# version 05
# speedup vs baseline: 1.5000x; 1.1190x over previous
import math
from contextlib import ExitStack, contextmanager

import numpy as np
import concourse.bass as bass
import concourse.mybir as mybir
from concourse.bass_utils import run_bass_kernel_spmd

F32 = mybir.dt.float32
BF16 = mybir.dt.bfloat16
I32 = mybir.dt.int32
AF = mybir.ActivationFunctionType
ALU = mybir.AluOpType
AX = mybir.AxisListType

D = 1024
NB = 4
TT = 4096
DC = D // 128
LN_EPS = 1e-5
DEC95 = float(np.float32(0.95))
DEC05 = float(np.float32(np.float32(1.0) - np.float32(0.95)))

_uid = [0]


def _nm(prefix):
    _uid[0] += 1
    return f"{prefix}{_uid[0]}"


def chunk(ap, c, w=128):
    return ap[c * w:(c + 1) * w, :]


class Prog:

    def __init__(self, nc):
        self.nc = nc
        self.ctx = ExitStack()
        self.dma_sem = self.ctx.enter_context(nc.semaphore("dmas"))
        self.dma_issued = 0
        self.pb = [self.ctx.enter_context(nc.psum_tensor(f"pb{i}", [128, 512], F32))
                   for i in range(7)]
        self.pbt = self.ctx.enter_context(nc.psum_tensor("pbt", [128, 1024], BF16))
        self.pbt2 = self.pbt.bitcast(F32)
        self._scopes = [self.ctx]
        self.ones_col = self.sb([128, 1], F32, name="onescol")
        nc.gpsimd.memset(self.ones_col[:, :], 1.0)
        self.ones_col_b = self.sb([128, 1], BF16, name="onescolb")
        nc.gpsimd.memset(self.ones_col_b[:, :], 1.0)
        self.ones_row = self.sb([1, 128], F32, name="onesrow")
        nc.gpsimd.memset(self.ones_row[:, :], 1.0)
        self.identf = self.sb([128, 128], F32, name="identf")
        iot2 = self.sb([128, 128], F32, name="iota2")
        pid = self.sb([128, 1], F32, name="pid")
        nc.gpsimd.iota(iot2[:, :], [[1, 128]], channel_multiplier=0,
                       allow_small_or_imprecise_dtypes=True)
        nc.gpsimd.iota(pid[:, :], [[0, 1]], channel_multiplier=1,
                       allow_small_or_imprecise_dtypes=True)
        self.bar()
        nc.vector.tensor_scalar(self.identf[:, :], iot2[:, :], pid[:, :], None,
                                ALU.is_equal)
        self.bar()
        self.identb = self.sb([128, 128], BF16, name="identb")
        nc.vector.tensor_copy(self.identb[:, :], self.identf[:, :])
        self.bar()

    def close(self):
        self.bar()
        self.ctx.close()

    def sem(self, name):
        if not hasattr(self, "sems"):
            self.sems, self.cnt = {}, {}
        if name not in self.sems:
            self.sems[name] = self.ctx.enter_context(self.nc.semaphore(name))
            self.cnt[name] = 0
        return self.sems[name]

    @contextmanager
    def scope(self):
        es = ExitStack()
        self._scopes.append(es)
        try:
            yield es
        finally:
            self.bar()
            self._scopes.pop()
            es.close()

    def sb(self, shape, dt=F32, name=None):
        return self._scopes[-1].enter_context(
            self.nc.sbuf_tensor(name or _nm("sb"), shape, dt))

    def dram(self, shape, dt=F32, name=None):
        return self.nc.dram_tensor(name or _nm("scr"), shape, dt, kind="Internal")

    def dma(self, out, in_, eng=None, **kw):
        eng = eng or self.nc.sync
        eng.dma_start(out=out, in_=in_, **kw).then_inc(self.dma_sem, 16)
        self.dma_issued += 16

    def cdma(self, out, in_, **kw):
        self.dma(out, in_, eng=self.nc.gpsimd, **kw)

    def bar(self):
        if self.dma_issued:
            self.nc.sync.wait_ge(self.dma_sem, self.dma_issued)
        self.nc.all_engine_barrier()

    def cst(self, v, name=None):
        t = self.sb([128, 1], F32, name=name or _nm("cst"))
        self.nc.gpsimd.memset(t[:, :], float(v))
        self.bar()
        return t

    def selmat(self, G, dv, name=None):
        nc = self.nc
        sel = self.sb([G, G * dv], F32, name=name or _nm("sel"))
        rowp = self.sb([G, 1], F32, name=_nm("selr"))
        nc.gpsimd.iota(sel.ap().rearrange("p (a b) -> p a b", b=dv),
                       [[1, G], [0, dv]], channel_multiplier=0,
                       allow_small_or_imprecise_dtypes=True)
        nc.gpsimd.iota(rowp[:, :], [[0, 1]], channel_multiplier=1,
                       allow_small_or_imprecise_dtypes=True)
        self.bar()
        nc.vector.tensor_scalar(sel[:, :], sel[:, :], rowp[:, :], None,
                                ALU.is_equal)
        self.bar()
        return sel

    def bcast_row(self, out_sb, row_ap, T, bank=6):
        nc = self.nc
        pbk = self.pb[bank]
        for t0 in range(0, T, 512):
            w = min(512, T - t0)
            nc.tensor.matmul(pbk[:, :w], self.ones_row[:1, :], row_ap[:1, t0:t0 + w],
                             start=True, stop=True)
            self.bar()
            nc.vector.tensor_copy(out_sb[:, t0:t0 + w], pbk[:, :w])
            self.bar()


class WMat:

    def __init__(self, p, WT_hbm_ap, K, N, name=None):
        self.K, self.N, self.KC = K, N, K // 128
        self.sb = p.sb([128, self.KC * N], BF16, name=name or _nm("wm"))
        p.cdma(self.sb.ap().rearrange("p (c n) -> p c n", c=self.KC),
               WT_hbm_ap.rearrange("(c p) n -> p c n", p=128))

    def lhsT(self, c, n0, nw=128):
        base = self.sb.ap().rearrange("p (c n) -> p c n", c=self.KC)
        return base[:, c, n0:n0 + nw]


class WSlice:

    def __init__(self, p, WT_hbm_ap, K, name=None):
        self.p, self.WT, self.K, self.KC = p, WT_hbm_ap, K, K // 128
        self.sb = p.sb([128, self.KC * 128], BF16, name=name or _nm("ws"))

    def load(self, n0, nw=128):
        self.p.cdma(self.sb.ap().rearrange("p (c n) -> p c n", c=self.KC)[:, :, :nw],
                    self.WT[:, n0:n0 + nw].rearrange("(c p) n -> p c n", p=128))

    def lhsT(self, c, nw=128):
        base = self.sb.ap().rearrange("p (c n) -> p c n", c=self.KC)
        return base[:, c, 0:nw]


def emit_proj(p, wm, rhs_chunks, T, consume, banks=(0,), tb=512, n0=0, n1=None):
    nc = p.nc
    if n1 is None:
        n1 = wm.N // 128
    bi = 0
    for e in range(n0, n1):
        for t0 in range(0, T, tb):
            w = min(tb, T - t0)
            pbk = p.pb[banks[bi % len(banks)]]
            bi += 1
            for c in range(wm.KC):
                nc.tensor.matmul(pbk[:, :w], wm.lhsT(c, e * 128),
                                 rhs_chunks[c][:, t0:t0 + w],
                                 start=(c == 0), stop=(c == wm.KC - 1))
            p.bar()
            consume(e, t0, w, pbk)
            p.bar()


def emit_ln_stats(p, load_tile, T, mean_b, rstd_b, eps=LN_EPS):
    nc = p.nc
    with p.scope():
        xc = p.sb([128, 512], F32, name=_nm("lnx"))
        sq = p.sb([128, 512], F32, name=_nm("lnsq"))
        mrow = p.sb([1, T], F32, name=_nm("lnm"))
        rrow = p.sb([1, T], F32, name=_nm("lnr"))
        pm, pv = p.pb[5], p.pb[6]
        for t0 in range(0, T, 512):
            for c in range(DC):
                load_tile(c, t0, 512, xc)
                p.bar()
                nc.scalar.activation(sq[:, :], xc[:, :], AF.Square)
                p.bar()
                nc.tensor.matmul(pm[:1, :], p.ones_col[:, :], xc[:, :],
                                 start=(c == 0), stop=(c == DC - 1))
                nc.tensor.matmul(pv[:1, :], p.ones_col[:, :], sq[:, :],
                                 start=(c == 0), stop=(c == DC - 1))
                p.bar()
            nc.vector.tensor_scalar(mrow[:1, t0:t0 + 512], pm[:1, :512],
                                    1.0 / D, None, ALU.mult)
            nc.vector.tensor_scalar(rrow[:1, t0:t0 + 512], pv[:1, :512],
                                    1.0 / D, None, ALU.mult)
            p.bar()
        msq = p.sb([1, T], F32, name=_nm("lnms"))
        nc.scalar.activation(msq[:1, :], mrow[:1, :], AF.Square)
        p.bar()
        nc.vector.tensor_tensor(rrow[:1, :], rrow[:1, :], msq[:1, :], ALU.subtract)
        p.bar()
        nc.vector.tensor_scalar(rrow[:1, :], rrow[:1, :], float(eps), None, ALU.add)
        p.bar()
        nc.scalar.activation(rrow[:1, :], rrow[:1, :], AF.Sqrt)
        p.bar()
        nc.vector.reciprocal(rrow[:1, :], rrow[:1, :])
        p.bar()
        p.bcast_row(mean_b, mrow, T)
        p.bcast_row(rstd_b, rrow, T)


def emit_prelude(p, xT_hbm, xnT_f32_hbm, xnT_bf16_hbm, T=TT, need_f32=True):
    nc = p.nc
    with p.scope():
        mean_b = p.sb([128, T], F32, name=_nm("pmb"))
        rstd_b = p.sb([128, T], F32, name=_nm("prb"))

        def load_tile(c, t0, w, tile):
            p.dma(tile[:, :w], xT_hbm[c * 128:(c + 1) * 128, t0:t0 + w])

        emit_ln_stats(p, load_tile, T, mean_b, rstd_b)
        xc = p.sb([128, T], F32, name=_nm("pxc"))
        xnb = p.sb([128, T], BF16, name=_nm("pxb"))
        for c in range(DC):
            p.dma(xc[:, :], chunk(xT_hbm, c))
            p.bar()
            nc.vector.tensor_tensor(xc[:, :], xc[:, :], mean_b[:, :], ALU.subtract)
            p.bar()
            nc.vector.tensor_tensor(xc[:, :], xc[:, :], rstd_b[:, :], ALU.mult)
            p.bar()
            nc.vector.tensor_copy(xnb[:, :], xc[:, :])
            if need_f32:
                p.dma(chunk(xnT_f32_hbm, c), xc[:, :])
            p.bar()
            p.dma(chunk(xnT_bf16_hbm, c), xnb[:, :])
            p.bar()


def emit_glue(p, xT, sT, gT, xoutT, T=TT):
    nc = p.nc
    with p.scope():
        a = p.sb([128, T], F32, name=_nm("ga"))
        b = p.sb([128, T], F32, name=_nm("gb"))
        cc = p.sb([128, T], F32, name=_nm("gc"))
        for c in range(DC):
            p.dma(a[:, :], chunk(xT, c))
            p.dma(b[:, :], chunk(sT, c))
            p.dma(cc[:, :], chunk(gT, c))
            p.bar()
            nc.vector.tensor_tensor(b[:, :], b[:, :], cc[:, :], ALU.mult)
            p.bar()
            nc.vector.tensor_tensor(a[:, :], a[:, :], b[:, :], ALU.add)
            p.bar()
            p.dma(chunk(xoutT, c), a[:, :])
            p.bar()


def load_chunks_bf16(p, hbm_ap, T, names="xc"):
    tiles = [p.sb([128, T], BF16, name=f"{_nm(names)}_{c}") for c in range(DC)]
    for c in range(DC):
        p.dma(tiles[c][:, :], chunk(hbm_ap, c))
    p.bar()
    return tiles


def emit_attn_head(p, kT, qT, kaux, qaux, vtile, nkv_tiles, Tq, out_cb,
                   dh=64, dv=64, causal_t0=None, dmasks=None, gate_sel=None,
                   q_tile=512, chunk_plan=None):
    nc = p.nc
    with p.scope():
        praw = p.sb([128, 4 * q_tile], F32, name=_nm("apr"))
        ptil = p.sb([128, 4 * q_tile], BF16, name=_nm("apt"))
        rrow = p.sb([1, q_tile], F32, name=_nm("arr"))
        o1 = p.sb([128, q_tile], F32, name=_nm("ao1"))
        oacc = p.sb([128, q_tile], F32, name=_nm("aoa"))
        dacc = p.sb([1, q_tile], F32, name=_nm("ada"))
        for t0 in range(0, Tq, q_tile):
            w = min(q_tile, Tq - t0)
            if chunk_plan is not None:
                plan = chunk_plan(t0, w)
            elif causal_t0 is None:
                plan = [(i, None) for i in range(nkv_tiles)]
            else:
                qhi = causal_t0(t0) + w
                nkv = min(nkv_tiles, (qhi + 127) // 128)
                plan = []
                for i in range(nkv):
                    kv0 = i * 128
                    m = None
                    qpos0 = causal_t0(t0)
                    r = (kv0 - qpos0) // 128
                    if 0 <= r <= 3 and kv0 + 128 > qpos0:
                        m = dmasks[:, r * q_tile:r * q_tile + w]
                    plan.append((i, m))
            first = True
            for g0 in range(0, len(plan), 4):
                grp = plan[g0:g0 + 4]
                gn = len(grp)
                for u in range(gn):
                    kv0 = grp[u][0] * 128
                    nc.tensor.matmul(p.pb[u][:, :w], kT[:, kv0:kv0 + 128],
                                     qT[:, t0:t0 + w], start=True, stop=False)
                    nc.tensor.matmul(p.pb[u][:, :w], kaux[:2, kv0:kv0 + 128],
                                     qaux[:2, t0:t0 + w], start=False, stop=True)
                p.bar()
                for u in range(gn):
                    nc.scalar.activation(praw[:, u * q_tile:u * q_tile + w],
                                         p.pb[u][:, :w], AF.Exp)
                p.bar()
                for u in range(gn):
                    m = grp[u][1]
                    sl = slice(u * q_tile, u * q_tile + w)
                    if m is not None:
                        nc.vector.tensor_tensor(ptil[:, sl], praw[:, sl], m,
                                                ALU.mult)
                    else:
                        nc.vector.tensor_copy(ptil[:, sl], praw[:, sl])
                p.bar()
                for u in range(gn):
                    i = grp[u][0]
                    nc.tensor.matmul(p.pb[4][:dv, :w], vtile(i),
                                     ptil[:, u * q_tile:u * q_tile + w],
                                     start=(u == 0), stop=(u == gn - 1))
                    nc.tensor.matmul(p.pb[5][:1, :w], p.ones_col_b[:, :],
                                     ptil[:, u * q_tile:u * q_tile + w],
                                     start=(u == 0), stop=(u == gn - 1))
                p.bar()
                if first:
                    nc.vector.tensor_copy(oacc[:dv, :w], p.pb[4][:dv, :w])
                    nc.scalar.copy(dacc[:1, :w], p.pb[5][:1, :w])
                    first = False
                else:
                    nc.vector.scalar_tensor_tensor(
                        oacc[:dv, :w], p.pb[4][:dv, :w], 1.0,
                        oacc[:dv, :w], ALU.mult, ALU.add)
                    p.bar()
                    nc.vector.scalar_tensor_tensor(
                        dacc[:1, :w], p.pb[5][:1, :w], 1.0,
                        dacc[:1, :w], ALU.mult, ALU.add)
                p.bar()
            nc.vector.reciprocal(rrow[:1, :w], dacc[:1, :w])
            p.bar()
            nc.tensor.matmul(p.pb[5][:dv, :w], p.ones_row[:1, :dv],
                             rrow[:1, :w], start=True, stop=True)
            if gate_sel is not None:
                gates_ap, selmat_ap, col = gate_sel
                nc.tensor.matmul(p.pb[6][:dv, :w],
                                 selmat_ap[:, col * dv:(col + 1) * dv],
                                 gates_ap[:, t0:t0 + w], start=True, stop=True)
            p.bar()
            nc.vector.tensor_tensor(o1[:dv, :w], oacc[:dv, :w], p.pb[5][:dv, :w],
                                    ALU.mult)
            p.bar()
            if gate_sel is not None:
                nc.vector.tensor_tensor(o1[:dv, :w], o1[:dv, :w],
                                        p.pb[6][:dv, :w], ALU.mult)
                p.bar()
            out_cb(t0, w, o1)
            p.bar()


def _head_proj2(p, wcols_sb, KC, rhs_chunks, T, out_sb, row0, nw, scale=None,
                wait_dma=True, bias_col=None):
    nc = p.nc
    ssim, sev = p.sem("s_proj"), p.sem("s_evac")
    bp, be = p.cnt["s_proj"], p.cnt["s_evac"]
    nt = T // 512
    for ti in range(nt):
        t0 = ti * 512
        bank = p.pb[ti % 2]
        if ti == 0 and wait_dma:
            nc.tensor.wait_ge(p.dma_sem, p.dma_issued)
        if ti >= 2:
            nc.tensor.wait_ge(sev, be + ti - 1)
        for c in range(KC):
            mm = nc.tensor.matmul(bank[:nw, :], wcols_sb[:, c * nw:(c + 1) * nw],
                                  rhs_chunks[c][:, t0:t0 + 512],
                                  start=(c == 0), stop=(c == KC - 1))
        mm.then_inc(ssim, 1)
        p.cnt["s_proj"] += 1
        nc.vector.wait_ge(ssim, bp + ti + 1)
        dst = out_sb[row0:row0 + nw, t0:t0 + 512]
        if bias_col is not None:
            if scale is None:
                cp = nc.vector.tensor_scalar(dst, bank[:nw, :], bias_col, None,
                                             ALU.add)
            else:
                cp = nc.vector.tensor_scalar(dst, bank[:nw, :], bias_col,
                                             float(scale), ALU.add, ALU.mult)
        elif scale is None:
            cp = nc.vector.tensor_copy(dst, bank[:nw, :])
        else:
            cp = nc.vector.tensor_scalar(dst, bank[:nw, :], float(scale), None,
                                         ALU.mult)
        cp.then_inc(sev, 1)
        p.cnt["s_evac"] += 1


def emit_vbuf2(p, vT, vbuf3, n, dv, f32=False):
    nc = p.nc
    stp, svc = p.sem("s_tp"), p.sem("s_vc")
    bt, bc = p.cnt["s_tp"], p.cnt["s_vc"]
    if f32:
        banks = [p.pb[5], p.pb[6]]
        ident = p.identf
    else:
        banks = [p.pb[5].bitcast(BF16), p.pb[6].bitcast(BF16)]
        ident = p.identb
    for i in range(n):
        bk = banks[i % 2]
        if i >= 2:
            nc.tensor.wait_ge(svc, bc + i - 1)
        tp = nc.tensor.transpose(bk[:, 0:dv], vT[:dv, i * 128:(i + 1) * 128],
                                 ident[:dv, :dv])
        tp.then_inc(stp, 1)
        p.cnt["s_tp"] += 1
        nc.vector.wait_ge(stp, bt + i + 1)
        nc.vector.tensor_copy(vbuf3[:, i, 0:dv], bk[:, 0:dv]).then_inc(svc, 1)
        p.cnt["s_vc"] += 1
    p.bar()


def emit_attn_head2(p, kaug, qaug, vtile, nkv_tiles, Tq, out_cb, dh, dv,
                    causal_t0=None, dmaskadd=None, gate_sel=None,
                    chunk_plan=None):
    nc = p.nc
    QT = 512
    ssim, sexp, sav = p.sem("s_sim"), p.sem("s_exp"), p.sem("s_av")
    with p.scope():
        ptil = p.sb([128, 2 * 3 * QT], BF16, name=_nm("a2p"))
        rrow = p.sb([1, QT], F32, name=_nm("a2r"))
        oacc = p.sb([128, QT], F32, name=_nm("a2o"))
        o1 = p.sb([128, QT], F32, name=_nm("a2q"))
        banksets = [(0, 1, 2), (3, 5, 6)]
        for t0 in range(0, Tq, QT):
            w = QT
            if chunk_plan is not None:
                plan = chunk_plan(t0, w)
            elif causal_t0 is None:
                plan = [(i, None) for i in range(nkv_tiles)]
            else:
                qpos0 = causal_t0(t0)
                nkv = min(nkv_tiles, (qpos0 + w + 127) // 128)
                plan = []
                for i in range(nkv):
                    kv0 = i * 128
                    r = (kv0 - qpos0) // 128
                    m = r if (0 <= r <= 3 and kv0 + 128 > qpos0) else None
                    plan.append((i, m))
            groups = [plan[i:i + 3] for i in range(0, len(plan), 3)]
            G = len(groups)
            nch = len(plan)
            b_sim, b_exp, b_av = p.cnt["s_sim"], p.cnt["s_exp"], p.cnt["s_av"]

            def emit_sim(g):
                grp, bs = groups[g], banksets[g % 2]
                if g >= 2:
                    nc.tensor.wait_ge(sexp, b_exp + g - 1)
                for u, (i, r) in enumerate(grp):
                    bk = p.pb[bs[u]]
                    mm = nc.tensor.matmul(bk[:, :w], kaug[0:66, i * 128:(i + 1) * 128],
                                          qaug[0:66, t0:t0 + w],
                                          start=True, stop=(r is None))
                    if r is not None:
                        mm = nc.tensor.matmul(bk[:, :w], p.identb[:, :],
                                              dmaskadd[:, r * QT:r * QT + w],
                                              start=False, stop=True)
                    if u == len(grp) - 1:
                        mm.then_inc(ssim, 1)
                p.cnt["s_sim"] += 1

            def emit_exp(g):
                grp, bs = groups[g], banksets[g % 2]
                pp = (g % 2) * 3 * QT
                nc.scalar.wait_ge(ssim, b_sim + g + 1)
                if g >= 2:
                    nc.scalar.wait_ge(sav, b_av + g - 1)
                for u in range(len(grp)):
                    a = nc.scalar.activation(ptil[:, pp + u * QT:pp + u * QT + w],
                                             p.pb[bs[u]][:, :w], AF.Exp)
                a.then_inc(sexp, 1)
                p.cnt["s_exp"] += 1

            ci = 0

            def emit_av(g):
                nonlocal ci
                grp = groups[g]
                pp = (g % 2) * 3 * QT
                nc.tensor.wait_ge(sexp, b_exp + g + 1)
                for u, (i, r) in enumerate(grp):
                    mm = nc.tensor.matmul(p.pb[4][:dv + 2, :w], vtile(i),
                                          ptil[:, pp + u * QT:pp + u * QT + w],
                                          start=(ci == 0), stop=(ci == nch - 1))
                    ci += 1
                mm.then_inc(sav, 1)
                p.cnt["s_av"] += 1

            emit_sim(0)
            if G > 1:
                emit_sim(1)
            for g in range(G):
                emit_exp(g)
                emit_av(g)
                if g + 2 < G:
                    emit_sim(g + 2)
            p.bar()
            nc.vector.reciprocal(rrow[:1, :w], p.pb[4][dv:dv + 1, :w])
            nc.scalar.copy(oacc[:dv, :w], p.pb[4][:dv, :w])
            p.bar()
            nc.tensor.matmul(p.pb[5][:dv, :w], p.ones_row[:1, :dv],
                             rrow[:1, :w], start=True, stop=True)
            if gate_sel is not None:
                gates_ap, selmat_ap, col = gate_sel
                nc.tensor.matmul(p.pb[6][:dv, :w],
                                 selmat_ap[:, col * dv:(col + 1) * dv],
                                 gates_ap[:, t0:t0 + w], start=True, stop=True)
            p.bar()
            nc.vector.tensor_tensor(o1[:dv, :w], oacc[:dv, :w], p.pb[5][:dv, :w],
                                    ALU.mult)
            p.bar()
            if gate_sel is not None:
                nc.vector.tensor_tensor(o1[:dv, :w], o1[:dv, :w],
                                        p.pb[6][:dv, :w], ALU.mult)
                p.bar()
            out_cb(t0, w, o1)
            p.bar()


def emit_bbound(p, qpT, kpT, dh, Tq, Tkv, negb_out, margin=2.0):
    nc = p.nc
    ssq, sev = p.sem("s_proj"), p.sem("s_evac")
    with p.scope():
        sq = p.sb([128, max(Tq, Tkv)], F32, name=_nm("bbs"))
        krow = p.sb([1, Tkv], F32, name=_nm("bbr"))
        kmax = p.sb([1, 1], F32, name=_nm("bbk"))

        def norms(src, T_, out_row):
            bq, be = p.cnt["s_proj"], p.cnt["s_evac"]
            nc.scalar.activation(sq[:dh, :T_], src[:dh, :], AF.Square)\
                .then_inc(ssq, 1)
            p.cnt["s_proj"] += 1
            nc.tensor.wait_ge(ssq, bq + 1)
            nt = T_ // 512
            for ti in range(nt):
                if ti >= 2:
                    nc.tensor.wait_ge(sev, be + ti - 1)
                nc.tensor.matmul(p.pb[ti % 2][:1, :], p.ones_col[:dh, :],
                                 sq[:dh, ti * 512:(ti + 1) * 512],
                                 start=True, stop=True).then_inc(ssq, 1)
                p.cnt["s_proj"] += 1
                nc.vector.wait_ge(ssq, bq + ti + 2)
                nc.vector.tensor_copy(out_row[:1, ti * 512:(ti + 1) * 512],
                                      p.pb[ti % 2][:1, :512]).then_inc(sev, 1)
                p.cnt["s_evac"] += 1
            nc.vector.wait_ge(sev, p.cnt["s_evac"])

        norms(kpT, Tkv, krow)
        nc.vector.tensor_reduce(kmax[:1, :1], krow[:1, :], AX.X, ALU.max)
        p.bar()
        nc.scalar.activation(kmax[:1, :1], kmax[:1, :1], AF.Sqrt)
        norms(qpT, Tq, negb_out)
        p.bar()
        nc.scalar.activation(negb_out[:1, :Tq], negb_out[:1, :Tq], AF.Sqrt)
        p.bar()
        nc.vector.tensor_scalar(negb_out[:1, :Tq], negb_out[:1, :Tq],
                                kmax[:1, :1], None, ALU.mult)
        p.bar()
        nc.vector.tensor_scalar(negb_out[:1, :Tq], negb_out[:1, :Tq], -1.0,
                                float(-margin), ALU.mult, ALU.add)
        p.bar()


def emit_l2_attn(p, xnT_f32, u_a, u_b, WkT, WvT, WrT, WoT, lnw, lnb, T=TT):
    nc = p.nc
    u_cur, u_next = xnT_f32, u_a
    yT_hbm = p.dram([D, T], BF16).ap()
    for sub in range(2):
        with p.scope():
            normedT = [p.sb([128, T], BF16, name=f"l2n{sub}_{c}") for c in range(DC)]
            with p.scope():
                mean_b = p.sb([128, T], F32, name=_nm("l2mb"))
                rstd_b = p.sb([128, T], F32, name=_nm("l2rb"))

                def load_u(c, t0, w, tile, u=u_cur):
                    p.dma(tile[:, :w], u[c * 128:(c + 1) * 128, t0:t0 + w])

                emit_ln_stats(p, load_u, T, mean_b, rstd_b)
                uc = p.sb([128, T], F32, name=_nm("l2uc"))
                lnw_sb = p.sb([128, 1], F32, name=_nm("l2lw"))
                lnb_sb = p.sb([128, 1], F32, name=_nm("l2lb"))
                for c in range(DC):
                    p.dma(uc[:, :], chunk(u_cur, c))
                    p.dma(lnw_sb[:, :], lnw[sub, c * 128:(c + 1) * 128]
                          .rearrange("(a b) -> a b", b=1))
                    p.dma(lnb_sb[:, :], lnb[sub, c * 128:(c + 1) * 128]
                          .rearrange("(a b) -> a b", b=1))
                    p.bar()
                    nc.vector.tensor_tensor(uc[:, :], uc[:, :], mean_b[:, :],
                                            ALU.subtract)
                    p.bar()
                    nc.vector.tensor_tensor(uc[:, :], uc[:, :], rstd_b[:, :],
                                            ALU.mult)
                    p.bar()
                    nc.vector.tensor_scalar(uc[:, :], uc[:, :], lnw_sb[:, :],
                                            lnb_sb[:, :], ALU.mult, ALU.add)
                    p.bar()
                    nc.vector.tensor_copy(normedT[c][:, :], uc[:, :])
                    p.bar()
            with p.scope():
                wk = WMat(p, WkT[sub, :, :], D, D, name=_nm("l2wk"))
                wv = WMat(p, WvT[sub, :, :], D, D, name=_nm("l2wv"))
                wr = WMat(p, WrT[sub, :, :], D, D, name=_nm("l2wr"))
                kv05 = p.sb([128, T], F32, name=_nm("l2kv"))
                rsg = p.sb([128, T], F32, name=_nm("l2rs"))
                sts = p.sb([128, T], F32, name=_nm("l2st"))
                dec = p.sb([128, T], F32, name=_nm("l2dc"))
                ksb = p.sb([128, 512], F32, name=_nm("l2ks"))
                ybf = p.sb([128, T], BF16, name=_nm("l2yb"))
                nc.gpsimd.memset(dec[:, :], DEC95)
                p.bar()
                for e in range(DC):
                    for t0 in range(0, T, 512):
                        kp, vp, rp = p.pb[0], p.pb[1], p.pb[2]
                        for c in range(DC):
                            st, sp = (c == 0), (c == DC - 1)
                            nc.tensor.matmul(kp[:, :], wk.lhsT(c, e * 128),
                                             normedT[c][:, t0:t0 + 512],
                                             start=st, stop=sp)
                            nc.tensor.matmul(vp[:, :], wv.lhsT(c, e * 128),
                                             normedT[c][:, t0:t0 + 512],
                                             start=st, stop=sp)
                            nc.tensor.matmul(rp[:, :], wr.lhsT(c, e * 128),
                                             normedT[c][:, t0:t0 + 512],
                                             start=st, stop=sp)
                        p.bar()
                        nc.scalar.copy(ksb[:, :], kp[:, :])
                        nc.vector.tensor_copy(rsg[:, t0:t0 + 512], rp[:, :])
                        p.bar()
                        nc.scalar.activation(rsg[:, t0:t0 + 512], rsg[:, t0:t0 + 512],
                                             AF.Sigmoid)
                        nc.vector.scalar_tensor_tensor(kv05[:, t0:t0 + 512], vp[:, :],
                                                       DEC05, ksb[:, :],
                                                       ALU.mult, ALU.mult)
                        p.bar()
                    nc.vector.tensor_tensor_scan(sts[:, :], dec[:, :], kv05[:, :],
                                                 0.0, ALU.mult, ALU.add)
                    p.bar()
                    nc.vector.tensor_tensor(sts[:, :], sts[:, :], rsg[:, :], ALU.mult)
                    p.bar()
                    nc.vector.tensor_copy(ybf[:, :], sts[:, :])
                    p.bar()
                    p.dma(chunk(yT_hbm, e), ybf[:, :])
                    p.bar()
            with p.scope():
                wo = WMat(p, WoT[sub, :, :], D, D, name=_nm("l2wo"))
                ybuf = [p.sb([128, 512], BF16, name=_nm("l2yt")) for _ in range(DC)]
                usb = p.sb([128, 512], F32, name=_nm("l2us"))
                for t0 in range(0, T, 512):
                    for c in range(DC):
                        p.dma(ybuf[c][:, :], yT_hbm[c * 128:(c + 1) * 128, t0:t0 + 512])
                    p.bar()
                    for e in range(DC):
                        pbk = p.pb[e % 2]
                        for c in range(DC):
                            nc.tensor.matmul(pbk[:, :], wo.lhsT(c, e * 128),
                                             ybuf[c][:, :],
                                             start=(c == 0), stop=(c == DC - 1))
                        p.dma(usb[:, :], u_cur[e * 128:(e + 1) * 128, t0:t0 + 512])
                        p.bar()
                        nc.vector.tensor_tensor(usb[:, :], usb[:, :], pbk[:, :],
                                                ALU.add)
                        p.bar()
                        p.dma(u_next[e * 128:(e + 1) * 128, t0:t0 + 512], usb[:, :])
                        p.bar()
        u_cur, u_next = u_next, u_b
    return u_cur


def emit_l2_ffn(p, xnT_bf16, wgT, wuT, wdT, gT_hbm, T=TT):
    nc = p.nc
    H3 = 3072
    inv127 = 1.0 / 127.0
    with p.scope():
        xnb = load_chunks_bf16(p, xnT_bf16, T, "f2x")
        wg = WSlice(p, wgT, D, name="f2wg")
        wu = WSlice(p, wuT, D, name="f2wu")
        wd = WSlice(p, wdT, H3, name="f2wd")
        TB_ = 1024
        hT = [p.sb([128, TB_], BF16, name=f"f2ht{i}") for i in range(H3 // 128)]
        h1 = p.sb([128, 512], F32, name="f2h1")
        gsb = p.sb([128, 512], F32, name="f2g")
        c127 = p.cst(inv127, name="c127")
        for tb0 in range(0, T, TB_):
            for hc in range(H3 // 128):
                wg.load(hc * 128)
                wu.load(hc * 128)
                p.bar()
                for t0 in range(0, TB_, 512):
                    gp, up = p.pb[0], p.pb[1]
                    for c in range(DC):
                        st, sp = (c == 0), (c == DC - 1)
                        nc.tensor.matmul(gp[:, :], wg.lhsT(c),
                                         xnb[c][:, tb0 + t0:tb0 + t0 + 512],
                                         start=st, stop=sp)
                        nc.tensor.matmul(up[:, :], wu.lhsT(c),
                                         xnb[c][:, tb0 + t0:tb0 + t0 + 512],
                                         start=st, stop=sp)
                    p.bar()
                    nc.scalar.activation(h1[:, :], gp[:, :], AF.Silu,
                                         scale=c127[:, :])
                    p.bar()
                    nc.vector.scalar_tensor_tensor(hT[hc][:, t0:t0 + 512], up[:, :],
                                                   inv127, h1[:, :], ALU.mult, ALU.mult)
                    p.bar()
            for e in range(DC):
                wd.load(e * 128)
                p.bar()
                for t0 in range(0, TB_, 512):
                    dp = p.pb[2]
                    for hc in range(H3 // 128):
                        nc.tensor.matmul(dp[:, :], wd.lhsT(hc), hT[hc][:, t0:t0 + 512],
                                         start=(hc == 0), stop=(hc == H3 // 128 - 1))
                    p.bar()
                    nc.scalar.activation(gsb[:, :], dp[:, :], AF.Silu,
                                         scale=c127[:, :])
                    p.bar()
                    p.dma(gT_hbm[e * 128:(e + 1) * 128, tb0 + t0:tb0 + t0 + 512],
                          gsb[:, :])
                    p.bar()


def build_layer2(T=TT, dbg=False):
    nc = bass.Bass("TRN2", target_bir_lowering=False)
    xT = nc.dram_tensor("xT", [D, T], F32, kind="ExternalInput").ap()
    WkT = nc.dram_tensor("WkT", [2, D, D], F32, kind="ExternalInput").ap()
    WvT = nc.dram_tensor("WvT", [2, D, D], F32, kind="ExternalInput").ap()
    WrT = nc.dram_tensor("WrT", [2, D, D], F32, kind="ExternalInput").ap()
    WoT = nc.dram_tensor("WoT", [2, D, D], F32, kind="ExternalInput").ap()
    lnw = nc.dram_tensor("lnw", [2, D], F32, kind="ExternalInput").ap()
    lnb = nc.dram_tensor("lnb", [2, D], F32, kind="ExternalInput").ap()
    wgT = nc.dram_tensor("wgT", [D, 3072], I32, kind="ExternalInput").ap()
    wuT = nc.dram_tensor("wuT", [D, 3072], I32, kind="ExternalInput").ap()
    wdT = nc.dram_tensor("wdT", [3072, D], I32, kind="ExternalInput").ap()
    xoutT = nc.dram_tensor("xoutT", [D, T], F32, kind="ExternalOutput").ap()

    p = Prog(nc)
    xnT_f32 = p.dram([D, T], F32).ap()
    xnT_bf16 = p.dram([D, T], BF16).ap()
    u_a = p.dram([D, T], F32).ap()
    u_b = p.dram([D, T], F32).ap()
    gT = p.dram([D, T], F32).ap()
    emit_prelude(p, xT, xnT_f32, xnT_bf16, T)
    sT = emit_l2_attn(p, xnT_f32, u_a, u_b, WkT, WvT, WrT, WoT, lnw, lnb, T)
    emit_l2_ffn(p, xnT_bf16, wgT, wuT, wdT, gT, T)
    emit_glue(p, xT, sT, gT, xoutT, T)
    if dbg:
        for nm, src in (("s_dbg", sT), ("g_dbg", gT), ("xn_dbg", xnT_f32)):
            o = nc.dram_tensor(nm, [D, T], F32, kind="ExternalOutput").ap()
            p.dma(o, src)
        p.bar()
    p.close()
    return nc


def emit_l0_route(p, xnT_f32, rts, qmask_row, m_row_out, T=TT, NSEL=1024):
    nc = p.nc
    scr_hbm = p.dram([2, T], F32).ap()
    with p.scope():
        rtsb = p.sb([128, DC * 2], F32, name="l0rt")
        p.dma(rtsb.ap().rearrange("p (c n) -> p c n", c=DC),
              rts.rearrange("(c p) n -> p c n", p=128))
        xt = p.sb([128, 512], F32, name="l0xt")
        scores = p.sb([2, T], F32, name="l0sc")
        p.bar()
        for t0 in range(0, T, 512):
            pbk = p.pb[0]
            for c in range(DC):
                p.dma(xt[:, :], xnT_f32[c * 128:(c + 1) * 128, t0:t0 + 512])
                p.bar()
                nc.tensor.matmul(pbk[:2, :], rtsb.ap().rearrange(
                    "p (c n) -> p c n", c=DC)[:, c, :], xt[:, :],
                    start=(c == 0), stop=(c == DC - 1))
                p.bar()
            nc.vector.tensor_copy(scores[:2, t0:t0 + 512], pbk[:2, :512])
            p.bar()
        p.dma(scr_hbm, scores[:2, :])
        p.bar()
        s2 = p.sb([128, 64], F32, name="l0s2")
        p.dma(s2[:, 0:32], scr_hbm[0:1, :].rearrange("a (q c) -> (a q) c", q=128))
        p.dma(s2[:, 32:64], scr_hbm[1:2, :].rearrange("a (q c) -> (a q) c", q=128))
        lo = p.sb([1, 2], F32, name="l0lo")
        hi = p.sb([1, 2], F32, name="l0hi")
        mid = p.sb([1, 2], F32, name="l0mi")
        thb = p.sb([128, 2], F32, name="l0th")
        ind = p.sb([128, 64], F32, name="l0in")
        cnt2 = p.sb([128, 2], F32, name="l0c2")
        cntc = p.sb([2, 1], F32, name="l0cc")
        crow = p.sb([1, 2], F32, name="l0cr")
        pred = p.sb([1, 2], F32, name="l0pd")
        nc.gpsimd.memset(lo[:1, :], -1e5)
        nc.gpsimd.memset(hi[:1, :], 1e5)
        p.bar()

        def count_gt(theta_row, out_crow):
            nc.tensor.matmul(p.pb[6][:, :2], p.ones_row[:1, :], theta_row[:1, :],
                             start=True, stop=True)
            p.bar()
            nc.vector.tensor_copy(thb[:, :], p.pb[6][:, :2])
            p.bar()
            nc.vector.tensor_scalar(ind[:, 0:32], s2[:, 0:32], thb[:, 0:1],
                                    None, ALU.is_gt)
            p.bar()
            nc.vector.tensor_scalar(ind[:, 32:64], s2[:, 32:64], thb[:, 1:2],
                                    None, ALU.is_gt)
            p.bar()
            nc.vector.tensor_reduce(cnt2[:, 0:1], ind[:, 0:32], AX.X, ALU.add)
            p.bar()
            nc.vector.tensor_reduce(cnt2[:, 1:2], ind[:, 32:64], AX.X, ALU.add)
            p.bar()
            nc.tensor.matmul(p.pb[5][:2, :1], cnt2[:, :], p.ones_col[:, :],
                             start=True, stop=True)
            p.bar()
            nc.vector.tensor_copy(cntc[:2, :1], p.pb[5][:2, :1])
            p.bar()
            nc.tensor.transpose(p.pbt2[:1, :2], cntc[:2, :1], p.identf[:2, :2])
            p.bar()
            nc.vector.tensor_copy(out_crow[:1, :2], p.pbt2[:1, :2])
            p.bar()

        for _ in range(46):
            nc.vector.tensor_tensor(mid[:1, :], lo[:1, :], hi[:1, :], ALU.add)
            p.bar()
            nc.vector.tensor_scalar(mid[:1, :], mid[:1, :], 0.5, None, ALU.mult)
            p.bar()
            count_gt(mid, crow)
            nc.vector.tensor_scalar(pred[:1, :], crow[:1, :], float(NSEL) - 0.5,
                                    None, ALU.is_gt)
            p.bar()
            nc.vector.tensor_tensor(crow[:1, :], mid[:1, :], lo[:1, :],
                                    ALU.subtract)
            p.bar()
            nc.vector.tensor_tensor(crow[:1, :], crow[:1, :], pred[:1, :],
                                    ALU.mult)
            p.bar()
            nc.vector.tensor_tensor(lo[:1, :], lo[:1, :], crow[:1, :], ALU.add)
            p.bar()
            nc.vector.tensor_tensor(crow[:1, :], hi[:1, :], mid[:1, :],
                                    ALU.subtract)
            p.bar()
            nc.vector.tensor_tensor(crow[:1, :], crow[:1, :], pred[:1, :],
                                    ALU.mult)
            p.bar()
            nc.vector.tensor_tensor(hi[:1, :], mid[:1, :], crow[:1, :], ALU.add)
            p.bar()
        neg = p.sb([128, 32], F32, name="l0ng")
        mh = p.sb([128, 2], F32, name="l0mh")
        tau_row = p.sb([1, 2], F32, name="l0ta")
        nc.gpsimd.memset(neg[:, :], -3e38)
        p.bar()
        nc.tensor.matmul(p.pb[6][:, :2], p.ones_row[:1, :], hi[:1, :],
                         start=True, stop=True)
        p.bar()
        nc.vector.tensor_copy(thb[:, :], p.pb[6][:, :2])
        p.bar()
        for r in range(2):
            half = s2[:, 32 * r:32 * (r + 1)]
            nc.vector.tensor_scalar(ind[:, 0:32], half, thb[:, r:r + 1], None,
                                    ALU.is_le)
            p.bar()
            nc.vector.tensor_tensor(ind[:, 32:64], half, ind[:, 0:32], ALU.mult)
            p.bar()
            nc.vector.tensor_scalar(ind[:, 0:32], ind[:, 0:32], 1.0, 3e38,
                                    ALU.subtract, ALU.mult)
            p.bar()
            nc.vector.tensor_tensor(ind[:, 32:64], ind[:, 32:64], ind[:, 0:32],
                                    ALU.add)
            p.bar()
            nc.vector.tensor_reduce(mh[:, r:r + 1], ind[:, 32:64], AX.X, ALU.max)
            p.bar()
        mh2 = p.sb([2, 128], F32, name="l0m2")
        nc.tensor.transpose(p.pbt2[:2, :128], mh[:, :2], p.identf[:, :])
        p.bar()
        nc.vector.tensor_copy(mh2[:2, :], p.pbt2[:2, :128])
        p.bar()
        nc.vector.tensor_reduce(cntc[:2, 0:1], mh2[:2, :], AX.X, ALU.max)
        p.bar()
        nc.tensor.transpose(p.pbt2[:1, :2], cntc[:2, :1], p.identf[:2, :2])
        p.bar()
        nc.vector.tensor_copy(tau_row[:1, :2], p.pbt2[:1, :2])
        p.bar()
        count_gt(tau_row, crow)
        k2row = p.sb([1, 2], F32, name="l0k2")
        nc.vector.tensor_scalar(k2row[:1, :], crow[:1, :], -1.0, float(NSEL),
                                ALU.mult, ALU.add)
        p.bar()
        tauc = p.sb([2, 1], F32, name="l0tc")
        k2c = p.sb([2, 1], F32, name="l0kc")
        nc.tensor.transpose(p.pbt2[:2, :1], tau_row[:1, :2], p.identf[:1, :1])
        p.bar()
        nc.vector.tensor_copy(tauc[:2, :1], p.pbt2[:2, :1])
        p.bar()
        nc.tensor.transpose(p.pbt2[:2, :1], k2row[:1, :2], p.identf[:1, :1])
        p.bar()
        nc.vector.tensor_copy(k2c[:2, :1], p.pbt2[:2, :1])
        p.bar()
        eq = p.sb([2, T], F32, name="l0eq")
        ones2 = p.sb([2, T], F32, name="l0o2")
        ps = p.sb([2, T], F32, name="l0ps")
        tot = p.sb([2, 1], F32, name="l0tt")
        msk = p.sb([2, T], F32, name="l0mk")
        nc.gpsimd.memset(ones2[:2, :], 1.0)
        p.bar()
        nc.vector.tensor_scalar(eq[:2, :], scores[:2, :], tauc[:2, :1], None,
                                ALU.is_equal)
        p.bar()
        nc.vector.tensor_tensor_scan(ps[:2, :], ones2[:2, :], eq[:2, :], 0.0,
                                     ALU.mult, ALU.add)
        p.bar()
        nc.vector.tensor_reduce(tot[:2, :1], eq[:2, :], AX.X, ALU.add)
        p.bar()
        nc.vector.tensor_scalar(ps[:2, :], ps[:2, :], tot[:2, :1], -1.0,
                                ALU.subtract, ALU.mult)
        p.bar()
        nc.vector.tensor_scalar(ps[:2, :], ps[:2, :], k2c[:2, :1], None,
                                ALU.is_lt)
        p.bar()
        nc.vector.tensor_tensor(ps[:2, :], ps[:2, :], eq[:2, :], ALU.mult)
        p.bar()
        nc.vector.tensor_scalar(msk[:2, :], scores[:2, :], tauc[:2, :1], None,
                                ALU.is_gt)
        p.bar()
        nc.vector.tensor_tensor(msk[:2, :], msk[:2, :], ps[:2, :], ALU.add)
        p.bar()
        nc.vector.tensor_copy(qmask_row[:1, :], msk[0:1, :])
        p.bar()
        nc.vector.tensor_scalar(msk[:2, :], msk[:2, :], 1.0, 30000.0,
                                ALU.subtract, ALU.mult)
        p.bar()
        p.cdma(m_row_out, msk[1:2, :])
        p.bar()


def emit_l0_attn(p, xnT_f32, xnT_bf16, gamma_h, wq_h, wkv_h, wo_h, ln_g, ln_b,
                 wqkv_l, wo_l, rts, bandmask_hbm, sT_hbm, T=TT, dbg_outs=None):
    nc = p.nc
    H, DH = 8, 64
    SC = 0.125
    outl = p.dram([512, T], BF16).ap()
    outh = p.dram([512, T], BF16).ap()
    with p.scope():
        xlT = [p.sb([128, T], BF16, name=f"l0xl{c}") for c in range(DC)]
        with p.scope():
            mean_b = p.sb([128, T], F32, name="l0mb")
            rstd_b = p.sb([128, T], F32, name="l0rb")

            def load_tile(c, t0, w, tile):
                p.dma(tile[:, :w], xnT_f32[c * 128:(c + 1) * 128, t0:t0 + w])

            emit_ln_stats(p, load_tile, T, mean_b, rstd_b)
            xc = p.sb([128, T], F32, name="l0xc")
            gcol = p.sb([128, 1], F32, name="l0gc")
            bcol = p.sb([128, 1], F32, name="l0bc")
            for c in range(DC):
                p.dma(xc[:, :], xnT_f32[c * 128:(c + 1) * 128, :])
                p.dma(gcol[:, :], ln_g[c * 128:(c + 1) * 128]
                      .rearrange("(a b) -> a b", b=1))
                p.dma(bcol[:, :], ln_b[c * 128:(c + 1) * 128]
                      .rearrange("(a b) -> a b", b=1))
                p.bar()
                nc.vector.tensor_tensor(xc[:, :], xc[:, :], mean_b[:, :],
                                        ALU.subtract)
                p.bar()
                nc.vector.tensor_tensor(xc[:, :], xc[:, :], rstd_b[:, :], ALU.mult)
                p.bar()
                nc.vector.tensor_scalar(xc[:, :], xc[:, :], gcol[:, :],
                                        bcol[:, :], ALU.mult, ALU.add)
                p.bar()
                nc.vector.tensor_copy(xlT[c][:, :], xc[:, :])
                p.bar()
        with p.scope():
            wcols = p.sb([128, DC * DH], BF16, name="l0wc")
            qT = p.sb([DH, T], BF16, name="l0q")
            kT = p.sb([DH, T], BF16, name="l0k")
            vT = p.sb([DH, T], F32, name="l0v")
            vbuf = p.sb([128, (T // 128) * DH], BF16, name="l0vb")
            vbuf3 = vbuf.ap().rearrange("p (i v) -> p i v", v=DH)
            qaux = p.sb([2, T], BF16, name="l0qx")
            kaux = p.sb([2, T], BF16, name="l0kx")
            negb = p.sb([1, T], F32, name="l0nb")
            ob = p.sb([DH, 512], BF16, name="l0ob")
            bandm = p.sb([128, 6 * 512], BF16, name="l0bm")
            p.cdma(bandm[:, :], bandmask_hbm)
            nc.gpsimd.memset(kaux[:, :], 1.0)
            nc.gpsimd.memset(qaux[:, :], 1.0)
            p.bar()

            def bandplan(t0, w):
                c0 = t0 // 128
                plan = []
                for r in range(6):
                    ci = c0 - 1 + r
                    if 0 <= ci < T // 128:
                        plan.append((ci, bandm[:, r * 512:r * 512 + w]))
                return plan

            for hh in range(H):
                load_wcols(p, wcols, wqkv_l, D, hh * DH, DH)
                p.bar()
                _head_proj(p, wcols, DC, xlT, T, qT, 0, DH, scale=SC)
                load_wcols(p, wcols, wqkv_l, D, 512 + hh * DH, DH)
                p.bar()
                _head_proj(p, wcols, DC, xlT, T, kT, 0, DH)
                load_wcols(p, wcols, wqkv_l, D, 1024 + hh * DH, DH)
                p.bar()
                _head_proj(p, wcols, DC, xlT, T, vT, 0, DH)
                for i in range(T // 128):
                    nc.tensor.transpose(p.pbt2[:, :DH],
                                        vT[:DH, i * 128:(i + 1) * 128],
                                        p.identf[:DH, :DH])
                    p.bar()
                    nc.vector.tensor_copy(vbuf3[:, i, :], p.pbt2[:, :DH])
                    p.bar()
                emit_bbound(p, qT, kT, DH, T, T, negb)
                nc.vector.tensor_copy(qaux[0:1, :], negb[:1, :])
                p.bar()

                def out_cb(t0, w, o1, hh=hh):
                    nc.vector.tensor_copy(ob[:DH, :w], o1[:DH, :w])
                    p.bar()
                    p.dma(outl[hh * DH:(hh + 1) * DH, t0:t0 + w], ob[:DH, :w])

                emit_attn_head(p, kT, qT, kaux, qaux, lambda i: vbuf3[:, i, :],
                               T // 128, T, out_cb, dh=DH, dv=DH,
                               chunk_plan=bandplan)
    with p.scope():
        qmask_row = p.sb([1, T], F32, name="l0qm")
        kaugh = p.sb([66, T], BF16, name="l0kxh")
        qaugh = p.sb([66, T], BF16, name="l0qxh")
        nc.gpsimd.memset(kaugh[64:66, :], 1.0)
        nc.gpsimd.memset(qaugh[64:66, :], 1.0)
        p.bar()
        emit_l0_route(p, xnT_f32, rts, qmask_row, kaugh[65:66, :], T)
        if dbg_outs is not None:
            p.dma(dbg_outs["qm"], qmask_row[:1, :])
            p.dma(dbg_outs["kx"], kaugh[64:66, :])
            p.bar()
        hxT = [p.sb([128, T], BF16, name=f"l0hx{c}") for c in range(DC)]
        with p.scope():
            rstd_b = p.sb([128, T], F32, name="l0hr")
            with p.scope():
                xc2 = p.sb([128, 512], F32, name="l0x2")
                sq2 = p.sb([128, 512], F32, name="l0s2b")
                rrow = p.sb([1, T], F32, name="l0rr")
                pv = p.pb[5]
                for t0 in range(0, T, 512):
                    for c in range(DC):
                        p.dma(xc2[:, :], xnT_f32[c * 128:(c + 1) * 128,
                                                 t0:t0 + 512])
                        p.bar()
                        nc.scalar.activation(sq2[:, :], xc2[:, :], AF.Square)
                        p.bar()
                        nc.tensor.matmul(pv[:1, :], p.ones_col[:, :], sq2[:, :],
                                         start=(c == 0), stop=(c == DC - 1))
                        p.bar()
                    nc.vector.tensor_copy(rrow[:1, t0:t0 + 512], pv[:1, :512])
                    p.bar()
                nc.scalar.activation(rrow[:1, :], rrow[:1, :], AF.Sqrt)
                p.bar()
                nc.vector.tensor_scalar(rrow[:1, :], rrow[:1, :], 1e-12, None,
                                        ALU.max)
                p.bar()
                nc.vector.reciprocal(rrow[:1, :], rrow[:1, :])
                p.bar()
                nc.vector.tensor_scalar(rrow[:1, :], rrow[:1, :], 32.0, None,
                                        ALU.mult)
                p.bar()
                p.bcast_row(rstd_b, rrow, T)
            xc = p.sb([128, T], F32, name="l0xch")
            gcol = p.sb([128, 1], F32, name="l0gch")
            for c in range(DC):
                p.dma(xc[:, :], xnT_f32[c * 128:(c + 1) * 128, :])
                p.dma(gcol[:, :], gamma_h[c * 128:(c + 1) * 128]
                      .rearrange("(a b) -> a b", b=1))
                p.bar()
                nc.vector.tensor_tensor(xc[:, :], xc[:, :], rstd_b[:, :], ALU.mult)
                p.bar()
                nc.vector.tensor_scalar(hxT[c][:, :], xc[:, :], gcol[:, :], None,
                                        ALU.mult)
                p.bar()
        with p.scope():
            wcols = p.sb([128, DC * 64], BF16, name="l0wch")
            vT = p.sb([64, T], F32, name="l0vh")
            vbuf = p.sb([128, (T // 128) * 66], BF16, name="l0vbh")
            vbuf3 = vbuf.ap().rearrange("p (i v) -> p i v", v=66)
            negb = p.sb([1, T], F32, name="l0nbh")
            ob = p.sb([64, 512], BF16, name="l0obh")
            nc.gpsimd.memset(vbuf[:, :], 1.0)
            p.bar()
            for hh in range(H):
                load_wcols(p, wcols, wq_h, D, hh * 64, 64)
                _head_proj2(p, wcols, DC, hxT, T, qaugh, 0, 64, scale=SC)
                p.bar()
                load_wcols(p, wcols, wkv_h, D, hh * 64, 64)
                _head_proj2(p, wcols, DC, hxT, T, kaugh, 0, 64)
                p.bar()
                load_wcols(p, wcols, wkv_h, D, 512 + hh * 64, 64)
                _head_proj2(p, wcols, DC, hxT, T, vT, 0, 64)
                p.bar()
                emit_vbuf2(p, vT, vbuf3, T // 128, 64, f32=True)
                emit_bbound(p, qaugh[0:64, :], kaugh[0:64, :], 64, T, T, negb)
                nc.vector.tensor_copy(qaugh[64:65, :], negb[:1, :])
                p.bar()

                def out_cb(t0, w, o1, hh=hh):
                    nc.vector.tensor_copy(ob[:64, :w], o1[:64, :w])
                    p.bar()
                    p.dma(outh[hh * 64:(hh + 1) * 64, t0:t0 + w], ob[:64, :w])

                emit_attn_head2(p, kaugh, qaugh,
                                lambda i: vbuf3[:, i, :], T // 128, T, out_cb,
                                dh=64, dv=64,
                                gate_sel=(qmask_row, p.ones_row, 0))
    with p.scope():
        wol = p.sb([128, 4 * D], BF16, name="l0wol")
        woh = p.sb([128, 4 * D], BF16, name="l0woh")
        load_wmat(p, wol, wo_l, 512, D)
        load_wmat(p, woh, wo_h, 512, D)
        rl = [p.sb([128, 512], BF16, name=f"l0rl{c}") for c in range(4)]
        rh = [p.sb([128, 512], BF16, name=f"l0rh{c}") for c in range(4)]
        ssb = p.sb([128, 512], F32, name="l0ss")
        p.bar()
        for t0 in range(0, T, 512):
            for c in range(4):
                p.dma(rl[c][:, :], outl[c * 128:(c + 1) * 128, t0:t0 + 512])
                p.dma(rh[c][:, :], outh[c * 128:(c + 1) * 128, t0:t0 + 512])
            p.bar()
            for e in range(DC):
                pbk = p.pb[e % 2]
                for c in range(4):
                    nc.tensor.matmul(pbk[:, :], wm_lhsT(wol, 4, c, e * 128),
                                     rl[c][:, :], start=(c == 0), stop=False)
                for c in range(4):
                    nc.tensor.matmul(pbk[:, :], wm_lhsT(woh, 4, c, e * 128),
                                     rh[c][:, :], start=False, stop=(c == 3))
                p.bar()
                nc.vector.tensor_copy(ssb[:, :], pbk[:, :])
                p.bar()
                p.dma(sT_hbm[e * 128:(e + 1) * 128, t0:t0 + 512], ssb[:, :])
                p.bar()


def emit_fp4_quant(p, x_ap, out_ap, W, sc):
    nc = p.nc
    nb = W // 16
    x3 = x_ap.rearrange("p (b k) -> p b k", k=16)
    nc.vector.tensor_reduce(sc["bm"][:, :nb], x3, AX.X, ALU.max,
                            apply_absolute_value=True)
    p.bar()
    nc.vector.tensor_scalar(sc["bm"][:, :nb], sc["bm"][:, :nb], 1.0 / 6.0,
                            1e-12, ALU.mult, ALU.max)
    p.bar()
    nc.vector.reciprocal(sc["rc"][:, :nb], sc["bm"][:, :nb])
    p.bar()
    re3 = sc["re"].ap().rearrange("p (b k) -> p b k", k=16)
    for j in range(16):
        nc.vector.tensor_copy(re3[:, :nb, j], sc["rc"][:, :nb])
    p.bar()
    nc.vector.tensor_tensor(sc["xn"][:, :W], x_ap, sc["re"][:, :W], ALU.mult)
    p.bar()
    nc.scalar.activation(sc["a"][:, :W], sc["xn"][:, :W], AF.Abs)
    p.bar()
    nc.vector.tensor_scalar(sc["acc"][:, :W], sc["a"][:, :W], 0.25, None,
                            ALU.is_gt)
    p.bar()
    for m in (0.75, 1.25, 1.75):
        nc.vector.scalar_tensor_tensor(sc["acc"][:, :W], sc["a"][:, :W],
                                       float(m), sc["acc"][:, :W],
                                       ALU.is_gt, ALU.add)
        p.bar()
    nc.vector.tensor_scalar(sc["acc"][:, :W], sc["acc"][:, :W], 0.5, None,
                            ALU.mult)
    p.bar()
    for m in (2.5, 3.5):
        nc.vector.scalar_tensor_tensor(sc["acc"][:, :W], sc["a"][:, :W],
                                       float(m), sc["acc"][:, :W],
                                       ALU.is_gt, ALU.add)
        p.bar()
    nc.vector.tensor_scalar(sc["v"][:, :W], sc["a"][:, :W], 5.0, None, ALU.is_gt)
    p.bar()
    nc.vector.scalar_tensor_tensor(sc["acc"][:, :W], sc["v"][:, :W], 2.0,
                                   sc["acc"][:, :W], ALU.mult, ALU.add)
    p.bar()
    nc.scalar.activation(sc["v"][:, :W], sc["xn"][:, :W], AF.Sign)
    p.bar()
    nc.vector.tensor_tensor(sc["acc"][:, :W], sc["acc"][:, :W], sc["v"][:, :W],
                            ALU.mult)
    p.bar()
    nc.vector.reciprocal(sc["re"][:, :W], sc["re"][:, :W])
    p.bar()
    nc.vector.tensor_tensor(out_ap, sc["acc"][:, :W], sc["re"][:, :W], ALU.mult)
    p.bar()


def mk_fp4_scratch(p, W, tag="q4"):
    return {
        "bm": p.sb([128, W // 16], F32, name=_nm(tag + "b")),
        "rc": p.sb([128, W // 16], F32, name=_nm(tag + "r")),
        "re": p.sb([128, W], F32, name=_nm(tag + "e")),
        "xn": p.sb([128, W], F32, name=_nm(tag + "n")),
        "a": p.sb([128, W], F32, name=_nm(tag + "a")),
        "acc": p.sb([128, W], F32, name=_nm(tag + "c")),
        "v": p.sb([128, W], F32, name=_nm(tag + "v")),
    }


def emit_l0_ffn(p, xnT_f32, w_gate, w_up, w_down, gT_hbm, T=TT):
    nc = p.nc
    HN = 2816
    wqgT = p.dram([D, HN], BF16).ap()
    wquT = p.dram([D, HN], BF16).ap()
    wqdT = p.dram([HN, D], BF16).ap()
    hqT_hbm = p.dram([HN, T], BF16).ap()
    xqT_hbm = p.dram([D, T], BF16).ap()
    with p.scope():
        sc = mk_fp4_scratch(p, HN, "q4w")
        wt = p.sb([128, HN], F32, name="q4wt")
        wq = p.sb([128, HN], BF16, name="q4wq")
        for (w_src, w_dstT, nrow, ncol) in ((w_gate, wqgT, HN, D),
                                            (w_up, wquT, HN, D),
                                            (w_down, wqdT, D, HN)):
            for r0 in range(0, nrow, 128):
                p.dma(wt[:, :ncol], w_src[r0:r0 + 128, :])
                p.bar()
                emit_fp4_quant(p, wt[:, :ncol], wq[:, :ncol], ncol, sc)
                for c0 in range(0, ncol, 128):
                    nc.tensor.transpose(p.pbt[:, :128], wq[:, c0:c0 + 128],
                                        p.identb[:, :])
                    p.bar()
                    nc.vector.tensor_copy(wq[:, HN - 128:HN]
                                          if False else sc["v"]
                                          .ap().bitcast(BF16)[:, 0:128],
                                          p.pbt[:, :128])
                    p.bar()
                    p.dma(w_dstT[c0:c0 + 128, r0:r0 + 128],
                          sc["v"].ap().bitcast(BF16)[:, 0:128])
                    p.bar()
    with p.scope():
        sc = mk_fp4_scratch(p, D, "q4a")
        xtok = p.sb([128, D], F32, name="q4xt")
        xqtok = p.sb([128, D], BF16, name="q4xo")
        for i in range(T // 128):
            for c in range(DC):
                p.dma(xtok[:, c * 128:(c + 1) * 128]
                      if False else sc["xn"][:, 0:128],
                      xnT_f32[c * 128:(c + 1) * 128, i * 128:(i + 1) * 128])
                p.bar()
                nc.tensor.transpose(p.pbt2[:, :128], sc["xn"][:, 0:128],
                                    p.identf[:, :])
                p.bar()
                nc.vector.tensor_copy(xtok[:, c * 128:(c + 1) * 128],
                                      p.pbt2[:, :128])
                p.bar()
            emit_fp4_quant(p, xtok[:, :], xqtok[:, :], D, sc)
            for c in range(DC):
                nc.tensor.transpose(p.pbt[:, :128], xqtok[:, c * 128:(c + 1) * 128],
                                    p.identb[:, :])
                p.bar()
                nc.vector.tensor_copy(sc["v"].ap().bitcast(BF16)[:, 0:128],
                                      p.pbt[:, :128])
                p.bar()
                p.dma(xqT_hbm[c * 128:(c + 1) * 128, i * 128:(i + 1) * 128],
                      sc["v"].ap().bitcast(BF16)[:, 0:128])
                p.bar()
    with p.scope():
        wg = p.sb([128, DC * HN], BF16, name="q4wg")
        wu = p.sb([128, DC * HN], BF16, name="q4wu")
        load_wmat(p, wg, wqgT, D, HN)
        load_wmat(p, wu, wquT, D, HN)
        sc = mk_fp4_scratch(p, HN, "q4h")
        htok = p.sb([128, HN], F32, name="q4ht")
        hq = p.sb([128, HN], BF16, name="q4hq")
        h1 = p.sb([128, 512], F32, name="q4h1")
        xqt = p.sb([128, DC * 128], BF16, name="q4xi")
        p.bar()
        for i in range(T // 128):
            p.dma(xqt.ap().rearrange("p (c n) -> p c n", c=DC),
                  xqT_hbm[:, i * 128:(i + 1) * 128]
                  .rearrange("(c p) n -> p c n", p=128))
            p.bar()
            for n0 in range(0, HN, 512):
                nw = min(512, HN - n0)
                gp, up = p.pb[0], p.pb[1]
                for c in range(DC):
                    nc.tensor.matmul(gp[:, :nw], wm_lhsT(xqt, DC, c, 0),
                                     wm_lhsT(wg, DC, c, n0, nw),
                                     start=(c == 0), stop=(c == DC - 1))
                    nc.tensor.matmul(up[:, :nw], wm_lhsT(xqt, DC, c, 0),
                                     wm_lhsT(wu, DC, c, n0, nw),
                                     start=(c == 0), stop=(c == DC - 1))
                p.bar()
                nc.scalar.activation(h1[:, :nw], gp[:, :nw], AF.Silu)
                p.bar()
                nc.vector.tensor_tensor(htok[:, n0:n0 + nw], h1[:, :nw],
                                        up[:, :nw], ALU.mult)
                p.bar()
            emit_fp4_quant(p, htok[:, :], hq[:, :], HN, sc)
            for c0 in range(0, HN, 128):
                nc.tensor.transpose(p.pbt[:, :128], hq[:, c0:c0 + 128],
                                    p.identb[:, :])
                p.bar()
                nc.vector.tensor_copy(sc["v"].ap().bitcast(BF16)[:, 0:128],
                                      p.pbt[:, :128])
                p.bar()
                p.dma(hqT_hbm[c0:c0 + 128, i * 128:(i + 1) * 128],
                      sc["v"].ap().bitcast(BF16)[:, 0:128])
                p.bar()
    with p.scope():
        wd = p.sb([128, (HN // 128) * D], BF16, name="q4wd")
        load_wmat(p, wd, wqdT, HN, D)
        hti = [p.sb([128, 512], BF16, name=f"q4hi{c}") for c in range(HN // 128)]
        gsb = p.sb([128, 512], F32, name="q4g")
        p.bar()
        for t0 in range(0, T, 512):
            for c in range(HN // 128):
                p.dma(hti[c][:, :], hqT_hbm[c * 128:(c + 1) * 128, t0:t0 + 512])
            p.bar()
            for e in range(DC):
                pbk = p.pb[e % 2]
                for c in range(HN // 128):
                    nc.tensor.matmul(pbk[:, :], wm_lhsT(wd, HN // 128, c, e * 128),
                                     hti[c][:, :], start=(c == 0),
                                     stop=(c == HN // 128 - 1))
                p.bar()
                nc.scalar.activation(gsb[:, :], pbk[:, :], AF.Silu)
                p.bar()
                p.dma(gT_hbm[e * 128:(e + 1) * 128, t0:t0 + 512], gsb[:, :])
                p.bar()


def build_layer0(T=TT, dbg=False):
    nc = bass.Bass("TRN2", target_bir_lowering=False)
    xT = nc.dram_tensor("xT", [D, T], F32, kind="ExternalInput").ap()
    gamma_h = nc.dram_tensor("gamma_h", [D], F32, kind="ExternalInput").ap()
    wq_h = nc.dram_tensor("wq_h", [D, 512], F32, kind="ExternalInput").ap()
    wkv_h = nc.dram_tensor("wkv_h", [D, 1024], F32, kind="ExternalInput").ap()
    wo_h = nc.dram_tensor("wo_h", [512, D], F32, kind="ExternalInput").ap()
    ln_g = nc.dram_tensor("ln_g", [D], F32, kind="ExternalInput").ap()
    ln_b = nc.dram_tensor("ln_b", [D], F32, kind="ExternalInput").ap()
    wqkv_l = nc.dram_tensor("wqkv_l", [D, 1536], F32, kind="ExternalInput").ap()
    wo_l = nc.dram_tensor("wo_l", [512, D], F32, kind="ExternalInput").ap()
    rts = nc.dram_tensor("rts", [D, 2], F32, kind="ExternalInput").ap()
    bandmask = nc.dram_tensor("bandmask", [128, 6 * 512], F32,
                              kind="ExternalInput").ap()
    w_gate = nc.dram_tensor("w_gate", [2816, D], F32, kind="ExternalInput").ap()
    w_up = nc.dram_tensor("w_up", [2816, D], F32, kind="ExternalInput").ap()
    w_down = nc.dram_tensor("w_down", [D, 2816], F32, kind="ExternalInput").ap()
    xoutT = nc.dram_tensor("xoutT", [D, T], F32, kind="ExternalOutput").ap()

    p = Prog(nc)
    xnT_f32 = p.dram([D, T], F32).ap()
    xnT_bf16 = p.dram([D, T], BF16).ap()
    sT = p.dram([D, T], F32).ap()
    gT = p.dram([D, T], F32).ap()
    emit_prelude(p, xT, xnT_f32, xnT_bf16, T)
    dbg_outs = None
    if dbg:
        dbg_outs = {
            "qm": nc.dram_tensor("qm_dbg", [1, T], F32, kind="ExternalOutput").ap(),
            "kx": nc.dram_tensor("kx_dbg", [2, T], BF16, kind="ExternalOutput").ap(),
        }
    emit_l0_attn(p, xnT_f32, xnT_bf16, gamma_h, wq_h, wkv_h, wo_h, ln_g, ln_b,
                 wqkv_l, wo_l, rts, bandmask, sT, T, dbg_outs=dbg_outs)
    emit_l0_ffn(p, xnT_f32, w_gate, w_up, w_down, gT, T)
    emit_glue(p, xT, sT, gT, xoutT, T)
    if dbg:
        for nm, srcb in (("s_dbg", sT), ("g_dbg", gT), ("xn_dbg", xnT_f32)):
            o = nc.dram_tensor(nm, [D, T], F32, kind="ExternalOutput").ap()
            p.dma(o, srcb)
        p.bar()
    p.close()
    return nc


def make_bandmask():
    bm = np.zeros((128, 6 * 512), np.float32)
    i = np.arange(128)
    j = np.arange(512)
    for r in range(6):
        kvw = (r - 1) * 2 + i[:, None] // 64
        qw = j[None, :] // 64
        bm[:, r * 512:(r + 1) * 512] = (np.abs(kvw - qw) <= 1)
    return bm


def _run_layer0(xT_all, inputs, dbg=False):
    nc = build_layer0(dbg=dbg)
    base = {
        "gamma_h": inputs["L0_attn_gamma_h"],
        "wq_h": inputs["L0_attn_wq_h"],
        "wkv_h": inputs["L0_attn_wkv_h"],
        "wo_h": inputs["L0_attn_wo_h"],
        "ln_g": inputs["L0_attn_ln_g"],
        "ln_b": inputs["L0_attn_ln_b"],
        "wqkv_l": inputs["L0_attn_wqkv_l"],
        "wo_l": inputs["L0_attn_wo_l"],
        "rts": np.stack([inputs["L0_attn_rt_q"], inputs["L0_attn_rt_kv"]], 1),
        "bandmask": make_bandmask(),
        "w_gate": inputs["L0_ffn_w_gate"],
        "w_up": inputs["L0_ffn_w_up"],
        "w_down": inputs["L0_ffn_w_down"],
    }
    base = {k: np.ascontiguousarray(v) for k, v in base.items()}
    in_maps = []
    for c in range(8):
        m = dict(base)
        m["xT"] = xT_all[c % 4]
        in_maps.append(m)
    res = run_bass_kernel_spmd(nc, in_maps, core_ids=list(range(8)))
    if dbg:
        return res.results
    return [np.asarray(res.results[b]["xoutT"]) for b in range(4)]


def load_wmat(p, buf, WT_ap, K, N):
    p.cdma(buf.ap().rearrange("p (c n) -> p c n", c=K // 128),
           WT_ap.rearrange("(c p) n -> p c n", p=128))


def wm_lhsT(buf, KC, c, n0, nw=128):
    return buf.ap().rearrange("p (c n) -> p c n", c=KC)[:, c, n0:n0 + nw]


def emit_l1_attn(p, xnT_bf16, W1T, W2T, W3T, WoT2, ropeC, ropeS, sT_hbm, T=TT):
    nc = p.nc
    H, DH = 8, 128
    SC = 1.0 / math.sqrt(8.0)
    outTh = p.dram([D, T], BF16).ap()
    with p.scope():
        rc = p.sb([128, T], F32, name="a1rc")
        rs = p.sb([128, T], F32, name="a1rs")
        p.dma(rc[:, :], ropeC)
        p.dma(rs[:, :], ropeS)
        p.bar()
        with p.scope():
            xnb_h = [p.sb([128, 512], BF16, name=f"a1xh{c}") for c in range(DC)]
            wcols = p.sb([128, DC * 128], BF16, name="a1wc")
            pq = p.sb([128, DC * 512], F32, name="a1pq")
            hT = p.sb([128, T], F32, name="a1h")
            hS = p.sb([128, T], F32, name="a1hs")
            qT = p.sb([DH, T], BF16, name="a1q")
            kT = p.sb([DH, T], BF16, name="a1k")
            vbuf = p.sb([128, (T // 128) * DH], BF16, name="a1vb")
            vbuf3 = vbuf.ap().rearrange("p (i v) -> p i v", v=DH)
            qaux = p.sb([2, T], BF16, name="a1qx")
            kaux = p.sb([2, T], BF16, name="a1kx")
            negb = p.sb([1, T], F32, name="a1nb")
            ob = p.sb([DH, 512], BF16, name="a1ob")
            nc.gpsimd.memset(kaux[:, :], 1.0)
            nc.gpsimd.memset(qaux[:, :], 1.0)
            p.bar()
            swapmask = []
            for i in range(0, 32, 2):
                swapmask += [i + 1, i]

            def mk_headspace(WT, out_bf, do_rope, scale):
                for cb in range(DC):
                    p.cdma(wcols.ap().rearrange("p (c n) -> p c n", c=DC),
                           WT[:, cb * 128:(cb + 1) * 128]
                           .rearrange("(c p) n -> p c n", p=128))
                    p.bar()
                    pbk = p.pb[0]
                    for tt in range(1):
                        for c in range(DC):
                            nc.tensor.matmul(
                                pbk[:, :], wm_lhsT(wcols, DC, c, 0),
                                xnb_h[c][:, :],
                                start=(c == 0), stop=(c == DC - 1))
                        p.bar()
                        nc.vector.tensor_copy(pq[:, cb * 512:(cb + 1) * 512],
                                              pbk[:, :])
                        p.bar()
                for cb in range(DC):
                    nc.vector.tensor_copy(
                        hT.ap().rearrange("p (t c) -> p t c", c=8)[:, :, cb],
                        pq[:, cb * 512:(cb + 1) * 512])
                p.bar()
                if do_rope:
                    nc.vector.stream_shuffle(hS[:, :], hT[:, :], swapmask)
                    p.bar()
                    nc.vector.tensor_tensor(hS[:, :], hS[:, :], rs[:, :], ALU.mult)
                    p.bar()
                    nc.vector.tensor_tensor(hT[:, :], hT[:, :], rc[:, :], ALU.mult)
                    p.bar()
                    nc.vector.tensor_tensor(hT[:, :], hT[:, :], hS[:, :], ALU.add)
                    p.bar()
                if out_bf is not None:
                    if scale is None:
                        nc.vector.tensor_copy(out_bf[:, :], hT[:, :])
                    else:
                        nc.vector.tensor_scalar(out_bf[:, :], hT[:, :],
                                                float(scale), None, ALU.mult)
                    p.bar()

            for hh in range(H):
                for c in range(DC):
                    p.dma(xnb_h[c][:, :],
                          xnT_bf16[c * 128:(c + 1) * 128,
                                   512 * hh:512 * hh + 512])
                p.bar()
                mk_headspace(W1T, qT, True, SC)
                mk_headspace(W2T, kT, True, None)
                mk_headspace(W3T, None, False, None)
                for i in range(T // 128):
                    nc.tensor.transpose(p.pbt2[:, :128],
                                        hT[:, i * 128:(i + 1) * 128],
                                        p.identf[:, :])
                    p.bar()
                    nc.vector.tensor_copy(vbuf3[:, i, :], p.pbt2[:, :128])
                    p.bar()
                emit_bbound(p, qT, kT, DH, T, T, negb)
                nc.vector.tensor_copy(qaux[0:1, :], negb[:1, :])
                p.bar()

                def out_cb(t0, w, o1, hh=hh):
                    nc.vector.tensor_copy(ob[:DH, :w], o1[:DH, :w])
                    p.bar()
                    p.dma(outTh[hh * DH:(hh + 1) * DH, t0:t0 + w], ob[:DH, :w])

                emit_attn_head(p, kT, qT, kaux, qaux,
                               lambda i: vbuf3[:, i, :], T // 128, T, out_cb,
                               dh=DH, dv=DH)
        with p.scope():
            wo = p.sb([128, DC * D], BF16, name="a1wo")
            load_wmat(p, wo, WoT2, D, D)
            qa = [p.sb([128, 1024], BF16, name=f"a1qa{c}") for c in range(DC)]
            otile = p.sb([128, 1024], BF16, name="a1ot")
            ssb = p.sb([128, 512], F32, name="a1ss")
            p.bar()
            for a in range(4):
                for fc in range(DC):
                    p.dma(otile[:, :], outTh[fc * 128:(fc + 1) * 128,
                                             a * 1024:(a + 1) * 1024])
                    p.bar()
                    for jt in range(8):
                        nc.tensor.transpose(p.pbt[:, :128],
                                            otile[:, jt * 128:(jt + 1) * 128],
                                            p.identb[:, :])
                        p.bar()
                        nc.vector.tensor_copy(
                            qa[jt][:, fc * 128:(fc + 1) * 128], p.pbt[:, :128])
                        p.bar()
                for fc in range(DC):
                    for e0 in range(0, D, 512):
                        pbk = p.pb[fc % 2]
                        for jc in range(DC):
                            nc.tensor.matmul(
                                pbk[:, :], qa[jc][:, fc * 128:(fc + 1) * 128],
                                wm_lhsT(wo, DC, jc, e0, 512), start=(jc == 0),
                                stop=(jc == DC - 1))
                        p.bar()
                        nc.vector.tensor_copy(ssb[:, :], pbk[:, :])
                        p.bar()
                        p.dma(sT_hbm[fc * 128:(fc + 1) * 128,
                                     a * 1024 + e0:a * 1024 + e0 + 512],
                              ssb[:, :])
                        p.bar()


def emit_l1_ffn(p, xnT_f32, xnT_bf16, q_w, q_b, k_w, k_b, v_w, v_b, o_w, o_b,
                router_w, router_b, w1, b1, w2, b2, gT_hbm, T=TT):
    nc = p.nc
    H, KPH, E = 16, 32, 8
    SC = 1.0 / math.sqrt(32.0)
    aoT_hbm = p.dram([512, T], BF16).ap()
    mlaT_hbm = p.dram([D, T], BF16).ap()
    with p.scope():
        xnb = load_chunks_bf16(p, xnT_bf16, T, "f1x")
        wcols = p.sb([128, DC * 32], BF16, name="f1wc")
        bcol = p.sb([32, 1], F32, name="f1bc")
        qaug = p.sb([66, T], BF16, name="f1q")
        kaug = p.sb([66, T], BF16, name="f1k")
        vT = p.sb([32, T], F32, name="f1v")
        vbuf = p.sb([128, (T // 128) * 34], BF16, name="f1vb")
        vbuf3 = vbuf.ap().rearrange("p (i v) -> p i v", v=34)
        negb = p.sb([1, T], F32, name="f1nb")
        ob = p.sb([32, 512], BF16, name="f1ob")
        nc.gpsimd.memset(kaug[32:64, :], 0.0)
        nc.gpsimd.memset(qaug[32:64, :], 0.0)
        nc.gpsimd.memset(kaug[64:66, :], 1.0)
        nc.gpsimd.memset(qaug[64:66, :], 1.0)
        nc.gpsimd.memset(vbuf[:, :], 1.0)
        p.bar()

        def proj32(W_ap, b_ap, col0, out_sb, scale):
            p.cdma(wcols.ap().rearrange("p (c n) -> p c n", c=DC),
                   W_ap[:, col0:col0 + 32].rearrange("(c p) n -> p c n", p=128))
            p.dma(bcol[:, :], b_ap[col0:col0 + 32].rearrange("(a b) -> a b", b=1))
            _head_proj2(p, wcols, DC, xnb, T, out_sb, 0, 32, scale=scale,
                        bias_col=bcol[:32, :])
            p.bar()

        for hh in range(H):
            proj32(q_w, q_b, 64 * hh, qaug, SC)
            proj32(k_w, k_b, 32 * hh, kaug, None)
            proj32(v_w, v_b, 32 * hh, vT, None)
            emit_vbuf2(p, vT, vbuf3, T // 128, 32, f32=True)
            emit_bbound(p, qaug[0:32, :], kaug[0:32, :], 32, T, T, negb)
            nc.vector.tensor_copy(qaug[64:65, :], negb[:1, :])
            p.bar()

            def out_cb(t0, w, o1, hh=hh):
                nc.vector.tensor_copy(ob[:32, :w], o1[:32, :w])
                p.bar()
                p.dma(aoT_hbm[hh * 32:(hh + 1) * 32, t0:t0 + w], ob[:32, :w])

            emit_attn_head2(p, kaug, qaug, lambda i: vbuf3[:, i, :],
                            T // 128, T, out_cb, dh=32, dv=32)
    with p.scope():
        wo = p.sb([128, 4 * D], BF16, name="f1wo")
        load_wmat(p, wo, o_w[0:512, :], 512, D)
        ao = [p.sb([128, 512], BF16, name=f"f1ao{c}") for c in range(4)]
        ocol = p.sb([128, 1], F32, name="f1oc")
        msb = p.sb([128, 512], BF16, name="f1ms")
        p.bar()
        for t0 in range(0, T, 512):
            for c in range(4):
                p.dma(ao[c][:, :], aoT_hbm[c * 128:(c + 1) * 128, t0:t0 + 512])
            p.bar()
            for e in range(DC):
                p.dma(ocol[:, :], o_b[e * 128:(e + 1) * 128]
                      .rearrange("(a b) -> a b", b=1))
                pbk = p.pb[e % 2]
                for c in range(4):
                    nc.tensor.matmul(pbk[:, :], wm_lhsT(wo, 4, c, e * 128),
                                     ao[c][:, :], start=(c == 0), stop=(c == 3))
                p.bar()
                nc.vector.tensor_scalar(msb[:, :], pbk[:, :], ocol[:, :], None,
                                        ALU.add)
                p.bar()
                p.dma(mlaT_hbm[e * 128:(e + 1) * 128, t0:t0 + 512], msb[:, :])
                p.bar()
    _oc = p.scope()
    _oc.__enter__()
    combT = p.sb([E, T], F32, name="f1cT")
    with p.scope():
        mlab = load_chunks_bf16(p, mlaT_hbm, T, "f1m")
        rw = p.sb([128, DC * E], BF16, name="f1rw")
        load_wmat(p, rw, router_w, D, E)
        rbb = p.sb([128, E], F32, name="f1rb")
        probs = p.sb([128, E], F32, name="f1pr")
        rank = p.sb([128, E], F32, name="f1rk")
        tmp = p.sb([128, E], F32, name="f1tp")
        mrow = p.sb([128, 1], F32, name="f1mr")
        srow = p.sb([128, 1], F32, name="f1sr")
        p.bar()
        rb1 = p.sb([1, E], F32, name="f1rb1")
        p.dma(rb1[:1, :], router_b.rearrange("(a b) -> a b", a=1))
        p.bar()
        nc.tensor.matmul(p.pb[6][:, :E], p.ones_row[:1, :], rb1[:1, :],
                         start=True, stop=True)
        p.bar()
        nc.vector.tensor_copy(rbb[:, :], p.pb[6][:, :E])
        p.bar()
        for i in range(T // 128):
            pbk = p.pb[0]
            for c in range(DC):
                nc.tensor.matmul(pbk[:, :E], mlab[c][:, i * 128:(i + 1) * 128],
                                 wm_lhsT(rw, DC, c, 0, E),
                                 start=(c == 0), stop=(c == DC - 1))
            p.bar()
            nc.vector.tensor_tensor(probs[:, :], pbk[:128, :E], rbb[:, :], ALU.add)
            p.bar()
            nc.vector.tensor_reduce(mrow[:, :1], probs[:, :], AX.X, ALU.max,
                                    negate=True)
            p.bar()
            nc.scalar.activation(probs[:, :], probs[:, :], AF.Exp,
                                 bias=mrow[:, :], accum_out=srow[:, :])
            p.bar()
            nc.vector.reciprocal(srow[:, :1], srow[:, :1])
            p.bar()
            nc.vector.tensor_scalar(probs[:, :], probs[:, :], srow[:, :], None,
                                    ALU.mult)
            p.bar()
            nc.gpsimd.memset(rank[:, :], 0.0)
            p.bar()
            for s in range(1, E):
                lo = slice(0, E - s)
                hi = slice(s, E)
                nc.vector.tensor_tensor(tmp[:, lo], probs[:, hi], probs[:, lo],
                                        ALU.is_gt)
                p.bar()
                nc.vector.tensor_tensor(rank[:, lo], rank[:, lo], tmp[:, lo],
                                        ALU.add)
                p.bar()
                nc.vector.tensor_tensor(tmp[:, hi], probs[:, lo], probs[:, hi],
                                        ALU.is_ge)
                p.bar()
                nc.vector.tensor_tensor(rank[:, hi], rank[:, hi], tmp[:, hi],
                                        ALU.add)
                p.bar()
            nc.vector.tensor_scalar(tmp[:, :], rank[:, :], 1.5, None, ALU.is_lt)
            p.bar()
            nc.vector.tensor_tensor(tmp[:, :], tmp[:, :], probs[:, :], ALU.mult)
            p.bar()
            nc.vector.tensor_reduce(srow[:, :1], tmp[:, :], AX.X, ALU.add)
            p.bar()
            nc.vector.tensor_scalar(srow[:, :1], srow[:, :1], 1e-8, None, ALU.add)
            p.bar()
            nc.vector.reciprocal(srow[:, :1], srow[:, :1])
            p.bar()
            nc.vector.tensor_scalar(tmp[:, :], tmp[:, :], srow[:, :], None,
                                    ALU.mult)
            p.bar()
            nc.tensor.transpose(p.pbt2[:E, :128], tmp[:, :E], p.identf[:, :])
            p.bar()
            nc.vector.tensor_copy(combT[:, i * 128:(i + 1) * 128],
                                  p.pbt2[:E, :128])
            p.bar()
    with p.scope():
        w1b = p.sb([128, DC * 2048], BF16, name="f1w1")
        w2b = p.sb([128, 16 * D], BF16, name="f1w2")
        mlat = [p.sb([128, 1024], BF16, name=f"f1mt{c}") for c in range(DC)]
        h1T = [p.sb([128, 1024], BF16, name=f"f1h{i}") for i in range(16)]
        moe = [p.sb([128, 1024], F32, name=f"f1mo{c}") for c in range(DC)]
        b1all = p.sb([128, E * 16], F32, name="f1b1")
        b2row = p.sb([1, E * D], F32, name="f1b2")
        ones512 = p.sb([1, 512], F32, name="f1o5")
        t1 = p.sb([128, 512], F32, name="f1t1")
        gc_sb = p.sb([128, 1024], F32, name="f1gc")
        gsel = p.selmat(E, 128, name="f1gs")
        xnc = gc_sb
        nc.gpsimd.memset(ones512[:1, :], 1.0)
        for e in range(E):
            for hc in range(16):
                p.dma(b1all[:, e * 16 + hc:e * 16 + hc + 1],
                      b1[e, hc * 128:(hc + 1) * 128].rearrange("(a b) -> a b", b=1))
        p.dma(b2row[:1, :],
              b2.rearrange("e d -> (e d)").rearrange("(a b) -> a b", a=1))
        p.bar()
        s_pe, s_act, s_dve = p.sem("s_moe"), p.sem("s_mact"), p.sem("s_mdve")
        for tb in range(0, T, 1024):
            for c in range(DC):
                nc.vector.memset(moe[c][:, :], 0.0)
                p.dma(mlat[c][:, :], mlaT_hbm[c * 128:(c + 1) * 128, tb:tb + 1024])
            p.bar()
            if tb == 0:
                load_wmat(p, w1b, w1[0], D, 2048)
                load_wmat(p, w2b, w2[0], 2048, D)
                p.bar()
            for e in range(E):
                for t5 in range(2):
                    nc.tensor.wait_ge(s_dve, p.cnt["s_mdve"])
                    nc.tensor.matmul(p.pb[6][:, :], gsel[:, e * 128:(e + 1) * 128],
                                     combT[:, tb + t5 * 512:tb + t5 * 512 + 512],
                                     start=True, stop=True).then_inc(s_pe, 1)
                    p.cnt["s_moe"] += 1
                    nc.vector.wait_ge(s_pe, p.cnt["s_moe"])
                    nc.vector.tensor_copy(gc_sb[:, t5 * 512:(t5 + 1) * 512],
                                          p.pb[6][:, :]).then_inc(s_dve, 1)
                    p.cnt["s_mdve"] += 1
                pe_b, act_b = p.cnt["s_moe"], p.cnt["s_mact"]
                u = 0
                for hc in range(16):
                    for t5 in range(2):
                        bank = p.pb[u % 2]
                        if u == 0:
                            nc.tensor.wait_ge(p.dma_sem, p.dma_issued)
                        if u >= 2:
                            nc.tensor.wait_ge(s_act, act_b + u - 1)
                        for c in range(DC):
                            mm = nc.tensor.matmul(
                                bank[:, :], wm_lhsT(w1b, DC, c, hc * 128),
                                mlat[c][:, t5 * 512:(t5 + 1) * 512],
                                start=(c == 0), stop=(c == DC - 1))
                        mm.then_inc(s_pe, 1)
                        p.cnt["s_moe"] += 1
                        nc.scalar.wait_ge(s_pe, pe_b + u + 1)
                        nc.scalar.activation(
                            h1T[hc][:, t5 * 512:(t5 + 1) * 512], bank[:, :],
                            AF.Gelu,
                            bias=b1all[:, e * 16 + hc:e * 16 + hc + 1])\
                            .then_inc(s_act, 1)
                        p.cnt["s_mact"] += 1
                        u += 1
                cnt_hc_end = p.cnt["s_moe"]
                ep1 = e + 1
                if ep1 < E or tb + 1024 < T:
                    nc.gpsimd.wait_ge(s_pe, cnt_hc_end)
                    load_wmat(p, w1b, w1[ep1 % E], D, 2048)
                pe_b2, act_b2, dve_b2 = (p.cnt["s_moe"], p.cnt["s_mact"],
                                         p.cnt["s_mdve"])
                for v in range(16):
                    e2, t5 = v // 2, v % 2
                    bank = p.pb[2 + v % 2]
                    if v == 0:
                        nc.tensor.wait_ge(s_act, act_b2)
                        nc.tensor.wait_ge(p.dma_sem, p.dma_issued)
                    if v >= 2:
                        nc.tensor.wait_ge(s_dve, dve_b2 + 2 * (v - 2) + 1)
                    for hc in range(16):
                        nc.tensor.matmul(
                            bank[:, :], wm_lhsT(w2b, 16, hc, e2 * 128),
                            h1T[hc][:, t5 * 512:(t5 + 1) * 512],
                            start=(hc == 0), stop=False)
                    mm = nc.tensor.matmul(
                        bank[:, :], b2row[:1, e * D + e2 * 128:e * D + e2 * 128 + 128],
                        ones512[:1, :], start=False, stop=True)
                    mm.then_inc(s_pe, 1)
                    p.cnt["s_moe"] += 1
                    t1v = t1[:, 0:512]
                    nc.vector.wait_ge(s_pe, pe_b2 + v + 1)
                    nc.vector.tensor_tensor(t1v, bank[:, :],
                                            gc_sb[:, t5 * 512:(t5 + 1) * 512],
                                            ALU.mult).then_inc(s_dve, 1)
                    p.cnt["s_mdve"] += 1
                    nc.vector.wait_ge(s_dve, p.cnt["s_mdve"])
                    nc.vector.tensor_tensor(
                        moe[e2][:, t5 * 512:(t5 + 1) * 512],
                        moe[e2][:, t5 * 512:(t5 + 1) * 512], t1v, ALU.add)\
                        .then_inc(s_dve, 1)
                    p.cnt["s_mdve"] += 1
                if ep1 < E or tb + 1024 < T:
                    nc.gpsimd.wait_ge(s_pe, p.cnt["s_moe"])
                    load_wmat(p, w2b, w2[ep1 % E], 2048, D)
            p.bar()
            for c in range(DC):
                p.dma(xnc[:, :], xnT_f32[c * 128:(c + 1) * 128, tb:tb + 1024])
                p.bar()
                nc.vector.scalar_tensor_tensor(xnc[:, :], moe[c][:, :], 0.3,
                                               xnc[:, :], ALU.mult, ALU.add)
                p.bar()
                nc.scalar.activation(xnc[:, :], xnc[:, :], AF.Silu)
                p.bar()
                p.dma(gT_hbm[c * 128:(c + 1) * 128, tb:tb + 1024], xnc[:, :])
                p.bar()
    _oc.__exit__(None, None, None)


def build_layer1(T=TT, dbg=False):
    nc = bass.Bass("TRN2", target_bir_lowering=False)
    xT = nc.dram_tensor("xT", [D, T], F32, kind="ExternalInput").ap()
    W1T = nc.dram_tensor("W1T", [D, D], F32, kind="ExternalInput").ap()
    W2T = nc.dram_tensor("W2T", [D, D], F32, kind="ExternalInput").ap()
    W3T = nc.dram_tensor("W3T", [D, D], F32, kind="ExternalInput").ap()
    WoT2 = nc.dram_tensor("WoT2", [D, D], F32, kind="ExternalInput").ap()
    ropeC = nc.dram_tensor("ropeC", [128, T], F32, kind="ExternalInput").ap()
    ropeS = nc.dram_tensor("ropeS", [128, T], F32, kind="ExternalInput").ap()
    q_w = nc.dram_tensor("q_w", [D, D], F32, kind="ExternalInput").ap()
    q_b = nc.dram_tensor("q_b", [D], F32, kind="ExternalInput").ap()
    k_w = nc.dram_tensor("k_w", [D, 512], F32, kind="ExternalInput").ap()
    k_b = nc.dram_tensor("k_b", [512], F32, kind="ExternalInput").ap()
    v_w = nc.dram_tensor("v_w", [D, 512], F32, kind="ExternalInput").ap()
    v_b = nc.dram_tensor("v_b", [512], F32, kind="ExternalInput").ap()
    o_w = nc.dram_tensor("o_w", [D, D], F32, kind="ExternalInput").ap()
    o_b = nc.dram_tensor("o_b", [D], F32, kind="ExternalInput").ap()
    router_w = nc.dram_tensor("router_w", [D, 8], F32, kind="ExternalInput").ap()
    router_b = nc.dram_tensor("router_b", [8], F32, kind="ExternalInput").ap()
    w1 = nc.dram_tensor("w1", [8, D, 2048], F32, kind="ExternalInput").ap()
    b1 = nc.dram_tensor("b1", [8, 2048], F32, kind="ExternalInput").ap()
    w2 = nc.dram_tensor("w2", [8, 2048, D], F32, kind="ExternalInput").ap()
    b2 = nc.dram_tensor("b2", [8, D], F32, kind="ExternalInput").ap()
    xoutT = nc.dram_tensor("xoutT", [D, T], F32, kind="ExternalOutput").ap()

    p = Prog(nc)
    xnT_f32 = p.dram([D, T], F32).ap()
    xnT_bf16 = p.dram([D, T], BF16).ap()
    sT = p.dram([D, T], F32).ap()
    gT = p.dram([D, T], F32).ap()
    emit_prelude(p, xT, xnT_f32, xnT_bf16, T)
    emit_l1_attn(p, xnT_bf16, W1T, W2T, W3T, WoT2, ropeC, ropeS, sT, T)
    emit_l1_ffn(p, xnT_f32, xnT_bf16, q_w, q_b, k_w, k_b, v_w, v_b, o_w, o_b,
                router_w, router_b, w1, b1, w2, b2, gT, T)
    emit_glue(p, xT, sT, gT, xoutT, T)
    if dbg:
        for nm, srcb in (("s_dbg", sT), ("g_dbg", gT), ("xn_dbg", xnT_f32)):
            o = nc.dram_tensor(nm, [D, T], F32, kind="ExternalOutput").ap()
            p.dma(o, srcb)
        p.bar()
    p.close()
    return nc


def make_rope_tables(T=TT, DH=128):
    inv = 1.0 / 10000.0 ** (np.arange(0, DH, 2, dtype=np.float64) / DH)
    t = np.arange(T, dtype=np.float64)
    fr = np.repeat(t[:, None] * inv[None, :], 2, axis=-1)
    cos = np.cos(fr).T.astype(np.float32)
    sin = np.sin(fr).T.astype(np.float32)
    sgn = np.where(np.arange(DH)[:, None] % 2 == 0, -1.0, 1.0).astype(np.float32)
    return np.ascontiguousarray(cos), np.ascontiguousarray(sin * sgn)


def _run_layer1(xT_all, inputs, dbg=False):
    nc = build_layer1(dbg=dbg)
    rc, rs = make_rope_tables()
    base = {
        "W1T": inputs["L1_attn_W1"].T, "W2T": inputs["L1_attn_W2"].T,
        "W3T": inputs["L1_attn_W3"].T, "WoT2": inputs["L1_attn_Wo"].T,
        "ropeC": rc, "ropeS": rs,
        "q_w": inputs["L1_ffn_q_w"], "q_b": inputs["L1_ffn_q_b"],
        "k_w": inputs["L1_ffn_k_w"], "k_b": inputs["L1_ffn_k_b"],
        "v_w": inputs["L1_ffn_v_w"], "v_b": inputs["L1_ffn_v_b"],
        "o_w": inputs["L1_ffn_o_w"], "o_b": inputs["L1_ffn_o_b"],
        "router_w": inputs["L1_ffn_router_w"],
        "router_b": inputs["L1_ffn_router_b"],
        "w1": inputs["L1_ffn_w1"], "b1": inputs["L1_ffn_b1"],
        "w2": inputs["L1_ffn_w2"], "b2": inputs["L1_ffn_b2"],
    }
    base = {k: np.ascontiguousarray(v) for k, v in base.items()}
    in_maps = []
    for c in range(8):
        m = dict(base)
        m["xT"] = xT_all[c % 4]
        in_maps.append(m)
    res = run_bass_kernel_spmd(nc, in_maps, core_ids=list(range(8)))
    if dbg:
        return res.results
    return [np.asarray(res.results[b]["xoutT"]) for b in range(4)]


F32EPS = float(np.finfo(np.float32).eps)


def emit_rms_bcast(p, srcT_hbm, T, rstd_b, eps):
    nc = p.nc
    with p.scope():
        xc = p.sb([128, 512], F32, name=_nm("rmx"))
        sq = p.sb([128, 512], F32, name=_nm("rmsq"))
        rrow = p.sb([1, T], F32, name=_nm("rmr"))
        pv = p.pb[5]
        for t0 in range(0, T, 512):
            for c in range(DC):
                p.dma(xc[:, :], srcT_hbm[c * 128:(c + 1) * 128, t0:t0 + 512])
                p.bar()
                nc.scalar.activation(sq[:, :], xc[:, :], AF.Square)
                p.bar()
                nc.tensor.matmul(pv[:1, :], p.ones_col[:, :], sq[:, :],
                                 start=(c == 0), stop=(c == DC - 1))
                p.bar()
            nc.vector.tensor_scalar(rrow[:1, t0:t0 + 512], pv[:1, :512],
                                    1.0 / D, float(eps), ALU.mult, ALU.add)
            p.bar()
        nc.scalar.activation(rrow[:1, :], rrow[:1, :], AF.Sqrt)
        p.bar()
        nc.vector.reciprocal(rrow[:1, :], rrow[:1, :])
        p.bar()
        p.bcast_row(rstd_b, rrow, T)


def _head_proj(p, wcols_sb, KC, rhs_chunks, T, out_sb, out_row0, nw, bank=0,
               scale=None, out_dt_via=None):
    nc = p.nc
    for t0 in range(0, T, 512):
        pbk = p.pb[bank]
        for c in range(KC):
            nc.tensor.matmul(pbk[:nw, :], wcols_sb[:, c * nw:(c + 1) * nw],
                             rhs_chunks[c][:, t0:t0 + 512],
                             start=(c == 0), stop=(c == KC - 1))
        p.bar()
        if scale is None:
            nc.vector.tensor_copy(out_sb[out_row0:out_row0 + nw, t0:t0 + 512],
                                  pbk[:nw, :])
        else:
            nc.vector.tensor_scalar(out_sb[out_row0:out_row0 + nw, t0:t0 + 512],
                                    pbk[:nw, :], float(scale), None, ALU.mult)
        p.bar()


def load_wcols(p, dst_sb, W_hbm, K, col0, nw):
    p.cdma(dst_sb.ap().rearrange("p (c n) -> p c n", c=K // 128)[:, :, :nw],
           W_hbm[:, col0:col0 + nw].rearrange("(c p) n -> p c n", p=128))


def emit_l3_attn(p, xnT_f32, xnT_bf16, norm_w, Wq, Wkv, Wout, Wg, dmask_hbm,
                 sT_hbm, T=TT):
    nc = p.nc
    HEADS, DH = 16, 64
    outTh = p.dram([HEADS * DH, T], BF16).ap()
    with p.scope():
        rstd_b = p.sb([128, T], F32, name="l3rb")
        emit_rms_bcast(p, xnT_f32, T, rstd_b, F32EPS)
        xl3 = [p.sb([128, T], BF16, name=f"l3x{c}") for c in range(DC)]
        with p.scope():
            xc = p.sb([128, T], F32, name="l3xc")
            nw_sb = p.sb([128, 1], F32, name="l3nw")
            for c in range(DC):
                p.dma(xc[:, :], chunk(xnT_f32, c))
                p.dma(nw_sb[:, :], norm_w[c * 128:(c + 1) * 128]
                      .rearrange("(a b) -> a b", b=1))
                p.bar()
                nc.vector.tensor_tensor(xc[:, :], xc[:, :], rstd_b[:, :], ALU.mult)
                p.bar()
                nc.vector.tensor_scalar(xl3[c][:, :], xc[:, :], nw_sb[:, :], None,
                                        ALU.mult)
                p.bar()
        gates = p.sb([16, T], F32, name="l3gt")
        with p.scope():
            wg_sb = p.sb([128, DC * 16], BF16, name="l3wg")
            load_wcols(p, wg_sb, Wg, D, 0, 16)
            p.bar()
            for t0 in range(0, T, 512):
                pbk = p.pb[0]
                for c in range(DC):
                    nc.tensor.matmul(pbk[:16, :], wg_sb[:, c * 16:(c + 1) * 16],
                                     xl3[c][:, t0:t0 + 512],
                                     start=(c == 0), stop=(c == DC - 1))
                p.bar()
                nc.scalar.activation(gates[:16, t0:t0 + 512], pbk[:16, :], AF.Sigmoid)
                p.bar()
        dmasks = p.sb([128, 4 * 512], BF16, name="l3dm")
        p.cdma(dmasks[:, :], dmask_hbm)
        p.bar()
        gsel = p.selmat(HEADS, DH, name="l3sel")
        with p.scope():
            wcols = p.sb([128, DC * DH], BF16, name="l3wc")
            qaug = p.sb([66, T], BF16, name="l3qa")
            kaug = p.sb([66, T], BF16, name="l3ka")
            vT = p.sb([DH, T], BF16, name="l3vt")
            VX = DH + 2
            vbuf = p.sb([128, (T // 128) * VX], BF16, name="l3vb")
            vbuf3 = vbuf.ap().rearrange("p (i v) -> p i v", v=VX)
            negb = p.sb([1, T], F32, name="l3nb")
            ob = p.sb([DH, 512], BF16, name="l3ob")
            nc.gpsimd.memset(kaug[64:66, :], 1.0)
            nc.gpsimd.memset(qaug[64:66, :], 1.0)
            nc.gpsimd.memset(vbuf[:, :], 1.0)
            p.bar()
            for h in range(HEADS):
                load_wcols(p, wcols, Wq, D, h * DH, DH)
                _head_proj2(p, wcols, DC, xl3, T, qaug, 0, DH, scale=0.125)
                p.bar()
                load_wcols(p, wcols, Wkv, D, h * DH, DH)
                _head_proj2(p, wcols, DC, xl3, T, kaug, 0, DH)
                p.bar()
                load_wcols(p, wcols, Wkv, D, D + h * DH, DH)
                _head_proj2(p, wcols, DC, xl3, T, vT, 0, DH)
                p.bar()
                emit_vbuf2(p, vT, vbuf3, T // 128, DH)
                emit_bbound(p, qaug[0:DH, :], kaug[0:DH, :], DH, T, T, negb)
                nc.vector.tensor_copy(qaug[64:65, :], negb[:1, :])
                p.bar()

                def out_cb(t0, w, o1, h=h):
                    nc.vector.tensor_copy(ob[:DH, :w], o1[:DH, :w])
                    p.bar()
                    p.dma(outTh[h * DH:(h + 1) * DH, t0:t0 + w], ob[:DH, :w])

                emit_attn_head2(p, kaug, qaug,
                                lambda i: vbuf3[:, i, :], T // 128, T, out_cb,
                                dh=DH, dv=DH, causal_t0=lambda t0: t0,
                                dmaskadd=dmasks, gate_sel=(gates, gsel, h))
        with p.scope():
            wo = WMat(p, Wout, HEADS * DH, D, name="l3wo")
            rhs = [p.sb([128, 512], BF16, name=f"l3r{c}") for c in range(DC)]
            ssb = p.sb([128, 512], F32, name="l3ss")
            for t0 in range(0, T, 512):
                for c in range(DC):
                    p.dma(rhs[c][:, :], outTh[c * 128:(c + 1) * 128, t0:t0 + 512])
                p.bar()
                for e in range(DC):
                    pbk = p.pb[e % 2]
                    for c in range(DC):
                        nc.tensor.matmul(pbk[:, :], wo.lhsT(c, e * 128), rhs[c][:, :],
                                         start=(c == 0), stop=(c == DC - 1))
                    p.bar()
                    nc.vector.tensor_copy(ssb[:, :], pbk[:, :])
                    p.bar()
                    p.dma(sT_hbm[e * 128:(e + 1) * 128, t0:t0 + 512], ssb[:, :])
                    p.bar()


def emit_l3_ffn(p, xnT_bf16, w_q3, w_k3, w_gate3, vexp, oexp, dmask_hbm,
                gT_hbm, T=TT, dbg_outs=None):
    nc = p.nc
    H, E, DH = 8, 8, 64
    NKV = T // 128
    zw_hbm = p.dram([H * E * DH, T], BF16).ap()
    with p.scope():
        xnb = load_chunks_bf16(p, xnT_bf16, T, "f3x")
        dmasks = p.sb([128, 4 * 512], BF16, name="f3dm")
        p.cdma(dmasks[:, :], dmask_hbm)
        gmT = p.sb([H * E, T], BF16, name="f3gmt")
        p.bar()
        VX3 = DH + 2
        vbuf = p.sb([128, NKV * H * VX3], BF16, name="f3vb")
        vbuf3 = vbuf.ap().rearrange("p (i x) -> p i x", i=NKV)
        nc.gpsimd.memset(vbuf[:, :], 1.0)
        with p.scope():
          gmbuf = p.sb([128, NKV * H * E], F32, name="f3gm")
          if True:
            wg_sb = p.sb([128, DC * H * E], BF16, name="f3wg")
            load_wcols(p, wg_sb, w_gate3, D, 0, H * E)
            gsig = p.sb([128, H * E], F32, name="f3gs")
            rank = p.sb([128, H * E], F32, name="f3rk")
            tmp = p.sb([128, H * E], F32, name="f3tp")
            p.bar()
            for i in range(NKV):
                pbk = p.pb[0]
                for c in range(DC):
                    nc.tensor.matmul(pbk[:, :H * E],
                                     xnb[c][:, i * 128:(i + 1) * 128],
                                     wg_sb.ap().rearrange(
                                         "p (c n) -> p c n", c=DC)[:, c, :],
                                     start=(c == 0), stop=(c == DC - 1))
                p.bar()
                nc.scalar.activation(gsig[:, :], pbk[:128, :H * E], AF.Sigmoid)
                p.bar()
                nc.gpsimd.memset(rank[:, :], 0.0)
                p.bar()
                g3 = gsig.ap().rearrange("p (h e) -> p h e", h=H)
                r3 = rank.ap().rearrange("p (h e) -> p h e", h=H)
                t3 = tmp.ap().rearrange("p (h e) -> p h e", h=H)
                for s in range(1, E):
                    lo = (slice(None), slice(None), slice(0, E - s))
                    hi = (slice(None), slice(None), slice(s, E))
                    nc.vector.tensor_tensor(t3[lo], g3[hi], g3[lo], ALU.is_gt)
                    p.bar()
                    nc.vector.tensor_tensor(r3[lo], r3[lo], t3[lo], ALU.add)
                    p.bar()
                    nc.vector.tensor_tensor(t3[hi], g3[lo], g3[hi], ALU.is_ge)
                    p.bar()
                    nc.vector.tensor_tensor(r3[hi], r3[hi], t3[hi], ALU.add)
                    p.bar()
                nc.vector.tensor_scalar(tmp[:, :], rank[:, :], 0.0, None, ALU.is_equal)
                p.bar()
                nc.vector.tensor_tensor(gmbuf[:, i * H * E:(i + 1) * H * E],
                                        tmp[:, :], gsig[:, :], ALU.mult)
                p.bar()
                nc.tensor.transpose(p.pbt2[:H * E, :128],
                                    gmbuf[:, i * H * E:(i + 1) * H * E],
                                    p.identf[:, :])
                p.bar()
                nc.vector.tensor_copy(gmT[:, i * 128:(i + 1) * 128],
                                      p.pbt2[:H * E, :128])
                p.bar()
          import os as _os
          if _os.environ.get("FFN_SKIP_VALUES") and dbg_outs:
              p.cdma(vbuf[:, :], dbg_outs["vbuf_in"])
              p.bar()
          elif True:
            NH = NKV // 2
            vacc = p.sb([128, NH * 512], F32, name="f3va")
            vex = p.sb([128, DC * 512], BF16, name="f3ve")
            for half in range(2):
                i0 = half * NH
                nc.gpsimd.memset(vacc[:, :], 0.0)
                p.bar()
                for e in range(E):
                    for h in range(H):
                        p.cdma(vex.ap().rearrange("p (c h v) -> p c h v",
                                                  c=DC, h=H)[:, :, h, :],
                               vexp[e, h].rearrange("(c p) v -> p c v", p=128))
                    p.bar()
                    for il in range(NH):
                        i = i0 + il
                        pbk = p.pb[e % 2]
                        for c in range(DC):
                            nc.tensor.matmul(pbk[:, :],
                                             xnb[c][:, i * 128:(i + 1) * 128],
                                             vex[:, c * 512:(c + 1) * 512],
                                             start=(c == 0), stop=(c == DC - 1))
                        p.bar()
                        for h in range(H):
                            gcol = gmbuf[:, i * H * E + h * E + e:
                                         i * H * E + h * E + e + 1]
                            sl = slice(il * 512 + h * DH, il * 512 + (h + 1) * DH)
                            nc.vector.scalar_tensor_tensor(
                                vacc[:, sl], pbk[:, h * DH:(h + 1) * DH], gcol,
                                vacc[:, sl], ALU.mult, ALU.add)
                        p.bar()
                vbuf4 = vbuf.ap().rearrange("p (i h v) -> p i h v", i=NKV, h=H)
                for il in range(NH):
                    i = i0 + il
                    for hh in range(H):
                        nc.vector.tensor_copy(
                            vbuf4[:, i, hh, 0:DH],
                            vacc[:, il * 512 + hh * DH:il * 512 + (hh + 1) * DH])
                p.bar()
            if dbg_outs and "vbuf" in dbg_outs:
                p.dma(dbg_outs["vbuf"]
                      .rearrange("p (i h x) -> p i h x", i=NKV, h=H)
                      .rearrange("p i h x -> p (i h x)") if False else
                      dbg_outs["vbuf"], vbuf[:, :])
                p.bar()
            import os
            if os.environ.get("VBUF_ROUNDTRIP"):
                vrt = p.dram([128, NKV * H * VX3], BF16).ap()
                p.dma(vrt, vbuf[:, :])
                p.bar()
                p.dma(vbuf[:, :], vrt)
                p.bar()
        with p.scope():
            wcols = p.sb([128, DC * DH], BF16, name="f3wc")
            qaug = p.sb([66, T], BF16, name="f3qa")
            kaug = p.sb([66, T], BF16, name="f3ka")
            negb = p.sb([1, T], F32, name="f3nb")
            zwb = p.sb([DH, 512], BF16, name="f3zw")
            allsel = p.sb([H * E, H * E * 64], BF16, name="f3as")
            with p.scope():
                rowp = p.sb([H * E, 1], F32, name="f3rp")
                asf = p.sb([H * E, H * E * 64], F32, name="f3asf")
                nc.gpsimd.iota(asf.ap().rearrange("p (a b) -> p a b", b=64),
                               [[1, H * E], [0, 64]], channel_multiplier=0,
                               allow_small_or_imprecise_dtypes=True)
                nc.gpsimd.iota(rowp[:, :], [[0, 1]], channel_multiplier=1,
                               allow_small_or_imprecise_dtypes=True)
                p.bar()
                nc.vector.tensor_scalar(asf[:, :], asf[:, :], rowp[:, :], None,
                                        ALU.is_equal)
                p.bar()
                nc.vector.tensor_copy(allsel[:, :], asf[:, :])
                p.bar()
            if dbg_outs and "gmT" in dbg_outs:
                p.dma(dbg_outs["gmT"], gmT[:, :])
                p.bar()
            nc.gpsimd.memset(kaug[64:66, :], 1.0)
            nc.gpsimd.memset(qaug[64:66, :], 1.0)
            p.bar()
            for h in range(H):
                load_wcols(p, wcols, w_q3, D, h * DH, DH)
                _head_proj2(p, wcols, DC, xnb, T, qaug, 0, DH, scale=0.125)
                p.bar()
                load_wcols(p, wcols, w_k3, D, h * DH, DH)
                _head_proj2(p, wcols, DC, xnb, T, kaug, 0, DH)
                p.bar()
                emit_bbound(p, qaug[0:DH, :], kaug[0:DH, :], DH, T, T, negb)
                nc.vector.tensor_copy(qaug[64:65, :], negb[:1, :])
                p.bar()
                if h == 0 and dbg_outs and "qaug" in dbg_outs:
                    p.dma(dbg_outs["qaug"], qaug[:, :])
                    p.dma(dbg_outs["kaug"], kaug[:, :])
                    p.bar()
                vb_h = vbuf.ap().rearrange(
                    "p (i h x) -> p (i h) x", h=H, x=VX3)

                def out_cb(t0, w, o1, h=h):
                    if h == 0 and dbg_outs and "o1" in dbg_outs:
                        nc.vector.tensor_copy(zwb[:DH, :w], o1[:DH, :w])
                        p.bar()
                        p.dma(dbg_outs["o1"][:, t0:t0 + w], zwb[:DH, :w])
                        p.bar()
                    for e in range(E):
                        col = h * E + e
                        nc.tensor.matmul(p.pb[6][:DH, :w],
                                         allsel[:, col * 64:col * 64 + DH],
                                         gmT[:, t0:t0 + w],
                                         start=True, stop=True)
                        p.bar()
                        nc.vector.tensor_tensor(zwb[:DH, :w], o1[:DH, :w],
                                                p.pb[6][:DH, :w], ALU.mult)
                        p.bar()
                        r0 = (h * E + e) * DH
                        p.dma(zw_hbm[r0:r0 + DH, t0:t0 + w], zwb[:DH, :w])
                        p.bar()

                emit_attn_head2(p, kaug, qaug,
                                lambda i, h=h: vb_h[:, i * H + h, :],
                                NKV, T, out_cb,
                                dh=DH, dv=DH, causal_t0=lambda t0: t0,
                                dmaskadd=dmasks)
    if dbg_outs and "zw" in dbg_outs:
        p.dma(dbg_outs["zw"], zw_hbm[0:512, :])
        p.bar()
    with p.scope():
        if True:
            oex = p.sb([DH, H * E * D], BF16, name="f3oe")
            p.cdma(oex.ap().rearrange("v (e h d) -> v e h d", e=E, h=H),
                   oexp.rearrange("e h v d -> v e h d"))
            zwt = p.sb([DH, H * E * 512], BF16, name="f3zt")
            zw3 = zwt.ap().rearrange("p (cc n) -> p cc n", n=512)
            gsb = p.sb([128, 512], F32, name="f3gb")
            p.bar()
            oex4 = oex.ap().rearrange("v (e h d) -> v e h d", e=E, h=H)
            for t0 in range(0, T, 512):
                p.dma(zw3, zw_hbm.rearrange("(cc p) t -> p cc t", p=DH)
                      [:, :, t0:t0 + 512])
                p.bar()
                for ep in range(DC):
                    pbk = p.pb[2 + (ep % 2)]
                    n = 0
                    for h in range(H):
                        for e in range(E):
                            pair = h * E + e
                            nc.tensor.matmul(
                                pbk[:, :], oex4[:DH, e, h, ep * 128:(ep + 1) * 128],
                                zwt[0:DH, pair * 512:(pair + 1) * 512],
                                start=(n == 0), stop=(n == H * E - 1))
                            n += 1
                    p.bar()
                    nc.scalar.activation(gsb[:, :], pbk[:, :], AF.Silu)
                    p.bar()
                    p.dma(gT_hbm[ep * 128:(ep + 1) * 128, t0:t0 + 512], gsb[:, :])
                    p.bar()


def build_layer3(T=TT, dbg=False):
    nc = bass.Bass("TRN2", target_bir_lowering=False)
    xT = nc.dram_tensor("xT", [D, T], F32, kind="ExternalInput").ap()
    norm_w = nc.dram_tensor("norm_w", [D], F32, kind="ExternalInput").ap()
    Wq = nc.dram_tensor("Wq", [D, 1024], F32, kind="ExternalInput").ap()
    Wkv = nc.dram_tensor("Wkv", [D, 2048], F32, kind="ExternalInput").ap()
    Wout = nc.dram_tensor("Wout", [1024, D], F32, kind="ExternalInput").ap()
    Wg = nc.dram_tensor("Wg", [D, 16], F32, kind="ExternalInput").ap()
    w_q3 = nc.dram_tensor("w_q3", [D, 512], F32, kind="ExternalInput").ap()
    w_k3 = nc.dram_tensor("w_k3", [D, 512], F32, kind="ExternalInput").ap()
    w_gate3 = nc.dram_tensor("w_gate3", [D, 64], F32, kind="ExternalInput").ap()
    vexp = nc.dram_tensor("vexp", [8, 8, D, 64], F32, kind="ExternalInput").ap()
    oexp = nc.dram_tensor("oexp", [8, 8, 64, D], F32, kind="ExternalInput").ap()
    dmask = nc.dram_tensor("dmask", [128, 4 * 512], F32, kind="ExternalInput").ap()
    xoutT = nc.dram_tensor("xoutT", [D, T], F32, kind="ExternalOutput").ap()

    p = Prog(nc)
    xnT_f32 = p.dram([D, T], F32).ap()
    xnT_bf16 = p.dram([D, T], BF16).ap()
    sT = p.dram([D, T], F32).ap()
    gT = p.dram([D, T], F32).ap()
    emit_prelude(p, xT, xnT_f32, xnT_bf16, T)
    import os
    if not os.environ.get("SKIP_L3_ATTN"):
        emit_l3_attn(p, xnT_f32, xnT_bf16, norm_w, Wq, Wkv, Wout, Wg, dmask, sT, T)
    dbg_outs = None
    if dbg == 2:
        dbg_outs = {
            "vbuf": nc.dram_tensor("vbuf_dbg", [128, (T // 128) * 8 * 64], BF16,
                                   kind="ExternalOutput").ap(),
            "vbuf_in": nc.dram_tensor("vbuf_in", [128, (T // 128) * 8 * 64], F32,
                                      kind="ExternalInput").ap(),
            "gmT": nc.dram_tensor("gmT_dbg", [64, T], BF16,
                                  kind="ExternalOutput").ap(),
            "zw": nc.dram_tensor("zw_dbg", [512, T], BF16,
                                 kind="ExternalOutput").ap(),
            "qaug": nc.dram_tensor("qaug_dbg", [66, T], BF16,
                                   kind="ExternalOutput").ap(),
            "kaug": nc.dram_tensor("kaug_dbg", [66, T], BF16,
                                   kind="ExternalOutput").ap(),
            "o1": nc.dram_tensor("o1_dbg", [64, T], BF16,
                                 kind="ExternalOutput").ap(),
            "praw": nc.dram_tensor("praw_dbg", [128, 4 * 512 * 5], F32,
                                   kind="ExternalOutput").ap(),
            "ptil": nc.dram_tensor("ptil_dbg", [128, 4 * 512 * 5], BF16,
                                   kind="ExternalOutput").ap(),
            "pacc": nc.dram_tensor("pacc_dbg", [65, T], F32,
                                   kind="ExternalOutput").ap(),
        }
    emit_l3_ffn(p, xnT_bf16, w_q3, w_k3, w_gate3, vexp, oexp, dmask, gT, T,
                dbg_outs=dbg_outs)
    emit_glue(p, xT, sT, gT, xoutT, T)
    if dbg:
        for nm, src in (("s_dbg", sT), ("g_dbg", gT), ("xn_dbg", xnT_f32)):
            o = nc.dram_tensor(nm, [D, T], F32, kind="ExternalOutput").ap()
            p.dma(o, src)
        p.bar()
    p.close()
    return nc


def make_dmask(q_tile=512):
    m = np.zeros((128, 4 * q_tile), np.float32)
    for r in range(4):
        i = np.arange(128)[:, None]
        j = np.arange(q_tile)[None, :]
        m[:, r * q_tile:(r + 1) * q_tile] = (i + 128 * r <= j)
    return m


def make_dmaskadd(q_tile=512):
    return (make_dmask(q_tile) - 1.0) * 30000.0


def _run_layer3(xT_all, inputs, dbg=False, extra=None):
    nc = build_layer3(dbg=dbg)
    base = {
        "norm_w": inputs["L3_attn_norm_w"],
        "Wq": inputs["L3_attn_Wq"],
        "Wkv": inputs["L3_attn_Wkv"],
        "Wout": inputs["L3_attn_Wout"],
        "Wg": inputs["L3_attn_Wg"],
        "w_q3": inputs["L3_ffn_w_q"],
        "w_k3": inputs["L3_ffn_w_k"],
        "w_gate3": inputs["L3_ffn_w_gate"],
        "vexp": inputs["L3_ffn_value_experts"],
        "oexp": inputs["L3_ffn_output_experts"],
        "dmask": make_dmaskadd(),
    }
    base = {k: np.ascontiguousarray(v) for k, v in base.items()}
    if dbg == 2:
        base["vbuf_in"] = np.zeros((128, 32 * 8 * 64), np.float32)
        if extra:
            base.update({k: np.ascontiguousarray(v) for k, v in extra.items()})
    in_maps = []
    for c in range(8):
        m = dict(base)
        m["xT"] = xT_all[c % 4]
        in_maps.append(m)
    res = run_bass_kernel_spmd(nc, in_maps, core_ids=list(range(8)))
    if dbg:
        return res.results
    return [np.asarray(res.results[b]["xoutT"]) for b in range(4)]


def _to_fm(x):
    return [np.ascontiguousarray(x[b].T) for b in range(x.shape[0])]


def _run_layer2(xT_all, inputs, dbg=False):
    nc = build_layer2(dbg=dbg)
    base = {
        "WkT": np.ascontiguousarray(np.transpose(inputs["L2_attn_Wk"], (0, 2, 1))),
        "WvT": np.ascontiguousarray(np.transpose(inputs["L2_attn_Wv"], (0, 2, 1))),
        "WrT": np.ascontiguousarray(np.transpose(inputs["L2_attn_Wr"], (0, 2, 1))),
        "WoT": np.ascontiguousarray(np.transpose(inputs["L2_attn_Wo"], (0, 2, 1))),
        "lnw": np.ascontiguousarray(inputs["L2_attn_ln_w"]),
        "lnb": np.ascontiguousarray(inputs["L2_attn_ln_b"]),
        "wgT": np.ascontiguousarray(inputs["L2_ffn_w_gate"].T),
        "wuT": np.ascontiguousarray(inputs["L2_ffn_w_up"].T),
        "wdT": np.ascontiguousarray(inputs["L2_ffn_w_down"].T),
    }
    in_maps = []
    for c in range(8):
        m = dict(base)
        m["xT"] = xT_all[c % 4]
        in_maps.append(m)
    res = run_bass_kernel_spmd(nc, in_maps, core_ids=list(range(8)))
    if dbg:
        return res.results
    return [np.asarray(res.results[b]["xoutT"]) for b in range(4)]


def build_module(T=TT):
    nc = bass.Bass("TRN2", target_bir_lowering=False)
    EI = "ExternalInput"
    xT = nc.dram_tensor("xT", [D, T], F32, kind=EI).ap()
    gamma_h = nc.dram_tensor("gamma_h", [D], F32, kind=EI).ap()
    wq_h = nc.dram_tensor("wq_h", [D, 512], F32, kind=EI).ap()
    wkv_h = nc.dram_tensor("wkv_h", [D, 1024], F32, kind=EI).ap()
    wo_h = nc.dram_tensor("wo_h", [512, D], F32, kind=EI).ap()
    ln_g = nc.dram_tensor("ln_g", [D], F32, kind=EI).ap()
    ln_b = nc.dram_tensor("ln_b", [D], F32, kind=EI).ap()
    wqkv_l = nc.dram_tensor("wqkv_l", [D, 1536], F32, kind=EI).ap()
    wo_l = nc.dram_tensor("wo_l", [512, D], F32, kind=EI).ap()
    rts = nc.dram_tensor("rts", [D, 2], F32, kind=EI).ap()
    bandmask = nc.dram_tensor("bandmask", [128, 6 * 512], F32, kind=EI).ap()
    w_gate = nc.dram_tensor("w_gate", [2816, D], F32, kind=EI).ap()
    w_up = nc.dram_tensor("w_up", [2816, D], F32, kind=EI).ap()
    w_down = nc.dram_tensor("w_down", [D, 2816], F32, kind=EI).ap()
    W1T = nc.dram_tensor("W1T", [D, D], F32, kind=EI).ap()
    W2T = nc.dram_tensor("W2T", [D, D], F32, kind=EI).ap()
    W3T = nc.dram_tensor("W3T", [D, D], F32, kind=EI).ap()
    WoT2 = nc.dram_tensor("WoT2", [D, D], F32, kind=EI).ap()
    ropeC = nc.dram_tensor("ropeC", [128, T], F32, kind=EI).ap()
    ropeS = nc.dram_tensor("ropeS", [128, T], F32, kind=EI).ap()
    q_w = nc.dram_tensor("q_w", [D, D], F32, kind=EI).ap()
    q_b = nc.dram_tensor("q_b", [D], F32, kind=EI).ap()
    k_w = nc.dram_tensor("k_w", [D, 512], F32, kind=EI).ap()
    k_b = nc.dram_tensor("k_b", [512], F32, kind=EI).ap()
    v_w = nc.dram_tensor("v_w", [D, 512], F32, kind=EI).ap()
    v_b = nc.dram_tensor("v_b", [512], F32, kind=EI).ap()
    o_w = nc.dram_tensor("o_w", [D, D], F32, kind=EI).ap()
    o_b = nc.dram_tensor("o_b", [D], F32, kind=EI).ap()
    router_w = nc.dram_tensor("router_w", [D, 8], F32, kind=EI).ap()
    router_b = nc.dram_tensor("router_b", [8], F32, kind=EI).ap()
    w1 = nc.dram_tensor("w1", [8, D, 2048], F32, kind=EI).ap()
    b1 = nc.dram_tensor("b1", [8, 2048], F32, kind=EI).ap()
    w2 = nc.dram_tensor("w2", [8, 2048, D], F32, kind=EI).ap()
    b2 = nc.dram_tensor("b2", [8, D], F32, kind=EI).ap()
    WkT = nc.dram_tensor("WkT", [2, D, D], F32, kind=EI).ap()
    WvT = nc.dram_tensor("WvT", [2, D, D], F32, kind=EI).ap()
    WrT = nc.dram_tensor("WrT", [2, D, D], F32, kind=EI).ap()
    WoT = nc.dram_tensor("WoT", [2, D, D], F32, kind=EI).ap()
    lnw = nc.dram_tensor("lnw", [2, D], F32, kind=EI).ap()
    lnb = nc.dram_tensor("lnb", [2, D], F32, kind=EI).ap()
    wgT = nc.dram_tensor("wgT", [D, 3072], I32, kind=EI).ap()
    wuT = nc.dram_tensor("wuT", [D, 3072], I32, kind=EI).ap()
    wdT = nc.dram_tensor("wdT", [3072, D], I32, kind=EI).ap()
    norm_w = nc.dram_tensor("norm_w", [D], F32, kind=EI).ap()
    Wq = nc.dram_tensor("Wq", [D, 1024], F32, kind=EI).ap()
    Wkv = nc.dram_tensor("Wkv", [D, 2048], F32, kind=EI).ap()
    Wout = nc.dram_tensor("Wout", [1024, D], F32, kind=EI).ap()
    Wg = nc.dram_tensor("Wg", [D, 16], F32, kind=EI).ap()
    w_q3 = nc.dram_tensor("w_q3", [D, 512], F32, kind=EI).ap()
    w_k3 = nc.dram_tensor("w_k3", [D, 512], F32, kind=EI).ap()
    w_gate3 = nc.dram_tensor("w_gate3", [D, 64], F32, kind=EI).ap()
    vexp = nc.dram_tensor("vexp", [8, 8, D, 64], F32, kind=EI).ap()
    oexp = nc.dram_tensor("oexp", [8, 8, 64, D], F32, kind=EI).ap()
    dmask = nc.dram_tensor("dmask", [128, 4 * 512], F32, kind=EI).ap()
    xoutT = nc.dram_tensor("xoutT", [D, T], F32, kind="ExternalOutput").ap()

    p = Prog(nc)
    x1T = p.dram([D, T], F32).ap()
    x2T = p.dram([D, T], F32).ap()
    x3T = p.dram([D, T], F32).ap()
    xnT_f32 = p.dram([D, T], F32).ap()
    xnT_bf16 = p.dram([D, T], BF16).ap()
    sT = p.dram([D, T], F32).ap()
    gT = p.dram([D, T], F32).ap()
    emit_prelude(p, xT, xnT_f32, xnT_bf16, T)
    emit_l0_attn(p, xnT_f32, xnT_bf16, gamma_h, wq_h, wkv_h, wo_h, ln_g, ln_b,
                 wqkv_l, wo_l, rts, bandmask, sT, T)
    emit_l0_ffn(p, xnT_f32, w_gate, w_up, w_down, gT, T)
    emit_glue(p, xT, sT, gT, x1T, T)
    xnT_f32 = p.dram([D, T], F32).ap()
    xnT_bf16 = p.dram([D, T], BF16).ap()
    sT = p.dram([D, T], F32).ap()
    gT = p.dram([D, T], F32).ap()
    emit_prelude(p, x1T, xnT_f32, xnT_bf16, T)
    emit_l1_attn(p, xnT_bf16, W1T, W2T, W3T, WoT2, ropeC, ropeS, sT, T)
    emit_l1_ffn(p, xnT_f32, xnT_bf16, q_w, q_b, k_w, k_b, v_w, v_b, o_w, o_b,
                router_w, router_b, w1, b1, w2, b2, gT, T)
    emit_glue(p, x1T, sT, gT, x2T, T)
    xnT_f32 = p.dram([D, T], F32).ap()
    xnT_bf16 = p.dram([D, T], BF16).ap()
    sT_scr = p.dram([D, T], F32).ap()
    u_b = p.dram([D, T], F32).ap()
    gT = p.dram([D, T], F32).ap()
    emit_prelude(p, x2T, xnT_f32, xnT_bf16, T)
    sT2 = emit_l2_attn(p, xnT_f32, sT_scr, u_b, WkT, WvT, WrT, WoT, lnw, lnb, T)
    emit_l2_ffn(p, xnT_bf16, wgT, wuT, wdT, gT, T)
    emit_glue(p, x2T, sT2, gT, x3T, T)
    xnT_f32 = p.dram([D, T], F32).ap()
    xnT_bf16 = p.dram([D, T], BF16).ap()
    sT = p.dram([D, T], F32).ap()
    gT = p.dram([D, T], F32).ap()
    emit_prelude(p, x3T, xnT_f32, xnT_bf16, T)
    emit_l3_attn(p, xnT_f32, xnT_bf16, norm_w, Wq, Wkv, Wout, Wg, dmask, sT, T)
    emit_l3_ffn(p, xnT_bf16, w_q3, w_k3, w_gate3, vexp, oexp, dmask, gT, T)
    emit_glue(p, x3T, sT, gT, xoutT, T)
    p.close()
    return nc


def kernel(**inputs):
    x = np.ascontiguousarray(np.asarray(inputs["L0_attn_x"], np.float32))
    xT_all = _to_fm(x)
    nc = build_module()
    rc, rs = make_rope_tables()
    base = {
        "gamma_h": inputs["L0_attn_gamma_h"],
        "wq_h": inputs["L0_attn_wq_h"],
        "wkv_h": inputs["L0_attn_wkv_h"],
        "wo_h": inputs["L0_attn_wo_h"],
        "ln_g": inputs["L0_attn_ln_g"],
        "ln_b": inputs["L0_attn_ln_b"],
        "wqkv_l": inputs["L0_attn_wqkv_l"],
        "wo_l": inputs["L0_attn_wo_l"],
        "rts": np.stack([inputs["L0_attn_rt_q"], inputs["L0_attn_rt_kv"]], 1),
        "bandmask": make_bandmask(),
        "w_gate": inputs["L0_ffn_w_gate"],
        "w_up": inputs["L0_ffn_w_up"],
        "w_down": inputs["L0_ffn_w_down"],
        "W1T": inputs["L1_attn_W1"].T, "W2T": inputs["L1_attn_W2"].T,
        "W3T": inputs["L1_attn_W3"].T, "WoT2": inputs["L1_attn_Wo"].T,
        "ropeC": rc, "ropeS": rs,
        "q_w": inputs["L1_ffn_q_w"], "q_b": inputs["L1_ffn_q_b"],
        "k_w": inputs["L1_ffn_k_w"], "k_b": inputs["L1_ffn_k_b"],
        "v_w": inputs["L1_ffn_v_w"], "v_b": inputs["L1_ffn_v_b"],
        "o_w": inputs["L1_ffn_o_w"], "o_b": inputs["L1_ffn_o_b"],
        "router_w": inputs["L1_ffn_router_w"],
        "router_b": inputs["L1_ffn_router_b"],
        "w1": inputs["L1_ffn_w1"], "b1": inputs["L1_ffn_b1"],
        "w2": inputs["L1_ffn_w2"], "b2": inputs["L1_ffn_b2"],
        "WkT": np.transpose(inputs["L2_attn_Wk"], (0, 2, 1)),
        "WvT": np.transpose(inputs["L2_attn_Wv"], (0, 2, 1)),
        "WrT": np.transpose(inputs["L2_attn_Wr"], (0, 2, 1)),
        "WoT": np.transpose(inputs["L2_attn_Wo"], (0, 2, 1)),
        "lnw": inputs["L2_attn_ln_w"],
        "lnb": inputs["L2_attn_ln_b"],
        "wgT": inputs["L2_ffn_w_gate"].T,
        "wuT": inputs["L2_ffn_w_up"].T,
        "wdT": inputs["L2_ffn_w_down"].T,
        "norm_w": inputs["L3_attn_norm_w"],
        "Wq": inputs["L3_attn_Wq"],
        "Wkv": inputs["L3_attn_Wkv"],
        "Wout": inputs["L3_attn_Wout"],
        "Wg": inputs["L3_attn_Wg"],
        "w_q3": inputs["L3_ffn_w_q"],
        "w_k3": inputs["L3_ffn_w_k"],
        "w_gate3": inputs["L3_ffn_w_gate"],
        "vexp": inputs["L3_ffn_value_experts"],
        "oexp": inputs["L3_ffn_output_experts"],
        "dmask": make_dmaskadd(),
    }
    base = {k: np.ascontiguousarray(v) for k, v in base.items()}
    in_maps = []
    for c in range(8):
        m = dict(base)
        m["xT"] = xT_all[c % 4]
        in_maps.append(m)
    res = run_bass_kernel_spmd(nc, in_maps, core_ids=list(range(8)))
    out = np.stack(
        [np.asarray(res.results[b]["xoutT"], np.float32).T for b in range(4)],
        axis=0)
    return np.ascontiguousarray(out)


# version 06
# speedup vs baseline: 1.5679x; 1.0452x over previous
import math
from contextlib import ExitStack, contextmanager

import numpy as np
import concourse.bass as bass
import concourse.mybir as mybir
from concourse.bass_utils import run_bass_kernel_spmd

F32 = mybir.dt.float32
BF16 = mybir.dt.bfloat16
I32 = mybir.dt.int32
AF = mybir.ActivationFunctionType
ALU = mybir.AluOpType
AX = mybir.AxisListType

D = 1024
NB = 4
TT = 4096
DC = D // 128
LN_EPS = 1e-5
DEC95 = float(np.float32(0.95))
DEC05 = float(np.float32(np.float32(1.0) - np.float32(0.95)))

_uid = [0]


def _nm(prefix):
    _uid[0] += 1
    return f"{prefix}{_uid[0]}"


def chunk(ap, c, w=128):
    return ap[c * w:(c + 1) * w, :]


class Prog:

    def __init__(self, nc):
        self.nc = nc
        self.ctx = ExitStack()
        self.dma_sem = self.ctx.enter_context(nc.semaphore("dmas"))
        self.dma_issued = 0
        self.pb = [self.ctx.enter_context(nc.psum_tensor(f"pb{i}", [128, 512], F32))
                   for i in range(7)]
        self.pbt = self.ctx.enter_context(nc.psum_tensor("pbt", [128, 1024], BF16))
        self.pbt2 = self.pbt.bitcast(F32)
        self._scopes = [self.ctx]
        self.ones_col = self.sb([128, 1], F32, name="onescol")
        nc.gpsimd.memset(self.ones_col[:, :], 1.0)
        self.ones_col_b = self.sb([128, 1], BF16, name="onescolb")
        nc.gpsimd.memset(self.ones_col_b[:, :], 1.0)
        self.ones_row = self.sb([1, 128], F32, name="onesrow")
        nc.gpsimd.memset(self.ones_row[:, :], 1.0)
        self.identf = self.sb([128, 128], F32, name="identf")
        iot2 = self.sb([128, 128], F32, name="iota2")
        pid = self.sb([128, 1], F32, name="pid")
        nc.gpsimd.iota(iot2[:, :], [[1, 128]], channel_multiplier=0,
                       allow_small_or_imprecise_dtypes=True)
        nc.gpsimd.iota(pid[:, :], [[0, 1]], channel_multiplier=1,
                       allow_small_or_imprecise_dtypes=True)
        self.bar()
        nc.vector.tensor_scalar(self.identf[:, :], iot2[:, :], pid[:, :], None,
                                ALU.is_equal)
        self.bar()
        self.identb = self.sb([128, 128], BF16, name="identb")
        nc.vector.tensor_copy(self.identb[:, :], self.identf[:, :])
        self.bar()

    def close(self):
        self.bar()
        self.ctx.close()

    def sem(self, name):
        if not hasattr(self, "sems"):
            self.sems, self.cnt = {}, {}
        if name not in self.sems:
            self.sems[name] = self.ctx.enter_context(self.nc.semaphore(name))
            self.cnt[name] = 0
        return self.sems[name]

    @contextmanager
    def scope(self):
        es = ExitStack()
        self._scopes.append(es)
        try:
            yield es
        finally:
            self.bar()
            self._scopes.pop()
            es.close()

    def sb(self, shape, dt=F32, name=None):
        return self._scopes[-1].enter_context(
            self.nc.sbuf_tensor(name or _nm("sb"), shape, dt))

    def dram(self, shape, dt=F32, name=None):
        return self.nc.dram_tensor(name or _nm("scr"), shape, dt, kind="Internal")

    def dma(self, out, in_, eng=None, **kw):
        eng = eng or self.nc.sync
        eng.dma_start(out=out, in_=in_, **kw).then_inc(self.dma_sem, 16)
        self.dma_issued += 16

    def cdma(self, out, in_, **kw):
        self.dma(out, in_, eng=self.nc.gpsimd, **kw)

    def bar(self):
        if self.dma_issued:
            self.nc.sync.wait_ge(self.dma_sem, self.dma_issued)
        self.nc.all_engine_barrier()

    def cst(self, v, name=None):
        t = self.sb([128, 1], F32, name=name or _nm("cst"))
        self.nc.gpsimd.memset(t[:, :], float(v))
        self.bar()
        return t

    def selmat(self, G, dv, name=None):
        nc = self.nc
        sel = self.sb([G, G * dv], F32, name=name or _nm("sel"))
        rowp = self.sb([G, 1], F32, name=_nm("selr"))
        nc.gpsimd.iota(sel.ap().rearrange("p (a b) -> p a b", b=dv),
                       [[1, G], [0, dv]], channel_multiplier=0,
                       allow_small_or_imprecise_dtypes=True)
        nc.gpsimd.iota(rowp[:, :], [[0, 1]], channel_multiplier=1,
                       allow_small_or_imprecise_dtypes=True)
        self.bar()
        nc.vector.tensor_scalar(sel[:, :], sel[:, :], rowp[:, :], None,
                                ALU.is_equal)
        self.bar()
        return sel

    def bcast_row(self, out_sb, row_ap, T, bank=6):
        nc = self.nc
        pbk = self.pb[bank]
        for t0 in range(0, T, 512):
            w = min(512, T - t0)
            nc.tensor.matmul(pbk[:, :w], self.ones_row[:1, :], row_ap[:1, t0:t0 + w],
                             start=True, stop=True)
            self.bar()
            nc.vector.tensor_copy(out_sb[:, t0:t0 + w], pbk[:, :w])
            self.bar()


class WMat:

    def __init__(self, p, WT_hbm_ap, K, N, name=None):
        self.K, self.N, self.KC = K, N, K // 128
        self.sb = p.sb([128, self.KC * N], BF16, name=name or _nm("wm"))
        p.cdma(self.sb.ap().rearrange("p (c n) -> p c n", c=self.KC),
               WT_hbm_ap.rearrange("(c p) n -> p c n", p=128))

    def lhsT(self, c, n0, nw=128):
        base = self.sb.ap().rearrange("p (c n) -> p c n", c=self.KC)
        return base[:, c, n0:n0 + nw]


class WSlice:

    def __init__(self, p, WT_hbm_ap, K, name=None):
        self.p, self.WT, self.K, self.KC = p, WT_hbm_ap, K, K // 128
        self.sb = p.sb([128, self.KC * 128], BF16, name=name or _nm("ws"))

    def load(self, n0, nw=128):
        self.p.cdma(self.sb.ap().rearrange("p (c n) -> p c n", c=self.KC)[:, :, :nw],
                    self.WT[:, n0:n0 + nw].rearrange("(c p) n -> p c n", p=128))

    def lhsT(self, c, nw=128):
        base = self.sb.ap().rearrange("p (c n) -> p c n", c=self.KC)
        return base[:, c, 0:nw]


def emit_proj(p, wm, rhs_chunks, T, consume, banks=(0,), tb=512, n0=0, n1=None):
    nc = p.nc
    if n1 is None:
        n1 = wm.N // 128
    bi = 0
    for e in range(n0, n1):
        for t0 in range(0, T, tb):
            w = min(tb, T - t0)
            pbk = p.pb[banks[bi % len(banks)]]
            bi += 1
            for c in range(wm.KC):
                nc.tensor.matmul(pbk[:, :w], wm.lhsT(c, e * 128),
                                 rhs_chunks[c][:, t0:t0 + w],
                                 start=(c == 0), stop=(c == wm.KC - 1))
            p.bar()
            consume(e, t0, w, pbk)
            p.bar()


def emit_ln_stats(p, load_tile, T, mean_b, rstd_b, eps=LN_EPS):
    nc = p.nc
    with p.scope():
        xc = p.sb([128, 512], F32, name=_nm("lnx"))
        sq = p.sb([128, 512], F32, name=_nm("lnsq"))
        mrow = p.sb([1, T], F32, name=_nm("lnm"))
        rrow = p.sb([1, T], F32, name=_nm("lnr"))
        pm, pv = p.pb[5], p.pb[6]
        for t0 in range(0, T, 512):
            for c in range(DC):
                load_tile(c, t0, 512, xc)
                p.bar()
                nc.scalar.activation(sq[:, :], xc[:, :], AF.Square)
                p.bar()
                nc.tensor.matmul(pm[:1, :], p.ones_col[:, :], xc[:, :],
                                 start=(c == 0), stop=(c == DC - 1))
                nc.tensor.matmul(pv[:1, :], p.ones_col[:, :], sq[:, :],
                                 start=(c == 0), stop=(c == DC - 1))
                p.bar()
            nc.vector.tensor_scalar(mrow[:1, t0:t0 + 512], pm[:1, :512],
                                    1.0 / D, None, ALU.mult)
            nc.vector.tensor_scalar(rrow[:1, t0:t0 + 512], pv[:1, :512],
                                    1.0 / D, None, ALU.mult)
            p.bar()
        msq = p.sb([1, T], F32, name=_nm("lnms"))
        nc.scalar.activation(msq[:1, :], mrow[:1, :], AF.Square)
        p.bar()
        nc.vector.tensor_tensor(rrow[:1, :], rrow[:1, :], msq[:1, :], ALU.subtract)
        p.bar()
        nc.vector.tensor_scalar(rrow[:1, :], rrow[:1, :], float(eps), None, ALU.add)
        p.bar()
        nc.scalar.activation(rrow[:1, :], rrow[:1, :], AF.Sqrt)
        p.bar()
        nc.vector.reciprocal(rrow[:1, :], rrow[:1, :])
        p.bar()
        p.bcast_row(mean_b, mrow, T)
        p.bcast_row(rstd_b, rrow, T)


def emit_prelude(p, xT_hbm, xnT_f32_hbm, xnT_bf16_hbm, T=TT, need_f32=True):
    nc = p.nc
    with p.scope():
        mean_b = p.sb([128, T], F32, name=_nm("pmb"))
        rstd_b = p.sb([128, T], F32, name=_nm("prb"))

        def load_tile(c, t0, w, tile):
            p.dma(tile[:, :w], xT_hbm[c * 128:(c + 1) * 128, t0:t0 + w])

        emit_ln_stats(p, load_tile, T, mean_b, rstd_b)
        xc = p.sb([128, T], F32, name=_nm("pxc"))
        xnb = p.sb([128, T], BF16, name=_nm("pxb"))
        for c in range(DC):
            p.dma(xc[:, :], chunk(xT_hbm, c))
            p.bar()
            nc.vector.tensor_tensor(xc[:, :], xc[:, :], mean_b[:, :], ALU.subtract)
            p.bar()
            nc.vector.tensor_tensor(xc[:, :], xc[:, :], rstd_b[:, :], ALU.mult)
            p.bar()
            nc.vector.tensor_copy(xnb[:, :], xc[:, :])
            if need_f32:
                p.dma(chunk(xnT_f32_hbm, c), xc[:, :])
            p.bar()
            p.dma(chunk(xnT_bf16_hbm, c), xnb[:, :])
            p.bar()


def emit_glue(p, xT, sT, gT, xoutT, T=TT):
    nc = p.nc
    with p.scope():
        a = p.sb([128, T], F32, name=_nm("ga"))
        b = p.sb([128, T], F32, name=_nm("gb"))
        cc = p.sb([128, T], F32, name=_nm("gc"))
        for c in range(DC):
            p.dma(a[:, :], chunk(xT, c))
            p.dma(b[:, :], chunk(sT, c))
            p.dma(cc[:, :], chunk(gT, c))
            p.bar()
            nc.vector.tensor_tensor(b[:, :], b[:, :], cc[:, :], ALU.mult)
            p.bar()
            nc.vector.tensor_tensor(a[:, :], a[:, :], b[:, :], ALU.add)
            p.bar()
            p.dma(chunk(xoutT, c), a[:, :])
            p.bar()


def load_chunks_bf16(p, hbm_ap, T, names="xc"):
    tiles = [p.sb([128, T], BF16, name=f"{_nm(names)}_{c}") for c in range(DC)]
    for c in range(DC):
        p.dma(tiles[c][:, :], chunk(hbm_ap, c))
    p.bar()
    return tiles


def emit_attn_head(p, kT, qT, kaux, qaux, vtile, nkv_tiles, Tq, out_cb,
                   dh=64, dv=64, causal_t0=None, dmasks=None, gate_sel=None,
                   q_tile=512, chunk_plan=None):
    nc = p.nc
    with p.scope():
        praw = p.sb([128, 4 * q_tile], F32, name=_nm("apr"))
        ptil = p.sb([128, 4 * q_tile], BF16, name=_nm("apt"))
        rrow = p.sb([1, q_tile], F32, name=_nm("arr"))
        o1 = p.sb([128, q_tile], F32, name=_nm("ao1"))
        oacc = p.sb([128, q_tile], F32, name=_nm("aoa"))
        dacc = p.sb([1, q_tile], F32, name=_nm("ada"))
        for t0 in range(0, Tq, q_tile):
            w = min(q_tile, Tq - t0)
            if chunk_plan is not None:
                plan = chunk_plan(t0, w)
            elif causal_t0 is None:
                plan = [(i, None) for i in range(nkv_tiles)]
            else:
                qhi = causal_t0(t0) + w
                nkv = min(nkv_tiles, (qhi + 127) // 128)
                plan = []
                for i in range(nkv):
                    kv0 = i * 128
                    m = None
                    qpos0 = causal_t0(t0)
                    r = (kv0 - qpos0) // 128
                    if 0 <= r <= 3 and kv0 + 128 > qpos0:
                        m = dmasks[:, r * q_tile:r * q_tile + w]
                    plan.append((i, m))
            first = True
            for g0 in range(0, len(plan), 4):
                grp = plan[g0:g0 + 4]
                gn = len(grp)
                for u in range(gn):
                    kv0 = grp[u][0] * 128
                    nc.tensor.matmul(p.pb[u][:, :w], kT[:, kv0:kv0 + 128],
                                     qT[:, t0:t0 + w], start=True, stop=False)
                    nc.tensor.matmul(p.pb[u][:, :w], kaux[:2, kv0:kv0 + 128],
                                     qaux[:2, t0:t0 + w], start=False, stop=True)
                p.bar()
                for u in range(gn):
                    nc.scalar.activation(praw[:, u * q_tile:u * q_tile + w],
                                         p.pb[u][:, :w], AF.Exp)
                p.bar()
                for u in range(gn):
                    m = grp[u][1]
                    sl = slice(u * q_tile, u * q_tile + w)
                    if m is not None:
                        nc.vector.tensor_tensor(ptil[:, sl], praw[:, sl], m,
                                                ALU.mult)
                    else:
                        nc.vector.tensor_copy(ptil[:, sl], praw[:, sl])
                p.bar()
                for u in range(gn):
                    i = grp[u][0]
                    nc.tensor.matmul(p.pb[4][:dv, :w], vtile(i),
                                     ptil[:, u * q_tile:u * q_tile + w],
                                     start=(u == 0), stop=(u == gn - 1))
                    nc.tensor.matmul(p.pb[5][:1, :w], p.ones_col_b[:, :],
                                     ptil[:, u * q_tile:u * q_tile + w],
                                     start=(u == 0), stop=(u == gn - 1))
                p.bar()
                if first:
                    nc.vector.tensor_copy(oacc[:dv, :w], p.pb[4][:dv, :w])
                    nc.scalar.copy(dacc[:1, :w], p.pb[5][:1, :w])
                    first = False
                else:
                    nc.vector.scalar_tensor_tensor(
                        oacc[:dv, :w], p.pb[4][:dv, :w], 1.0,
                        oacc[:dv, :w], ALU.mult, ALU.add)
                    p.bar()
                    nc.vector.scalar_tensor_tensor(
                        dacc[:1, :w], p.pb[5][:1, :w], 1.0,
                        dacc[:1, :w], ALU.mult, ALU.add)
                p.bar()
            nc.vector.reciprocal(rrow[:1, :w], dacc[:1, :w])
            p.bar()
            nc.tensor.matmul(p.pb[5][:dv, :w], p.ones_row[:1, :dv],
                             rrow[:1, :w], start=True, stop=True)
            if gate_sel is not None:
                gates_ap, selmat_ap, col = gate_sel
                nc.tensor.matmul(p.pb[6][:dv, :w],
                                 selmat_ap[:, col * dv:(col + 1) * dv],
                                 gates_ap[:, t0:t0 + w], start=True, stop=True)
            p.bar()
            nc.vector.tensor_tensor(o1[:dv, :w], oacc[:dv, :w], p.pb[5][:dv, :w],
                                    ALU.mult)
            p.bar()
            if gate_sel is not None:
                nc.vector.tensor_tensor(o1[:dv, :w], o1[:dv, :w],
                                        p.pb[6][:dv, :w], ALU.mult)
                p.bar()
            out_cb(t0, w, o1)
            p.bar()


def _head_proj2(p, wcols_sb, KC, rhs_chunks, T, out_sb, row0, nw, scale=None,
                wait_dma=True, bias_col=None):
    nc = p.nc
    ssim, sev = p.sem("s_proj"), p.sem("s_evac")
    bp, be = p.cnt["s_proj"], p.cnt["s_evac"]
    nt = T // 512
    for ti in range(nt):
        t0 = ti * 512
        bank = p.pb[ti % 2]
        if ti == 0 and wait_dma:
            nc.tensor.wait_ge(p.dma_sem, p.dma_issued)
        if ti >= 2:
            nc.tensor.wait_ge(sev, be + ti - 1)
        for c in range(KC):
            mm = nc.tensor.matmul(bank[:nw, :], wcols_sb[:, c * nw:(c + 1) * nw],
                                  rhs_chunks[c][:, t0:t0 + 512],
                                  start=(c == 0), stop=(c == KC - 1))
        mm.then_inc(ssim, 1)
        p.cnt["s_proj"] += 1
        nc.vector.wait_ge(ssim, bp + ti + 1)
        dst = out_sb[row0:row0 + nw, t0:t0 + 512]
        if bias_col is not None:
            if scale is None:
                cp = nc.vector.tensor_scalar(dst, bank[:nw, :], bias_col, None,
                                             ALU.add)
            else:
                cp = nc.vector.tensor_scalar(dst, bank[:nw, :], bias_col,
                                             float(scale), ALU.add, ALU.mult)
        elif scale is None:
            cp = nc.vector.tensor_copy(dst, bank[:nw, :])
        else:
            cp = nc.vector.tensor_scalar(dst, bank[:nw, :], float(scale), None,
                                         ALU.mult)
        cp.then_inc(sev, 1)
        p.cnt["s_evac"] += 1


def emit_vbuf2(p, vT, vbuf3, n, dv, f32=False):
    nc = p.nc
    stp, svc = p.sem("s_tp"), p.sem("s_vc")
    bt, bc = p.cnt["s_tp"], p.cnt["s_vc"]
    if f32:
        banks = [p.pb[5], p.pb[6]]
        ident = p.identf
    else:
        banks = [p.pb[5].bitcast(BF16), p.pb[6].bitcast(BF16)]
        ident = p.identb
    for i in range(n):
        bk = banks[i % 2]
        if i >= 2:
            nc.tensor.wait_ge(svc, bc + i - 1)
        tp = nc.tensor.transpose(bk[:, 0:dv], vT[:dv, i * 128:(i + 1) * 128],
                                 ident[:dv, :dv])
        tp.then_inc(stp, 1)
        p.cnt["s_tp"] += 1
        nc.vector.wait_ge(stp, bt + i + 1)
        nc.vector.tensor_copy(vbuf3[:, i, 0:dv], bk[:, 0:dv]).then_inc(svc, 1)
        p.cnt["s_vc"] += 1
    p.bar()


def emit_attn_head2(p, kaug, qaug, vtile, nkv_tiles, Tq, out_cb, dh, dv,
                    causal_t0=None, dmaskadd=None, gate_sel=None,
                    chunk_plan=None, aux2=None):
    nc = p.nc
    QT = 512
    big = aux2 is not None
    ssim, sexp, sav = p.sem("s_sim"), p.sem("s_exp"), p.sem("s_av")
    with p.scope():
        ptil = p.sb([128, 2 * 3 * QT], BF16, name=_nm("a2p"))
        rrow = p.sb([1, QT], F32, name=_nm("a2r"))
        oacc = p.sb([128, QT], F32, name=_nm("a2o"))
        o1 = p.sb([128, QT], F32, name=_nm("a2q"))
        banksets = [(0, 1), (2, 3)] if big else [(0, 1, 2), (3, 5, 6)]
        GSZ = 2 if big else 3
        for t0 in range(0, Tq, QT):
            w = QT
            if chunk_plan is not None:
                plan = chunk_plan(t0, w)
            elif causal_t0 is None:
                plan = [(i, None) for i in range(nkv_tiles)]
            else:
                qpos0 = causal_t0(t0)
                nkv = min(nkv_tiles, (qpos0 + w + 127) // 128)
                plan = []
                for i in range(nkv):
                    kv0 = i * 128
                    r = (kv0 - qpos0) // 128
                    m = r if (0 <= r <= 3 and kv0 + 128 > qpos0) else None
                    plan.append((i, m))
            groups = [plan[i:i + GSZ] for i in range(0, len(plan), GSZ)]
            G = len(groups)
            nch = len(plan)
            b_sim, b_exp, b_av = p.cnt["s_sim"], p.cnt["s_exp"], p.cnt["s_av"]

            def emit_sim(g):
                grp, bs = groups[g], banksets[g % 2]
                if g >= 2:
                    nc.tensor.wait_ge(sexp, b_exp + g - 1)
                for u, (i, r) in enumerate(grp):
                    bk = p.pb[bs[u]]
                    if big:
                        kaux, qaux = aux2
                        nc.tensor.matmul(bk[:, :w], kaug[:, i * 128:(i + 1) * 128],
                                         qaug[:, t0:t0 + w], start=True, stop=False)
                        mm = nc.tensor.matmul(bk[:, :w],
                                              kaux[0:2, i * 128:(i + 1) * 128],
                                              qaux[0:2, t0:t0 + w],
                                              start=False, stop=True)
                    else:
                        mm = nc.tensor.matmul(bk[:, :w],
                                              kaug[0:66, i * 128:(i + 1) * 128],
                                              qaug[0:66, t0:t0 + w],
                                              start=True, stop=(r is None))
                        if r is not None:
                            mm = nc.tensor.matmul(bk[:, :w], p.identb[:, :],
                                                  dmaskadd[:, r * QT:r * QT + w],
                                                  start=False, stop=True)
                    if u == len(grp) - 1:
                        mm.then_inc(ssim, 1)
                p.cnt["s_sim"] += 1

            def emit_exp(g):
                grp, bs = groups[g], banksets[g % 2]
                pp = (g % 2) * 3 * QT
                nc.scalar.wait_ge(ssim, b_sim + g + 1)
                if g >= 2:
                    nc.scalar.wait_ge(sav, b_av + g - 1)
                for u in range(len(grp)):
                    a = nc.scalar.activation(ptil[:, pp + u * QT:pp + u * QT + w],
                                             p.pb[bs[u]][:, :w], AF.Exp)
                a.then_inc(sexp, 1)
                p.cnt["s_exp"] += 1

            ci = 0

            def emit_av(g):
                nonlocal ci
                grp = groups[g]
                pp = (g % 2) * 3 * QT
                nc.tensor.wait_ge(sexp, b_exp + g + 1)
                for u, (i, r) in enumerate(grp):
                    st, sp = (ci == 0), (ci == nch - 1)
                    if big:
                        nc.tensor.matmul(p.pb[4][:dv, :w], vtile(i),
                                         ptil[:, pp + u * QT:pp + u * QT + w],
                                         start=st, stop=sp)
                        mm = nc.tensor.matmul(p.pb[5][:1, :w], p.ones_col_b[:, :],
                                              ptil[:, pp + u * QT:pp + u * QT + w],
                                              start=st, stop=sp)
                    else:
                        mm = nc.tensor.matmul(p.pb[4][:dv + 2, :w], vtile(i),
                                              ptil[:, pp + u * QT:pp + u * QT + w],
                                              start=st, stop=sp)
                    ci += 1
                mm.then_inc(sav, 1)
                p.cnt["s_av"] += 1

            emit_sim(0)
            if G > 1:
                emit_sim(1)
            for g in range(G):
                emit_exp(g)
                emit_av(g)
                if g + 2 < G:
                    emit_sim(g + 2)
            p.bar()
            if big:
                nc.vector.reciprocal(rrow[:1, :w], p.pb[5][0:1, :w])
            else:
                nc.vector.reciprocal(rrow[:1, :w], p.pb[4][dv:dv + 1, :w])
            nc.scalar.copy(oacc[:dv, :w], p.pb[4][:dv, :w])
            p.bar()
            nc.tensor.matmul(p.pb[5][:dv, :w], p.ones_row[:1, :dv],
                             rrow[:1, :w], start=True, stop=True)
            if gate_sel is not None:
                gates_ap, selmat_ap, col = gate_sel
                nc.tensor.matmul(p.pb[6][:dv, :w],
                                 selmat_ap[:, col * dv:(col + 1) * dv],
                                 gates_ap[:, t0:t0 + w], start=True, stop=True)
            p.bar()
            nc.vector.tensor_tensor(o1[:dv, :w], oacc[:dv, :w], p.pb[5][:dv, :w],
                                    ALU.mult)
            p.bar()
            if gate_sel is not None:
                nc.vector.tensor_tensor(o1[:dv, :w], o1[:dv, :w],
                                        p.pb[6][:dv, :w], ALU.mult)
                p.bar()
            out_cb(t0, w, o1)
            p.bar()


def emit_bbound(p, qpT, kpT, dh, Tq, Tkv, negb_out, margin=2.0):
    nc = p.nc
    ssq, sev = p.sem("s_proj"), p.sem("s_evac")
    with p.scope():
        sq = p.sb([128, max(Tq, Tkv)], F32, name=_nm("bbs"))
        krow = p.sb([1, Tkv], F32, name=_nm("bbr"))
        kmax = p.sb([1, 1], F32, name=_nm("bbk"))

        def norms(src, T_, out_row):
            bq, be = p.cnt["s_proj"], p.cnt["s_evac"]
            nc.scalar.activation(sq[:dh, :T_], src[:dh, :], AF.Square)\
                .then_inc(ssq, 1)
            p.cnt["s_proj"] += 1
            nc.tensor.wait_ge(ssq, bq + 1)
            nt = T_ // 512
            for ti in range(nt):
                if ti >= 2:
                    nc.tensor.wait_ge(sev, be + ti - 1)
                nc.tensor.matmul(p.pb[ti % 2][:1, :], p.ones_col[:dh, :],
                                 sq[:dh, ti * 512:(ti + 1) * 512],
                                 start=True, stop=True).then_inc(ssq, 1)
                p.cnt["s_proj"] += 1
                nc.vector.wait_ge(ssq, bq + ti + 2)
                nc.vector.tensor_copy(out_row[:1, ti * 512:(ti + 1) * 512],
                                      p.pb[ti % 2][:1, :512]).then_inc(sev, 1)
                p.cnt["s_evac"] += 1
            nc.vector.wait_ge(sev, p.cnt["s_evac"])

        norms(kpT, Tkv, krow)
        nc.vector.tensor_reduce(kmax[:1, :1], krow[:1, :], AX.X, ALU.max)
        p.bar()
        nc.scalar.activation(kmax[:1, :1], kmax[:1, :1], AF.Sqrt)
        norms(qpT, Tq, negb_out)
        p.bar()
        nc.scalar.activation(negb_out[:1, :Tq], negb_out[:1, :Tq], AF.Sqrt)
        p.bar()
        nc.vector.tensor_scalar(negb_out[:1, :Tq], negb_out[:1, :Tq],
                                kmax[:1, :1], None, ALU.mult)
        p.bar()
        nc.vector.tensor_scalar(negb_out[:1, :Tq], negb_out[:1, :Tq], -1.0,
                                float(-margin), ALU.mult, ALU.add)
        p.bar()


def emit_l2_attn(p, xnT_f32, u_a, u_b, WkT, WvT, WrT, WoT, lnw, lnb, T=TT):
    nc = p.nc
    u_cur, u_next = xnT_f32, u_a
    yT_hbm = p.dram([D, T], BF16).ap()
    for sub in range(2):
        with p.scope():
            normedT = [p.sb([128, T], BF16, name=f"l2n{sub}_{c}") for c in range(DC)]
            with p.scope():
                mean_b = p.sb([128, T], F32, name=_nm("l2mb"))
                rstd_b = p.sb([128, T], F32, name=_nm("l2rb"))

                def load_u(c, t0, w, tile, u=u_cur):
                    p.dma(tile[:, :w], u[c * 128:(c + 1) * 128, t0:t0 + w])

                emit_ln_stats(p, load_u, T, mean_b, rstd_b)
                uc = p.sb([128, T], F32, name=_nm("l2uc"))
                lnw_sb = p.sb([128, 1], F32, name=_nm("l2lw"))
                lnb_sb = p.sb([128, 1], F32, name=_nm("l2lb"))
                for c in range(DC):
                    p.dma(uc[:, :], chunk(u_cur, c))
                    p.dma(lnw_sb[:, :], lnw[sub, c * 128:(c + 1) * 128]
                          .rearrange("(a b) -> a b", b=1))
                    p.dma(lnb_sb[:, :], lnb[sub, c * 128:(c + 1) * 128]
                          .rearrange("(a b) -> a b", b=1))
                    p.bar()
                    nc.vector.tensor_tensor(uc[:, :], uc[:, :], mean_b[:, :],
                                            ALU.subtract)
                    p.bar()
                    nc.vector.tensor_tensor(uc[:, :], uc[:, :], rstd_b[:, :],
                                            ALU.mult)
                    p.bar()
                    nc.vector.tensor_scalar(uc[:, :], uc[:, :], lnw_sb[:, :],
                                            lnb_sb[:, :], ALU.mult, ALU.add)
                    p.bar()
                    nc.vector.tensor_copy(normedT[c][:, :], uc[:, :])
                    p.bar()
            with p.scope():
                wk = WMat(p, WkT[sub, :, :], D, D, name=_nm("l2wk"))
                wv = WMat(p, WvT[sub, :, :], D, D, name=_nm("l2wv"))
                wr = WMat(p, WrT[sub, :, :], D, D, name=_nm("l2wr"))
                kv05 = p.sb([128, T], F32, name=_nm("l2kv"))
                rsg = p.sb([128, T], F32, name=_nm("l2rs"))
                sts = p.sb([128, T], F32, name=_nm("l2st"))
                dec = p.sb([128, T], F32, name=_nm("l2dc"))
                ksb = p.sb([128, 512], F32, name=_nm("l2ks"))
                ybf = p.sb([128, T], BF16, name=_nm("l2yb"))
                nc.gpsimd.memset(dec[:, :], DEC95)
                p.bar()
                for e in range(DC):
                    for t0 in range(0, T, 512):
                        kp, vp, rp = p.pb[0], p.pb[1], p.pb[2]
                        for c in range(DC):
                            st, sp = (c == 0), (c == DC - 1)
                            nc.tensor.matmul(kp[:, :], wk.lhsT(c, e * 128),
                                             normedT[c][:, t0:t0 + 512],
                                             start=st, stop=sp)
                            nc.tensor.matmul(vp[:, :], wv.lhsT(c, e * 128),
                                             normedT[c][:, t0:t0 + 512],
                                             start=st, stop=sp)
                            nc.tensor.matmul(rp[:, :], wr.lhsT(c, e * 128),
                                             normedT[c][:, t0:t0 + 512],
                                             start=st, stop=sp)
                        p.bar()
                        nc.scalar.copy(ksb[:, :], kp[:, :])
                        nc.vector.tensor_copy(rsg[:, t0:t0 + 512], rp[:, :])
                        p.bar()
                        nc.scalar.activation(rsg[:, t0:t0 + 512], rsg[:, t0:t0 + 512],
                                             AF.Sigmoid)
                        nc.vector.scalar_tensor_tensor(kv05[:, t0:t0 + 512], vp[:, :],
                                                       DEC05, ksb[:, :],
                                                       ALU.mult, ALU.mult)
                        p.bar()
                    nc.vector.tensor_tensor_scan(sts[:, :], dec[:, :], kv05[:, :],
                                                 0.0, ALU.mult, ALU.add)
                    p.bar()
                    nc.vector.tensor_tensor(sts[:, :], sts[:, :], rsg[:, :], ALU.mult)
                    p.bar()
                    nc.vector.tensor_copy(ybf[:, :], sts[:, :])
                    p.bar()
                    p.dma(chunk(yT_hbm, e), ybf[:, :])
                    p.bar()
            with p.scope():
                wo = WMat(p, WoT[sub, :, :], D, D, name=_nm("l2wo"))
                ybuf = [p.sb([128, 512], BF16, name=_nm("l2yt")) for _ in range(DC)]
                usb = p.sb([128, 512], F32, name=_nm("l2us"))
                for t0 in range(0, T, 512):
                    for c in range(DC):
                        p.dma(ybuf[c][:, :], yT_hbm[c * 128:(c + 1) * 128, t0:t0 + 512])
                    p.bar()
                    for e in range(DC):
                        pbk = p.pb[e % 2]
                        for c in range(DC):
                            nc.tensor.matmul(pbk[:, :], wo.lhsT(c, e * 128),
                                             ybuf[c][:, :],
                                             start=(c == 0), stop=(c == DC - 1))
                        p.dma(usb[:, :], u_cur[e * 128:(e + 1) * 128, t0:t0 + 512])
                        p.bar()
                        nc.vector.tensor_tensor(usb[:, :], usb[:, :], pbk[:, :],
                                                ALU.add)
                        p.bar()
                        p.dma(u_next[e * 128:(e + 1) * 128, t0:t0 + 512], usb[:, :])
                        p.bar()
        u_cur, u_next = u_next, u_b
    return u_cur


def emit_l2_ffn(p, xnT_bf16, wgT, wuT, wdT, gT_hbm, T=TT):
    nc = p.nc
    H3 = 3072
    inv127 = 1.0 / 127.0
    with p.scope():
        xnb = load_chunks_bf16(p, xnT_bf16, T, "f2x")
        wg = WSlice(p, wgT, D, name="f2wg")
        wu = WSlice(p, wuT, D, name="f2wu")
        wd = WSlice(p, wdT, H3, name="f2wd")
        TB_ = 1024
        hT = [p.sb([128, TB_], BF16, name=f"f2ht{i}") for i in range(H3 // 128)]
        h1 = p.sb([128, 512], F32, name="f2h1")
        gsb = p.sb([128, 512], F32, name="f2g")
        c127 = p.cst(inv127, name="c127")
        for tb0 in range(0, T, TB_):
            for hc in range(H3 // 128):
                wg.load(hc * 128)
                wu.load(hc * 128)
                p.bar()
                for t0 in range(0, TB_, 512):
                    gp, up = p.pb[0], p.pb[1]
                    for c in range(DC):
                        st, sp = (c == 0), (c == DC - 1)
                        nc.tensor.matmul(gp[:, :], wg.lhsT(c),
                                         xnb[c][:, tb0 + t0:tb0 + t0 + 512],
                                         start=st, stop=sp)
                        nc.tensor.matmul(up[:, :], wu.lhsT(c),
                                         xnb[c][:, tb0 + t0:tb0 + t0 + 512],
                                         start=st, stop=sp)
                    p.bar()
                    nc.scalar.activation(h1[:, :], gp[:, :], AF.Silu,
                                         scale=c127[:, :])
                    p.bar()
                    nc.vector.scalar_tensor_tensor(hT[hc][:, t0:t0 + 512], up[:, :],
                                                   inv127, h1[:, :], ALU.mult, ALU.mult)
                    p.bar()
            for e in range(DC):
                wd.load(e * 128)
                p.bar()
                for t0 in range(0, TB_, 512):
                    dp = p.pb[2]
                    for hc in range(H3 // 128):
                        nc.tensor.matmul(dp[:, :], wd.lhsT(hc), hT[hc][:, t0:t0 + 512],
                                         start=(hc == 0), stop=(hc == H3 // 128 - 1))
                    p.bar()
                    nc.scalar.activation(gsb[:, :], dp[:, :], AF.Silu,
                                         scale=c127[:, :])
                    p.bar()
                    p.dma(gT_hbm[e * 128:(e + 1) * 128, tb0 + t0:tb0 + t0 + 512],
                          gsb[:, :])
                    p.bar()


def build_layer2(T=TT, dbg=False):
    nc = bass.Bass("TRN2", target_bir_lowering=False)
    xT = nc.dram_tensor("xT", [D, T], F32, kind="ExternalInput").ap()
    WkT = nc.dram_tensor("WkT", [2, D, D], F32, kind="ExternalInput").ap()
    WvT = nc.dram_tensor("WvT", [2, D, D], F32, kind="ExternalInput").ap()
    WrT = nc.dram_tensor("WrT", [2, D, D], F32, kind="ExternalInput").ap()
    WoT = nc.dram_tensor("WoT", [2, D, D], F32, kind="ExternalInput").ap()
    lnw = nc.dram_tensor("lnw", [2, D], F32, kind="ExternalInput").ap()
    lnb = nc.dram_tensor("lnb", [2, D], F32, kind="ExternalInput").ap()
    wgT = nc.dram_tensor("wgT", [D, 3072], I32, kind="ExternalInput").ap()
    wuT = nc.dram_tensor("wuT", [D, 3072], I32, kind="ExternalInput").ap()
    wdT = nc.dram_tensor("wdT", [3072, D], I32, kind="ExternalInput").ap()
    xoutT = nc.dram_tensor("xoutT", [D, T], F32, kind="ExternalOutput").ap()

    p = Prog(nc)
    xnT_f32 = p.dram([D, T], F32).ap()
    xnT_bf16 = p.dram([D, T], BF16).ap()
    u_a = p.dram([D, T], F32).ap()
    u_b = p.dram([D, T], F32).ap()
    gT = p.dram([D, T], F32).ap()
    emit_prelude(p, xT, xnT_f32, xnT_bf16, T)
    sT = emit_l2_attn(p, xnT_f32, u_a, u_b, WkT, WvT, WrT, WoT, lnw, lnb, T)
    emit_l2_ffn(p, xnT_bf16, wgT, wuT, wdT, gT, T)
    emit_glue(p, xT, sT, gT, xoutT, T)
    if dbg:
        for nm, src in (("s_dbg", sT), ("g_dbg", gT), ("xn_dbg", xnT_f32)):
            o = nc.dram_tensor(nm, [D, T], F32, kind="ExternalOutput").ap()
            p.dma(o, src)
        p.bar()
    p.close()
    return nc


def emit_l0_route(p, xnT_f32, rts, qmask_row, m_row_out, T=TT, NSEL=1024):
    nc = p.nc
    scr_hbm = p.dram([2, T], F32).ap()
    with p.scope():
        rtsb = p.sb([128, DC * 2], F32, name="l0rt")
        p.dma(rtsb.ap().rearrange("p (c n) -> p c n", c=DC),
              rts.rearrange("(c p) n -> p c n", p=128))
        xt = p.sb([128, 512], F32, name="l0xt")
        scores = p.sb([2, T], F32, name="l0sc")
        p.bar()
        for t0 in range(0, T, 512):
            pbk = p.pb[0]
            for c in range(DC):
                p.dma(xt[:, :], xnT_f32[c * 128:(c + 1) * 128, t0:t0 + 512])
                p.bar()
                nc.tensor.matmul(pbk[:2, :], rtsb.ap().rearrange(
                    "p (c n) -> p c n", c=DC)[:, c, :], xt[:, :],
                    start=(c == 0), stop=(c == DC - 1))
                p.bar()
            nc.vector.tensor_copy(scores[:2, t0:t0 + 512], pbk[:2, :512])
            p.bar()
        p.dma(scr_hbm, scores[:2, :])
        p.bar()
        s2 = p.sb([128, 64], F32, name="l0s2")
        p.dma(s2[:, 0:32], scr_hbm[0:1, :].rearrange("a (q c) -> (a q) c", q=128))
        p.dma(s2[:, 32:64], scr_hbm[1:2, :].rearrange("a (q c) -> (a q) c", q=128))
        lo = p.sb([1, 2], F32, name="l0lo")
        hi = p.sb([1, 2], F32, name="l0hi")
        mid = p.sb([1, 2], F32, name="l0mi")
        thb = p.sb([128, 2], F32, name="l0th")
        ind = p.sb([128, 64], F32, name="l0in")
        cnt2 = p.sb([128, 2], F32, name="l0c2")
        cntc = p.sb([2, 1], F32, name="l0cc")
        crow = p.sb([1, 2], F32, name="l0cr")
        pred = p.sb([1, 2], F32, name="l0pd")
        nc.gpsimd.memset(lo[:1, :], -1e5)
        nc.gpsimd.memset(hi[:1, :], 1e5)
        p.bar()

        def count_gt(theta_row, out_crow):
            nc.tensor.matmul(p.pb[6][:, :2], p.ones_row[:1, :], theta_row[:1, :],
                             start=True, stop=True)
            p.bar()
            nc.vector.tensor_copy(thb[:, :], p.pb[6][:, :2])
            p.bar()
            nc.vector.tensor_scalar(ind[:, 0:32], s2[:, 0:32], thb[:, 0:1],
                                    None, ALU.is_gt)
            p.bar()
            nc.vector.tensor_scalar(ind[:, 32:64], s2[:, 32:64], thb[:, 1:2],
                                    None, ALU.is_gt)
            p.bar()
            nc.vector.tensor_reduce(cnt2[:, 0:1], ind[:, 0:32], AX.X, ALU.add)
            p.bar()
            nc.vector.tensor_reduce(cnt2[:, 1:2], ind[:, 32:64], AX.X, ALU.add)
            p.bar()
            nc.tensor.matmul(p.pb[5][:2, :1], cnt2[:, :], p.ones_col[:, :],
                             start=True, stop=True)
            p.bar()
            nc.vector.tensor_copy(cntc[:2, :1], p.pb[5][:2, :1])
            p.bar()
            nc.tensor.transpose(p.pbt2[:1, :2], cntc[:2, :1], p.identf[:2, :2])
            p.bar()
            nc.vector.tensor_copy(out_crow[:1, :2], p.pbt2[:1, :2])
            p.bar()

        for _ in range(46):
            nc.vector.tensor_tensor(mid[:1, :], lo[:1, :], hi[:1, :], ALU.add)
            p.bar()
            nc.vector.tensor_scalar(mid[:1, :], mid[:1, :], 0.5, None, ALU.mult)
            p.bar()
            count_gt(mid, crow)
            nc.vector.tensor_scalar(pred[:1, :], crow[:1, :], float(NSEL) - 0.5,
                                    None, ALU.is_gt)
            p.bar()
            nc.vector.tensor_tensor(crow[:1, :], mid[:1, :], lo[:1, :],
                                    ALU.subtract)
            p.bar()
            nc.vector.tensor_tensor(crow[:1, :], crow[:1, :], pred[:1, :],
                                    ALU.mult)
            p.bar()
            nc.vector.tensor_tensor(lo[:1, :], lo[:1, :], crow[:1, :], ALU.add)
            p.bar()
            nc.vector.tensor_tensor(crow[:1, :], hi[:1, :], mid[:1, :],
                                    ALU.subtract)
            p.bar()
            nc.vector.tensor_tensor(crow[:1, :], crow[:1, :], pred[:1, :],
                                    ALU.mult)
            p.bar()
            nc.vector.tensor_tensor(hi[:1, :], mid[:1, :], crow[:1, :], ALU.add)
            p.bar()
        neg = p.sb([128, 32], F32, name="l0ng")
        mh = p.sb([128, 2], F32, name="l0mh")
        tau_row = p.sb([1, 2], F32, name="l0ta")
        nc.gpsimd.memset(neg[:, :], -3e38)
        p.bar()
        nc.tensor.matmul(p.pb[6][:, :2], p.ones_row[:1, :], hi[:1, :],
                         start=True, stop=True)
        p.bar()
        nc.vector.tensor_copy(thb[:, :], p.pb[6][:, :2])
        p.bar()
        for r in range(2):
            half = s2[:, 32 * r:32 * (r + 1)]
            nc.vector.tensor_scalar(ind[:, 0:32], half, thb[:, r:r + 1], None,
                                    ALU.is_le)
            p.bar()
            nc.vector.tensor_tensor(ind[:, 32:64], half, ind[:, 0:32], ALU.mult)
            p.bar()
            nc.vector.tensor_scalar(ind[:, 0:32], ind[:, 0:32], 1.0, 3e38,
                                    ALU.subtract, ALU.mult)
            p.bar()
            nc.vector.tensor_tensor(ind[:, 32:64], ind[:, 32:64], ind[:, 0:32],
                                    ALU.add)
            p.bar()
            nc.vector.tensor_reduce(mh[:, r:r + 1], ind[:, 32:64], AX.X, ALU.max)
            p.bar()
        mh2 = p.sb([2, 128], F32, name="l0m2")
        nc.tensor.transpose(p.pbt2[:2, :128], mh[:, :2], p.identf[:, :])
        p.bar()
        nc.vector.tensor_copy(mh2[:2, :], p.pbt2[:2, :128])
        p.bar()
        nc.vector.tensor_reduce(cntc[:2, 0:1], mh2[:2, :], AX.X, ALU.max)
        p.bar()
        nc.tensor.transpose(p.pbt2[:1, :2], cntc[:2, :1], p.identf[:2, :2])
        p.bar()
        nc.vector.tensor_copy(tau_row[:1, :2], p.pbt2[:1, :2])
        p.bar()
        count_gt(tau_row, crow)
        k2row = p.sb([1, 2], F32, name="l0k2")
        nc.vector.tensor_scalar(k2row[:1, :], crow[:1, :], -1.0, float(NSEL),
                                ALU.mult, ALU.add)
        p.bar()
        tauc = p.sb([2, 1], F32, name="l0tc")
        k2c = p.sb([2, 1], F32, name="l0kc")
        nc.tensor.transpose(p.pbt2[:2, :1], tau_row[:1, :2], p.identf[:1, :1])
        p.bar()
        nc.vector.tensor_copy(tauc[:2, :1], p.pbt2[:2, :1])
        p.bar()
        nc.tensor.transpose(p.pbt2[:2, :1], k2row[:1, :2], p.identf[:1, :1])
        p.bar()
        nc.vector.tensor_copy(k2c[:2, :1], p.pbt2[:2, :1])
        p.bar()
        eq = p.sb([2, T], F32, name="l0eq")
        ones2 = p.sb([2, T], F32, name="l0o2")
        ps = p.sb([2, T], F32, name="l0ps")
        tot = p.sb([2, 1], F32, name="l0tt")
        msk = p.sb([2, T], F32, name="l0mk")
        nc.gpsimd.memset(ones2[:2, :], 1.0)
        p.bar()
        nc.vector.tensor_scalar(eq[:2, :], scores[:2, :], tauc[:2, :1], None,
                                ALU.is_equal)
        p.bar()
        nc.vector.tensor_tensor_scan(ps[:2, :], ones2[:2, :], eq[:2, :], 0.0,
                                     ALU.mult, ALU.add)
        p.bar()
        nc.vector.tensor_reduce(tot[:2, :1], eq[:2, :], AX.X, ALU.add)
        p.bar()
        nc.vector.tensor_scalar(ps[:2, :], ps[:2, :], tot[:2, :1], -1.0,
                                ALU.subtract, ALU.mult)
        p.bar()
        nc.vector.tensor_scalar(ps[:2, :], ps[:2, :], k2c[:2, :1], None,
                                ALU.is_lt)
        p.bar()
        nc.vector.tensor_tensor(ps[:2, :], ps[:2, :], eq[:2, :], ALU.mult)
        p.bar()
        nc.vector.tensor_scalar(msk[:2, :], scores[:2, :], tauc[:2, :1], None,
                                ALU.is_gt)
        p.bar()
        nc.vector.tensor_tensor(msk[:2, :], msk[:2, :], ps[:2, :], ALU.add)
        p.bar()
        nc.vector.tensor_copy(qmask_row[:1, :], msk[0:1, :])
        p.bar()
        nc.vector.tensor_scalar(msk[:2, :], msk[:2, :], 1.0, 30000.0,
                                ALU.subtract, ALU.mult)
        p.bar()
        p.cdma(m_row_out, msk[1:2, :])
        p.bar()


def emit_l0_attn(p, xnT_f32, xnT_bf16, gamma_h, wq_h, wkv_h, wo_h, ln_g, ln_b,
                 wqkv_l, wo_l, rts, bandmask_hbm, sT_hbm, T=TT, dbg_outs=None):
    nc = p.nc
    H, DH = 8, 64
    SC = 0.125
    outl = p.dram([512, T], BF16).ap()
    outh = p.dram([512, T], BF16).ap()
    with p.scope():
        xlT = [p.sb([128, T], BF16, name=f"l0xl{c}") for c in range(DC)]
        with p.scope():
            mean_b = p.sb([128, T], F32, name="l0mb")
            rstd_b = p.sb([128, T], F32, name="l0rb")

            def load_tile(c, t0, w, tile):
                p.dma(tile[:, :w], xnT_f32[c * 128:(c + 1) * 128, t0:t0 + w])

            emit_ln_stats(p, load_tile, T, mean_b, rstd_b)
            xc = p.sb([128, T], F32, name="l0xc")
            gcol = p.sb([128, 1], F32, name="l0gc")
            bcol = p.sb([128, 1], F32, name="l0bc")
            for c in range(DC):
                p.dma(xc[:, :], xnT_f32[c * 128:(c + 1) * 128, :])
                p.dma(gcol[:, :], ln_g[c * 128:(c + 1) * 128]
                      .rearrange("(a b) -> a b", b=1))
                p.dma(bcol[:, :], ln_b[c * 128:(c + 1) * 128]
                      .rearrange("(a b) -> a b", b=1))
                p.bar()
                nc.vector.tensor_tensor(xc[:, :], xc[:, :], mean_b[:, :],
                                        ALU.subtract)
                p.bar()
                nc.vector.tensor_tensor(xc[:, :], xc[:, :], rstd_b[:, :], ALU.mult)
                p.bar()
                nc.vector.tensor_scalar(xc[:, :], xc[:, :], gcol[:, :],
                                        bcol[:, :], ALU.mult, ALU.add)
                p.bar()
                nc.vector.tensor_copy(xlT[c][:, :], xc[:, :])
                p.bar()
        with p.scope():
            wcols = p.sb([128, DC * DH], BF16, name="l0wc")
            qT = p.sb([DH, T], BF16, name="l0q")
            kT = p.sb([DH, T], BF16, name="l0k")
            vT = p.sb([DH, T], F32, name="l0v")
            vbuf = p.sb([128, (T // 128) * DH], BF16, name="l0vb")
            vbuf3 = vbuf.ap().rearrange("p (i v) -> p i v", v=DH)
            qaux = p.sb([2, T], BF16, name="l0qx")
            kaux = p.sb([2, T], BF16, name="l0kx")
            negb = p.sb([1, T], F32, name="l0nb")
            ob = p.sb([DH, 512], BF16, name="l0ob")
            bandm = p.sb([128, 6 * 512], BF16, name="l0bm")
            p.cdma(bandm[:, :], bandmask_hbm)
            nc.gpsimd.memset(kaux[:, :], 1.0)
            nc.gpsimd.memset(qaux[:, :], 1.0)
            p.bar()

            def bandplan(t0, w):
                c0 = t0 // 128
                plan = []
                for r in range(6):
                    ci = c0 - 1 + r
                    if 0 <= ci < T // 128:
                        plan.append((ci, bandm[:, r * 512:r * 512 + w]))
                return plan

            for hh in range(H):
                load_wcols(p, wcols, wqkv_l, D, hh * DH, DH)
                p.bar()
                _head_proj(p, wcols, DC, xlT, T, qT, 0, DH, scale=SC)
                load_wcols(p, wcols, wqkv_l, D, 512 + hh * DH, DH)
                p.bar()
                _head_proj(p, wcols, DC, xlT, T, kT, 0, DH)
                load_wcols(p, wcols, wqkv_l, D, 1024 + hh * DH, DH)
                p.bar()
                _head_proj(p, wcols, DC, xlT, T, vT, 0, DH)
                for i in range(T // 128):
                    nc.tensor.transpose(p.pbt2[:, :DH],
                                        vT[:DH, i * 128:(i + 1) * 128],
                                        p.identf[:DH, :DH])
                    p.bar()
                    nc.vector.tensor_copy(vbuf3[:, i, :], p.pbt2[:, :DH])
                    p.bar()
                emit_bbound(p, qT, kT, DH, T, T, negb)
                nc.vector.tensor_copy(qaux[0:1, :], negb[:1, :])
                p.bar()

                def out_cb(t0, w, o1, hh=hh):
                    nc.vector.tensor_copy(ob[:DH, :w], o1[:DH, :w])
                    p.bar()
                    p.dma(outl[hh * DH:(hh + 1) * DH, t0:t0 + w], ob[:DH, :w])

                emit_attn_head(p, kT, qT, kaux, qaux, lambda i: vbuf3[:, i, :],
                               T // 128, T, out_cb, dh=DH, dv=DH,
                               chunk_plan=bandplan)
    with p.scope():
        qmask_row = p.sb([1, T], F32, name="l0qm")
        kaugh = p.sb([66, T], BF16, name="l0kxh")
        qaugh = p.sb([66, T], BF16, name="l0qxh")
        nc.gpsimd.memset(kaugh[64:66, :], 1.0)
        nc.gpsimd.memset(qaugh[64:66, :], 1.0)
        p.bar()
        emit_l0_route(p, xnT_f32, rts, qmask_row, kaugh[65:66, :], T)
        if dbg_outs is not None:
            p.dma(dbg_outs["qm"], qmask_row[:1, :])
            p.dma(dbg_outs["kx"], kaugh[64:66, :])
            p.bar()
        hxT = [p.sb([128, T], BF16, name=f"l0hx{c}") for c in range(DC)]
        with p.scope():
            rstd_b = p.sb([128, T], F32, name="l0hr")
            with p.scope():
                xc2 = p.sb([128, 512], F32, name="l0x2")
                sq2 = p.sb([128, 512], F32, name="l0s2b")
                rrow = p.sb([1, T], F32, name="l0rr")
                pv = p.pb[5]
                for t0 in range(0, T, 512):
                    for c in range(DC):
                        p.dma(xc2[:, :], xnT_f32[c * 128:(c + 1) * 128,
                                                 t0:t0 + 512])
                        p.bar()
                        nc.scalar.activation(sq2[:, :], xc2[:, :], AF.Square)
                        p.bar()
                        nc.tensor.matmul(pv[:1, :], p.ones_col[:, :], sq2[:, :],
                                         start=(c == 0), stop=(c == DC - 1))
                        p.bar()
                    nc.vector.tensor_copy(rrow[:1, t0:t0 + 512], pv[:1, :512])
                    p.bar()
                nc.scalar.activation(rrow[:1, :], rrow[:1, :], AF.Sqrt)
                p.bar()
                nc.vector.tensor_scalar(rrow[:1, :], rrow[:1, :], 1e-12, None,
                                        ALU.max)
                p.bar()
                nc.vector.reciprocal(rrow[:1, :], rrow[:1, :])
                p.bar()
                nc.vector.tensor_scalar(rrow[:1, :], rrow[:1, :], 32.0, None,
                                        ALU.mult)
                p.bar()
                p.bcast_row(rstd_b, rrow, T)
            xc = p.sb([128, T], F32, name="l0xch")
            gcol = p.sb([128, 1], F32, name="l0gch")
            for c in range(DC):
                p.dma(xc[:, :], xnT_f32[c * 128:(c + 1) * 128, :])
                p.dma(gcol[:, :], gamma_h[c * 128:(c + 1) * 128]
                      .rearrange("(a b) -> a b", b=1))
                p.bar()
                nc.vector.tensor_tensor(xc[:, :], xc[:, :], rstd_b[:, :], ALU.mult)
                p.bar()
                nc.vector.tensor_scalar(hxT[c][:, :], xc[:, :], gcol[:, :], None,
                                        ALU.mult)
                p.bar()
        with p.scope():
            wcols = p.sb([128, DC * 64], BF16, name="l0wch")
            vT = p.sb([64, T], F32, name="l0vh")
            vbuf = p.sb([128, (T // 128) * 66], BF16, name="l0vbh")
            vbuf3 = vbuf.ap().rearrange("p (i v) -> p i v", v=66)
            negb = p.sb([1, T], F32, name="l0nbh")
            ob = p.sb([64, 512], BF16, name="l0obh")
            nc.gpsimd.memset(vbuf[:, :], 1.0)
            p.bar()
            for hh in range(H):
                load_wcols(p, wcols, wq_h, D, hh * 64, 64)
                _head_proj2(p, wcols, DC, hxT, T, qaugh, 0, 64, scale=SC)
                p.bar()
                load_wcols(p, wcols, wkv_h, D, hh * 64, 64)
                _head_proj2(p, wcols, DC, hxT, T, kaugh, 0, 64)
                p.bar()
                load_wcols(p, wcols, wkv_h, D, 512 + hh * 64, 64)
                _head_proj2(p, wcols, DC, hxT, T, vT, 0, 64)
                p.bar()
                emit_vbuf2(p, vT, vbuf3, T // 128, 64, f32=True)
                emit_bbound(p, qaugh[0:64, :], kaugh[0:64, :], 64, T, T, negb)
                nc.vector.tensor_copy(qaugh[64:65, :], negb[:1, :])
                p.bar()

                def out_cb(t0, w, o1, hh=hh):
                    nc.vector.tensor_copy(ob[:64, :w], o1[:64, :w])
                    p.bar()
                    p.dma(outh[hh * 64:(hh + 1) * 64, t0:t0 + w], ob[:64, :w])

                emit_attn_head2(p, kaugh, qaugh,
                                lambda i: vbuf3[:, i, :], T // 128, T, out_cb,
                                dh=64, dv=64,
                                gate_sel=(qmask_row, p.ones_row, 0))
    with p.scope():
        wol = p.sb([128, 4 * D], BF16, name="l0wol")
        woh = p.sb([128, 4 * D], BF16, name="l0woh")
        load_wmat(p, wol, wo_l, 512, D)
        load_wmat(p, woh, wo_h, 512, D)
        rl = [p.sb([128, 512], BF16, name=f"l0rl{c}") for c in range(4)]
        rh = [p.sb([128, 512], BF16, name=f"l0rh{c}") for c in range(4)]
        ssb = p.sb([128, 512], F32, name="l0ss")
        p.bar()
        for t0 in range(0, T, 512):
            for c in range(4):
                p.dma(rl[c][:, :], outl[c * 128:(c + 1) * 128, t0:t0 + 512])
                p.dma(rh[c][:, :], outh[c * 128:(c + 1) * 128, t0:t0 + 512])
            p.bar()
            for e in range(DC):
                pbk = p.pb[e % 2]
                for c in range(4):
                    nc.tensor.matmul(pbk[:, :], wm_lhsT(wol, 4, c, e * 128),
                                     rl[c][:, :], start=(c == 0), stop=False)
                for c in range(4):
                    nc.tensor.matmul(pbk[:, :], wm_lhsT(woh, 4, c, e * 128),
                                     rh[c][:, :], start=False, stop=(c == 3))
                p.bar()
                nc.vector.tensor_copy(ssb[:, :], pbk[:, :])
                p.bar()
                p.dma(sT_hbm[e * 128:(e + 1) * 128, t0:t0 + 512], ssb[:, :])
                p.bar()


def emit_fp4_quant(p, x_ap, out_ap, W, sc):
    nc = p.nc
    nb = W // 16
    x3 = x_ap.rearrange("p (b k) -> p b k", k=16)
    nc.vector.tensor_reduce(sc["bm"][:, :nb], x3, AX.X, ALU.max,
                            apply_absolute_value=True)
    p.bar()
    nc.vector.tensor_scalar(sc["bm"][:, :nb], sc["bm"][:, :nb], 1.0 / 6.0,
                            1e-12, ALU.mult, ALU.max)
    p.bar()
    nc.vector.reciprocal(sc["rc"][:, :nb], sc["bm"][:, :nb])
    p.bar()
    re3 = sc["re"].ap().rearrange("p (b k) -> p b k", k=16)
    for j in range(16):
        nc.vector.tensor_copy(re3[:, :nb, j], sc["rc"][:, :nb])
    p.bar()
    nc.vector.tensor_tensor(sc["xn"][:, :W], x_ap, sc["re"][:, :W], ALU.mult)
    p.bar()
    nc.scalar.activation(sc["a"][:, :W], sc["xn"][:, :W], AF.Abs)
    p.bar()
    nc.vector.tensor_scalar(sc["acc"][:, :W], sc["a"][:, :W], 0.25, None,
                            ALU.is_gt)
    p.bar()
    for m in (0.75, 1.25, 1.75):
        nc.vector.scalar_tensor_tensor(sc["acc"][:, :W], sc["a"][:, :W],
                                       float(m), sc["acc"][:, :W],
                                       ALU.is_gt, ALU.add)
        p.bar()
    nc.vector.tensor_scalar(sc["acc"][:, :W], sc["acc"][:, :W], 0.5, None,
                            ALU.mult)
    p.bar()
    for m in (2.5, 3.5):
        nc.vector.scalar_tensor_tensor(sc["acc"][:, :W], sc["a"][:, :W],
                                       float(m), sc["acc"][:, :W],
                                       ALU.is_gt, ALU.add)
        p.bar()
    nc.vector.tensor_scalar(sc["v"][:, :W], sc["a"][:, :W], 5.0, None, ALU.is_gt)
    p.bar()
    nc.vector.scalar_tensor_tensor(sc["acc"][:, :W], sc["v"][:, :W], 2.0,
                                   sc["acc"][:, :W], ALU.mult, ALU.add)
    p.bar()
    nc.scalar.activation(sc["v"][:, :W], sc["xn"][:, :W], AF.Sign)
    p.bar()
    nc.vector.tensor_tensor(sc["acc"][:, :W], sc["acc"][:, :W], sc["v"][:, :W],
                            ALU.mult)
    p.bar()
    nc.vector.reciprocal(sc["re"][:, :W], sc["re"][:, :W])
    p.bar()
    nc.vector.tensor_tensor(out_ap, sc["acc"][:, :W], sc["re"][:, :W], ALU.mult)
    p.bar()


def mk_fp4_scratch(p, W, tag="q4"):
    return {
        "bm": p.sb([128, W // 16], F32, name=_nm(tag + "b")),
        "rc": p.sb([128, W // 16], F32, name=_nm(tag + "r")),
        "re": p.sb([128, W], F32, name=_nm(tag + "e")),
        "xn": p.sb([128, W], F32, name=_nm(tag + "n")),
        "a": p.sb([128, W], F32, name=_nm(tag + "a")),
        "acc": p.sb([128, W], F32, name=_nm(tag + "c")),
        "v": p.sb([128, W], F32, name=_nm(tag + "v")),
    }


def emit_l0_ffn(p, xnT_f32, w_gate, w_up, w_down, gT_hbm, T=TT):
    nc = p.nc
    HN = 2816
    wqgT = p.dram([D, HN], BF16).ap()
    wquT = p.dram([D, HN], BF16).ap()
    wqdT = p.dram([HN, D], BF16).ap()
    hqT_hbm = p.dram([HN, T], BF16).ap()
    xqT_hbm = p.dram([D, T], BF16).ap()
    with p.scope():
        sc = mk_fp4_scratch(p, HN, "q4w")
        wt = p.sb([128, HN], F32, name="q4wt")
        wq = p.sb([128, HN], BF16, name="q4wq")
        for (w_src, w_dstT, nrow, ncol) in ((w_gate, wqgT, HN, D),
                                            (w_up, wquT, HN, D),
                                            (w_down, wqdT, D, HN)):
            for r0 in range(0, nrow, 128):
                p.dma(wt[:, :ncol], w_src[r0:r0 + 128, :])
                p.bar()
                emit_fp4_quant(p, wt[:, :ncol], wq[:, :ncol], ncol, sc)
                for c0 in range(0, ncol, 128):
                    nc.tensor.transpose(p.pbt[:, :128], wq[:, c0:c0 + 128],
                                        p.identb[:, :])
                    p.bar()
                    nc.vector.tensor_copy(wq[:, HN - 128:HN]
                                          if False else sc["v"]
                                          .ap().bitcast(BF16)[:, 0:128],
                                          p.pbt[:, :128])
                    p.bar()
                    p.dma(w_dstT[c0:c0 + 128, r0:r0 + 128],
                          sc["v"].ap().bitcast(BF16)[:, 0:128])
                    p.bar()
    with p.scope():
        sc = mk_fp4_scratch(p, D, "q4a")
        xtok = p.sb([128, D], F32, name="q4xt")
        xqtok = p.sb([128, D], BF16, name="q4xo")
        for i in range(T // 128):
            for c in range(DC):
                p.dma(xtok[:, c * 128:(c + 1) * 128]
                      if False else sc["xn"][:, 0:128],
                      xnT_f32[c * 128:(c + 1) * 128, i * 128:(i + 1) * 128])
                p.bar()
                nc.tensor.transpose(p.pbt2[:, :128], sc["xn"][:, 0:128],
                                    p.identf[:, :])
                p.bar()
                nc.vector.tensor_copy(xtok[:, c * 128:(c + 1) * 128],
                                      p.pbt2[:, :128])
                p.bar()
            emit_fp4_quant(p, xtok[:, :], xqtok[:, :], D, sc)
            for c in range(DC):
                nc.tensor.transpose(p.pbt[:, :128], xqtok[:, c * 128:(c + 1) * 128],
                                    p.identb[:, :])
                p.bar()
                nc.vector.tensor_copy(sc["v"].ap().bitcast(BF16)[:, 0:128],
                                      p.pbt[:, :128])
                p.bar()
                p.dma(xqT_hbm[c * 128:(c + 1) * 128, i * 128:(i + 1) * 128],
                      sc["v"].ap().bitcast(BF16)[:, 0:128])
                p.bar()
    with p.scope():
        wg = p.sb([128, DC * HN], BF16, name="q4wg")
        wu = p.sb([128, DC * HN], BF16, name="q4wu")
        load_wmat(p, wg, wqgT, D, HN)
        load_wmat(p, wu, wquT, D, HN)
        sc = mk_fp4_scratch(p, HN, "q4h")
        htok = p.sb([128, HN], F32, name="q4ht")
        hq = p.sb([128, HN], BF16, name="q4hq")
        h1 = p.sb([128, 512], F32, name="q4h1")
        xqt = p.sb([128, DC * 128], BF16, name="q4xi")
        p.bar()
        for i in range(T // 128):
            p.dma(xqt.ap().rearrange("p (c n) -> p c n", c=DC),
                  xqT_hbm[:, i * 128:(i + 1) * 128]
                  .rearrange("(c p) n -> p c n", p=128))
            p.bar()
            for n0 in range(0, HN, 512):
                nw = min(512, HN - n0)
                gp, up = p.pb[0], p.pb[1]
                for c in range(DC):
                    nc.tensor.matmul(gp[:, :nw], wm_lhsT(xqt, DC, c, 0),
                                     wm_lhsT(wg, DC, c, n0, nw),
                                     start=(c == 0), stop=(c == DC - 1))
                    nc.tensor.matmul(up[:, :nw], wm_lhsT(xqt, DC, c, 0),
                                     wm_lhsT(wu, DC, c, n0, nw),
                                     start=(c == 0), stop=(c == DC - 1))
                p.bar()
                nc.scalar.activation(h1[:, :nw], gp[:, :nw], AF.Silu)
                p.bar()
                nc.vector.tensor_tensor(htok[:, n0:n0 + nw], h1[:, :nw],
                                        up[:, :nw], ALU.mult)
                p.bar()
            emit_fp4_quant(p, htok[:, :], hq[:, :], HN, sc)
            for c0 in range(0, HN, 128):
                nc.tensor.transpose(p.pbt[:, :128], hq[:, c0:c0 + 128],
                                    p.identb[:, :])
                p.bar()
                nc.vector.tensor_copy(sc["v"].ap().bitcast(BF16)[:, 0:128],
                                      p.pbt[:, :128])
                p.bar()
                p.dma(hqT_hbm[c0:c0 + 128, i * 128:(i + 1) * 128],
                      sc["v"].ap().bitcast(BF16)[:, 0:128])
                p.bar()
    with p.scope():
        wd = p.sb([128, (HN // 128) * D], BF16, name="q4wd")
        load_wmat(p, wd, wqdT, HN, D)
        hti = [p.sb([128, 512], BF16, name=f"q4hi{c}") for c in range(HN // 128)]
        gsb = p.sb([128, 512], F32, name="q4g")
        p.bar()
        for t0 in range(0, T, 512):
            for c in range(HN // 128):
                p.dma(hti[c][:, :], hqT_hbm[c * 128:(c + 1) * 128, t0:t0 + 512])
            p.bar()
            for e in range(DC):
                pbk = p.pb[e % 2]
                for c in range(HN // 128):
                    nc.tensor.matmul(pbk[:, :], wm_lhsT(wd, HN // 128, c, e * 128),
                                     hti[c][:, :], start=(c == 0),
                                     stop=(c == HN // 128 - 1))
                p.bar()
                nc.scalar.activation(gsb[:, :], pbk[:, :], AF.Silu)
                p.bar()
                p.dma(gT_hbm[e * 128:(e + 1) * 128, t0:t0 + 512], gsb[:, :])
                p.bar()


def build_layer0(T=TT, dbg=False):
    nc = bass.Bass("TRN2", target_bir_lowering=False)
    xT = nc.dram_tensor("xT", [D, T], F32, kind="ExternalInput").ap()
    gamma_h = nc.dram_tensor("gamma_h", [D], F32, kind="ExternalInput").ap()
    wq_h = nc.dram_tensor("wq_h", [D, 512], F32, kind="ExternalInput").ap()
    wkv_h = nc.dram_tensor("wkv_h", [D, 1024], F32, kind="ExternalInput").ap()
    wo_h = nc.dram_tensor("wo_h", [512, D], F32, kind="ExternalInput").ap()
    ln_g = nc.dram_tensor("ln_g", [D], F32, kind="ExternalInput").ap()
    ln_b = nc.dram_tensor("ln_b", [D], F32, kind="ExternalInput").ap()
    wqkv_l = nc.dram_tensor("wqkv_l", [D, 1536], F32, kind="ExternalInput").ap()
    wo_l = nc.dram_tensor("wo_l", [512, D], F32, kind="ExternalInput").ap()
    rts = nc.dram_tensor("rts", [D, 2], F32, kind="ExternalInput").ap()
    bandmask = nc.dram_tensor("bandmask", [128, 6 * 512], F32,
                              kind="ExternalInput").ap()
    w_gate = nc.dram_tensor("w_gate", [2816, D], F32, kind="ExternalInput").ap()
    w_up = nc.dram_tensor("w_up", [2816, D], F32, kind="ExternalInput").ap()
    w_down = nc.dram_tensor("w_down", [D, 2816], F32, kind="ExternalInput").ap()
    xoutT = nc.dram_tensor("xoutT", [D, T], F32, kind="ExternalOutput").ap()

    p = Prog(nc)
    xnT_f32 = p.dram([D, T], F32).ap()
    xnT_bf16 = p.dram([D, T], BF16).ap()
    sT = p.dram([D, T], F32).ap()
    gT = p.dram([D, T], F32).ap()
    emit_prelude(p, xT, xnT_f32, xnT_bf16, T)
    dbg_outs = None
    if dbg:
        dbg_outs = {
            "qm": nc.dram_tensor("qm_dbg", [1, T], F32, kind="ExternalOutput").ap(),
            "kx": nc.dram_tensor("kx_dbg", [2, T], BF16, kind="ExternalOutput").ap(),
        }
    emit_l0_attn(p, xnT_f32, xnT_bf16, gamma_h, wq_h, wkv_h, wo_h, ln_g, ln_b,
                 wqkv_l, wo_l, rts, bandmask, sT, T, dbg_outs=dbg_outs)
    emit_l0_ffn(p, xnT_f32, w_gate, w_up, w_down, gT, T)
    emit_glue(p, xT, sT, gT, xoutT, T)
    if dbg:
        for nm, srcb in (("s_dbg", sT), ("g_dbg", gT), ("xn_dbg", xnT_f32)):
            o = nc.dram_tensor(nm, [D, T], F32, kind="ExternalOutput").ap()
            p.dma(o, srcb)
        p.bar()
    p.close()
    return nc


def make_bandmask():
    bm = np.zeros((128, 6 * 512), np.float32)
    i = np.arange(128)
    j = np.arange(512)
    for r in range(6):
        kvw = (r - 1) * 2 + i[:, None] // 64
        qw = j[None, :] // 64
        bm[:, r * 512:(r + 1) * 512] = (np.abs(kvw - qw) <= 1)
    return bm


def _run_layer0(xT_all, inputs, dbg=False):
    nc = build_layer0(dbg=dbg)
    base = {
        "gamma_h": inputs["L0_attn_gamma_h"],
        "wq_h": inputs["L0_attn_wq_h"],
        "wkv_h": inputs["L0_attn_wkv_h"],
        "wo_h": inputs["L0_attn_wo_h"],
        "ln_g": inputs["L0_attn_ln_g"],
        "ln_b": inputs["L0_attn_ln_b"],
        "wqkv_l": inputs["L0_attn_wqkv_l"],
        "wo_l": inputs["L0_attn_wo_l"],
        "rts": np.stack([inputs["L0_attn_rt_q"], inputs["L0_attn_rt_kv"]], 1),
        "bandmask": make_bandmask(),
        "w_gate": inputs["L0_ffn_w_gate"],
        "w_up": inputs["L0_ffn_w_up"],
        "w_down": inputs["L0_ffn_w_down"],
    }
    base = {k: np.ascontiguousarray(v) for k, v in base.items()}
    in_maps = []
    for c in range(8):
        m = dict(base)
        m["xT"] = xT_all[c % 4]
        in_maps.append(m)
    res = run_bass_kernel_spmd(nc, in_maps, core_ids=list(range(8)))
    if dbg:
        return res.results
    return [np.asarray(res.results[b]["xoutT"]) for b in range(4)]


def load_wmat(p, buf, WT_ap, K, N):
    p.cdma(buf.ap().rearrange("p (c n) -> p c n", c=K // 128),
           WT_ap.rearrange("(c p) n -> p c n", p=128))


def wm_lhsT(buf, KC, c, n0, nw=128):
    return buf.ap().rearrange("p (c n) -> p c n", c=KC)[:, c, n0:n0 + nw]


def emit_l1_attn(p, xnT_bf16, W1T, W2T, W3T, WoT2, ropeC, ropeS, sT_hbm, T=TT):
    nc = p.nc
    H, DH = 8, 128
    SC = 1.0 / math.sqrt(8.0)
    outTh = p.dram([D, T], BF16).ap()
    with p.scope():
        rc = p.sb([128, T], F32, name="a1rc")
        rs = p.sb([128, T], F32, name="a1rs")
        p.dma(rc[:, :], ropeC)
        p.dma(rs[:, :], ropeS)
        p.bar()
        with p.scope():
            xnb_h = [p.sb([128, 512], BF16, name=f"a1xh{c}") for c in range(DC)]
            wcols = p.sb([128, DC * 128], BF16, name="a1wc")
            pq = p.sb([128, DC * 512], F32, name="a1pq")
            hT = p.sb([128, T], F32, name="a1h")
            hS = p.sb([128, T], F32, name="a1hs")
            qT = p.sb([DH, T], BF16, name="a1q")
            kT = p.sb([DH, T], BF16, name="a1k")
            vbuf = p.sb([128, (T // 128) * DH], BF16, name="a1vb")
            vbuf3 = vbuf.ap().rearrange("p (i v) -> p i v", v=DH)
            qaux = p.sb([2, T], BF16, name="a1qx")
            kaux = p.sb([2, T], BF16, name="a1kx")
            negb = p.sb([1, T], F32, name="a1nb")
            ob = p.sb([DH, 512], BF16, name="a1ob")
            nc.gpsimd.memset(kaux[:, :], 1.0)
            nc.gpsimd.memset(qaux[:, :], 1.0)
            p.bar()
            swapmask = []
            for i in range(0, 32, 2):
                swapmask += [i + 1, i]

            def mk_headspace(WT, out_bf, do_rope, scale):
                for cb in range(DC):
                    p.cdma(wcols.ap().rearrange("p (c n) -> p c n", c=DC),
                           WT[:, cb * 128:(cb + 1) * 128]
                           .rearrange("(c p) n -> p c n", p=128))
                    p.bar()
                    pbk = p.pb[0]
                    for tt in range(1):
                        for c in range(DC):
                            nc.tensor.matmul(
                                pbk[:, :], wm_lhsT(wcols, DC, c, 0),
                                xnb_h[c][:, :],
                                start=(c == 0), stop=(c == DC - 1))
                        p.bar()
                        nc.vector.tensor_copy(pq[:, cb * 512:(cb + 1) * 512],
                                              pbk[:, :])
                        p.bar()
                for cb in range(DC):
                    nc.vector.tensor_copy(
                        hT.ap().rearrange("p (t c) -> p t c", c=8)[:, :, cb],
                        pq[:, cb * 512:(cb + 1) * 512])
                p.bar()
                if do_rope:
                    nc.vector.stream_shuffle(hS[:, :], hT[:, :], swapmask)
                    p.bar()
                    nc.vector.tensor_tensor(hS[:, :], hS[:, :], rs[:, :], ALU.mult)
                    p.bar()
                    nc.vector.tensor_tensor(hT[:, :], hT[:, :], rc[:, :], ALU.mult)
                    p.bar()
                    nc.vector.tensor_tensor(hT[:, :], hT[:, :], hS[:, :], ALU.add)
                    p.bar()
                if out_bf is not None:
                    if scale is None:
                        nc.vector.tensor_copy(out_bf[:, :], hT[:, :])
                    else:
                        nc.vector.tensor_scalar(out_bf[:, :], hT[:, :],
                                                float(scale), None, ALU.mult)
                    p.bar()

            for hh in range(H):
                for c in range(DC):
                    p.dma(xnb_h[c][:, :],
                          xnT_bf16[c * 128:(c + 1) * 128,
                                   512 * hh:512 * hh + 512])
                p.bar()
                mk_headspace(W1T, qT, True, SC)
                mk_headspace(W2T, kT, True, None)
                mk_headspace(W3T, None, False, None)
                emit_vbuf2(p, hT, vbuf3, T // 128, 128, f32=True)
                emit_bbound(p, qT, kT, DH, T, T, negb)
                nc.vector.tensor_copy(qaux[0:1, :], negb[:1, :])
                p.bar()

                def out_cb(t0, w, o1, hh=hh):
                    nc.vector.tensor_copy(ob[:DH, :w], o1[:DH, :w])
                    p.bar()
                    p.dma(outTh[hh * DH:(hh + 1) * DH, t0:t0 + w], ob[:DH, :w])

                emit_attn_head2(p, kT, qT,
                                lambda i: vbuf3[:, i, :], T // 128, T, out_cb,
                                dh=DH, dv=DH, aux2=(kaux, qaux))
        with p.scope():
            wo = p.sb([128, DC * D], BF16, name="a1wo")
            load_wmat(p, wo, WoT2, D, D)
            qa = [p.sb([128, 1024], BF16, name=f"a1qa{c}") for c in range(DC)]
            otile = p.sb([128, 1024], BF16, name="a1ot")
            ssb = p.sb([128, 512], F32, name="a1ss")
            p.bar()
            for a in range(4):
                for fc in range(DC):
                    p.dma(otile[:, :], outTh[fc * 128:(fc + 1) * 128,
                                             a * 1024:(a + 1) * 1024])
                    p.bar()
                    for jt in range(8):
                        nc.tensor.transpose(p.pbt[:, :128],
                                            otile[:, jt * 128:(jt + 1) * 128],
                                            p.identb[:, :])
                        p.bar()
                        nc.vector.tensor_copy(
                            qa[jt][:, fc * 128:(fc + 1) * 128], p.pbt[:, :128])
                        p.bar()
                for fc in range(DC):
                    for e0 in range(0, D, 512):
                        pbk = p.pb[fc % 2]
                        for jc in range(DC):
                            nc.tensor.matmul(
                                pbk[:, :], qa[jc][:, fc * 128:(fc + 1) * 128],
                                wm_lhsT(wo, DC, jc, e0, 512), start=(jc == 0),
                                stop=(jc == DC - 1))
                        p.bar()
                        nc.vector.tensor_copy(ssb[:, :], pbk[:, :])
                        p.bar()
                        p.dma(sT_hbm[fc * 128:(fc + 1) * 128,
                                     a * 1024 + e0:a * 1024 + e0 + 512],
                              ssb[:, :])
                        p.bar()


def emit_l1_ffn(p, xnT_f32, xnT_bf16, q_w, q_b, k_w, k_b, v_w, v_b, o_w, o_b,
                router_w, router_b, w1, b1, w2, b2, gT_hbm, T=TT):
    nc = p.nc
    H, KPH, E = 16, 32, 8
    SC = 1.0 / math.sqrt(32.0)
    aoT_hbm = p.dram([512, T], BF16).ap()
    mlaT_hbm = p.dram([D, T], BF16).ap()
    with p.scope():
        xnb = load_chunks_bf16(p, xnT_bf16, T, "f1x")
        wcols = p.sb([128, DC * 32], BF16, name="f1wc")
        bcol = p.sb([32, 1], F32, name="f1bc")
        qaug = p.sb([66, T], BF16, name="f1q")
        kaug = p.sb([66, T], BF16, name="f1k")
        vT = p.sb([32, T], F32, name="f1v")
        vbuf = p.sb([128, (T // 128) * 34], BF16, name="f1vb")
        vbuf3 = vbuf.ap().rearrange("p (i v) -> p i v", v=34)
        negb = p.sb([1, T], F32, name="f1nb")
        ob = p.sb([32, 512], BF16, name="f1ob")
        nc.gpsimd.memset(kaug[32:64, :], 0.0)
        nc.gpsimd.memset(qaug[32:64, :], 0.0)
        nc.gpsimd.memset(kaug[64:66, :], 1.0)
        nc.gpsimd.memset(qaug[64:66, :], 1.0)
        nc.gpsimd.memset(vbuf[:, :], 1.0)
        p.bar()

        def proj32(W_ap, b_ap, col0, out_sb, scale):
            p.cdma(wcols.ap().rearrange("p (c n) -> p c n", c=DC),
                   W_ap[:, col0:col0 + 32].rearrange("(c p) n -> p c n", p=128))
            p.dma(bcol[:, :], b_ap[col0:col0 + 32].rearrange("(a b) -> a b", b=1))
            _head_proj2(p, wcols, DC, xnb, T, out_sb, 0, 32, scale=scale,
                        bias_col=bcol[:32, :])
            p.bar()

        for hh in range(H):
            proj32(q_w, q_b, 64 * hh, qaug, SC)
            proj32(k_w, k_b, 32 * hh, kaug, None)
            proj32(v_w, v_b, 32 * hh, vT, None)
            emit_vbuf2(p, vT, vbuf3, T // 128, 32, f32=True)
            emit_bbound(p, qaug[0:32, :], kaug[0:32, :], 32, T, T, negb)
            nc.vector.tensor_copy(qaug[64:65, :], negb[:1, :])
            p.bar()

            def out_cb(t0, w, o1, hh=hh):
                nc.vector.tensor_copy(ob[:32, :w], o1[:32, :w])
                p.bar()
                p.dma(aoT_hbm[hh * 32:(hh + 1) * 32, t0:t0 + w], ob[:32, :w])

            emit_attn_head2(p, kaug, qaug, lambda i: vbuf3[:, i, :],
                            T // 128, T, out_cb, dh=32, dv=32)
    with p.scope():
        wo = p.sb([128, 4 * D], BF16, name="f1wo")
        load_wmat(p, wo, o_w[0:512, :], 512, D)
        ao = [p.sb([128, 512], BF16, name=f"f1ao{c}") for c in range(4)]
        ocol = p.sb([128, 1], F32, name="f1oc")
        msb = p.sb([128, 512], BF16, name="f1ms")
        p.bar()
        for t0 in range(0, T, 512):
            for c in range(4):
                p.dma(ao[c][:, :], aoT_hbm[c * 128:(c + 1) * 128, t0:t0 + 512])
            p.bar()
            for e in range(DC):
                p.dma(ocol[:, :], o_b[e * 128:(e + 1) * 128]
                      .rearrange("(a b) -> a b", b=1))
                pbk = p.pb[e % 2]
                for c in range(4):
                    nc.tensor.matmul(pbk[:, :], wm_lhsT(wo, 4, c, e * 128),
                                     ao[c][:, :], start=(c == 0), stop=(c == 3))
                p.bar()
                nc.vector.tensor_scalar(msb[:, :], pbk[:, :], ocol[:, :], None,
                                        ALU.add)
                p.bar()
                p.dma(mlaT_hbm[e * 128:(e + 1) * 128, t0:t0 + 512], msb[:, :])
                p.bar()
    _oc = p.scope()
    _oc.__enter__()
    combT = p.sb([E, T], F32, name="f1cT")
    with p.scope():
        mlab = load_chunks_bf16(p, mlaT_hbm, T, "f1m")
        rw = p.sb([128, DC * E], BF16, name="f1rw")
        load_wmat(p, rw, router_w, D, E)
        rbb = p.sb([128, E], F32, name="f1rb")
        probs = p.sb([128, E], F32, name="f1pr")
        rank = p.sb([128, E], F32, name="f1rk")
        tmp = p.sb([128, E], F32, name="f1tp")
        mrow = p.sb([128, 1], F32, name="f1mr")
        srow = p.sb([128, 1], F32, name="f1sr")
        p.bar()
        rb1 = p.sb([1, E], F32, name="f1rb1")
        p.dma(rb1[:1, :], router_b.rearrange("(a b) -> a b", a=1))
        p.bar()
        nc.tensor.matmul(p.pb[6][:, :E], p.ones_row[:1, :], rb1[:1, :],
                         start=True, stop=True)
        p.bar()
        nc.vector.tensor_copy(rbb[:, :], p.pb[6][:, :E])
        p.bar()
        for i in range(T // 128):
            pbk = p.pb[0]
            for c in range(DC):
                nc.tensor.matmul(pbk[:, :E], mlab[c][:, i * 128:(i + 1) * 128],
                                 wm_lhsT(rw, DC, c, 0, E),
                                 start=(c == 0), stop=(c == DC - 1))
            p.bar()
            nc.vector.tensor_tensor(probs[:, :], pbk[:128, :E], rbb[:, :], ALU.add)
            p.bar()
            nc.vector.tensor_reduce(mrow[:, :1], probs[:, :], AX.X, ALU.max,
                                    negate=True)
            p.bar()
            nc.scalar.activation(probs[:, :], probs[:, :], AF.Exp,
                                 bias=mrow[:, :], accum_out=srow[:, :])
            p.bar()
            nc.vector.reciprocal(srow[:, :1], srow[:, :1])
            p.bar()
            nc.vector.tensor_scalar(probs[:, :], probs[:, :], srow[:, :], None,
                                    ALU.mult)
            p.bar()
            nc.gpsimd.memset(rank[:, :], 0.0)
            p.bar()
            for s in range(1, E):
                lo = slice(0, E - s)
                hi = slice(s, E)
                nc.vector.tensor_tensor(tmp[:, lo], probs[:, hi], probs[:, lo],
                                        ALU.is_gt)
                p.bar()
                nc.vector.tensor_tensor(rank[:, lo], rank[:, lo], tmp[:, lo],
                                        ALU.add)
                p.bar()
                nc.vector.tensor_tensor(tmp[:, hi], probs[:, lo], probs[:, hi],
                                        ALU.is_ge)
                p.bar()
                nc.vector.tensor_tensor(rank[:, hi], rank[:, hi], tmp[:, hi],
                                        ALU.add)
                p.bar()
            nc.vector.tensor_scalar(tmp[:, :], rank[:, :], 1.5, None, ALU.is_lt)
            p.bar()
            nc.vector.tensor_tensor(tmp[:, :], tmp[:, :], probs[:, :], ALU.mult)
            p.bar()
            nc.vector.tensor_reduce(srow[:, :1], tmp[:, :], AX.X, ALU.add)
            p.bar()
            nc.vector.tensor_scalar(srow[:, :1], srow[:, :1], 1e-8, None, ALU.add)
            p.bar()
            nc.vector.reciprocal(srow[:, :1], srow[:, :1])
            p.bar()
            nc.vector.tensor_scalar(tmp[:, :], tmp[:, :], srow[:, :], None,
                                    ALU.mult)
            p.bar()
            nc.tensor.transpose(p.pbt2[:E, :128], tmp[:, :E], p.identf[:, :])
            p.bar()
            nc.vector.tensor_copy(combT[:, i * 128:(i + 1) * 128],
                                  p.pbt2[:E, :128])
            p.bar()
    with p.scope():
        w1b = p.sb([128, DC * 2048], BF16, name="f1w1")
        w2b = p.sb([128, 16 * D], BF16, name="f1w2")
        mlat = [p.sb([128, 1024], BF16, name=f"f1mt{c}") for c in range(DC)]
        h1T = [p.sb([128, 1024], BF16, name=f"f1h{i}") for i in range(16)]
        moe = [p.sb([128, 1024], F32, name=f"f1mo{c}") for c in range(DC)]
        b1all = p.sb([128, E * 16], F32, name="f1b1")
        b2row = p.sb([1, E * D], F32, name="f1b2")
        ones512 = p.sb([1, 512], F32, name="f1o5")
        t1 = p.sb([128, 512], F32, name="f1t1")
        gc_sb = p.sb([128, 1024], F32, name="f1gc")
        gsel = p.selmat(E, 128, name="f1gs")
        xnc = gc_sb
        nc.gpsimd.memset(ones512[:1, :], 1.0)
        for e in range(E):
            for hc in range(16):
                p.dma(b1all[:, e * 16 + hc:e * 16 + hc + 1],
                      b1[e, hc * 128:(hc + 1) * 128].rearrange("(a b) -> a b", b=1))
        p.dma(b2row[:1, :],
              b2.rearrange("e d -> (e d)").rearrange("(a b) -> a b", a=1))
        p.bar()
        s_pe, s_act, s_dve = p.sem("s_moe"), p.sem("s_mact"), p.sem("s_mdve")
        for tb in range(0, T, 1024):
            for c in range(DC):
                nc.vector.memset(moe[c][:, :], 0.0)
                p.dma(mlat[c][:, :], mlaT_hbm[c * 128:(c + 1) * 128, tb:tb + 1024])
            p.bar()
            if tb == 0:
                load_wmat(p, w1b, w1[0], D, 2048)
                load_wmat(p, w2b, w2[0], 2048, D)
                p.bar()
            for e in range(E):
                for t5 in range(2):
                    nc.tensor.wait_ge(s_dve, p.cnt["s_mdve"])
                    nc.tensor.matmul(p.pb[6][:, :], gsel[:, e * 128:(e + 1) * 128],
                                     combT[:, tb + t5 * 512:tb + t5 * 512 + 512],
                                     start=True, stop=True).then_inc(s_pe, 1)
                    p.cnt["s_moe"] += 1
                    nc.vector.wait_ge(s_pe, p.cnt["s_moe"])
                    nc.vector.tensor_copy(gc_sb[:, t5 * 512:(t5 + 1) * 512],
                                          p.pb[6][:, :]).then_inc(s_dve, 1)
                    p.cnt["s_mdve"] += 1
                pe_b, act_b = p.cnt["s_moe"], p.cnt["s_mact"]
                u = 0
                for hc in range(16):
                    for t5 in range(2):
                        bank = p.pb[u % 2]
                        if u == 0:
                            nc.tensor.wait_ge(p.dma_sem, p.dma_issued)
                        if u >= 2:
                            nc.tensor.wait_ge(s_act, act_b + u - 1)
                        for c in range(DC):
                            mm = nc.tensor.matmul(
                                bank[:, :], wm_lhsT(w1b, DC, c, hc * 128),
                                mlat[c][:, t5 * 512:(t5 + 1) * 512],
                                start=(c == 0), stop=(c == DC - 1))
                        mm.then_inc(s_pe, 1)
                        p.cnt["s_moe"] += 1
                        nc.scalar.wait_ge(s_pe, pe_b + u + 1)
                        nc.scalar.activation(
                            h1T[hc][:, t5 * 512:(t5 + 1) * 512], bank[:, :],
                            AF.Gelu,
                            bias=b1all[:, e * 16 + hc:e * 16 + hc + 1])\
                            .then_inc(s_act, 1)
                        p.cnt["s_mact"] += 1
                        u += 1
                cnt_hc_end = p.cnt["s_moe"]
                ep1 = e + 1
                if ep1 < E or tb + 1024 < T:
                    nc.gpsimd.wait_ge(s_pe, cnt_hc_end)
                    load_wmat(p, w1b, w1[ep1 % E], D, 2048)
                pe_b2, act_b2, dve_b2 = (p.cnt["s_moe"], p.cnt["s_mact"],
                                         p.cnt["s_mdve"])
                for v in range(16):
                    e2, t5 = v // 2, v % 2
                    bank = p.pb[2 + v % 2]
                    if v == 0:
                        nc.tensor.wait_ge(s_act, act_b2)
                        nc.tensor.wait_ge(p.dma_sem, p.dma_issued)
                    if v >= 2:
                        nc.tensor.wait_ge(s_dve, dve_b2 + 2 * (v - 2) + 1)
                    for hc in range(16):
                        nc.tensor.matmul(
                            bank[:, :], wm_lhsT(w2b, 16, hc, e2 * 128),
                            h1T[hc][:, t5 * 512:(t5 + 1) * 512],
                            start=(hc == 0), stop=False)
                    mm = nc.tensor.matmul(
                        bank[:, :], b2row[:1, e * D + e2 * 128:e * D + e2 * 128 + 128],
                        ones512[:1, :], start=False, stop=True)
                    mm.then_inc(s_pe, 1)
                    p.cnt["s_moe"] += 1
                    t1v = t1[:, 0:512]
                    nc.vector.wait_ge(s_pe, pe_b2 + v + 1)
                    nc.vector.tensor_tensor(t1v, bank[:, :],
                                            gc_sb[:, t5 * 512:(t5 + 1) * 512],
                                            ALU.mult).then_inc(s_dve, 1)
                    p.cnt["s_mdve"] += 1
                    nc.vector.wait_ge(s_dve, p.cnt["s_mdve"])
                    nc.vector.tensor_tensor(
                        moe[e2][:, t5 * 512:(t5 + 1) * 512],
                        moe[e2][:, t5 * 512:(t5 + 1) * 512], t1v, ALU.add)\
                        .then_inc(s_dve, 1)
                    p.cnt["s_mdve"] += 1
                if ep1 < E or tb + 1024 < T:
                    nc.gpsimd.wait_ge(s_pe, p.cnt["s_moe"])
                    load_wmat(p, w2b, w2[ep1 % E], 2048, D)
            p.bar()
            for c in range(DC):
                p.dma(xnc[:, :], xnT_f32[c * 128:(c + 1) * 128, tb:tb + 1024])
                p.bar()
                nc.vector.scalar_tensor_tensor(xnc[:, :], moe[c][:, :], 0.3,
                                               xnc[:, :], ALU.mult, ALU.add)
                p.bar()
                nc.scalar.activation(xnc[:, :], xnc[:, :], AF.Silu)
                p.bar()
                p.dma(gT_hbm[c * 128:(c + 1) * 128, tb:tb + 1024], xnc[:, :])
                p.bar()
    _oc.__exit__(None, None, None)


def build_layer1(T=TT, dbg=False):
    nc = bass.Bass("TRN2", target_bir_lowering=False)
    xT = nc.dram_tensor("xT", [D, T], F32, kind="ExternalInput").ap()
    W1T = nc.dram_tensor("W1T", [D, D], F32, kind="ExternalInput").ap()
    W2T = nc.dram_tensor("W2T", [D, D], F32, kind="ExternalInput").ap()
    W3T = nc.dram_tensor("W3T", [D, D], F32, kind="ExternalInput").ap()
    WoT2 = nc.dram_tensor("WoT2", [D, D], F32, kind="ExternalInput").ap()
    ropeC = nc.dram_tensor("ropeC", [128, T], F32, kind="ExternalInput").ap()
    ropeS = nc.dram_tensor("ropeS", [128, T], F32, kind="ExternalInput").ap()
    q_w = nc.dram_tensor("q_w", [D, D], F32, kind="ExternalInput").ap()
    q_b = nc.dram_tensor("q_b", [D], F32, kind="ExternalInput").ap()
    k_w = nc.dram_tensor("k_w", [D, 512], F32, kind="ExternalInput").ap()
    k_b = nc.dram_tensor("k_b", [512], F32, kind="ExternalInput").ap()
    v_w = nc.dram_tensor("v_w", [D, 512], F32, kind="ExternalInput").ap()
    v_b = nc.dram_tensor("v_b", [512], F32, kind="ExternalInput").ap()
    o_w = nc.dram_tensor("o_w", [D, D], F32, kind="ExternalInput").ap()
    o_b = nc.dram_tensor("o_b", [D], F32, kind="ExternalInput").ap()
    router_w = nc.dram_tensor("router_w", [D, 8], F32, kind="ExternalInput").ap()
    router_b = nc.dram_tensor("router_b", [8], F32, kind="ExternalInput").ap()
    w1 = nc.dram_tensor("w1", [8, D, 2048], F32, kind="ExternalInput").ap()
    b1 = nc.dram_tensor("b1", [8, 2048], F32, kind="ExternalInput").ap()
    w2 = nc.dram_tensor("w2", [8, 2048, D], F32, kind="ExternalInput").ap()
    b2 = nc.dram_tensor("b2", [8, D], F32, kind="ExternalInput").ap()
    xoutT = nc.dram_tensor("xoutT", [D, T], F32, kind="ExternalOutput").ap()

    p = Prog(nc)
    xnT_f32 = p.dram([D, T], F32).ap()
    xnT_bf16 = p.dram([D, T], BF16).ap()
    sT = p.dram([D, T], F32).ap()
    gT = p.dram([D, T], F32).ap()
    emit_prelude(p, xT, xnT_f32, xnT_bf16, T)
    emit_l1_attn(p, xnT_bf16, W1T, W2T, W3T, WoT2, ropeC, ropeS, sT, T)
    emit_l1_ffn(p, xnT_f32, xnT_bf16, q_w, q_b, k_w, k_b, v_w, v_b, o_w, o_b,
                router_w, router_b, w1, b1, w2, b2, gT, T)
    emit_glue(p, xT, sT, gT, xoutT, T)
    if dbg:
        for nm, srcb in (("s_dbg", sT), ("g_dbg", gT), ("xn_dbg", xnT_f32)):
            o = nc.dram_tensor(nm, [D, T], F32, kind="ExternalOutput").ap()
            p.dma(o, srcb)
        p.bar()
    p.close()
    return nc


def make_rope_tables(T=TT, DH=128):
    inv = 1.0 / 10000.0 ** (np.arange(0, DH, 2, dtype=np.float64) / DH)
    t = np.arange(T, dtype=np.float64)
    fr = np.repeat(t[:, None] * inv[None, :], 2, axis=-1)
    cos = np.cos(fr).T.astype(np.float32)
    sin = np.sin(fr).T.astype(np.float32)
    sgn = np.where(np.arange(DH)[:, None] % 2 == 0, -1.0, 1.0).astype(np.float32)
    return np.ascontiguousarray(cos), np.ascontiguousarray(sin * sgn)


def _run_layer1(xT_all, inputs, dbg=False):
    nc = build_layer1(dbg=dbg)
    rc, rs = make_rope_tables()
    base = {
        "W1T": inputs["L1_attn_W1"].T, "W2T": inputs["L1_attn_W2"].T,
        "W3T": inputs["L1_attn_W3"].T, "WoT2": inputs["L1_attn_Wo"].T,
        "ropeC": rc, "ropeS": rs,
        "q_w": inputs["L1_ffn_q_w"], "q_b": inputs["L1_ffn_q_b"],
        "k_w": inputs["L1_ffn_k_w"], "k_b": inputs["L1_ffn_k_b"],
        "v_w": inputs["L1_ffn_v_w"], "v_b": inputs["L1_ffn_v_b"],
        "o_w": inputs["L1_ffn_o_w"], "o_b": inputs["L1_ffn_o_b"],
        "router_w": inputs["L1_ffn_router_w"],
        "router_b": inputs["L1_ffn_router_b"],
        "w1": inputs["L1_ffn_w1"], "b1": inputs["L1_ffn_b1"],
        "w2": inputs["L1_ffn_w2"], "b2": inputs["L1_ffn_b2"],
    }
    base = {k: np.ascontiguousarray(v) for k, v in base.items()}
    in_maps = []
    for c in range(8):
        m = dict(base)
        m["xT"] = xT_all[c % 4]
        in_maps.append(m)
    res = run_bass_kernel_spmd(nc, in_maps, core_ids=list(range(8)))
    if dbg:
        return res.results
    return [np.asarray(res.results[b]["xoutT"]) for b in range(4)]


F32EPS = float(np.finfo(np.float32).eps)


def emit_rms_bcast(p, srcT_hbm, T, rstd_b, eps):
    nc = p.nc
    with p.scope():
        xc = p.sb([128, 512], F32, name=_nm("rmx"))
        sq = p.sb([128, 512], F32, name=_nm("rmsq"))
        rrow = p.sb([1, T], F32, name=_nm("rmr"))
        pv = p.pb[5]
        for t0 in range(0, T, 512):
            for c in range(DC):
                p.dma(xc[:, :], srcT_hbm[c * 128:(c + 1) * 128, t0:t0 + 512])
                p.bar()
                nc.scalar.activation(sq[:, :], xc[:, :], AF.Square)
                p.bar()
                nc.tensor.matmul(pv[:1, :], p.ones_col[:, :], sq[:, :],
                                 start=(c == 0), stop=(c == DC - 1))
                p.bar()
            nc.vector.tensor_scalar(rrow[:1, t0:t0 + 512], pv[:1, :512],
                                    1.0 / D, float(eps), ALU.mult, ALU.add)
            p.bar()
        nc.scalar.activation(rrow[:1, :], rrow[:1, :], AF.Sqrt)
        p.bar()
        nc.vector.reciprocal(rrow[:1, :], rrow[:1, :])
        p.bar()
        p.bcast_row(rstd_b, rrow, T)


def _head_proj(p, wcols_sb, KC, rhs_chunks, T, out_sb, out_row0, nw, bank=0,
               scale=None, out_dt_via=None):
    nc = p.nc
    for t0 in range(0, T, 512):
        pbk = p.pb[bank]
        for c in range(KC):
            nc.tensor.matmul(pbk[:nw, :], wcols_sb[:, c * nw:(c + 1) * nw],
                             rhs_chunks[c][:, t0:t0 + 512],
                             start=(c == 0), stop=(c == KC - 1))
        p.bar()
        if scale is None:
            nc.vector.tensor_copy(out_sb[out_row0:out_row0 + nw, t0:t0 + 512],
                                  pbk[:nw, :])
        else:
            nc.vector.tensor_scalar(out_sb[out_row0:out_row0 + nw, t0:t0 + 512],
                                    pbk[:nw, :], float(scale), None, ALU.mult)
        p.bar()


def load_wcols(p, dst_sb, W_hbm, K, col0, nw):
    p.cdma(dst_sb.ap().rearrange("p (c n) -> p c n", c=K // 128)[:, :, :nw],
           W_hbm[:, col0:col0 + nw].rearrange("(c p) n -> p c n", p=128))


def emit_l3_attn(p, xnT_f32, xnT_bf16, norm_w, Wq, Wkv, Wout, Wg, dmask_hbm,
                 sT_hbm, T=TT):
    nc = p.nc
    HEADS, DH = 16, 64
    outTh = p.dram([HEADS * DH, T], BF16).ap()
    with p.scope():
        rstd_b = p.sb([128, T], F32, name="l3rb")
        emit_rms_bcast(p, xnT_f32, T, rstd_b, F32EPS)
        xl3 = [p.sb([128, T], BF16, name=f"l3x{c}") for c in range(DC)]
        with p.scope():
            xc = p.sb([128, T], F32, name="l3xc")
            nw_sb = p.sb([128, 1], F32, name="l3nw")
            for c in range(DC):
                p.dma(xc[:, :], chunk(xnT_f32, c))
                p.dma(nw_sb[:, :], norm_w[c * 128:(c + 1) * 128]
                      .rearrange("(a b) -> a b", b=1))
                p.bar()
                nc.vector.tensor_tensor(xc[:, :], xc[:, :], rstd_b[:, :], ALU.mult)
                p.bar()
                nc.vector.tensor_scalar(xl3[c][:, :], xc[:, :], nw_sb[:, :], None,
                                        ALU.mult)
                p.bar()
        gates = p.sb([16, T], F32, name="l3gt")
        with p.scope():
            wg_sb = p.sb([128, DC * 16], BF16, name="l3wg")
            load_wcols(p, wg_sb, Wg, D, 0, 16)
            p.bar()
            for t0 in range(0, T, 512):
                pbk = p.pb[0]
                for c in range(DC):
                    nc.tensor.matmul(pbk[:16, :], wg_sb[:, c * 16:(c + 1) * 16],
                                     xl3[c][:, t0:t0 + 512],
                                     start=(c == 0), stop=(c == DC - 1))
                p.bar()
                nc.scalar.activation(gates[:16, t0:t0 + 512], pbk[:16, :], AF.Sigmoid)
                p.bar()
        dmasks = p.sb([128, 4 * 512], BF16, name="l3dm")
        p.cdma(dmasks[:, :], dmask_hbm)
        p.bar()
        gsel = p.selmat(HEADS, DH, name="l3sel")
        with p.scope():
            wcols = p.sb([128, DC * DH], BF16, name="l3wc")
            qaug = p.sb([66, T], BF16, name="l3qa")
            kaug = p.sb([66, T], BF16, name="l3ka")
            vT = p.sb([DH, T], BF16, name="l3vt")
            VX = DH + 2
            vbuf = p.sb([128, (T // 128) * VX], BF16, name="l3vb")
            vbuf3 = vbuf.ap().rearrange("p (i v) -> p i v", v=VX)
            negb = p.sb([1, T], F32, name="l3nb")
            ob = p.sb([DH, 512], BF16, name="l3ob")
            nc.gpsimd.memset(kaug[64:66, :], 1.0)
            nc.gpsimd.memset(qaug[64:66, :], 1.0)
            nc.gpsimd.memset(vbuf[:, :], 1.0)
            p.bar()
            for h in range(HEADS):
                load_wcols(p, wcols, Wq, D, h * DH, DH)
                _head_proj2(p, wcols, DC, xl3, T, qaug, 0, DH, scale=0.125)
                p.bar()
                load_wcols(p, wcols, Wkv, D, h * DH, DH)
                _head_proj2(p, wcols, DC, xl3, T, kaug, 0, DH)
                p.bar()
                load_wcols(p, wcols, Wkv, D, D + h * DH, DH)
                _head_proj2(p, wcols, DC, xl3, T, vT, 0, DH)
                p.bar()
                emit_vbuf2(p, vT, vbuf3, T // 128, DH)
                emit_bbound(p, qaug[0:DH, :], kaug[0:DH, :], DH, T, T, negb)
                nc.vector.tensor_copy(qaug[64:65, :], negb[:1, :])
                p.bar()

                def out_cb(t0, w, o1, h=h):
                    nc.vector.tensor_copy(ob[:DH, :w], o1[:DH, :w])
                    p.bar()
                    p.dma(outTh[h * DH:(h + 1) * DH, t0:t0 + w], ob[:DH, :w])

                emit_attn_head2(p, kaug, qaug,
                                lambda i: vbuf3[:, i, :], T // 128, T, out_cb,
                                dh=DH, dv=DH, causal_t0=lambda t0: t0,
                                dmaskadd=dmasks, gate_sel=(gates, gsel, h))
        with p.scope():
            wo = WMat(p, Wout, HEADS * DH, D, name="l3wo")
            rhs = [p.sb([128, 512], BF16, name=f"l3r{c}") for c in range(DC)]
            ssb = p.sb([128, 512], F32, name="l3ss")
            for t0 in range(0, T, 512):
                for c in range(DC):
                    p.dma(rhs[c][:, :], outTh[c * 128:(c + 1) * 128, t0:t0 + 512])
                p.bar()
                for e in range(DC):
                    pbk = p.pb[e % 2]
                    for c in range(DC):
                        nc.tensor.matmul(pbk[:, :], wo.lhsT(c, e * 128), rhs[c][:, :],
                                         start=(c == 0), stop=(c == DC - 1))
                    p.bar()
                    nc.vector.tensor_copy(ssb[:, :], pbk[:, :])
                    p.bar()
                    p.dma(sT_hbm[e * 128:(e + 1) * 128, t0:t0 + 512], ssb[:, :])
                    p.bar()


def emit_l3_ffn(p, xnT_bf16, w_q3, w_k3, w_gate3, vexp, oexp, dmask_hbm,
                gT_hbm, T=TT, dbg_outs=None):
    nc = p.nc
    H, E, DH = 8, 8, 64
    NKV = T // 128
    zw_hbm = p.dram([H * E * DH, T], BF16).ap()
    with p.scope():
        xnb = load_chunks_bf16(p, xnT_bf16, T, "f3x")
        dmasks = p.sb([128, 4 * 512], BF16, name="f3dm")
        p.cdma(dmasks[:, :], dmask_hbm)
        gmT = p.sb([H * E, T], BF16, name="f3gmt")
        p.bar()
        VX3 = DH + 2
        vbuf = p.sb([128, NKV * H * VX3], BF16, name="f3vb")
        vbuf3 = vbuf.ap().rearrange("p (i x) -> p i x", i=NKV)
        nc.gpsimd.memset(vbuf[:, :], 1.0)
        with p.scope():
          gmbuf = p.sb([128, NKV * H * E], F32, name="f3gm")
          if True:
            wg_sb = p.sb([128, DC * H * E], BF16, name="f3wg")
            load_wcols(p, wg_sb, w_gate3, D, 0, H * E)
            gsig = p.sb([128, H * E], F32, name="f3gs")
            rank = p.sb([128, H * E], F32, name="f3rk")
            tmp = p.sb([128, H * E], F32, name="f3tp")
            p.bar()
            for i in range(NKV):
                pbk = p.pb[0]
                for c in range(DC):
                    nc.tensor.matmul(pbk[:, :H * E],
                                     xnb[c][:, i * 128:(i + 1) * 128],
                                     wg_sb.ap().rearrange(
                                         "p (c n) -> p c n", c=DC)[:, c, :],
                                     start=(c == 0), stop=(c == DC - 1))
                p.bar()
                nc.scalar.activation(gsig[:, :], pbk[:128, :H * E], AF.Sigmoid)
                p.bar()
                nc.gpsimd.memset(rank[:, :], 0.0)
                p.bar()
                g3 = gsig.ap().rearrange("p (h e) -> p h e", h=H)
                r3 = rank.ap().rearrange("p (h e) -> p h e", h=H)
                t3 = tmp.ap().rearrange("p (h e) -> p h e", h=H)
                for s in range(1, E):
                    lo = (slice(None), slice(None), slice(0, E - s))
                    hi = (slice(None), slice(None), slice(s, E))
                    nc.vector.tensor_tensor(t3[lo], g3[hi], g3[lo], ALU.is_gt)
                    p.bar()
                    nc.vector.tensor_tensor(r3[lo], r3[lo], t3[lo], ALU.add)
                    p.bar()
                    nc.vector.tensor_tensor(t3[hi], g3[lo], g3[hi], ALU.is_ge)
                    p.bar()
                    nc.vector.tensor_tensor(r3[hi], r3[hi], t3[hi], ALU.add)
                    p.bar()
                nc.vector.tensor_scalar(tmp[:, :], rank[:, :], 0.0, None, ALU.is_equal)
                p.bar()
                nc.vector.tensor_tensor(gmbuf[:, i * H * E:(i + 1) * H * E],
                                        tmp[:, :], gsig[:, :], ALU.mult)
                p.bar()
                nc.tensor.transpose(p.pbt2[:H * E, :128],
                                    gmbuf[:, i * H * E:(i + 1) * H * E],
                                    p.identf[:, :])
                p.bar()
                nc.vector.tensor_copy(gmT[:, i * 128:(i + 1) * 128],
                                      p.pbt2[:H * E, :128])
                p.bar()
          import os as _os
          if _os.environ.get("FFN_SKIP_VALUES") and dbg_outs:
              p.cdma(vbuf[:, :], dbg_outs["vbuf_in"])
              p.bar()
          elif True:
            NH = NKV // 2
            vacc = p.sb([128, NH * 512], F32, name="f3va")
            vex = p.sb([128, DC * 512], BF16, name="f3ve")
            for half in range(2):
                i0 = half * NH
                nc.gpsimd.memset(vacc[:, :], 0.0)
                p.bar()
                for e in range(E):
                    for h in range(H):
                        p.cdma(vex.ap().rearrange("p (c h v) -> p c h v",
                                                  c=DC, h=H)[:, :, h, :],
                               vexp[e, h].rearrange("(c p) v -> p c v", p=128))
                    p.bar()
                    for il in range(NH):
                        i = i0 + il
                        pbk = p.pb[e % 2]
                        for c in range(DC):
                            nc.tensor.matmul(pbk[:, :],
                                             xnb[c][:, i * 128:(i + 1) * 128],
                                             vex[:, c * 512:(c + 1) * 512],
                                             start=(c == 0), stop=(c == DC - 1))
                        p.bar()
                        for h in range(H):
                            gcol = gmbuf[:, i * H * E + h * E + e:
                                         i * H * E + h * E + e + 1]
                            sl = slice(il * 512 + h * DH, il * 512 + (h + 1) * DH)
                            nc.vector.scalar_tensor_tensor(
                                vacc[:, sl], pbk[:, h * DH:(h + 1) * DH], gcol,
                                vacc[:, sl], ALU.mult, ALU.add)
                        p.bar()
                vbuf4 = vbuf.ap().rearrange("p (i h v) -> p i h v", i=NKV, h=H)
                for il in range(NH):
                    i = i0 + il
                    for hh in range(H):
                        nc.vector.tensor_copy(
                            vbuf4[:, i, hh, 0:DH],
                            vacc[:, il * 512 + hh * DH:il * 512 + (hh + 1) * DH])
                p.bar()
            if dbg_outs and "vbuf" in dbg_outs:
                p.dma(dbg_outs["vbuf"]
                      .rearrange("p (i h x) -> p i h x", i=NKV, h=H)
                      .rearrange("p i h x -> p (i h x)") if False else
                      dbg_outs["vbuf"], vbuf[:, :])
                p.bar()
            import os
            if os.environ.get("VBUF_ROUNDTRIP"):
                vrt = p.dram([128, NKV * H * VX3], BF16).ap()
                p.dma(vrt, vbuf[:, :])
                p.bar()
                p.dma(vbuf[:, :], vrt)
                p.bar()
        with p.scope():
            wcols = p.sb([128, DC * DH], BF16, name="f3wc")
            qaug = p.sb([66, T], BF16, name="f3qa")
            kaug = p.sb([66, T], BF16, name="f3ka")
            negb = p.sb([1, T], F32, name="f3nb")
            zwb = p.sb([DH, 512], BF16, name="f3zw")
            allsel = p.sb([H * E, H * E * 64], BF16, name="f3as")
            with p.scope():
                rowp = p.sb([H * E, 1], F32, name="f3rp")
                asf = p.sb([H * E, H * E * 64], F32, name="f3asf")
                nc.gpsimd.iota(asf.ap().rearrange("p (a b) -> p a b", b=64),
                               [[1, H * E], [0, 64]], channel_multiplier=0,
                               allow_small_or_imprecise_dtypes=True)
                nc.gpsimd.iota(rowp[:, :], [[0, 1]], channel_multiplier=1,
                               allow_small_or_imprecise_dtypes=True)
                p.bar()
                nc.vector.tensor_scalar(asf[:, :], asf[:, :], rowp[:, :], None,
                                        ALU.is_equal)
                p.bar()
                nc.vector.tensor_copy(allsel[:, :], asf[:, :])
                p.bar()
            if dbg_outs and "gmT" in dbg_outs:
                p.dma(dbg_outs["gmT"], gmT[:, :])
                p.bar()
            nc.gpsimd.memset(kaug[64:66, :], 1.0)
            nc.gpsimd.memset(qaug[64:66, :], 1.0)
            p.bar()
            for h in range(H):
                load_wcols(p, wcols, w_q3, D, h * DH, DH)
                _head_proj2(p, wcols, DC, xnb, T, qaug, 0, DH, scale=0.125)
                p.bar()
                load_wcols(p, wcols, w_k3, D, h * DH, DH)
                _head_proj2(p, wcols, DC, xnb, T, kaug, 0, DH)
                p.bar()
                emit_bbound(p, qaug[0:DH, :], kaug[0:DH, :], DH, T, T, negb)
                nc.vector.tensor_copy(qaug[64:65, :], negb[:1, :])
                p.bar()
                if h == 0 and dbg_outs and "qaug" in dbg_outs:
                    p.dma(dbg_outs["qaug"], qaug[:, :])
                    p.dma(dbg_outs["kaug"], kaug[:, :])
                    p.bar()
                vb_h = vbuf.ap().rearrange(
                    "p (i h x) -> p (i h) x", h=H, x=VX3)

                def out_cb(t0, w, o1, h=h):
                    if h == 0 and dbg_outs and "o1" in dbg_outs:
                        nc.vector.tensor_copy(zwb[:DH, :w], o1[:DH, :w])
                        p.bar()
                        p.dma(dbg_outs["o1"][:, t0:t0 + w], zwb[:DH, :w])
                        p.bar()
                    for e in range(E):
                        col = h * E + e
                        nc.tensor.matmul(p.pb[6][:DH, :w],
                                         allsel[:, col * 64:col * 64 + DH],
                                         gmT[:, t0:t0 + w],
                                         start=True, stop=True)
                        p.bar()
                        nc.vector.tensor_tensor(zwb[:DH, :w], o1[:DH, :w],
                                                p.pb[6][:DH, :w], ALU.mult)
                        p.bar()
                        r0 = (h * E + e) * DH
                        p.dma(zw_hbm[r0:r0 + DH, t0:t0 + w], zwb[:DH, :w])
                        p.bar()

                emit_attn_head2(p, kaug, qaug,
                                lambda i, h=h: vb_h[:, i * H + h, :],
                                NKV, T, out_cb,
                                dh=DH, dv=DH, causal_t0=lambda t0: t0,
                                dmaskadd=dmasks)
    if dbg_outs and "zw" in dbg_outs:
        p.dma(dbg_outs["zw"], zw_hbm[0:512, :])
        p.bar()
    with p.scope():
        if True:
            oex = p.sb([DH, H * E * D], BF16, name="f3oe")
            p.cdma(oex.ap().rearrange("v (e h d) -> v e h d", e=E, h=H),
                   oexp.rearrange("e h v d -> v e h d"))
            zwt = p.sb([DH, H * E * 512], BF16, name="f3zt")
            zw3 = zwt.ap().rearrange("p (cc n) -> p cc n", n=512)
            gsb = p.sb([128, 512], F32, name="f3gb")
            p.bar()
            oex4 = oex.ap().rearrange("v (e h d) -> v e h d", e=E, h=H)
            for t0 in range(0, T, 512):
                p.dma(zw3, zw_hbm.rearrange("(cc p) t -> p cc t", p=DH)
                      [:, :, t0:t0 + 512])
                p.bar()
                for ep in range(DC):
                    pbk = p.pb[2 + (ep % 2)]
                    n = 0
                    for h in range(H):
                        for e in range(E):
                            pair = h * E + e
                            nc.tensor.matmul(
                                pbk[:, :], oex4[:DH, e, h, ep * 128:(ep + 1) * 128],
                                zwt[0:DH, pair * 512:(pair + 1) * 512],
                                start=(n == 0), stop=(n == H * E - 1))
                            n += 1
                    p.bar()
                    nc.scalar.activation(gsb[:, :], pbk[:, :], AF.Silu)
                    p.bar()
                    p.dma(gT_hbm[ep * 128:(ep + 1) * 128, t0:t0 + 512], gsb[:, :])
                    p.bar()


def build_layer3(T=TT, dbg=False):
    nc = bass.Bass("TRN2", target_bir_lowering=False)
    xT = nc.dram_tensor("xT", [D, T], F32, kind="ExternalInput").ap()
    norm_w = nc.dram_tensor("norm_w", [D], F32, kind="ExternalInput").ap()
    Wq = nc.dram_tensor("Wq", [D, 1024], F32, kind="ExternalInput").ap()
    Wkv = nc.dram_tensor("Wkv", [D, 2048], F32, kind="ExternalInput").ap()
    Wout = nc.dram_tensor("Wout", [1024, D], F32, kind="ExternalInput").ap()
    Wg = nc.dram_tensor("Wg", [D, 16], F32, kind="ExternalInput").ap()
    w_q3 = nc.dram_tensor("w_q3", [D, 512], F32, kind="ExternalInput").ap()
    w_k3 = nc.dram_tensor("w_k3", [D, 512], F32, kind="ExternalInput").ap()
    w_gate3 = nc.dram_tensor("w_gate3", [D, 64], F32, kind="ExternalInput").ap()
    vexp = nc.dram_tensor("vexp", [8, 8, D, 64], F32, kind="ExternalInput").ap()
    oexp = nc.dram_tensor("oexp", [8, 8, 64, D], F32, kind="ExternalInput").ap()
    dmask = nc.dram_tensor("dmask", [128, 4 * 512], F32, kind="ExternalInput").ap()
    xoutT = nc.dram_tensor("xoutT", [D, T], F32, kind="ExternalOutput").ap()

    p = Prog(nc)
    xnT_f32 = p.dram([D, T], F32).ap()
    xnT_bf16 = p.dram([D, T], BF16).ap()
    sT = p.dram([D, T], F32).ap()
    gT = p.dram([D, T], F32).ap()
    emit_prelude(p, xT, xnT_f32, xnT_bf16, T)
    import os
    if not os.environ.get("SKIP_L3_ATTN"):
        emit_l3_attn(p, xnT_f32, xnT_bf16, norm_w, Wq, Wkv, Wout, Wg, dmask, sT, T)
    dbg_outs = None
    if dbg == 2:
        dbg_outs = {
            "vbuf": nc.dram_tensor("vbuf_dbg", [128, (T // 128) * 8 * 64], BF16,
                                   kind="ExternalOutput").ap(),
            "vbuf_in": nc.dram_tensor("vbuf_in", [128, (T // 128) * 8 * 64], F32,
                                      kind="ExternalInput").ap(),
            "gmT": nc.dram_tensor("gmT_dbg", [64, T], BF16,
                                  kind="ExternalOutput").ap(),
            "zw": nc.dram_tensor("zw_dbg", [512, T], BF16,
                                 kind="ExternalOutput").ap(),
            "qaug": nc.dram_tensor("qaug_dbg", [66, T], BF16,
                                   kind="ExternalOutput").ap(),
            "kaug": nc.dram_tensor("kaug_dbg", [66, T], BF16,
                                   kind="ExternalOutput").ap(),
            "o1": nc.dram_tensor("o1_dbg", [64, T], BF16,
                                 kind="ExternalOutput").ap(),
            "praw": nc.dram_tensor("praw_dbg", [128, 4 * 512 * 5], F32,
                                   kind="ExternalOutput").ap(),
            "ptil": nc.dram_tensor("ptil_dbg", [128, 4 * 512 * 5], BF16,
                                   kind="ExternalOutput").ap(),
            "pacc": nc.dram_tensor("pacc_dbg", [65, T], F32,
                                   kind="ExternalOutput").ap(),
        }
    emit_l3_ffn(p, xnT_bf16, w_q3, w_k3, w_gate3, vexp, oexp, dmask, gT, T,
                dbg_outs=dbg_outs)
    emit_glue(p, xT, sT, gT, xoutT, T)
    if dbg:
        for nm, src in (("s_dbg", sT), ("g_dbg", gT), ("xn_dbg", xnT_f32)):
            o = nc.dram_tensor(nm, [D, T], F32, kind="ExternalOutput").ap()
            p.dma(o, src)
        p.bar()
    p.close()
    return nc


def make_dmask(q_tile=512):
    m = np.zeros((128, 4 * q_tile), np.float32)
    for r in range(4):
        i = np.arange(128)[:, None]
        j = np.arange(q_tile)[None, :]
        m[:, r * q_tile:(r + 1) * q_tile] = (i + 128 * r <= j)
    return m


def make_dmaskadd(q_tile=512):
    return (make_dmask(q_tile) - 1.0) * 30000.0


def _run_layer3(xT_all, inputs, dbg=False, extra=None):
    nc = build_layer3(dbg=dbg)
    base = {
        "norm_w": inputs["L3_attn_norm_w"],
        "Wq": inputs["L3_attn_Wq"],
        "Wkv": inputs["L3_attn_Wkv"],
        "Wout": inputs["L3_attn_Wout"],
        "Wg": inputs["L3_attn_Wg"],
        "w_q3": inputs["L3_ffn_w_q"],
        "w_k3": inputs["L3_ffn_w_k"],
        "w_gate3": inputs["L3_ffn_w_gate"],
        "vexp": inputs["L3_ffn_value_experts"],
        "oexp": inputs["L3_ffn_output_experts"],
        "dmask": make_dmaskadd(),
    }
    base = {k: np.ascontiguousarray(v) for k, v in base.items()}
    if dbg == 2:
        base["vbuf_in"] = np.zeros((128, 32 * 8 * 64), np.float32)
        if extra:
            base.update({k: np.ascontiguousarray(v) for k, v in extra.items()})
    in_maps = []
    for c in range(8):
        m = dict(base)
        m["xT"] = xT_all[c % 4]
        in_maps.append(m)
    res = run_bass_kernel_spmd(nc, in_maps, core_ids=list(range(8)))
    if dbg:
        return res.results
    return [np.asarray(res.results[b]["xoutT"]) for b in range(4)]


def _to_fm(x):
    return [np.ascontiguousarray(x[b].T) for b in range(x.shape[0])]


def _run_layer2(xT_all, inputs, dbg=False):
    nc = build_layer2(dbg=dbg)
    base = {
        "WkT": np.ascontiguousarray(np.transpose(inputs["L2_attn_Wk"], (0, 2, 1))),
        "WvT": np.ascontiguousarray(np.transpose(inputs["L2_attn_Wv"], (0, 2, 1))),
        "WrT": np.ascontiguousarray(np.transpose(inputs["L2_attn_Wr"], (0, 2, 1))),
        "WoT": np.ascontiguousarray(np.transpose(inputs["L2_attn_Wo"], (0, 2, 1))),
        "lnw": np.ascontiguousarray(inputs["L2_attn_ln_w"]),
        "lnb": np.ascontiguousarray(inputs["L2_attn_ln_b"]),
        "wgT": np.ascontiguousarray(inputs["L2_ffn_w_gate"].T),
        "wuT": np.ascontiguousarray(inputs["L2_ffn_w_up"].T),
        "wdT": np.ascontiguousarray(inputs["L2_ffn_w_down"].T),
    }
    in_maps = []
    for c in range(8):
        m = dict(base)
        m["xT"] = xT_all[c % 4]
        in_maps.append(m)
    res = run_bass_kernel_spmd(nc, in_maps, core_ids=list(range(8)))
    if dbg:
        return res.results
    return [np.asarray(res.results[b]["xoutT"]) for b in range(4)]


def build_module(T=TT):
    nc = bass.Bass("TRN2", target_bir_lowering=False)
    EI = "ExternalInput"
    xT = nc.dram_tensor("xT", [D, T], F32, kind=EI).ap()
    gamma_h = nc.dram_tensor("gamma_h", [D], F32, kind=EI).ap()
    wq_h = nc.dram_tensor("wq_h", [D, 512], F32, kind=EI).ap()
    wkv_h = nc.dram_tensor("wkv_h", [D, 1024], F32, kind=EI).ap()
    wo_h = nc.dram_tensor("wo_h", [512, D], F32, kind=EI).ap()
    ln_g = nc.dram_tensor("ln_g", [D], F32, kind=EI).ap()
    ln_b = nc.dram_tensor("ln_b", [D], F32, kind=EI).ap()
    wqkv_l = nc.dram_tensor("wqkv_l", [D, 1536], F32, kind=EI).ap()
    wo_l = nc.dram_tensor("wo_l", [512, D], F32, kind=EI).ap()
    rts = nc.dram_tensor("rts", [D, 2], F32, kind=EI).ap()
    bandmask = nc.dram_tensor("bandmask", [128, 6 * 512], F32, kind=EI).ap()
    w_gate = nc.dram_tensor("w_gate", [2816, D], F32, kind=EI).ap()
    w_up = nc.dram_tensor("w_up", [2816, D], F32, kind=EI).ap()
    w_down = nc.dram_tensor("w_down", [D, 2816], F32, kind=EI).ap()
    W1T = nc.dram_tensor("W1T", [D, D], F32, kind=EI).ap()
    W2T = nc.dram_tensor("W2T", [D, D], F32, kind=EI).ap()
    W3T = nc.dram_tensor("W3T", [D, D], F32, kind=EI).ap()
    WoT2 = nc.dram_tensor("WoT2", [D, D], F32, kind=EI).ap()
    ropeC = nc.dram_tensor("ropeC", [128, T], F32, kind=EI).ap()
    ropeS = nc.dram_tensor("ropeS", [128, T], F32, kind=EI).ap()
    q_w = nc.dram_tensor("q_w", [D, D], F32, kind=EI).ap()
    q_b = nc.dram_tensor("q_b", [D], F32, kind=EI).ap()
    k_w = nc.dram_tensor("k_w", [D, 512], F32, kind=EI).ap()
    k_b = nc.dram_tensor("k_b", [512], F32, kind=EI).ap()
    v_w = nc.dram_tensor("v_w", [D, 512], F32, kind=EI).ap()
    v_b = nc.dram_tensor("v_b", [512], F32, kind=EI).ap()
    o_w = nc.dram_tensor("o_w", [D, D], F32, kind=EI).ap()
    o_b = nc.dram_tensor("o_b", [D], F32, kind=EI).ap()
    router_w = nc.dram_tensor("router_w", [D, 8], F32, kind=EI).ap()
    router_b = nc.dram_tensor("router_b", [8], F32, kind=EI).ap()
    w1 = nc.dram_tensor("w1", [8, D, 2048], F32, kind=EI).ap()
    b1 = nc.dram_tensor("b1", [8, 2048], F32, kind=EI).ap()
    w2 = nc.dram_tensor("w2", [8, 2048, D], F32, kind=EI).ap()
    b2 = nc.dram_tensor("b2", [8, D], F32, kind=EI).ap()
    WkT = nc.dram_tensor("WkT", [2, D, D], F32, kind=EI).ap()
    WvT = nc.dram_tensor("WvT", [2, D, D], F32, kind=EI).ap()
    WrT = nc.dram_tensor("WrT", [2, D, D], F32, kind=EI).ap()
    WoT = nc.dram_tensor("WoT", [2, D, D], F32, kind=EI).ap()
    lnw = nc.dram_tensor("lnw", [2, D], F32, kind=EI).ap()
    lnb = nc.dram_tensor("lnb", [2, D], F32, kind=EI).ap()
    wgT = nc.dram_tensor("wgT", [D, 3072], I32, kind=EI).ap()
    wuT = nc.dram_tensor("wuT", [D, 3072], I32, kind=EI).ap()
    wdT = nc.dram_tensor("wdT", [3072, D], I32, kind=EI).ap()
    norm_w = nc.dram_tensor("norm_w", [D], F32, kind=EI).ap()
    Wq = nc.dram_tensor("Wq", [D, 1024], F32, kind=EI).ap()
    Wkv = nc.dram_tensor("Wkv", [D, 2048], F32, kind=EI).ap()
    Wout = nc.dram_tensor("Wout", [1024, D], F32, kind=EI).ap()
    Wg = nc.dram_tensor("Wg", [D, 16], F32, kind=EI).ap()
    w_q3 = nc.dram_tensor("w_q3", [D, 512], F32, kind=EI).ap()
    w_k3 = nc.dram_tensor("w_k3", [D, 512], F32, kind=EI).ap()
    w_gate3 = nc.dram_tensor("w_gate3", [D, 64], F32, kind=EI).ap()
    vexp = nc.dram_tensor("vexp", [8, 8, D, 64], F32, kind=EI).ap()
    oexp = nc.dram_tensor("oexp", [8, 8, 64, D], F32, kind=EI).ap()
    dmask = nc.dram_tensor("dmask", [128, 4 * 512], F32, kind=EI).ap()
    xoutT = nc.dram_tensor("xoutT", [D, T], F32, kind="ExternalOutput").ap()

    p = Prog(nc)
    x1T = p.dram([D, T], F32).ap()
    x2T = p.dram([D, T], F32).ap()
    x3T = p.dram([D, T], F32).ap()
    xnT_f32 = p.dram([D, T], F32).ap()
    xnT_bf16 = p.dram([D, T], BF16).ap()
    sT = p.dram([D, T], F32).ap()
    gT = p.dram([D, T], F32).ap()
    emit_prelude(p, xT, xnT_f32, xnT_bf16, T)
    emit_l0_attn(p, xnT_f32, xnT_bf16, gamma_h, wq_h, wkv_h, wo_h, ln_g, ln_b,
                 wqkv_l, wo_l, rts, bandmask, sT, T)
    emit_l0_ffn(p, xnT_f32, w_gate, w_up, w_down, gT, T)
    emit_glue(p, xT, sT, gT, x1T, T)
    xnT_f32 = p.dram([D, T], F32).ap()
    xnT_bf16 = p.dram([D, T], BF16).ap()
    sT = p.dram([D, T], F32).ap()
    gT = p.dram([D, T], F32).ap()
    emit_prelude(p, x1T, xnT_f32, xnT_bf16, T)
    emit_l1_attn(p, xnT_bf16, W1T, W2T, W3T, WoT2, ropeC, ropeS, sT, T)
    emit_l1_ffn(p, xnT_f32, xnT_bf16, q_w, q_b, k_w, k_b, v_w, v_b, o_w, o_b,
                router_w, router_b, w1, b1, w2, b2, gT, T)
    emit_glue(p, x1T, sT, gT, x2T, T)
    xnT_f32 = p.dram([D, T], F32).ap()
    xnT_bf16 = p.dram([D, T], BF16).ap()
    sT_scr = p.dram([D, T], F32).ap()
    u_b = p.dram([D, T], F32).ap()
    gT = p.dram([D, T], F32).ap()
    emit_prelude(p, x2T, xnT_f32, xnT_bf16, T)
    sT2 = emit_l2_attn(p, xnT_f32, sT_scr, u_b, WkT, WvT, WrT, WoT, lnw, lnb, T)
    emit_l2_ffn(p, xnT_bf16, wgT, wuT, wdT, gT, T)
    emit_glue(p, x2T, sT2, gT, x3T, T)
    xnT_f32 = p.dram([D, T], F32).ap()
    xnT_bf16 = p.dram([D, T], BF16).ap()
    sT = p.dram([D, T], F32).ap()
    gT = p.dram([D, T], F32).ap()
    emit_prelude(p, x3T, xnT_f32, xnT_bf16, T)
    emit_l3_attn(p, xnT_f32, xnT_bf16, norm_w, Wq, Wkv, Wout, Wg, dmask, sT, T)
    emit_l3_ffn(p, xnT_bf16, w_q3, w_k3, w_gate3, vexp, oexp, dmask, gT, T)
    emit_glue(p, x3T, sT, gT, xoutT, T)
    p.close()
    return nc


def kernel(**inputs):
    x = np.ascontiguousarray(np.asarray(inputs["L0_attn_x"], np.float32))
    xT_all = _to_fm(x)
    nc = build_module()
    rc, rs = make_rope_tables()
    base = {
        "gamma_h": inputs["L0_attn_gamma_h"],
        "wq_h": inputs["L0_attn_wq_h"],
        "wkv_h": inputs["L0_attn_wkv_h"],
        "wo_h": inputs["L0_attn_wo_h"],
        "ln_g": inputs["L0_attn_ln_g"],
        "ln_b": inputs["L0_attn_ln_b"],
        "wqkv_l": inputs["L0_attn_wqkv_l"],
        "wo_l": inputs["L0_attn_wo_l"],
        "rts": np.stack([inputs["L0_attn_rt_q"], inputs["L0_attn_rt_kv"]], 1),
        "bandmask": make_bandmask(),
        "w_gate": inputs["L0_ffn_w_gate"],
        "w_up": inputs["L0_ffn_w_up"],
        "w_down": inputs["L0_ffn_w_down"],
        "W1T": inputs["L1_attn_W1"].T, "W2T": inputs["L1_attn_W2"].T,
        "W3T": inputs["L1_attn_W3"].T, "WoT2": inputs["L1_attn_Wo"].T,
        "ropeC": rc, "ropeS": rs,
        "q_w": inputs["L1_ffn_q_w"], "q_b": inputs["L1_ffn_q_b"],
        "k_w": inputs["L1_ffn_k_w"], "k_b": inputs["L1_ffn_k_b"],
        "v_w": inputs["L1_ffn_v_w"], "v_b": inputs["L1_ffn_v_b"],
        "o_w": inputs["L1_ffn_o_w"], "o_b": inputs["L1_ffn_o_b"],
        "router_w": inputs["L1_ffn_router_w"],
        "router_b": inputs["L1_ffn_router_b"],
        "w1": inputs["L1_ffn_w1"], "b1": inputs["L1_ffn_b1"],
        "w2": inputs["L1_ffn_w2"], "b2": inputs["L1_ffn_b2"],
        "WkT": np.transpose(inputs["L2_attn_Wk"], (0, 2, 1)),
        "WvT": np.transpose(inputs["L2_attn_Wv"], (0, 2, 1)),
        "WrT": np.transpose(inputs["L2_attn_Wr"], (0, 2, 1)),
        "WoT": np.transpose(inputs["L2_attn_Wo"], (0, 2, 1)),
        "lnw": inputs["L2_attn_ln_w"],
        "lnb": inputs["L2_attn_ln_b"],
        "wgT": inputs["L2_ffn_w_gate"].T,
        "wuT": inputs["L2_ffn_w_up"].T,
        "wdT": inputs["L2_ffn_w_down"].T,
        "norm_w": inputs["L3_attn_norm_w"],
        "Wq": inputs["L3_attn_Wq"],
        "Wkv": inputs["L3_attn_Wkv"],
        "Wout": inputs["L3_attn_Wout"],
        "Wg": inputs["L3_attn_Wg"],
        "w_q3": inputs["L3_ffn_w_q"],
        "w_k3": inputs["L3_ffn_w_k"],
        "w_gate3": inputs["L3_ffn_w_gate"],
        "vexp": inputs["L3_ffn_value_experts"],
        "oexp": inputs["L3_ffn_output_experts"],
        "dmask": make_dmaskadd(),
    }
    base = {k: np.ascontiguousarray(v) for k, v in base.items()}
    in_maps = []
    for c in range(8):
        m = dict(base)
        m["xT"] = xT_all[c % 4]
        in_maps.append(m)
    res = run_bass_kernel_spmd(nc, in_maps, core_ids=list(range(8)))
    out = np.stack(
        [np.asarray(res.results[b]["xoutT"], np.float32).T for b in range(4)],
        axis=0)
    return np.ascontiguousarray(out)


# version 07
# speedup vs baseline: 1.6300x; 1.0396x over previous
import math
from contextlib import ExitStack, contextmanager

import numpy as np
import concourse.bass as bass
import concourse.mybir as mybir
from concourse.bass_utils import run_bass_kernel_spmd

F32 = mybir.dt.float32
BF16 = mybir.dt.bfloat16
I32 = mybir.dt.int32
AF = mybir.ActivationFunctionType
ALU = mybir.AluOpType
AX = mybir.AxisListType

D = 1024
NB = 4
TT = 4096
DC = D // 128
LN_EPS = 1e-5
DEC95 = float(np.float32(0.95))
DEC05 = float(np.float32(np.float32(1.0) - np.float32(0.95)))

_uid = [0]


def _nm(prefix):
    _uid[0] += 1
    return f"{prefix}{_uid[0]}"


def chunk(ap, c, w=128):
    return ap[c * w:(c + 1) * w, :]


class Prog:

    def __init__(self, nc):
        self.nc = nc
        self.ctx = ExitStack()
        self.dma_sem = self.ctx.enter_context(nc.semaphore("dmas"))
        self.dma_issued = 0
        self.pb = [self.ctx.enter_context(nc.psum_tensor(f"pb{i}", [128, 512], F32))
                   for i in range(7)]
        self.pbt = self.ctx.enter_context(nc.psum_tensor("pbt", [128, 1024], BF16))
        self.pbt2 = self.pbt.bitcast(F32)
        self._scopes = [self.ctx]
        self.ones_col = self.sb([128, 1], F32, name="onescol")
        nc.gpsimd.memset(self.ones_col[:, :], 1.0)
        self.ones_col_b = self.sb([128, 1], BF16, name="onescolb")
        nc.gpsimd.memset(self.ones_col_b[:, :], 1.0)
        self.ones_row = self.sb([1, 128], F32, name="onesrow")
        nc.gpsimd.memset(self.ones_row[:, :], 1.0)
        self.identf = self.sb([128, 128], F32, name="identf")
        iot2 = self.sb([128, 128], F32, name="iota2")
        pid = self.sb([128, 1], F32, name="pid")
        nc.gpsimd.iota(iot2[:, :], [[1, 128]], channel_multiplier=0,
                       allow_small_or_imprecise_dtypes=True)
        nc.gpsimd.iota(pid[:, :], [[0, 1]], channel_multiplier=1,
                       allow_small_or_imprecise_dtypes=True)
        self.bar()
        nc.vector.tensor_scalar(self.identf[:, :], iot2[:, :], pid[:, :], None,
                                ALU.is_equal)
        self.bar()
        self.identb = self.sb([128, 128], BF16, name="identb")
        nc.vector.tensor_copy(self.identb[:, :], self.identf[:, :])
        self.bar()

    def close(self):
        self.bar()
        self.ctx.close()

    def sem(self, name):
        if not hasattr(self, "sems"):
            self.sems, self.cnt = {}, {}
        if name not in self.sems:
            self.sems[name] = self.ctx.enter_context(self.nc.semaphore(name))
            self.cnt[name] = 0
        return self.sems[name]

    @contextmanager
    def scope(self):
        es = ExitStack()
        self._scopes.append(es)
        try:
            yield es
        finally:
            self.bar()
            self._scopes.pop()
            es.close()

    def sb(self, shape, dt=F32, name=None):
        return self._scopes[-1].enter_context(
            self.nc.sbuf_tensor(name or _nm("sb"), shape, dt))

    def dram(self, shape, dt=F32, name=None):
        return self.nc.dram_tensor(name or _nm("scr"), shape, dt, kind="Internal")

    def dma(self, out, in_, eng=None, **kw):
        eng = eng or self.nc.sync
        eng.dma_start(out=out, in_=in_, **kw).then_inc(self.dma_sem, 16)
        self.dma_issued += 16

    def cdma(self, out, in_, **kw):
        self.dma(out, in_, eng=self.nc.gpsimd, **kw)

    def bar(self):
        if self.dma_issued:
            self.nc.sync.wait_ge(self.dma_sem, self.dma_issued)
        self.nc.all_engine_barrier()

    def cst(self, v, name=None):
        t = self.sb([128, 1], F32, name=name or _nm("cst"))
        self.nc.gpsimd.memset(t[:, :], float(v))
        self.bar()
        return t

    def selmat(self, G, dv, name=None):
        nc = self.nc
        sel = self.sb([G, G * dv], F32, name=name or _nm("sel"))
        rowp = self.sb([G, 1], F32, name=_nm("selr"))
        nc.gpsimd.iota(sel.ap().rearrange("p (a b) -> p a b", b=dv),
                       [[1, G], [0, dv]], channel_multiplier=0,
                       allow_small_or_imprecise_dtypes=True)
        nc.gpsimd.iota(rowp[:, :], [[0, 1]], channel_multiplier=1,
                       allow_small_or_imprecise_dtypes=True)
        self.bar()
        nc.vector.tensor_scalar(sel[:, :], sel[:, :], rowp[:, :], None,
                                ALU.is_equal)
        self.bar()
        return sel

    def bcast_row(self, out_sb, row_ap, T, bank=6):
        nc = self.nc
        pbk = self.pb[bank]
        for t0 in range(0, T, 512):
            w = min(512, T - t0)
            nc.tensor.matmul(pbk[:, :w], self.ones_row[:1, :], row_ap[:1, t0:t0 + w],
                             start=True, stop=True)
            self.bar()
            nc.vector.tensor_copy(out_sb[:, t0:t0 + w], pbk[:, :w])
            self.bar()


class WMat:

    def __init__(self, p, WT_hbm_ap, K, N, name=None):
        self.K, self.N, self.KC = K, N, K // 128
        self.sb = p.sb([128, self.KC * N], BF16, name=name or _nm("wm"))
        p.cdma(self.sb.ap().rearrange("p (c n) -> p c n", c=self.KC),
               WT_hbm_ap.rearrange("(c p) n -> p c n", p=128))

    def lhsT(self, c, n0, nw=128):
        base = self.sb.ap().rearrange("p (c n) -> p c n", c=self.KC)
        return base[:, c, n0:n0 + nw]


class WSlice:

    def __init__(self, p, WT_hbm_ap, K, name=None):
        self.p, self.WT, self.K, self.KC = p, WT_hbm_ap, K, K // 128
        self.sb = p.sb([128, self.KC * 128], BF16, name=name or _nm("ws"))

    def load(self, n0, nw=128):
        self.p.cdma(self.sb.ap().rearrange("p (c n) -> p c n", c=self.KC)[:, :, :nw],
                    self.WT[:, n0:n0 + nw].rearrange("(c p) n -> p c n", p=128))

    def lhsT(self, c, nw=128):
        base = self.sb.ap().rearrange("p (c n) -> p c n", c=self.KC)
        return base[:, c, 0:nw]


def emit_proj(p, wm, rhs_chunks, T, consume, banks=(0,), tb=512, n0=0, n1=None):
    nc = p.nc
    if n1 is None:
        n1 = wm.N // 128
    bi = 0
    for e in range(n0, n1):
        for t0 in range(0, T, tb):
            w = min(tb, T - t0)
            pbk = p.pb[banks[bi % len(banks)]]
            bi += 1
            for c in range(wm.KC):
                nc.tensor.matmul(pbk[:, :w], wm.lhsT(c, e * 128),
                                 rhs_chunks[c][:, t0:t0 + w],
                                 start=(c == 0), stop=(c == wm.KC - 1))
            p.bar()
            consume(e, t0, w, pbk)
            p.bar()


def emit_ln_stats(p, load_tile, T, mean_b, rstd_b, eps=LN_EPS):
    nc = p.nc
    with p.scope():
        xc = p.sb([128, 512], F32, name=_nm("lnx"))
        sq = p.sb([128, 512], F32, name=_nm("lnsq"))
        mrow = p.sb([1, T], F32, name=_nm("lnm"))
        rrow = p.sb([1, T], F32, name=_nm("lnr"))
        pm, pv = p.pb[5], p.pb[6]
        for t0 in range(0, T, 512):
            for c in range(DC):
                load_tile(c, t0, 512, xc)
                p.bar()
                nc.scalar.activation(sq[:, :], xc[:, :], AF.Square)
                p.bar()
                nc.tensor.matmul(pm[:1, :], p.ones_col[:, :], xc[:, :],
                                 start=(c == 0), stop=(c == DC - 1))
                nc.tensor.matmul(pv[:1, :], p.ones_col[:, :], sq[:, :],
                                 start=(c == 0), stop=(c == DC - 1))
                p.bar()
            nc.vector.tensor_scalar(mrow[:1, t0:t0 + 512], pm[:1, :512],
                                    1.0 / D, None, ALU.mult)
            nc.vector.tensor_scalar(rrow[:1, t0:t0 + 512], pv[:1, :512],
                                    1.0 / D, None, ALU.mult)
            p.bar()
        msq = p.sb([1, T], F32, name=_nm("lnms"))
        nc.scalar.activation(msq[:1, :], mrow[:1, :], AF.Square)
        p.bar()
        nc.vector.tensor_tensor(rrow[:1, :], rrow[:1, :], msq[:1, :], ALU.subtract)
        p.bar()
        nc.vector.tensor_scalar(rrow[:1, :], rrow[:1, :], float(eps), None, ALU.add)
        p.bar()
        nc.scalar.activation(rrow[:1, :], rrow[:1, :], AF.Sqrt)
        p.bar()
        nc.vector.reciprocal(rrow[:1, :], rrow[:1, :])
        p.bar()
        p.bcast_row(mean_b, mrow, T)
        p.bcast_row(rstd_b, rrow, T)


def emit_prelude(p, xT_hbm, xnT_f32_hbm, xnT_bf16_hbm, T=TT, need_f32=True):
    nc = p.nc
    with p.scope():
        mean_b = p.sb([128, T], F32, name=_nm("pmb"))
        rstd_b = p.sb([128, T], F32, name=_nm("prb"))

        def load_tile(c, t0, w, tile):
            p.dma(tile[:, :w], xT_hbm[c * 128:(c + 1) * 128, t0:t0 + w])

        emit_ln_stats(p, load_tile, T, mean_b, rstd_b)
        xc = p.sb([128, T], F32, name=_nm("pxc"))
        xnb = p.sb([128, T], BF16, name=_nm("pxb"))
        for c in range(DC):
            p.dma(xc[:, :], chunk(xT_hbm, c))
            p.bar()
            nc.vector.tensor_tensor(xc[:, :], xc[:, :], mean_b[:, :], ALU.subtract)
            p.bar()
            nc.vector.tensor_tensor(xc[:, :], xc[:, :], rstd_b[:, :], ALU.mult)
            p.bar()
            nc.vector.tensor_copy(xnb[:, :], xc[:, :])
            if need_f32:
                p.dma(chunk(xnT_f32_hbm, c), xc[:, :])
            p.bar()
            p.dma(chunk(xnT_bf16_hbm, c), xnb[:, :])
            p.bar()


def emit_glue(p, xT, sT, gT, xoutT, T=TT):
    nc = p.nc
    with p.scope():
        a = p.sb([128, T], F32, name=_nm("ga"))
        b = p.sb([128, T], F32, name=_nm("gb"))
        cc = p.sb([128, T], F32, name=_nm("gc"))
        for c in range(DC):
            p.dma(a[:, :], chunk(xT, c))
            p.dma(b[:, :], chunk(sT, c))
            p.dma(cc[:, :], chunk(gT, c))
            p.bar()
            nc.vector.tensor_tensor(b[:, :], b[:, :], cc[:, :], ALU.mult)
            p.bar()
            nc.vector.tensor_tensor(a[:, :], a[:, :], b[:, :], ALU.add)
            p.bar()
            p.dma(chunk(xoutT, c), a[:, :])
            p.bar()


def load_chunks_bf16(p, hbm_ap, T, names="xc"):
    tiles = [p.sb([128, T], BF16, name=f"{_nm(names)}_{c}") for c in range(DC)]
    for c in range(DC):
        p.dma(tiles[c][:, :], chunk(hbm_ap, c))
    p.bar()
    return tiles


def emit_attn_head(p, kT, qT, kaux, qaux, vtile, nkv_tiles, Tq, out_cb,
                   dh=64, dv=64, causal_t0=None, dmasks=None, gate_sel=None,
                   q_tile=512, chunk_plan=None):
    nc = p.nc
    with p.scope():
        praw = p.sb([128, 4 * q_tile], F32, name=_nm("apr"))
        ptil = p.sb([128, 4 * q_tile], BF16, name=_nm("apt"))
        rrow = p.sb([1, q_tile], F32, name=_nm("arr"))
        o1 = p.sb([128, q_tile], F32, name=_nm("ao1"))
        oacc = p.sb([128, q_tile], F32, name=_nm("aoa"))
        dacc = p.sb([1, q_tile], F32, name=_nm("ada"))
        for t0 in range(0, Tq, q_tile):
            w = min(q_tile, Tq - t0)
            if chunk_plan is not None:
                plan = chunk_plan(t0, w)
            elif causal_t0 is None:
                plan = [(i, None) for i in range(nkv_tiles)]
            else:
                qhi = causal_t0(t0) + w
                nkv = min(nkv_tiles, (qhi + 127) // 128)
                plan = []
                for i in range(nkv):
                    kv0 = i * 128
                    m = None
                    qpos0 = causal_t0(t0)
                    r = (kv0 - qpos0) // 128
                    if 0 <= r <= 3 and kv0 + 128 > qpos0:
                        m = dmasks[:, r * q_tile:r * q_tile + w]
                    plan.append((i, m))
            first = True
            for g0 in range(0, len(plan), 4):
                grp = plan[g0:g0 + 4]
                gn = len(grp)
                for u in range(gn):
                    kv0 = grp[u][0] * 128
                    nc.tensor.matmul(p.pb[u][:, :w], kT[:, kv0:kv0 + 128],
                                     qT[:, t0:t0 + w], start=True, stop=False)
                    nc.tensor.matmul(p.pb[u][:, :w], kaux[:2, kv0:kv0 + 128],
                                     qaux[:2, t0:t0 + w], start=False, stop=True)
                p.bar()
                for u in range(gn):
                    nc.scalar.activation(praw[:, u * q_tile:u * q_tile + w],
                                         p.pb[u][:, :w], AF.Exp)
                p.bar()
                for u in range(gn):
                    m = grp[u][1]
                    sl = slice(u * q_tile, u * q_tile + w)
                    if m is not None:
                        nc.vector.tensor_tensor(ptil[:, sl], praw[:, sl], m,
                                                ALU.mult)
                    else:
                        nc.vector.tensor_copy(ptil[:, sl], praw[:, sl])
                p.bar()
                for u in range(gn):
                    i = grp[u][0]
                    nc.tensor.matmul(p.pb[4][:dv, :w], vtile(i),
                                     ptil[:, u * q_tile:u * q_tile + w],
                                     start=(u == 0), stop=(u == gn - 1))
                    nc.tensor.matmul(p.pb[5][:1, :w], p.ones_col_b[:, :],
                                     ptil[:, u * q_tile:u * q_tile + w],
                                     start=(u == 0), stop=(u == gn - 1))
                p.bar()
                if first:
                    nc.vector.tensor_copy(oacc[:dv, :w], p.pb[4][:dv, :w])
                    nc.scalar.copy(dacc[:1, :w], p.pb[5][:1, :w])
                    first = False
                else:
                    nc.vector.scalar_tensor_tensor(
                        oacc[:dv, :w], p.pb[4][:dv, :w], 1.0,
                        oacc[:dv, :w], ALU.mult, ALU.add)
                    p.bar()
                    nc.vector.scalar_tensor_tensor(
                        dacc[:1, :w], p.pb[5][:1, :w], 1.0,
                        dacc[:1, :w], ALU.mult, ALU.add)
                p.bar()
            nc.vector.reciprocal(rrow[:1, :w], dacc[:1, :w])
            p.bar()
            nc.tensor.matmul(p.pb[5][:dv, :w], p.ones_row[:1, :dv],
                             rrow[:1, :w], start=True, stop=True)
            if gate_sel is not None:
                gates_ap, selmat_ap, col = gate_sel
                nc.tensor.matmul(p.pb[6][:dv, :w],
                                 selmat_ap[:, col * dv:(col + 1) * dv],
                                 gates_ap[:, t0:t0 + w], start=True, stop=True)
            p.bar()
            nc.vector.tensor_tensor(o1[:dv, :w], oacc[:dv, :w], p.pb[5][:dv, :w],
                                    ALU.mult)
            p.bar()
            if gate_sel is not None:
                nc.vector.tensor_tensor(o1[:dv, :w], o1[:dv, :w],
                                        p.pb[6][:dv, :w], ALU.mult)
                p.bar()
            out_cb(t0, w, o1)
            p.bar()


def _head_proj2(p, wcols_sb, KC, rhs_chunks, T, out_sb, row0, nw, scale=None,
                wait_dma=True, bias_col=None):
    nc = p.nc
    ssim, sev = p.sem("s_proj"), p.sem("s_evac")
    bp, be = p.cnt["s_proj"], p.cnt["s_evac"]
    nt = T // 512
    for ti in range(nt):
        t0 = ti * 512
        bank = p.pb[ti % 2]
        if ti == 0 and wait_dma:
            nc.tensor.wait_ge(p.dma_sem, p.dma_issued)
        if ti >= 2:
            nc.tensor.wait_ge(sev, be + ti - 1)
        for c in range(KC):
            mm = nc.tensor.matmul(bank[:nw, :], wcols_sb[:, c * nw:(c + 1) * nw],
                                  rhs_chunks[c][:, t0:t0 + 512],
                                  start=(c == 0), stop=(c == KC - 1))
        mm.then_inc(ssim, 1)
        p.cnt["s_proj"] += 1
        nc.vector.wait_ge(ssim, bp + ti + 1)
        dst = out_sb[row0:row0 + nw, t0:t0 + 512]
        if bias_col is not None:
            if scale is None:
                cp = nc.vector.tensor_scalar(dst, bank[:nw, :], bias_col, None,
                                             ALU.add)
            else:
                cp = nc.vector.tensor_scalar(dst, bank[:nw, :], bias_col,
                                             float(scale), ALU.add, ALU.mult)
        elif scale is None:
            cp = nc.vector.tensor_copy(dst, bank[:nw, :])
        else:
            cp = nc.vector.tensor_scalar(dst, bank[:nw, :], float(scale), None,
                                         ALU.mult)
        cp.then_inc(sev, 1)
        p.cnt["s_evac"] += 1


def emit_vbuf2(p, vT, vbuf3, n, dv, f32=False):
    nc = p.nc
    stp, svc = p.sem("s_tp"), p.sem("s_vc")
    bt, bc = p.cnt["s_tp"], p.cnt["s_vc"]
    if f32:
        banks = [p.pb[5], p.pb[6]]
        ident = p.identf
    else:
        banks = [p.pb[5].bitcast(BF16), p.pb[6].bitcast(BF16)]
        ident = p.identb
    for i in range(n):
        bk = banks[i % 2]
        if i >= 2:
            nc.tensor.wait_ge(svc, bc + i - 1)
        tp = nc.tensor.transpose(bk[:, 0:dv], vT[:dv, i * 128:(i + 1) * 128],
                                 ident[:dv, :dv])
        tp.then_inc(stp, 1)
        p.cnt["s_tp"] += 1
        nc.vector.wait_ge(stp, bt + i + 1)
        nc.vector.tensor_copy(vbuf3[:, i, 0:dv], bk[:, 0:dv]).then_inc(svc, 1)
        p.cnt["s_vc"] += 1
    p.bar()


def emit_attn_head2(p, kaug, qaug, vtile, nkv_tiles, Tq, out_cb, dh, dv,
                    causal_t0=None, dmaskadd=None, gate_sel=None,
                    chunk_plan=None, aux2=None):
    nc = p.nc
    QT = 512
    big = aux2 is not None
    ssim, sexp, sav = p.sem("s_sim"), p.sem("s_exp"), p.sem("s_av")
    with p.scope():
        ptil = p.sb([128, 2 * 3 * QT], BF16, name=_nm("a2p"))
        rrow = p.sb([1, QT], F32, name=_nm("a2r"))
        oacc = p.sb([128, QT], F32, name=_nm("a2o"))
        o1 = p.sb([128, QT], F32, name=_nm("a2q"))
        banksets = [(0, 1), (2, 3)] if big else [(0, 1, 2), (3, 5, 6)]
        GSZ = 2 if big else 3
        for t0 in range(0, Tq, QT):
            w = QT
            if chunk_plan is not None:
                plan = chunk_plan(t0, w)
            elif causal_t0 is None:
                plan = [(i, None) for i in range(nkv_tiles)]
            else:
                qpos0 = causal_t0(t0)
                nkv = min(nkv_tiles, (qpos0 + w + 127) // 128)
                plan = []
                for i in range(nkv):
                    kv0 = i * 128
                    r = (kv0 - qpos0) // 128
                    m = r if (0 <= r <= 3 and kv0 + 128 > qpos0) else None
                    plan.append((i, m))
            groups = [plan[i:i + GSZ] for i in range(0, len(plan), GSZ)]
            G = len(groups)
            nch = len(plan)
            b_sim, b_exp, b_av = p.cnt["s_sim"], p.cnt["s_exp"], p.cnt["s_av"]

            def emit_sim(g):
                grp, bs = groups[g], banksets[g % 2]
                if g >= 2:
                    nc.tensor.wait_ge(sexp, b_exp + g - 1)
                for u, (i, r) in enumerate(grp):
                    bk = p.pb[bs[u]]
                    if big:
                        kaux, qaux = aux2
                        nc.tensor.matmul(bk[:, :w], kaug[:, i * 128:(i + 1) * 128],
                                         qaug[:, t0:t0 + w], start=True, stop=False)
                        mm = nc.tensor.matmul(bk[:, :w],
                                              kaux[0:2, i * 128:(i + 1) * 128],
                                              qaux[0:2, t0:t0 + w],
                                              start=False, stop=True)
                    else:
                        mm = nc.tensor.matmul(bk[:, :w],
                                              kaug[0:66, i * 128:(i + 1) * 128],
                                              qaug[0:66, t0:t0 + w],
                                              start=True, stop=(r is None))
                        if r is not None:
                            mm = nc.tensor.matmul(bk[:, :w], p.identb[:, :],
                                                  dmaskadd[:, r * QT:r * QT + w],
                                                  start=False, stop=True)
                    if u == len(grp) - 1:
                        mm.then_inc(ssim, 1)
                p.cnt["s_sim"] += 1

            def emit_exp(g):
                grp, bs = groups[g], banksets[g % 2]
                pp = (g % 2) * 3 * QT
                nc.scalar.wait_ge(ssim, b_sim + g + 1)
                if g >= 2:
                    nc.scalar.wait_ge(sav, b_av + g - 1)
                for u in range(len(grp)):
                    a = nc.scalar.activation(ptil[:, pp + u * QT:pp + u * QT + w],
                                             p.pb[bs[u]][:, :w], AF.Exp)
                a.then_inc(sexp, 1)
                p.cnt["s_exp"] += 1

            ci = 0

            def emit_av(g):
                nonlocal ci
                grp = groups[g]
                pp = (g % 2) * 3 * QT
                nc.tensor.wait_ge(sexp, b_exp + g + 1)
                for u, (i, r) in enumerate(grp):
                    st, sp = (ci == 0), (ci == nch - 1)
                    if big:
                        nc.tensor.matmul(p.pb[4][:dv, :w], vtile(i),
                                         ptil[:, pp + u * QT:pp + u * QT + w],
                                         start=st, stop=sp)
                        mm = nc.tensor.matmul(p.pb[5][:1, :w], p.ones_col_b[:, :],
                                              ptil[:, pp + u * QT:pp + u * QT + w],
                                              start=st, stop=sp)
                    else:
                        mm = nc.tensor.matmul(p.pb[4][:dv + 2, :w], vtile(i),
                                              ptil[:, pp + u * QT:pp + u * QT + w],
                                              start=st, stop=sp)
                    ci += 1
                mm.then_inc(sav, 1)
                p.cnt["s_av"] += 1

            emit_sim(0)
            if G > 1:
                emit_sim(1)
            for g in range(G):
                emit_exp(g)
                emit_av(g)
                if g + 2 < G:
                    emit_sim(g + 2)
            p.bar()
            if big:
                nc.vector.reciprocal(rrow[:1, :w], p.pb[5][0:1, :w])
            else:
                nc.vector.reciprocal(rrow[:1, :w], p.pb[4][dv:dv + 1, :w])
            nc.scalar.copy(oacc[:dv, :w], p.pb[4][:dv, :w])
            p.bar()
            nc.tensor.matmul(p.pb[5][:dv, :w], p.ones_row[:1, :dv],
                             rrow[:1, :w], start=True, stop=True)
            if gate_sel is not None:
                gates_ap, selmat_ap, col = gate_sel
                nc.tensor.matmul(p.pb[6][:dv, :w],
                                 selmat_ap[:, col * dv:(col + 1) * dv],
                                 gates_ap[:, t0:t0 + w], start=True, stop=True)
            p.bar()
            nc.vector.tensor_tensor(o1[:dv, :w], oacc[:dv, :w], p.pb[5][:dv, :w],
                                    ALU.mult)
            p.bar()
            if gate_sel is not None:
                nc.vector.tensor_tensor(o1[:dv, :w], o1[:dv, :w],
                                        p.pb[6][:dv, :w], ALU.mult)
                p.bar()
            out_cb(t0, w, o1)
            p.bar()


def emit_bbound(p, qpT, kpT, dh, Tq, Tkv, negb_out, margin=2.0):
    nc = p.nc
    ssq, sev = p.sem("s_proj"), p.sem("s_evac")
    with p.scope():
        sq = p.sb([128, max(Tq, Tkv)], F32, name=_nm("bbs"))
        krow = p.sb([1, Tkv], F32, name=_nm("bbr"))
        kmax = p.sb([1, 1], F32, name=_nm("bbk"))

        def norms(src, T_, out_row):
            bq, be = p.cnt["s_proj"], p.cnt["s_evac"]
            nc.scalar.activation(sq[:dh, :T_], src[:dh, :], AF.Square)\
                .then_inc(ssq, 1)
            p.cnt["s_proj"] += 1
            nc.tensor.wait_ge(ssq, bq + 1)
            nt = T_ // 512
            for ti in range(nt):
                if ti >= 2:
                    nc.tensor.wait_ge(sev, be + ti - 1)
                nc.tensor.matmul(p.pb[ti % 2][:1, :], p.ones_col[:dh, :],
                                 sq[:dh, ti * 512:(ti + 1) * 512],
                                 start=True, stop=True).then_inc(ssq, 1)
                p.cnt["s_proj"] += 1
                nc.vector.wait_ge(ssq, bq + ti + 2)
                nc.vector.tensor_copy(out_row[:1, ti * 512:(ti + 1) * 512],
                                      p.pb[ti % 2][:1, :512]).then_inc(sev, 1)
                p.cnt["s_evac"] += 1
            nc.vector.wait_ge(sev, p.cnt["s_evac"])

        norms(kpT, Tkv, krow)
        nc.vector.tensor_reduce(kmax[:1, :1], krow[:1, :], AX.X, ALU.max)
        p.bar()
        nc.scalar.activation(kmax[:1, :1], kmax[:1, :1], AF.Sqrt)
        norms(qpT, Tq, negb_out)
        p.bar()
        nc.scalar.activation(negb_out[:1, :Tq], negb_out[:1, :Tq], AF.Sqrt)
        p.bar()
        nc.vector.tensor_scalar(negb_out[:1, :Tq], negb_out[:1, :Tq],
                                kmax[:1, :1], None, ALU.mult)
        p.bar()
        nc.vector.tensor_scalar(negb_out[:1, :Tq], negb_out[:1, :Tq], -1.0,
                                float(-margin), ALU.mult, ALU.add)
        p.bar()


def emit_l2_attn(p, xnT_f32, u_a, u_b, WkT, WvT, WrT, WoT, lnw, lnb, T=TT):
    nc = p.nc
    u_cur, u_next = xnT_f32, u_a
    yT_hbm = p.dram([D, T], BF16).ap()
    for sub in range(2):
        with p.scope():
            normedT = [p.sb([128, T], BF16, name=f"l2n{sub}_{c}") for c in range(DC)]
            with p.scope():
                mean_b = p.sb([128, T], F32, name=_nm("l2mb"))
                rstd_b = p.sb([128, T], F32, name=_nm("l2rb"))

                def load_u(c, t0, w, tile, u=u_cur):
                    p.dma(tile[:, :w], u[c * 128:(c + 1) * 128, t0:t0 + w])

                emit_ln_stats(p, load_u, T, mean_b, rstd_b)
                uc = p.sb([128, T], F32, name=_nm("l2uc"))
                lnw_sb = p.sb([128, 1], F32, name=_nm("l2lw"))
                lnb_sb = p.sb([128, 1], F32, name=_nm("l2lb"))
                for c in range(DC):
                    p.dma(uc[:, :], chunk(u_cur, c))
                    p.dma(lnw_sb[:, :], lnw[sub, c * 128:(c + 1) * 128]
                          .rearrange("(a b) -> a b", b=1))
                    p.dma(lnb_sb[:, :], lnb[sub, c * 128:(c + 1) * 128]
                          .rearrange("(a b) -> a b", b=1))
                    p.bar()
                    nc.vector.tensor_tensor(uc[:, :], uc[:, :], mean_b[:, :],
                                            ALU.subtract)
                    p.bar()
                    nc.vector.tensor_tensor(uc[:, :], uc[:, :], rstd_b[:, :],
                                            ALU.mult)
                    p.bar()
                    nc.vector.tensor_scalar(uc[:, :], uc[:, :], lnw_sb[:, :],
                                            lnb_sb[:, :], ALU.mult, ALU.add)
                    p.bar()
                    nc.vector.tensor_copy(normedT[c][:, :], uc[:, :])
                    p.bar()
            with p.scope():
                wk = WMat(p, WkT[sub, :, :], D, D, name=_nm("l2wk"))
                wv = WMat(p, WvT[sub, :, :], D, D, name=_nm("l2wv"))
                wr = WMat(p, WrT[sub, :, :], D, D, name=_nm("l2wr"))
                kv05 = p.sb([128, T], F32, name=_nm("l2kv"))
                rsg = p.sb([128, T], F32, name=_nm("l2rs"))
                sts = p.sb([128, T], F32, name=_nm("l2st"))
                dec = p.sb([128, T], F32, name=_nm("l2dc"))
                ksb = p.sb([128, 512], F32, name=_nm("l2ks"))
                ybf = p.sb([128, T], BF16, name=_nm("l2yb"))
                nc.gpsimd.memset(dec[:, :], DEC95)
                p.bar()
                for e in range(DC):
                    for t0 in range(0, T, 512):
                        kp, vp, rp = p.pb[0], p.pb[1], p.pb[2]
                        for c in range(DC):
                            st, sp = (c == 0), (c == DC - 1)
                            nc.tensor.matmul(kp[:, :], wk.lhsT(c, e * 128),
                                             normedT[c][:, t0:t0 + 512],
                                             start=st, stop=sp)
                            nc.tensor.matmul(vp[:, :], wv.lhsT(c, e * 128),
                                             normedT[c][:, t0:t0 + 512],
                                             start=st, stop=sp)
                            nc.tensor.matmul(rp[:, :], wr.lhsT(c, e * 128),
                                             normedT[c][:, t0:t0 + 512],
                                             start=st, stop=sp)
                        p.bar()
                        nc.scalar.copy(ksb[:, :], kp[:, :])
                        nc.vector.tensor_copy(rsg[:, t0:t0 + 512], rp[:, :])
                        p.bar()
                        nc.scalar.activation(rsg[:, t0:t0 + 512], rsg[:, t0:t0 + 512],
                                             AF.Sigmoid)
                        nc.vector.scalar_tensor_tensor(kv05[:, t0:t0 + 512], vp[:, :],
                                                       DEC05, ksb[:, :],
                                                       ALU.mult, ALU.mult)
                        p.bar()
                    nc.vector.tensor_tensor_scan(sts[:, :], dec[:, :], kv05[:, :],
                                                 0.0, ALU.mult, ALU.add)
                    p.bar()
                    nc.vector.tensor_tensor(sts[:, :], sts[:, :], rsg[:, :], ALU.mult)
                    p.bar()
                    nc.vector.tensor_copy(ybf[:, :], sts[:, :])
                    p.bar()
                    p.dma(chunk(yT_hbm, e), ybf[:, :])
                    p.bar()
            with p.scope():
                wo = WMat(p, WoT[sub, :, :], D, D, name=_nm("l2wo"))
                ybuf = [p.sb([128, 512], BF16, name=_nm("l2yt")) for _ in range(DC)]
                usb = p.sb([128, 512], F32, name=_nm("l2us"))
                for t0 in range(0, T, 512):
                    for c in range(DC):
                        p.dma(ybuf[c][:, :], yT_hbm[c * 128:(c + 1) * 128, t0:t0 + 512])
                    p.bar()
                    for e in range(DC):
                        pbk = p.pb[e % 2]
                        for c in range(DC):
                            nc.tensor.matmul(pbk[:, :], wo.lhsT(c, e * 128),
                                             ybuf[c][:, :],
                                             start=(c == 0), stop=(c == DC - 1))
                        p.dma(usb[:, :], u_cur[e * 128:(e + 1) * 128, t0:t0 + 512])
                        p.bar()
                        nc.vector.tensor_tensor(usb[:, :], usb[:, :], pbk[:, :],
                                                ALU.add)
                        p.bar()
                        p.dma(u_next[e * 128:(e + 1) * 128, t0:t0 + 512], usb[:, :])
                        p.bar()
        u_cur, u_next = u_next, u_b
    return u_cur


def emit_l2_ffn(p, xnT_bf16, wgT, wuT, wdT, gT_hbm, T=TT):
    nc = p.nc
    H3 = 3072
    inv127 = 1.0 / 127.0
    with p.scope():
        xnb = load_chunks_bf16(p, xnT_bf16, T, "f2x")
        wg = WSlice(p, wgT, D, name="f2wg")
        wu = WSlice(p, wuT, D, name="f2wu")
        wd = WSlice(p, wdT, H3, name="f2wd")
        TB_ = 1024
        hT = [p.sb([128, TB_], BF16, name=f"f2ht{i}") for i in range(H3 // 128)]
        h1 = p.sb([128, 512], F32, name="f2h1")
        gsb = p.sb([128, 512], F32, name="f2g")
        c127 = p.cst(inv127, name="c127")
        for tb0 in range(0, T, TB_):
            for hc in range(H3 // 128):
                wg.load(hc * 128)
                wu.load(hc * 128)
                p.bar()
                for t0 in range(0, TB_, 512):
                    gp, up = p.pb[0], p.pb[1]
                    for c in range(DC):
                        st, sp = (c == 0), (c == DC - 1)
                        nc.tensor.matmul(gp[:, :], wg.lhsT(c),
                                         xnb[c][:, tb0 + t0:tb0 + t0 + 512],
                                         start=st, stop=sp)
                        nc.tensor.matmul(up[:, :], wu.lhsT(c),
                                         xnb[c][:, tb0 + t0:tb0 + t0 + 512],
                                         start=st, stop=sp)
                    p.bar()
                    nc.scalar.activation(h1[:, :], gp[:, :], AF.Silu,
                                         scale=c127[:, :])
                    p.bar()
                    nc.vector.scalar_tensor_tensor(hT[hc][:, t0:t0 + 512], up[:, :],
                                                   inv127, h1[:, :], ALU.mult, ALU.mult)
                    p.bar()
            for e in range(DC):
                wd.load(e * 128)
                p.bar()
                for t0 in range(0, TB_, 512):
                    dp = p.pb[2]
                    for hc in range(H3 // 128):
                        nc.tensor.matmul(dp[:, :], wd.lhsT(hc), hT[hc][:, t0:t0 + 512],
                                         start=(hc == 0), stop=(hc == H3 // 128 - 1))
                    p.bar()
                    nc.scalar.activation(gsb[:, :], dp[:, :], AF.Silu,
                                         scale=c127[:, :])
                    p.bar()
                    p.dma(gT_hbm[e * 128:(e + 1) * 128, tb0 + t0:tb0 + t0 + 512],
                          gsb[:, :])
                    p.bar()


def build_layer2(T=TT, dbg=False):
    nc = bass.Bass("TRN2", target_bir_lowering=False)
    xT = nc.dram_tensor("xT", [D, T], F32, kind="ExternalInput").ap()
    WkT = nc.dram_tensor("WkT", [2, D, D], F32, kind="ExternalInput").ap()
    WvT = nc.dram_tensor("WvT", [2, D, D], F32, kind="ExternalInput").ap()
    WrT = nc.dram_tensor("WrT", [2, D, D], F32, kind="ExternalInput").ap()
    WoT = nc.dram_tensor("WoT", [2, D, D], F32, kind="ExternalInput").ap()
    lnw = nc.dram_tensor("lnw", [2, D], F32, kind="ExternalInput").ap()
    lnb = nc.dram_tensor("lnb", [2, D], F32, kind="ExternalInput").ap()
    wgT = nc.dram_tensor("wgT", [D, 3072], I32, kind="ExternalInput").ap()
    wuT = nc.dram_tensor("wuT", [D, 3072], I32, kind="ExternalInput").ap()
    wdT = nc.dram_tensor("wdT", [3072, D], I32, kind="ExternalInput").ap()
    xoutT = nc.dram_tensor("xoutT", [D, T], F32, kind="ExternalOutput").ap()

    p = Prog(nc)
    xnT_f32 = p.dram([D, T], F32).ap()
    xnT_bf16 = p.dram([D, T], BF16).ap()
    u_a = p.dram([D, T], F32).ap()
    u_b = p.dram([D, T], F32).ap()
    gT = p.dram([D, T], F32).ap()
    emit_prelude(p, xT, xnT_f32, xnT_bf16, T)
    sT = emit_l2_attn(p, xnT_f32, u_a, u_b, WkT, WvT, WrT, WoT, lnw, lnb, T)
    emit_l2_ffn(p, xnT_bf16, wgT, wuT, wdT, gT, T)
    emit_glue(p, xT, sT, gT, xoutT, T)
    if dbg:
        for nm, src in (("s_dbg", sT), ("g_dbg", gT), ("xn_dbg", xnT_f32)):
            o = nc.dram_tensor(nm, [D, T], F32, kind="ExternalOutput").ap()
            p.dma(o, src)
        p.bar()
    p.close()
    return nc


def emit_l0_route(p, xnT_f32, rts, qmask_row, m_row_out, T=TT, NSEL=1024):
    nc = p.nc
    scr_hbm = p.dram([2, T], F32).ap()
    with p.scope():
        rtsb = p.sb([128, DC * 2], F32, name="l0rt")
        p.dma(rtsb.ap().rearrange("p (c n) -> p c n", c=DC),
              rts.rearrange("(c p) n -> p c n", p=128))
        xt = p.sb([128, 512], F32, name="l0xt")
        scores = p.sb([2, T], F32, name="l0sc")
        p.bar()
        for t0 in range(0, T, 512):
            pbk = p.pb[0]
            for c in range(DC):
                p.dma(xt[:, :], xnT_f32[c * 128:(c + 1) * 128, t0:t0 + 512])
                p.bar()
                nc.tensor.matmul(pbk[:2, :], rtsb.ap().rearrange(
                    "p (c n) -> p c n", c=DC)[:, c, :], xt[:, :],
                    start=(c == 0), stop=(c == DC - 1))
                p.bar()
            nc.vector.tensor_copy(scores[:2, t0:t0 + 512], pbk[:2, :512])
            p.bar()
        p.dma(scr_hbm, scores[:2, :])
        p.bar()
        s2 = p.sb([128, 64], F32, name="l0s2")
        p.dma(s2[:, 0:32], scr_hbm[0:1, :].rearrange("a (q c) -> (a q) c", q=128))
        p.dma(s2[:, 32:64], scr_hbm[1:2, :].rearrange("a (q c) -> (a q) c", q=128))
        lo = p.sb([1, 2], F32, name="l0lo")
        hi = p.sb([1, 2], F32, name="l0hi")
        mid = p.sb([1, 2], F32, name="l0mi")
        thb = p.sb([128, 2], F32, name="l0th")
        ind = p.sb([128, 64], F32, name="l0in")
        cnt2 = p.sb([128, 2], F32, name="l0c2")
        cntc = p.sb([2, 1], F32, name="l0cc")
        crow = p.sb([1, 2], F32, name="l0cr")
        pred = p.sb([1, 2], F32, name="l0pd")
        nc.gpsimd.memset(lo[:1, :], -1e5)
        nc.gpsimd.memset(hi[:1, :], 1e5)
        p.bar()

        def count_gt(theta_row, out_crow):
            nc.tensor.matmul(p.pb[6][:, :2], p.ones_row[:1, :], theta_row[:1, :],
                             start=True, stop=True)
            p.bar()
            nc.vector.tensor_copy(thb[:, :], p.pb[6][:, :2])
            p.bar()
            nc.vector.tensor_scalar(ind[:, 0:32], s2[:, 0:32], thb[:, 0:1],
                                    None, ALU.is_gt)
            p.bar()
            nc.vector.tensor_scalar(ind[:, 32:64], s2[:, 32:64], thb[:, 1:2],
                                    None, ALU.is_gt)
            p.bar()
            nc.vector.tensor_reduce(cnt2[:, 0:1], ind[:, 0:32], AX.X, ALU.add)
            p.bar()
            nc.vector.tensor_reduce(cnt2[:, 1:2], ind[:, 32:64], AX.X, ALU.add)
            p.bar()
            nc.tensor.matmul(p.pb[5][:2, :1], cnt2[:, :], p.ones_col[:, :],
                             start=True, stop=True)
            p.bar()
            nc.vector.tensor_copy(cntc[:2, :1], p.pb[5][:2, :1])
            p.bar()
            nc.tensor.transpose(p.pbt2[:1, :2], cntc[:2, :1], p.identf[:2, :2])
            p.bar()
            nc.vector.tensor_copy(out_crow[:1, :2], p.pbt2[:1, :2])
            p.bar()

        for _ in range(46):
            nc.vector.tensor_tensor(mid[:1, :], lo[:1, :], hi[:1, :], ALU.add)
            p.bar()
            nc.vector.tensor_scalar(mid[:1, :], mid[:1, :], 0.5, None, ALU.mult)
            p.bar()
            count_gt(mid, crow)
            nc.vector.tensor_scalar(pred[:1, :], crow[:1, :], float(NSEL) - 0.5,
                                    None, ALU.is_gt)
            p.bar()
            nc.vector.tensor_tensor(crow[:1, :], mid[:1, :], lo[:1, :],
                                    ALU.subtract)
            p.bar()
            nc.vector.tensor_tensor(crow[:1, :], crow[:1, :], pred[:1, :],
                                    ALU.mult)
            p.bar()
            nc.vector.tensor_tensor(lo[:1, :], lo[:1, :], crow[:1, :], ALU.add)
            p.bar()
            nc.vector.tensor_tensor(crow[:1, :], hi[:1, :], mid[:1, :],
                                    ALU.subtract)
            p.bar()
            nc.vector.tensor_tensor(crow[:1, :], crow[:1, :], pred[:1, :],
                                    ALU.mult)
            p.bar()
            nc.vector.tensor_tensor(hi[:1, :], mid[:1, :], crow[:1, :], ALU.add)
            p.bar()
        neg = p.sb([128, 32], F32, name="l0ng")
        mh = p.sb([128, 2], F32, name="l0mh")
        tau_row = p.sb([1, 2], F32, name="l0ta")
        nc.gpsimd.memset(neg[:, :], -3e38)
        p.bar()
        nc.tensor.matmul(p.pb[6][:, :2], p.ones_row[:1, :], hi[:1, :],
                         start=True, stop=True)
        p.bar()
        nc.vector.tensor_copy(thb[:, :], p.pb[6][:, :2])
        p.bar()
        for r in range(2):
            half = s2[:, 32 * r:32 * (r + 1)]
            nc.vector.tensor_scalar(ind[:, 0:32], half, thb[:, r:r + 1], None,
                                    ALU.is_le)
            p.bar()
            nc.vector.tensor_tensor(ind[:, 32:64], half, ind[:, 0:32], ALU.mult)
            p.bar()
            nc.vector.tensor_scalar(ind[:, 0:32], ind[:, 0:32], 1.0, 3e38,
                                    ALU.subtract, ALU.mult)
            p.bar()
            nc.vector.tensor_tensor(ind[:, 32:64], ind[:, 32:64], ind[:, 0:32],
                                    ALU.add)
            p.bar()
            nc.vector.tensor_reduce(mh[:, r:r + 1], ind[:, 32:64], AX.X, ALU.max)
            p.bar()
        mh2 = p.sb([2, 128], F32, name="l0m2")
        nc.tensor.transpose(p.pbt2[:2, :128], mh[:, :2], p.identf[:, :])
        p.bar()
        nc.vector.tensor_copy(mh2[:2, :], p.pbt2[:2, :128])
        p.bar()
        nc.vector.tensor_reduce(cntc[:2, 0:1], mh2[:2, :], AX.X, ALU.max)
        p.bar()
        nc.tensor.transpose(p.pbt2[:1, :2], cntc[:2, :1], p.identf[:2, :2])
        p.bar()
        nc.vector.tensor_copy(tau_row[:1, :2], p.pbt2[:1, :2])
        p.bar()
        count_gt(tau_row, crow)
        k2row = p.sb([1, 2], F32, name="l0k2")
        nc.vector.tensor_scalar(k2row[:1, :], crow[:1, :], -1.0, float(NSEL),
                                ALU.mult, ALU.add)
        p.bar()
        tauc = p.sb([2, 1], F32, name="l0tc")
        k2c = p.sb([2, 1], F32, name="l0kc")
        nc.tensor.transpose(p.pbt2[:2, :1], tau_row[:1, :2], p.identf[:1, :1])
        p.bar()
        nc.vector.tensor_copy(tauc[:2, :1], p.pbt2[:2, :1])
        p.bar()
        nc.tensor.transpose(p.pbt2[:2, :1], k2row[:1, :2], p.identf[:1, :1])
        p.bar()
        nc.vector.tensor_copy(k2c[:2, :1], p.pbt2[:2, :1])
        p.bar()
        eq = p.sb([2, T], F32, name="l0eq")
        ones2 = p.sb([2, T], F32, name="l0o2")
        ps = p.sb([2, T], F32, name="l0ps")
        tot = p.sb([2, 1], F32, name="l0tt")
        msk = p.sb([2, T], F32, name="l0mk")
        nc.gpsimd.memset(ones2[:2, :], 1.0)
        p.bar()
        nc.vector.tensor_scalar(eq[:2, :], scores[:2, :], tauc[:2, :1], None,
                                ALU.is_equal)
        p.bar()
        nc.vector.tensor_tensor_scan(ps[:2, :], ones2[:2, :], eq[:2, :], 0.0,
                                     ALU.mult, ALU.add)
        p.bar()
        nc.vector.tensor_reduce(tot[:2, :1], eq[:2, :], AX.X, ALU.add)
        p.bar()
        nc.vector.tensor_scalar(ps[:2, :], ps[:2, :], tot[:2, :1], -1.0,
                                ALU.subtract, ALU.mult)
        p.bar()
        nc.vector.tensor_scalar(ps[:2, :], ps[:2, :], k2c[:2, :1], None,
                                ALU.is_lt)
        p.bar()
        nc.vector.tensor_tensor(ps[:2, :], ps[:2, :], eq[:2, :], ALU.mult)
        p.bar()
        nc.vector.tensor_scalar(msk[:2, :], scores[:2, :], tauc[:2, :1], None,
                                ALU.is_gt)
        p.bar()
        nc.vector.tensor_tensor(msk[:2, :], msk[:2, :], ps[:2, :], ALU.add)
        p.bar()
        nc.vector.tensor_copy(qmask_row[:1, :], msk[0:1, :])
        p.bar()
        nc.vector.tensor_scalar(msk[:2, :], msk[:2, :], 1.0, 30000.0,
                                ALU.subtract, ALU.mult)
        p.bar()
        p.cdma(m_row_out, msk[1:2, :])
        p.bar()


def emit_l0_attn(p, xnT_f32, xnT_bf16, gamma_h, wq_h, wkv_h, wo_h, ln_g, ln_b,
                 wqkv_l, wo_l, rts, bandmask_hbm, sT_hbm, T=TT, dbg_outs=None):
    nc = p.nc
    H, DH = 8, 64
    SC = 0.125
    outl = p.dram([512, T], BF16).ap()
    outh = p.dram([512, T], BF16).ap()
    with p.scope():
        xlT = [p.sb([128, T], BF16, name=f"l0xl{c}") for c in range(DC)]
        with p.scope():
            mean_b = p.sb([128, T], F32, name="l0mb")
            rstd_b = p.sb([128, T], F32, name="l0rb")

            def load_tile(c, t0, w, tile):
                p.dma(tile[:, :w], xnT_f32[c * 128:(c + 1) * 128, t0:t0 + w])

            emit_ln_stats(p, load_tile, T, mean_b, rstd_b)
            xc = p.sb([128, T], F32, name="l0xc")
            gcol = p.sb([128, 1], F32, name="l0gc")
            bcol = p.sb([128, 1], F32, name="l0bc")
            for c in range(DC):
                p.dma(xc[:, :], xnT_f32[c * 128:(c + 1) * 128, :])
                p.dma(gcol[:, :], ln_g[c * 128:(c + 1) * 128]
                      .rearrange("(a b) -> a b", b=1))
                p.dma(bcol[:, :], ln_b[c * 128:(c + 1) * 128]
                      .rearrange("(a b) -> a b", b=1))
                p.bar()
                nc.vector.tensor_tensor(xc[:, :], xc[:, :], mean_b[:, :],
                                        ALU.subtract)
                p.bar()
                nc.vector.tensor_tensor(xc[:, :], xc[:, :], rstd_b[:, :], ALU.mult)
                p.bar()
                nc.vector.tensor_scalar(xc[:, :], xc[:, :], gcol[:, :],
                                        bcol[:, :], ALU.mult, ALU.add)
                p.bar()
                nc.vector.tensor_copy(xlT[c][:, :], xc[:, :])
                p.bar()
        with p.scope():
            wcols = p.sb([128, DC * DH], BF16, name="l0wc")
            qaug = p.sb([66, T], BF16, name="l0q")
            kaug = p.sb([66, T], BF16, name="l0k")
            vT = p.sb([DH, T], F32, name="l0v")
            vbuf = p.sb([128, (T // 128) * 66], BF16, name="l0vb")
            vbuf3 = vbuf.ap().rearrange("p (i v) -> p i v", v=66)
            negb = p.sb([1, T], F32, name="l0nb")
            ob = p.sb([DH, 512], BF16, name="l0ob")
            bandm = p.sb([128, 6 * 512], BF16, name="l0bm")
            p.cdma(bandm[:, :], bandmask_hbm)
            nc.gpsimd.memset(kaug[64:66, :], 1.0)
            nc.gpsimd.memset(qaug[64:66, :], 1.0)
            nc.gpsimd.memset(vbuf[:, :], 1.0)
            p.bar()

            def bandplan(t0, w):
                c0 = t0 // 128
                plan = []
                for r in range(6):
                    ci = c0 - 1 + r
                    if 0 <= ci < T // 128:
                        plan.append((ci, r))
                return plan

            for hh in range(H):
                load_wcols(p, wcols, wqkv_l, D, hh * DH, DH)
                _head_proj2(p, wcols, DC, xlT, T, qaug, 0, DH, scale=SC)
                p.bar()
                load_wcols(p, wcols, wqkv_l, D, 512 + hh * DH, DH)
                _head_proj2(p, wcols, DC, xlT, T, kaug, 0, DH)
                p.bar()
                load_wcols(p, wcols, wqkv_l, D, 1024 + hh * DH, DH)
                _head_proj2(p, wcols, DC, xlT, T, vT, 0, DH)
                p.bar()
                emit_vbuf2(p, vT, vbuf3, T // 128, DH, f32=True)
                emit_bbound(p, qaug[0:DH, :], kaug[0:DH, :], DH, T, T, negb)
                nc.vector.tensor_copy(qaug[64:65, :], negb[:1, :])
                p.bar()

                def out_cb(t0, w, o1, hh=hh):
                    nc.vector.tensor_copy(ob[:DH, :w], o1[:DH, :w])
                    p.bar()
                    p.dma(outl[hh * DH:(hh + 1) * DH, t0:t0 + w], ob[:DH, :w])

                emit_attn_head2(p, kaug, qaug, lambda i: vbuf3[:, i, :],
                                T // 128, T, out_cb, dh=DH, dv=DH,
                                chunk_plan=bandplan, dmaskadd=bandm)
    with p.scope():
        qmask_row = p.sb([1, T], F32, name="l0qm")
        kaugh = p.sb([66, T], BF16, name="l0kxh")
        qaugh = p.sb([66, T], BF16, name="l0qxh")
        nc.gpsimd.memset(kaugh[64:66, :], 1.0)
        nc.gpsimd.memset(qaugh[64:66, :], 1.0)
        p.bar()
        emit_l0_route(p, xnT_f32, rts, qmask_row, kaugh[65:66, :], T)
        if dbg_outs is not None:
            p.dma(dbg_outs["qm"], qmask_row[:1, :])
            p.dma(dbg_outs["kx"], kaugh[64:66, :])
            p.bar()
        hxT = [p.sb([128, T], BF16, name=f"l0hx{c}") for c in range(DC)]
        with p.scope():
            rstd_b = p.sb([128, T], F32, name="l0hr")
            with p.scope():
                xc2 = p.sb([128, 512], F32, name="l0x2")
                sq2 = p.sb([128, 512], F32, name="l0s2b")
                rrow = p.sb([1, T], F32, name="l0rr")
                pv = p.pb[5]
                for t0 in range(0, T, 512):
                    for c in range(DC):
                        p.dma(xc2[:, :], xnT_f32[c * 128:(c + 1) * 128,
                                                 t0:t0 + 512])
                        p.bar()
                        nc.scalar.activation(sq2[:, :], xc2[:, :], AF.Square)
                        p.bar()
                        nc.tensor.matmul(pv[:1, :], p.ones_col[:, :], sq2[:, :],
                                         start=(c == 0), stop=(c == DC - 1))
                        p.bar()
                    nc.vector.tensor_copy(rrow[:1, t0:t0 + 512], pv[:1, :512])
                    p.bar()
                nc.scalar.activation(rrow[:1, :], rrow[:1, :], AF.Sqrt)
                p.bar()
                nc.vector.tensor_scalar(rrow[:1, :], rrow[:1, :], 1e-12, None,
                                        ALU.max)
                p.bar()
                nc.vector.reciprocal(rrow[:1, :], rrow[:1, :])
                p.bar()
                nc.vector.tensor_scalar(rrow[:1, :], rrow[:1, :], 32.0, None,
                                        ALU.mult)
                p.bar()
                p.bcast_row(rstd_b, rrow, T)
            xc = p.sb([128, T], F32, name="l0xch")
            gcol = p.sb([128, 1], F32, name="l0gch")
            for c in range(DC):
                p.dma(xc[:, :], xnT_f32[c * 128:(c + 1) * 128, :])
                p.dma(gcol[:, :], gamma_h[c * 128:(c + 1) * 128]
                      .rearrange("(a b) -> a b", b=1))
                p.bar()
                nc.vector.tensor_tensor(xc[:, :], xc[:, :], rstd_b[:, :], ALU.mult)
                p.bar()
                nc.vector.tensor_scalar(hxT[c][:, :], xc[:, :], gcol[:, :], None,
                                        ALU.mult)
                p.bar()
        with p.scope():
            wcols = p.sb([128, DC * 64], BF16, name="l0wch")
            vT = p.sb([64, T], F32, name="l0vh")
            vbuf = p.sb([128, (T // 128) * 66], BF16, name="l0vbh")
            vbuf3 = vbuf.ap().rearrange("p (i v) -> p i v", v=66)
            negb = p.sb([1, T], F32, name="l0nbh")
            ob = p.sb([64, 512], BF16, name="l0obh")
            nc.gpsimd.memset(vbuf[:, :], 1.0)
            p.bar()
            for hh in range(H):
                load_wcols(p, wcols, wq_h, D, hh * 64, 64)
                _head_proj2(p, wcols, DC, hxT, T, qaugh, 0, 64, scale=SC)
                p.bar()
                load_wcols(p, wcols, wkv_h, D, hh * 64, 64)
                _head_proj2(p, wcols, DC, hxT, T, kaugh, 0, 64)
                p.bar()
                load_wcols(p, wcols, wkv_h, D, 512 + hh * 64, 64)
                _head_proj2(p, wcols, DC, hxT, T, vT, 0, 64)
                p.bar()
                emit_vbuf2(p, vT, vbuf3, T // 128, 64, f32=True)
                emit_bbound(p, qaugh[0:64, :], kaugh[0:64, :], 64, T, T, negb)
                nc.vector.tensor_copy(qaugh[64:65, :], negb[:1, :])
                p.bar()

                def out_cb(t0, w, o1, hh=hh):
                    nc.vector.tensor_copy(ob[:64, :w], o1[:64, :w])
                    p.bar()
                    p.dma(outh[hh * 64:(hh + 1) * 64, t0:t0 + w], ob[:64, :w])

                emit_attn_head2(p, kaugh, qaugh,
                                lambda i: vbuf3[:, i, :], T // 128, T, out_cb,
                                dh=64, dv=64,
                                gate_sel=(qmask_row, p.ones_row, 0))
    with p.scope():
        wol = p.sb([128, 4 * D], BF16, name="l0wol")
        woh = p.sb([128, 4 * D], BF16, name="l0woh")
        load_wmat(p, wol, wo_l, 512, D)
        load_wmat(p, woh, wo_h, 512, D)
        rl = [p.sb([128, 512], BF16, name=f"l0rl{c}") for c in range(4)]
        rh = [p.sb([128, 512], BF16, name=f"l0rh{c}") for c in range(4)]
        ssb = p.sb([128, 512], F32, name="l0ss")
        p.bar()
        for t0 in range(0, T, 512):
            for c in range(4):
                p.dma(rl[c][:, :], outl[c * 128:(c + 1) * 128, t0:t0 + 512])
                p.dma(rh[c][:, :], outh[c * 128:(c + 1) * 128, t0:t0 + 512])
            p.bar()
            for e in range(DC):
                pbk = p.pb[e % 2]
                for c in range(4):
                    nc.tensor.matmul(pbk[:, :], wm_lhsT(wol, 4, c, e * 128),
                                     rl[c][:, :], start=(c == 0), stop=False)
                for c in range(4):
                    nc.tensor.matmul(pbk[:, :], wm_lhsT(woh, 4, c, e * 128),
                                     rh[c][:, :], start=False, stop=(c == 3))
                p.bar()
                nc.vector.tensor_copy(ssb[:, :], pbk[:, :])
                p.bar()
                p.dma(sT_hbm[e * 128:(e + 1) * 128, t0:t0 + 512], ssb[:, :])
                p.bar()


def emit_fp4_quant(p, x_ap, out_ap, W, sc):
    nc = p.nc
    nb = W // 16
    x3 = x_ap.rearrange("p (b k) -> p b k", k=16)
    nc.vector.tensor_reduce(sc["bm"][:, :nb], x3, AX.X, ALU.max,
                            apply_absolute_value=True)
    p.bar()
    nc.vector.tensor_scalar(sc["bm"][:, :nb], sc["bm"][:, :nb], 1.0 / 6.0,
                            1e-12, ALU.mult, ALU.max)
    p.bar()
    nc.vector.reciprocal(sc["rc"][:, :nb], sc["bm"][:, :nb])
    p.bar()
    re3 = sc["re"].ap().rearrange("p (b k) -> p b k", k=16)
    for j in range(16):
        nc.vector.tensor_copy(re3[:, :nb, j], sc["rc"][:, :nb])
    p.bar()
    nc.vector.tensor_tensor(sc["xn"][:, :W], x_ap, sc["re"][:, :W], ALU.mult)
    p.bar()
    nc.scalar.activation(sc["a"][:, :W], sc["xn"][:, :W], AF.Abs)
    p.bar()
    nc.vector.tensor_scalar(sc["acc"][:, :W], sc["a"][:, :W], 0.25, None,
                            ALU.is_gt)
    p.bar()
    for m in (0.75, 1.25, 1.75):
        nc.vector.scalar_tensor_tensor(sc["acc"][:, :W], sc["a"][:, :W],
                                       float(m), sc["acc"][:, :W],
                                       ALU.is_gt, ALU.add)
        p.bar()
    nc.vector.tensor_scalar(sc["acc"][:, :W], sc["acc"][:, :W], 0.5, None,
                            ALU.mult)
    p.bar()
    for m in (2.5, 3.5):
        nc.vector.scalar_tensor_tensor(sc["acc"][:, :W], sc["a"][:, :W],
                                       float(m), sc["acc"][:, :W],
                                       ALU.is_gt, ALU.add)
        p.bar()
    nc.vector.tensor_scalar(sc["v"][:, :W], sc["a"][:, :W], 5.0, None, ALU.is_gt)
    p.bar()
    nc.vector.scalar_tensor_tensor(sc["acc"][:, :W], sc["v"][:, :W], 2.0,
                                   sc["acc"][:, :W], ALU.mult, ALU.add)
    p.bar()
    nc.scalar.activation(sc["v"][:, :W], sc["xn"][:, :W], AF.Sign)
    p.bar()
    nc.vector.tensor_tensor(sc["acc"][:, :W], sc["acc"][:, :W], sc["v"][:, :W],
                            ALU.mult)
    p.bar()
    nc.vector.reciprocal(sc["re"][:, :W], sc["re"][:, :W])
    p.bar()
    nc.vector.tensor_tensor(out_ap, sc["acc"][:, :W], sc["re"][:, :W], ALU.mult)
    p.bar()


def mk_fp4_scratch(p, W, tag="q4"):
    return {
        "bm": p.sb([128, W // 16], F32, name=_nm(tag + "b")),
        "rc": p.sb([128, W // 16], F32, name=_nm(tag + "r")),
        "re": p.sb([128, W], F32, name=_nm(tag + "e")),
        "xn": p.sb([128, W], F32, name=_nm(tag + "n")),
        "a": p.sb([128, W], F32, name=_nm(tag + "a")),
        "acc": p.sb([128, W], F32, name=_nm(tag + "c")),
        "v": p.sb([128, W], F32, name=_nm(tag + "v")),
    }


def emit_l0_ffn(p, xnT_f32, w_gate, w_up, w_down, gT_hbm, T=TT):
    nc = p.nc
    HN = 2816
    wqgT = p.dram([D, HN], BF16).ap()
    wquT = p.dram([D, HN], BF16).ap()
    wqdT = p.dram([HN, D], BF16).ap()
    hqT_hbm = p.dram([HN, T], BF16).ap()
    xqT_hbm = p.dram([D, T], BF16).ap()
    with p.scope():
        sc = mk_fp4_scratch(p, HN, "q4w")
        wt = p.sb([128, HN], F32, name="q4wt")
        wq = p.sb([128, HN], BF16, name="q4wq")
        for (w_src, w_dstT, nrow, ncol) in ((w_gate, wqgT, HN, D),
                                            (w_up, wquT, HN, D),
                                            (w_down, wqdT, D, HN)):
            for r0 in range(0, nrow, 128):
                p.dma(wt[:, :ncol], w_src[r0:r0 + 128, :])
                p.bar()
                emit_fp4_quant(p, wt[:, :ncol], wq[:, :ncol], ncol, sc)
                for c0 in range(0, ncol, 128):
                    nc.tensor.transpose(p.pbt[:, :128], wq[:, c0:c0 + 128],
                                        p.identb[:, :])
                    p.bar()
                    nc.vector.tensor_copy(wq[:, HN - 128:HN]
                                          if False else sc["v"]
                                          .ap().bitcast(BF16)[:, 0:128],
                                          p.pbt[:, :128])
                    p.bar()
                    p.dma(w_dstT[c0:c0 + 128, r0:r0 + 128],
                          sc["v"].ap().bitcast(BF16)[:, 0:128])
                    p.bar()
    with p.scope():
        sc = mk_fp4_scratch(p, D, "q4a")
        xtok = p.sb([128, D], F32, name="q4xt")
        xqtok = p.sb([128, D], BF16, name="q4xo")
        for i in range(T // 128):
            for c in range(DC):
                p.dma(xtok[:, c * 128:(c + 1) * 128]
                      if False else sc["xn"][:, 0:128],
                      xnT_f32[c * 128:(c + 1) * 128, i * 128:(i + 1) * 128])
                p.bar()
                nc.tensor.transpose(p.pbt2[:, :128], sc["xn"][:, 0:128],
                                    p.identf[:, :])
                p.bar()
                nc.vector.tensor_copy(xtok[:, c * 128:(c + 1) * 128],
                                      p.pbt2[:, :128])
                p.bar()
            emit_fp4_quant(p, xtok[:, :], xqtok[:, :], D, sc)
            for c in range(DC):
                nc.tensor.transpose(p.pbt[:, :128], xqtok[:, c * 128:(c + 1) * 128],
                                    p.identb[:, :])
                p.bar()
                nc.vector.tensor_copy(sc["v"].ap().bitcast(BF16)[:, 0:128],
                                      p.pbt[:, :128])
                p.bar()
                p.dma(xqT_hbm[c * 128:(c + 1) * 128, i * 128:(i + 1) * 128],
                      sc["v"].ap().bitcast(BF16)[:, 0:128])
                p.bar()
    with p.scope():
        wg = p.sb([128, DC * HN], BF16, name="q4wg")
        wu = p.sb([128, DC * HN], BF16, name="q4wu")
        load_wmat(p, wg, wqgT, D, HN)
        load_wmat(p, wu, wquT, D, HN)
        sc = mk_fp4_scratch(p, HN, "q4h")
        htok = p.sb([128, HN], F32, name="q4ht")
        hq = p.sb([128, HN], BF16, name="q4hq")
        h1 = p.sb([128, 512], F32, name="q4h1")
        xqt = p.sb([128, DC * 128], BF16, name="q4xi")
        p.bar()
        for i in range(T // 128):
            p.dma(xqt.ap().rearrange("p (c n) -> p c n", c=DC),
                  xqT_hbm[:, i * 128:(i + 1) * 128]
                  .rearrange("(c p) n -> p c n", p=128))
            p.bar()
            for n0 in range(0, HN, 512):
                nw = min(512, HN - n0)
                gp, up = p.pb[0], p.pb[1]
                for c in range(DC):
                    nc.tensor.matmul(gp[:, :nw], wm_lhsT(xqt, DC, c, 0),
                                     wm_lhsT(wg, DC, c, n0, nw),
                                     start=(c == 0), stop=(c == DC - 1))
                    nc.tensor.matmul(up[:, :nw], wm_lhsT(xqt, DC, c, 0),
                                     wm_lhsT(wu, DC, c, n0, nw),
                                     start=(c == 0), stop=(c == DC - 1))
                p.bar()
                nc.scalar.activation(h1[:, :nw], gp[:, :nw], AF.Silu)
                p.bar()
                nc.vector.tensor_tensor(htok[:, n0:n0 + nw], h1[:, :nw],
                                        up[:, :nw], ALU.mult)
                p.bar()
            emit_fp4_quant(p, htok[:, :], hq[:, :], HN, sc)
            for c0 in range(0, HN, 128):
                nc.tensor.transpose(p.pbt[:, :128], hq[:, c0:c0 + 128],
                                    p.identb[:, :])
                p.bar()
                nc.vector.tensor_copy(sc["v"].ap().bitcast(BF16)[:, 0:128],
                                      p.pbt[:, :128])
                p.bar()
                p.dma(hqT_hbm[c0:c0 + 128, i * 128:(i + 1) * 128],
                      sc["v"].ap().bitcast(BF16)[:, 0:128])
                p.bar()
    with p.scope():
        wd = p.sb([128, (HN // 128) * D], BF16, name="q4wd")
        load_wmat(p, wd, wqdT, HN, D)
        hti = [p.sb([128, 512], BF16, name=f"q4hi{c}") for c in range(HN // 128)]
        gsb = p.sb([128, 512], F32, name="q4g")
        p.bar()
        for t0 in range(0, T, 512):
            for c in range(HN // 128):
                p.dma(hti[c][:, :], hqT_hbm[c * 128:(c + 1) * 128, t0:t0 + 512])
            p.bar()
            for e in range(DC):
                pbk = p.pb[e % 2]
                for c in range(HN // 128):
                    nc.tensor.matmul(pbk[:, :], wm_lhsT(wd, HN // 128, c, e * 128),
                                     hti[c][:, :], start=(c == 0),
                                     stop=(c == HN // 128 - 1))
                p.bar()
                nc.scalar.activation(gsb[:, :], pbk[:, :], AF.Silu)
                p.bar()
                p.dma(gT_hbm[e * 128:(e + 1) * 128, t0:t0 + 512], gsb[:, :])
                p.bar()


def build_layer0(T=TT, dbg=False):
    nc = bass.Bass("TRN2", target_bir_lowering=False)
    xT = nc.dram_tensor("xT", [D, T], F32, kind="ExternalInput").ap()
    gamma_h = nc.dram_tensor("gamma_h", [D], F32, kind="ExternalInput").ap()
    wq_h = nc.dram_tensor("wq_h", [D, 512], F32, kind="ExternalInput").ap()
    wkv_h = nc.dram_tensor("wkv_h", [D, 1024], F32, kind="ExternalInput").ap()
    wo_h = nc.dram_tensor("wo_h", [512, D], F32, kind="ExternalInput").ap()
    ln_g = nc.dram_tensor("ln_g", [D], F32, kind="ExternalInput").ap()
    ln_b = nc.dram_tensor("ln_b", [D], F32, kind="ExternalInput").ap()
    wqkv_l = nc.dram_tensor("wqkv_l", [D, 1536], F32, kind="ExternalInput").ap()
    wo_l = nc.dram_tensor("wo_l", [512, D], F32, kind="ExternalInput").ap()
    rts = nc.dram_tensor("rts", [D, 2], F32, kind="ExternalInput").ap()
    bandmask = nc.dram_tensor("bandmask", [128, 6 * 512], F32,
                              kind="ExternalInput").ap()
    w_gate = nc.dram_tensor("w_gate", [2816, D], F32, kind="ExternalInput").ap()
    w_up = nc.dram_tensor("w_up", [2816, D], F32, kind="ExternalInput").ap()
    w_down = nc.dram_tensor("w_down", [D, 2816], F32, kind="ExternalInput").ap()
    xoutT = nc.dram_tensor("xoutT", [D, T], F32, kind="ExternalOutput").ap()

    p = Prog(nc)
    xnT_f32 = p.dram([D, T], F32).ap()
    xnT_bf16 = p.dram([D, T], BF16).ap()
    sT = p.dram([D, T], F32).ap()
    gT = p.dram([D, T], F32).ap()
    emit_prelude(p, xT, xnT_f32, xnT_bf16, T)
    dbg_outs = None
    if dbg:
        dbg_outs = {
            "qm": nc.dram_tensor("qm_dbg", [1, T], F32, kind="ExternalOutput").ap(),
            "kx": nc.dram_tensor("kx_dbg", [2, T], BF16, kind="ExternalOutput").ap(),
        }
    emit_l0_attn(p, xnT_f32, xnT_bf16, gamma_h, wq_h, wkv_h, wo_h, ln_g, ln_b,
                 wqkv_l, wo_l, rts, bandmask, sT, T, dbg_outs=dbg_outs)
    emit_l0_ffn(p, xnT_f32, w_gate, w_up, w_down, gT, T)
    emit_glue(p, xT, sT, gT, xoutT, T)
    if dbg:
        for nm, srcb in (("s_dbg", sT), ("g_dbg", gT), ("xn_dbg", xnT_f32)):
            o = nc.dram_tensor(nm, [D, T], F32, kind="ExternalOutput").ap()
            p.dma(o, srcb)
        p.bar()
    p.close()
    return nc


def make_bandmask():
    return (make_bandmask_mult() - 1.0) * 30000.0


def make_bandmask_mult():
    bm = np.zeros((128, 6 * 512), np.float32)
    i = np.arange(128)
    j = np.arange(512)
    for r in range(6):
        kvw = (r - 1) * 2 + i[:, None] // 64
        qw = j[None, :] // 64
        bm[:, r * 512:(r + 1) * 512] = (np.abs(kvw - qw) <= 1)
    return bm


def _run_layer0(xT_all, inputs, dbg=False):
    nc = build_layer0(dbg=dbg)
    base = {
        "gamma_h": inputs["L0_attn_gamma_h"],
        "wq_h": inputs["L0_attn_wq_h"],
        "wkv_h": inputs["L0_attn_wkv_h"],
        "wo_h": inputs["L0_attn_wo_h"],
        "ln_g": inputs["L0_attn_ln_g"],
        "ln_b": inputs["L0_attn_ln_b"],
        "wqkv_l": inputs["L0_attn_wqkv_l"],
        "wo_l": inputs["L0_attn_wo_l"],
        "rts": np.stack([inputs["L0_attn_rt_q"], inputs["L0_attn_rt_kv"]], 1),
        "bandmask": make_bandmask(),
        "w_gate": inputs["L0_ffn_w_gate"],
        "w_up": inputs["L0_ffn_w_up"],
        "w_down": inputs["L0_ffn_w_down"],
    }
    base = {k: np.ascontiguousarray(v) for k, v in base.items()}
    in_maps = []
    for c in range(8):
        m = dict(base)
        m["xT"] = xT_all[c % 4]
        in_maps.append(m)
    res = run_bass_kernel_spmd(nc, in_maps, core_ids=list(range(8)))
    if dbg:
        return res.results
    return [np.asarray(res.results[b]["xoutT"]) for b in range(4)]


def load_wmat(p, buf, WT_ap, K, N):
    p.cdma(buf.ap().rearrange("p (c n) -> p c n", c=K // 128),
           WT_ap.rearrange("(c p) n -> p c n", p=128))


def wm_lhsT(buf, KC, c, n0, nw=128):
    return buf.ap().rearrange("p (c n) -> p c n", c=KC)[:, c, n0:n0 + nw]


def emit_l1_attn(p, xnT_bf16, W1T, W2T, W3T, WoT2, ropeC, ropeS, sT_hbm, T=TT):
    nc = p.nc
    H, DH = 8, 128
    SC = 1.0 / math.sqrt(8.0)
    outTh = p.dram([D, T], BF16).ap()
    with p.scope():
        rc = p.sb([128, T], F32, name="a1rc")
        rs = p.sb([128, T], F32, name="a1rs")
        p.dma(rc[:, :], ropeC)
        p.dma(rs[:, :], ropeS)
        p.bar()
        with p.scope():
            xnb_h = [p.sb([128, 512], BF16, name=f"a1xh{c}") for c in range(DC)]
            wcols = p.sb([128, DC * 128], BF16, name="a1wc")
            pq = p.sb([128, DC * 512], F32, name="a1pq")
            hT = p.sb([128, T], F32, name="a1h")
            hS = p.sb([128, T], F32, name="a1hs")
            qT = p.sb([DH, T], BF16, name="a1q")
            kT = p.sb([DH, T], BF16, name="a1k")
            vbuf = p.sb([128, (T // 128) * DH], BF16, name="a1vb")
            vbuf3 = vbuf.ap().rearrange("p (i v) -> p i v", v=DH)
            qaux = p.sb([2, T], BF16, name="a1qx")
            kaux = p.sb([2, T], BF16, name="a1kx")
            negb = p.sb([1, T], F32, name="a1nb")
            ob = p.sb([DH, 512], BF16, name="a1ob")
            nc.gpsimd.memset(kaux[:, :], 1.0)
            nc.gpsimd.memset(qaux[:, :], 1.0)
            p.bar()
            swapmask = []
            for i in range(0, 32, 2):
                swapmask += [i + 1, i]

            def mk_headspace(WT, out_bf, do_rope, scale):
                for cb in range(DC):
                    p.cdma(wcols.ap().rearrange("p (c n) -> p c n", c=DC),
                           WT[:, cb * 128:(cb + 1) * 128]
                           .rearrange("(c p) n -> p c n", p=128))
                    p.bar()
                    pbk = p.pb[0]
                    for tt in range(1):
                        for c in range(DC):
                            nc.tensor.matmul(
                                pbk[:, :], wm_lhsT(wcols, DC, c, 0),
                                xnb_h[c][:, :],
                                start=(c == 0), stop=(c == DC - 1))
                        p.bar()
                        nc.vector.tensor_copy(pq[:, cb * 512:(cb + 1) * 512],
                                              pbk[:, :])
                        p.bar()
                for cb in range(DC):
                    nc.vector.tensor_copy(
                        hT.ap().rearrange("p (t c) -> p t c", c=8)[:, :, cb],
                        pq[:, cb * 512:(cb + 1) * 512])
                p.bar()
                if do_rope:
                    nc.vector.stream_shuffle(hS[:, :], hT[:, :], swapmask)
                    p.bar()
                    nc.vector.tensor_tensor(hS[:, :], hS[:, :], rs[:, :], ALU.mult)
                    p.bar()
                    nc.vector.tensor_tensor(hT[:, :], hT[:, :], rc[:, :], ALU.mult)
                    p.bar()
                    nc.vector.tensor_tensor(hT[:, :], hT[:, :], hS[:, :], ALU.add)
                    p.bar()
                if out_bf is not None:
                    if scale is None:
                        nc.vector.tensor_copy(out_bf[:, :], hT[:, :])
                    else:
                        nc.vector.tensor_scalar(out_bf[:, :], hT[:, :],
                                                float(scale), None, ALU.mult)
                    p.bar()

            for hh in range(H):
                for c in range(DC):
                    p.dma(xnb_h[c][:, :],
                          xnT_bf16[c * 128:(c + 1) * 128,
                                   512 * hh:512 * hh + 512])
                p.bar()
                mk_headspace(W1T, qT, True, SC)
                mk_headspace(W2T, kT, True, None)
                mk_headspace(W3T, None, False, None)
                emit_vbuf2(p, hT, vbuf3, T // 128, 128, f32=True)
                emit_bbound(p, qT, kT, DH, T, T, negb)
                nc.vector.tensor_copy(qaux[0:1, :], negb[:1, :])
                p.bar()

                def out_cb(t0, w, o1, hh=hh):
                    nc.vector.tensor_copy(ob[:DH, :w], o1[:DH, :w])
                    p.bar()
                    p.dma(outTh[hh * DH:(hh + 1) * DH, t0:t0 + w], ob[:DH, :w])

                emit_attn_head2(p, kT, qT,
                                lambda i: vbuf3[:, i, :], T // 128, T, out_cb,
                                dh=DH, dv=DH, aux2=(kaux, qaux))
        with p.scope():
            wo = p.sb([128, DC * D], BF16, name="a1wo")
            load_wmat(p, wo, WoT2, D, D)
            qa = [p.sb([128, 1024], BF16, name=f"a1qa{c}") for c in range(DC)]
            otile = p.sb([128, 1024], BF16, name="a1ot")
            ssb = p.sb([128, 512], F32, name="a1ss")
            p.bar()
            for a in range(4):
                for fc in range(DC):
                    p.dma(otile[:, :], outTh[fc * 128:(fc + 1) * 128,
                                             a * 1024:(a + 1) * 1024])
                    p.bar()
                    for jt in range(8):
                        nc.tensor.transpose(p.pbt[:, :128],
                                            otile[:, jt * 128:(jt + 1) * 128],
                                            p.identb[:, :])
                        p.bar()
                        nc.vector.tensor_copy(
                            qa[jt][:, fc * 128:(fc + 1) * 128], p.pbt[:, :128])
                        p.bar()
                for fc in range(DC):
                    for e0 in range(0, D, 512):
                        pbk = p.pb[fc % 2]
                        for jc in range(DC):
                            nc.tensor.matmul(
                                pbk[:, :], qa[jc][:, fc * 128:(fc + 1) * 128],
                                wm_lhsT(wo, DC, jc, e0, 512), start=(jc == 0),
                                stop=(jc == DC - 1))
                        p.bar()
                        nc.vector.tensor_copy(ssb[:, :], pbk[:, :])
                        p.bar()
                        p.dma(sT_hbm[fc * 128:(fc + 1) * 128,
                                     a * 1024 + e0:a * 1024 + e0 + 512],
                              ssb[:, :])
                        p.bar()


def emit_l1_ffn(p, xnT_f32, xnT_bf16, q_w, q_b, k_w, k_b, v_w, v_b, o_w, o_b,
                router_w, router_b, w1, b1, w2, b2, gT_hbm, T=TT):
    nc = p.nc
    H, KPH, E = 16, 32, 8
    SC = 1.0 / math.sqrt(32.0)
    aoT_hbm = p.dram([512, T], BF16).ap()
    mlaT_hbm = p.dram([D, T], BF16).ap()
    with p.scope():
        xnb = load_chunks_bf16(p, xnT_bf16, T, "f1x")
        wcols = p.sb([128, DC * 32], BF16, name="f1wc")
        bcol = p.sb([32, 1], F32, name="f1bc")
        qaug = p.sb([66, T], BF16, name="f1q")
        kaug = p.sb([66, T], BF16, name="f1k")
        vT = p.sb([32, T], F32, name="f1v")
        vbuf = p.sb([128, (T // 128) * 34], BF16, name="f1vb")
        vbuf3 = vbuf.ap().rearrange("p (i v) -> p i v", v=34)
        negb = p.sb([1, T], F32, name="f1nb")
        ob = p.sb([32, 512], BF16, name="f1ob")
        nc.gpsimd.memset(kaug[32:64, :], 0.0)
        nc.gpsimd.memset(qaug[32:64, :], 0.0)
        nc.gpsimd.memset(kaug[64:66, :], 1.0)
        nc.gpsimd.memset(qaug[64:66, :], 1.0)
        nc.gpsimd.memset(vbuf[:, :], 1.0)
        p.bar()

        def proj32(W_ap, b_ap, col0, out_sb, scale):
            p.cdma(wcols.ap().rearrange("p (c n) -> p c n", c=DC),
                   W_ap[:, col0:col0 + 32].rearrange("(c p) n -> p c n", p=128))
            p.dma(bcol[:, :], b_ap[col0:col0 + 32].rearrange("(a b) -> a b", b=1))
            _head_proj2(p, wcols, DC, xnb, T, out_sb, 0, 32, scale=scale,
                        bias_col=bcol[:32, :])
            p.bar()

        for hh in range(H):
            proj32(q_w, q_b, 64 * hh, qaug, SC)
            proj32(k_w, k_b, 32 * hh, kaug, None)
            proj32(v_w, v_b, 32 * hh, vT, None)
            emit_vbuf2(p, vT, vbuf3, T // 128, 32, f32=True)
            emit_bbound(p, qaug[0:32, :], kaug[0:32, :], 32, T, T, negb)
            nc.vector.tensor_copy(qaug[64:65, :], negb[:1, :])
            p.bar()

            def out_cb(t0, w, o1, hh=hh):
                nc.vector.tensor_copy(ob[:32, :w], o1[:32, :w])
                p.bar()
                p.dma(aoT_hbm[hh * 32:(hh + 1) * 32, t0:t0 + w], ob[:32, :w])

            emit_attn_head2(p, kaug, qaug, lambda i: vbuf3[:, i, :],
                            T // 128, T, out_cb, dh=32, dv=32)
    with p.scope():
        wo = p.sb([128, 4 * D], BF16, name="f1wo")
        load_wmat(p, wo, o_w[0:512, :], 512, D)
        ao = [p.sb([128, 512], BF16, name=f"f1ao{c}") for c in range(4)]
        ocol = p.sb([128, 1], F32, name="f1oc")
        msb = p.sb([128, 512], BF16, name="f1ms")
        p.bar()
        for t0 in range(0, T, 512):
            for c in range(4):
                p.dma(ao[c][:, :], aoT_hbm[c * 128:(c + 1) * 128, t0:t0 + 512])
            p.bar()
            for e in range(DC):
                p.dma(ocol[:, :], o_b[e * 128:(e + 1) * 128]
                      .rearrange("(a b) -> a b", b=1))
                pbk = p.pb[e % 2]
                for c in range(4):
                    nc.tensor.matmul(pbk[:, :], wm_lhsT(wo, 4, c, e * 128),
                                     ao[c][:, :], start=(c == 0), stop=(c == 3))
                p.bar()
                nc.vector.tensor_scalar(msb[:, :], pbk[:, :], ocol[:, :], None,
                                        ALU.add)
                p.bar()
                p.dma(mlaT_hbm[e * 128:(e + 1) * 128, t0:t0 + 512], msb[:, :])
                p.bar()
    _oc = p.scope()
    _oc.__enter__()
    combT = p.sb([E, T], F32, name="f1cT")
    with p.scope():
        mlab = load_chunks_bf16(p, mlaT_hbm, T, "f1m")
        rw = p.sb([128, DC * E], BF16, name="f1rw")
        load_wmat(p, rw, router_w, D, E)
        rbb = p.sb([128, E], F32, name="f1rb")
        probs = p.sb([128, E], F32, name="f1pr")
        rank = p.sb([128, E], F32, name="f1rk")
        tmp = p.sb([128, E], F32, name="f1tp")
        mrow = p.sb([128, 1], F32, name="f1mr")
        srow = p.sb([128, 1], F32, name="f1sr")
        p.bar()
        rb1 = p.sb([1, E], F32, name="f1rb1")
        p.dma(rb1[:1, :], router_b.rearrange("(a b) -> a b", a=1))
        p.bar()
        nc.tensor.matmul(p.pb[6][:, :E], p.ones_row[:1, :], rb1[:1, :],
                         start=True, stop=True)
        p.bar()
        nc.vector.tensor_copy(rbb[:, :], p.pb[6][:, :E])
        p.bar()
        for i in range(T // 128):
            pbk = p.pb[0]
            for c in range(DC):
                nc.tensor.matmul(pbk[:, :E], mlab[c][:, i * 128:(i + 1) * 128],
                                 wm_lhsT(rw, DC, c, 0, E),
                                 start=(c == 0), stop=(c == DC - 1))
            p.bar()
            nc.vector.tensor_tensor(probs[:, :], pbk[:128, :E], rbb[:, :], ALU.add)
            p.bar()
            nc.vector.tensor_reduce(mrow[:, :1], probs[:, :], AX.X, ALU.max,
                                    negate=True)
            p.bar()
            nc.scalar.activation(probs[:, :], probs[:, :], AF.Exp,
                                 bias=mrow[:, :], accum_out=srow[:, :])
            p.bar()
            nc.vector.reciprocal(srow[:, :1], srow[:, :1])
            p.bar()
            nc.vector.tensor_scalar(probs[:, :], probs[:, :], srow[:, :], None,
                                    ALU.mult)
            p.bar()
            nc.gpsimd.memset(rank[:, :], 0.0)
            p.bar()
            for s in range(1, E):
                lo = slice(0, E - s)
                hi = slice(s, E)
                nc.vector.tensor_tensor(tmp[:, lo], probs[:, hi], probs[:, lo],
                                        ALU.is_gt)
                p.bar()
                nc.vector.tensor_tensor(rank[:, lo], rank[:, lo], tmp[:, lo],
                                        ALU.add)
                p.bar()
                nc.vector.tensor_tensor(tmp[:, hi], probs[:, lo], probs[:, hi],
                                        ALU.is_ge)
                p.bar()
                nc.vector.tensor_tensor(rank[:, hi], rank[:, hi], tmp[:, hi],
                                        ALU.add)
                p.bar()
            nc.vector.tensor_scalar(tmp[:, :], rank[:, :], 1.5, None, ALU.is_lt)
            p.bar()
            nc.vector.tensor_tensor(tmp[:, :], tmp[:, :], probs[:, :], ALU.mult)
            p.bar()
            nc.vector.tensor_reduce(srow[:, :1], tmp[:, :], AX.X, ALU.add)
            p.bar()
            nc.vector.tensor_scalar(srow[:, :1], srow[:, :1], 1e-8, None, ALU.add)
            p.bar()
            nc.vector.reciprocal(srow[:, :1], srow[:, :1])
            p.bar()
            nc.vector.tensor_scalar(tmp[:, :], tmp[:, :], srow[:, :], None,
                                    ALU.mult)
            p.bar()
            nc.tensor.transpose(p.pbt2[:E, :128], tmp[:, :E], p.identf[:, :])
            p.bar()
            nc.vector.tensor_copy(combT[:, i * 128:(i + 1) * 128],
                                  p.pbt2[:E, :128])
            p.bar()
    with p.scope():
        w1b = p.sb([128, DC * 2048], BF16, name="f1w1")
        w2b = p.sb([128, 16 * D], BF16, name="f1w2")
        mlat = [p.sb([128, 1024], BF16, name=f"f1mt{c}") for c in range(DC)]
        h1T = [p.sb([128, 1024], BF16, name=f"f1h{i}") for i in range(16)]
        moe = [p.sb([128, 1024], F32, name=f"f1mo{c}") for c in range(DC)]
        b1all = p.sb([128, E * 16], F32, name="f1b1")
        b2row = p.sb([1, E * D], F32, name="f1b2")
        ones512 = p.sb([1, 512], F32, name="f1o5")
        t1 = p.sb([128, 512], F32, name="f1t1")
        gc_sb = p.sb([128, 1024], F32, name="f1gc")
        gsel = p.selmat(E, 128, name="f1gs")
        xnc = gc_sb
        nc.gpsimd.memset(ones512[:1, :], 1.0)
        for e in range(E):
            for hc in range(16):
                p.dma(b1all[:, e * 16 + hc:e * 16 + hc + 1],
                      b1[e, hc * 128:(hc + 1) * 128].rearrange("(a b) -> a b", b=1))
        p.dma(b2row[:1, :],
              b2.rearrange("e d -> (e d)").rearrange("(a b) -> a b", a=1))
        p.bar()
        s_pe, s_act, s_dve = p.sem("s_moe"), p.sem("s_mact"), p.sem("s_mdve")
        for tb in range(0, T, 1024):
            for c in range(DC):
                nc.vector.memset(moe[c][:, :], 0.0)
                p.dma(mlat[c][:, :], mlaT_hbm[c * 128:(c + 1) * 128, tb:tb + 1024])
            p.bar()
            if tb == 0:
                load_wmat(p, w1b, w1[0], D, 2048)
                load_wmat(p, w2b, w2[0], 2048, D)
                p.bar()
            for e in range(E):
                for t5 in range(2):
                    nc.tensor.wait_ge(s_dve, p.cnt["s_mdve"])
                    nc.tensor.matmul(p.pb[6][:, :], gsel[:, e * 128:(e + 1) * 128],
                                     combT[:, tb + t5 * 512:tb + t5 * 512 + 512],
                                     start=True, stop=True).then_inc(s_pe, 1)
                    p.cnt["s_moe"] += 1
                    nc.vector.wait_ge(s_pe, p.cnt["s_moe"])
                    nc.vector.tensor_copy(gc_sb[:, t5 * 512:(t5 + 1) * 512],
                                          p.pb[6][:, :]).then_inc(s_dve, 1)
                    p.cnt["s_mdve"] += 1
                pe_b, act_b = p.cnt["s_moe"], p.cnt["s_mact"]
                u = 0
                for hc in range(16):
                    for t5 in range(2):
                        bank = p.pb[u % 2]
                        if u == 0:
                            nc.tensor.wait_ge(p.dma_sem, p.dma_issued)
                        if u >= 2:
                            nc.tensor.wait_ge(s_act, act_b + u - 1)
                        for c in range(DC):
                            mm = nc.tensor.matmul(
                                bank[:, :], wm_lhsT(w1b, DC, c, hc * 128),
                                mlat[c][:, t5 * 512:(t5 + 1) * 512],
                                start=(c == 0), stop=(c == DC - 1))
                        mm.then_inc(s_pe, 1)
                        p.cnt["s_moe"] += 1
                        nc.scalar.wait_ge(s_pe, pe_b + u + 1)
                        nc.scalar.activation(
                            h1T[hc][:, t5 * 512:(t5 + 1) * 512], bank[:, :],
                            AF.Gelu,
                            bias=b1all[:, e * 16 + hc:e * 16 + hc + 1])\
                            .then_inc(s_act, 1)
                        p.cnt["s_mact"] += 1
                        u += 1
                cnt_hc_end = p.cnt["s_moe"]
                ep1 = e + 1
                if ep1 < E or tb + 1024 < T:
                    nc.gpsimd.wait_ge(s_pe, cnt_hc_end)
                    load_wmat(p, w1b, w1[ep1 % E], D, 2048)
                pe_b2, act_b2, dve_b2 = (p.cnt["s_moe"], p.cnt["s_mact"],
                                         p.cnt["s_mdve"])
                for v in range(16):
                    e2, t5 = v // 2, v % 2
                    bank = p.pb[2 + v % 2]
                    if v == 0:
                        nc.tensor.wait_ge(s_act, act_b2)
                        nc.tensor.wait_ge(p.dma_sem, p.dma_issued)
                    if v >= 2:
                        nc.tensor.wait_ge(s_dve, dve_b2 + 2 * (v - 2) + 1)
                    for hc in range(16):
                        nc.tensor.matmul(
                            bank[:, :], wm_lhsT(w2b, 16, hc, e2 * 128),
                            h1T[hc][:, t5 * 512:(t5 + 1) * 512],
                            start=(hc == 0), stop=False)
                    mm = nc.tensor.matmul(
                        bank[:, :], b2row[:1, e * D + e2 * 128:e * D + e2 * 128 + 128],
                        ones512[:1, :], start=False, stop=True)
                    mm.then_inc(s_pe, 1)
                    p.cnt["s_moe"] += 1
                    t1v = t1[:, 0:512]
                    nc.vector.wait_ge(s_pe, pe_b2 + v + 1)
                    nc.vector.tensor_tensor(t1v, bank[:, :],
                                            gc_sb[:, t5 * 512:(t5 + 1) * 512],
                                            ALU.mult).then_inc(s_dve, 1)
                    p.cnt["s_mdve"] += 1
                    nc.vector.wait_ge(s_dve, p.cnt["s_mdve"])
                    nc.vector.tensor_tensor(
                        moe[e2][:, t5 * 512:(t5 + 1) * 512],
                        moe[e2][:, t5 * 512:(t5 + 1) * 512], t1v, ALU.add)\
                        .then_inc(s_dve, 1)
                    p.cnt["s_mdve"] += 1
                if ep1 < E or tb + 1024 < T:
                    nc.gpsimd.wait_ge(s_pe, p.cnt["s_moe"])
                    load_wmat(p, w2b, w2[ep1 % E], 2048, D)
            p.bar()
            for c in range(DC):
                p.dma(xnc[:, :], xnT_f32[c * 128:(c + 1) * 128, tb:tb + 1024])
                p.bar()
                nc.vector.scalar_tensor_tensor(xnc[:, :], moe[c][:, :], 0.3,
                                               xnc[:, :], ALU.mult, ALU.add)
                p.bar()
                nc.scalar.activation(xnc[:, :], xnc[:, :], AF.Silu)
                p.bar()
                p.dma(gT_hbm[c * 128:(c + 1) * 128, tb:tb + 1024], xnc[:, :])
                p.bar()
    _oc.__exit__(None, None, None)


def build_layer1(T=TT, dbg=False):
    nc = bass.Bass("TRN2", target_bir_lowering=False)
    xT = nc.dram_tensor("xT", [D, T], F32, kind="ExternalInput").ap()
    W1T = nc.dram_tensor("W1T", [D, D], F32, kind="ExternalInput").ap()
    W2T = nc.dram_tensor("W2T", [D, D], F32, kind="ExternalInput").ap()
    W3T = nc.dram_tensor("W3T", [D, D], F32, kind="ExternalInput").ap()
    WoT2 = nc.dram_tensor("WoT2", [D, D], F32, kind="ExternalInput").ap()
    ropeC = nc.dram_tensor("ropeC", [128, T], F32, kind="ExternalInput").ap()
    ropeS = nc.dram_tensor("ropeS", [128, T], F32, kind="ExternalInput").ap()
    q_w = nc.dram_tensor("q_w", [D, D], F32, kind="ExternalInput").ap()
    q_b = nc.dram_tensor("q_b", [D], F32, kind="ExternalInput").ap()
    k_w = nc.dram_tensor("k_w", [D, 512], F32, kind="ExternalInput").ap()
    k_b = nc.dram_tensor("k_b", [512], F32, kind="ExternalInput").ap()
    v_w = nc.dram_tensor("v_w", [D, 512], F32, kind="ExternalInput").ap()
    v_b = nc.dram_tensor("v_b", [512], F32, kind="ExternalInput").ap()
    o_w = nc.dram_tensor("o_w", [D, D], F32, kind="ExternalInput").ap()
    o_b = nc.dram_tensor("o_b", [D], F32, kind="ExternalInput").ap()
    router_w = nc.dram_tensor("router_w", [D, 8], F32, kind="ExternalInput").ap()
    router_b = nc.dram_tensor("router_b", [8], F32, kind="ExternalInput").ap()
    w1 = nc.dram_tensor("w1", [8, D, 2048], F32, kind="ExternalInput").ap()
    b1 = nc.dram_tensor("b1", [8, 2048], F32, kind="ExternalInput").ap()
    w2 = nc.dram_tensor("w2", [8, 2048, D], F32, kind="ExternalInput").ap()
    b2 = nc.dram_tensor("b2", [8, D], F32, kind="ExternalInput").ap()
    xoutT = nc.dram_tensor("xoutT", [D, T], F32, kind="ExternalOutput").ap()

    p = Prog(nc)
    xnT_f32 = p.dram([D, T], F32).ap()
    xnT_bf16 = p.dram([D, T], BF16).ap()
    sT = p.dram([D, T], F32).ap()
    gT = p.dram([D, T], F32).ap()
    emit_prelude(p, xT, xnT_f32, xnT_bf16, T)
    emit_l1_attn(p, xnT_bf16, W1T, W2T, W3T, WoT2, ropeC, ropeS, sT, T)
    emit_l1_ffn(p, xnT_f32, xnT_bf16, q_w, q_b, k_w, k_b, v_w, v_b, o_w, o_b,
                router_w, router_b, w1, b1, w2, b2, gT, T)
    emit_glue(p, xT, sT, gT, xoutT, T)
    if dbg:
        for nm, srcb in (("s_dbg", sT), ("g_dbg", gT), ("xn_dbg", xnT_f32)):
            o = nc.dram_tensor(nm, [D, T], F32, kind="ExternalOutput").ap()
            p.dma(o, srcb)
        p.bar()
    p.close()
    return nc


def make_rope_tables(T=TT, DH=128):
    inv = 1.0 / 10000.0 ** (np.arange(0, DH, 2, dtype=np.float64) / DH)
    t = np.arange(T, dtype=np.float64)
    fr = np.repeat(t[:, None] * inv[None, :], 2, axis=-1)
    cos = np.cos(fr).T.astype(np.float32)
    sin = np.sin(fr).T.astype(np.float32)
    sgn = np.where(np.arange(DH)[:, None] % 2 == 0, -1.0, 1.0).astype(np.float32)
    return np.ascontiguousarray(cos), np.ascontiguousarray(sin * sgn)


def _run_layer1(xT_all, inputs, dbg=False):
    nc = build_layer1(dbg=dbg)
    rc, rs = make_rope_tables()
    base = {
        "W1T": inputs["L1_attn_W1"].T, "W2T": inputs["L1_attn_W2"].T,
        "W3T": inputs["L1_attn_W3"].T, "WoT2": inputs["L1_attn_Wo"].T,
        "ropeC": rc, "ropeS": rs,
        "q_w": inputs["L1_ffn_q_w"], "q_b": inputs["L1_ffn_q_b"],
        "k_w": inputs["L1_ffn_k_w"], "k_b": inputs["L1_ffn_k_b"],
        "v_w": inputs["L1_ffn_v_w"], "v_b": inputs["L1_ffn_v_b"],
        "o_w": inputs["L1_ffn_o_w"], "o_b": inputs["L1_ffn_o_b"],
        "router_w": inputs["L1_ffn_router_w"],
        "router_b": inputs["L1_ffn_router_b"],
        "w1": inputs["L1_ffn_w1"], "b1": inputs["L1_ffn_b1"],
        "w2": inputs["L1_ffn_w2"], "b2": inputs["L1_ffn_b2"],
    }
    base = {k: np.ascontiguousarray(v) for k, v in base.items()}
    in_maps = []
    for c in range(8):
        m = dict(base)
        m["xT"] = xT_all[c % 4]
        in_maps.append(m)
    res = run_bass_kernel_spmd(nc, in_maps, core_ids=list(range(8)))
    if dbg:
        return res.results
    return [np.asarray(res.results[b]["xoutT"]) for b in range(4)]


F32EPS = float(np.finfo(np.float32).eps)


def emit_rms_bcast(p, srcT_hbm, T, rstd_b, eps):
    nc = p.nc
    with p.scope():
        xc = p.sb([128, 512], F32, name=_nm("rmx"))
        sq = p.sb([128, 512], F32, name=_nm("rmsq"))
        rrow = p.sb([1, T], F32, name=_nm("rmr"))
        pv = p.pb[5]
        for t0 in range(0, T, 512):
            for c in range(DC):
                p.dma(xc[:, :], srcT_hbm[c * 128:(c + 1) * 128, t0:t0 + 512])
                p.bar()
                nc.scalar.activation(sq[:, :], xc[:, :], AF.Square)
                p.bar()
                nc.tensor.matmul(pv[:1, :], p.ones_col[:, :], sq[:, :],
                                 start=(c == 0), stop=(c == DC - 1))
                p.bar()
            nc.vector.tensor_scalar(rrow[:1, t0:t0 + 512], pv[:1, :512],
                                    1.0 / D, float(eps), ALU.mult, ALU.add)
            p.bar()
        nc.scalar.activation(rrow[:1, :], rrow[:1, :], AF.Sqrt)
        p.bar()
        nc.vector.reciprocal(rrow[:1, :], rrow[:1, :])
        p.bar()
        p.bcast_row(rstd_b, rrow, T)


def _head_proj(p, wcols_sb, KC, rhs_chunks, T, out_sb, out_row0, nw, bank=0,
               scale=None, out_dt_via=None):
    nc = p.nc
    for t0 in range(0, T, 512):
        pbk = p.pb[bank]
        for c in range(KC):
            nc.tensor.matmul(pbk[:nw, :], wcols_sb[:, c * nw:(c + 1) * nw],
                             rhs_chunks[c][:, t0:t0 + 512],
                             start=(c == 0), stop=(c == KC - 1))
        p.bar()
        if scale is None:
            nc.vector.tensor_copy(out_sb[out_row0:out_row0 + nw, t0:t0 + 512],
                                  pbk[:nw, :])
        else:
            nc.vector.tensor_scalar(out_sb[out_row0:out_row0 + nw, t0:t0 + 512],
                                    pbk[:nw, :], float(scale), None, ALU.mult)
        p.bar()


def load_wcols(p, dst_sb, W_hbm, K, col0, nw):
    p.cdma(dst_sb.ap().rearrange("p (c n) -> p c n", c=K // 128)[:, :, :nw],
           W_hbm[:, col0:col0 + nw].rearrange("(c p) n -> p c n", p=128))


def emit_l3_attn(p, xnT_f32, xnT_bf16, norm_w, Wq, Wkv, Wout, Wg, dmask_hbm,
                 sT_hbm, T=TT):
    nc = p.nc
    HEADS, DH = 16, 64
    outTh = p.dram([HEADS * DH, T], BF16).ap()
    with p.scope():
        rstd_b = p.sb([128, T], F32, name="l3rb")
        emit_rms_bcast(p, xnT_f32, T, rstd_b, F32EPS)
        xl3 = [p.sb([128, T], BF16, name=f"l3x{c}") for c in range(DC)]
        with p.scope():
            xc = p.sb([128, T], F32, name="l3xc")
            nw_sb = p.sb([128, 1], F32, name="l3nw")
            for c in range(DC):
                p.dma(xc[:, :], chunk(xnT_f32, c))
                p.dma(nw_sb[:, :], norm_w[c * 128:(c + 1) * 128]
                      .rearrange("(a b) -> a b", b=1))
                p.bar()
                nc.vector.tensor_tensor(xc[:, :], xc[:, :], rstd_b[:, :], ALU.mult)
                p.bar()
                nc.vector.tensor_scalar(xl3[c][:, :], xc[:, :], nw_sb[:, :], None,
                                        ALU.mult)
                p.bar()
        gates = p.sb([16, T], F32, name="l3gt")
        with p.scope():
            wg_sb = p.sb([128, DC * 16], BF16, name="l3wg")
            load_wcols(p, wg_sb, Wg, D, 0, 16)
            p.bar()
            for t0 in range(0, T, 512):
                pbk = p.pb[0]
                for c in range(DC):
                    nc.tensor.matmul(pbk[:16, :], wg_sb[:, c * 16:(c + 1) * 16],
                                     xl3[c][:, t0:t0 + 512],
                                     start=(c == 0), stop=(c == DC - 1))
                p.bar()
                nc.scalar.activation(gates[:16, t0:t0 + 512], pbk[:16, :], AF.Sigmoid)
                p.bar()
        dmasks = p.sb([128, 4 * 512], BF16, name="l3dm")
        p.cdma(dmasks[:, :], dmask_hbm)
        p.bar()
        gsel = p.selmat(HEADS, DH, name="l3sel")
        with p.scope():
            wcols = p.sb([128, DC * DH], BF16, name="l3wc")
            qaug = p.sb([66, T], BF16, name="l3qa")
            kaug = p.sb([66, T], BF16, name="l3ka")
            vT = p.sb([DH, T], BF16, name="l3vt")
            VX = DH + 2
            vbuf = p.sb([128, (T // 128) * VX], BF16, name="l3vb")
            vbuf3 = vbuf.ap().rearrange("p (i v) -> p i v", v=VX)
            negb = p.sb([1, T], F32, name="l3nb")
            ob = p.sb([DH, 512], BF16, name="l3ob")
            nc.gpsimd.memset(kaug[64:66, :], 1.0)
            nc.gpsimd.memset(qaug[64:66, :], 1.0)
            nc.gpsimd.memset(vbuf[:, :], 1.0)
            p.bar()
            for h in range(HEADS):
                load_wcols(p, wcols, Wq, D, h * DH, DH)
                _head_proj2(p, wcols, DC, xl3, T, qaug, 0, DH, scale=0.125)
                p.bar()
                load_wcols(p, wcols, Wkv, D, h * DH, DH)
                _head_proj2(p, wcols, DC, xl3, T, kaug, 0, DH)
                p.bar()
                load_wcols(p, wcols, Wkv, D, D + h * DH, DH)
                _head_proj2(p, wcols, DC, xl3, T, vT, 0, DH)
                p.bar()
                emit_vbuf2(p, vT, vbuf3, T // 128, DH)
                emit_bbound(p, qaug[0:DH, :], kaug[0:DH, :], DH, T, T, negb)
                nc.vector.tensor_copy(qaug[64:65, :], negb[:1, :])
                p.bar()

                def out_cb(t0, w, o1, h=h):
                    nc.vector.tensor_copy(ob[:DH, :w], o1[:DH, :w])
                    p.bar()
                    p.dma(outTh[h * DH:(h + 1) * DH, t0:t0 + w], ob[:DH, :w])

                emit_attn_head2(p, kaug, qaug,
                                lambda i: vbuf3[:, i, :], T // 128, T, out_cb,
                                dh=DH, dv=DH, causal_t0=lambda t0: t0,
                                dmaskadd=dmasks, gate_sel=(gates, gsel, h))
        with p.scope():
            wo = WMat(p, Wout, HEADS * DH, D, name="l3wo")
            rhs = [p.sb([128, 512], BF16, name=f"l3r{c}") for c in range(DC)]
            ssb = p.sb([128, 512], F32, name="l3ss")
            for t0 in range(0, T, 512):
                for c in range(DC):
                    p.dma(rhs[c][:, :], outTh[c * 128:(c + 1) * 128, t0:t0 + 512])
                p.bar()
                for e in range(DC):
                    pbk = p.pb[e % 2]
                    for c in range(DC):
                        nc.tensor.matmul(pbk[:, :], wo.lhsT(c, e * 128), rhs[c][:, :],
                                         start=(c == 0), stop=(c == DC - 1))
                    p.bar()
                    nc.vector.tensor_copy(ssb[:, :], pbk[:, :])
                    p.bar()
                    p.dma(sT_hbm[e * 128:(e + 1) * 128, t0:t0 + 512], ssb[:, :])
                    p.bar()


def emit_l3_ffn(p, xnT_bf16, w_q3, w_k3, w_gate3, vexp, oexp, dmask_hbm,
                gT_hbm, T=TT, dbg_outs=None):
    nc = p.nc
    H, E, DH = 8, 8, 64
    NKV = T // 128
    zw_hbm = p.dram([H * E * DH, T], BF16).ap()
    with p.scope():
        xnb = load_chunks_bf16(p, xnT_bf16, T, "f3x")
        dmasks = p.sb([128, 4 * 512], BF16, name="f3dm")
        p.cdma(dmasks[:, :], dmask_hbm)
        gmT = p.sb([H * E, T], BF16, name="f3gmt")
        p.bar()
        VX3 = DH + 2
        vbuf = p.sb([128, NKV * H * VX3], BF16, name="f3vb")
        vbuf3 = vbuf.ap().rearrange("p (i x) -> p i x", i=NKV)
        nc.gpsimd.memset(vbuf[:, :], 1.0)
        with p.scope():
          gmbuf = p.sb([128, NKV * H * E], F32, name="f3gm")
          if True:
            wg_sb = p.sb([128, DC * H * E], BF16, name="f3wg")
            load_wcols(p, wg_sb, w_gate3, D, 0, H * E)
            gsig = p.sb([128, H * E], F32, name="f3gs")
            rank = p.sb([128, H * E], F32, name="f3rk")
            tmp = p.sb([128, H * E], F32, name="f3tp")
            p.bar()
            for i in range(NKV):
                pbk = p.pb[0]
                for c in range(DC):
                    nc.tensor.matmul(pbk[:, :H * E],
                                     xnb[c][:, i * 128:(i + 1) * 128],
                                     wg_sb.ap().rearrange(
                                         "p (c n) -> p c n", c=DC)[:, c, :],
                                     start=(c == 0), stop=(c == DC - 1))
                p.bar()
                nc.scalar.activation(gsig[:, :], pbk[:128, :H * E], AF.Sigmoid)
                p.bar()
                nc.gpsimd.memset(rank[:, :], 0.0)
                p.bar()
                g3 = gsig.ap().rearrange("p (h e) -> p h e", h=H)
                r3 = rank.ap().rearrange("p (h e) -> p h e", h=H)
                t3 = tmp.ap().rearrange("p (h e) -> p h e", h=H)
                for s in range(1, E):
                    lo = (slice(None), slice(None), slice(0, E - s))
                    hi = (slice(None), slice(None), slice(s, E))
                    nc.vector.tensor_tensor(t3[lo], g3[hi], g3[lo], ALU.is_gt)
                    p.bar()
                    nc.vector.tensor_tensor(r3[lo], r3[lo], t3[lo], ALU.add)
                    p.bar()
                    nc.vector.tensor_tensor(t3[hi], g3[lo], g3[hi], ALU.is_ge)
                    p.bar()
                    nc.vector.tensor_tensor(r3[hi], r3[hi], t3[hi], ALU.add)
                    p.bar()
                nc.vector.tensor_scalar(tmp[:, :], rank[:, :], 0.0, None, ALU.is_equal)
                p.bar()
                nc.vector.tensor_tensor(gmbuf[:, i * H * E:(i + 1) * H * E],
                                        tmp[:, :], gsig[:, :], ALU.mult)
                p.bar()
                nc.tensor.transpose(p.pbt2[:H * E, :128],
                                    gmbuf[:, i * H * E:(i + 1) * H * E],
                                    p.identf[:, :])
                p.bar()
                nc.vector.tensor_copy(gmT[:, i * 128:(i + 1) * 128],
                                      p.pbt2[:H * E, :128])
                p.bar()
          import os as _os
          if _os.environ.get("FFN_SKIP_VALUES") and dbg_outs:
              p.cdma(vbuf[:, :], dbg_outs["vbuf_in"])
              p.bar()
          elif True:
            NH = NKV // 2
            vacc = p.sb([128, NH * 512], F32, name="f3va")
            vex = p.sb([128, DC * 512], BF16, name="f3ve")
            for half in range(2):
                i0 = half * NH
                nc.gpsimd.memset(vacc[:, :], 0.0)
                p.bar()
                for e in range(E):
                    for h in range(H):
                        p.cdma(vex.ap().rearrange("p (c h v) -> p c h v",
                                                  c=DC, h=H)[:, :, h, :],
                               vexp[e, h].rearrange("(c p) v -> p c v", p=128))
                    p.bar()
                    for il in range(NH):
                        i = i0 + il
                        pbk = p.pb[e % 2]
                        for c in range(DC):
                            nc.tensor.matmul(pbk[:, :],
                                             xnb[c][:, i * 128:(i + 1) * 128],
                                             vex[:, c * 512:(c + 1) * 512],
                                             start=(c == 0), stop=(c == DC - 1))
                        p.bar()
                        for h in range(H):
                            gcol = gmbuf[:, i * H * E + h * E + e:
                                         i * H * E + h * E + e + 1]
                            sl = slice(il * 512 + h * DH, il * 512 + (h + 1) * DH)
                            nc.vector.scalar_tensor_tensor(
                                vacc[:, sl], pbk[:, h * DH:(h + 1) * DH], gcol,
                                vacc[:, sl], ALU.mult, ALU.add)
                        p.bar()
                vbuf4 = vbuf.ap().rearrange("p (i h v) -> p i h v", i=NKV, h=H)
                for il in range(NH):
                    i = i0 + il
                    for hh in range(H):
                        nc.vector.tensor_copy(
                            vbuf4[:, i, hh, 0:DH],
                            vacc[:, il * 512 + hh * DH:il * 512 + (hh + 1) * DH])
                p.bar()
            if dbg_outs and "vbuf" in dbg_outs:
                p.dma(dbg_outs["vbuf"]
                      .rearrange("p (i h x) -> p i h x", i=NKV, h=H)
                      .rearrange("p i h x -> p (i h x)") if False else
                      dbg_outs["vbuf"], vbuf[:, :])
                p.bar()
            import os
            if os.environ.get("VBUF_ROUNDTRIP"):
                vrt = p.dram([128, NKV * H * VX3], BF16).ap()
                p.dma(vrt, vbuf[:, :])
                p.bar()
                p.dma(vbuf[:, :], vrt)
                p.bar()
        with p.scope():
            wcols = p.sb([128, DC * DH], BF16, name="f3wc")
            qaug = p.sb([66, T], BF16, name="f3qa")
            kaug = p.sb([66, T], BF16, name="f3ka")
            negb = p.sb([1, T], F32, name="f3nb")
            zwb = p.sb([DH, 8 * 512], BF16, name="f3zw")
            allsel = p.sb([H * E, H * E * 64], BF16, name="f3as")
            with p.scope():
                rowp = p.sb([H * E, 1], F32, name="f3rp")
                asf = p.sb([H * E, H * E * 64], F32, name="f3asf")
                nc.gpsimd.iota(asf.ap().rearrange("p (a b) -> p a b", b=64),
                               [[1, H * E], [0, 64]], channel_multiplier=0,
                               allow_small_or_imprecise_dtypes=True)
                nc.gpsimd.iota(rowp[:, :], [[0, 1]], channel_multiplier=1,
                               allow_small_or_imprecise_dtypes=True)
                p.bar()
                nc.vector.tensor_scalar(asf[:, :], asf[:, :], rowp[:, :], None,
                                        ALU.is_equal)
                p.bar()
                nc.vector.tensor_copy(allsel[:, :], asf[:, :])
                p.bar()
            if dbg_outs and "gmT" in dbg_outs:
                p.dma(dbg_outs["gmT"], gmT[:, :])
                p.bar()
            nc.gpsimd.memset(kaug[64:66, :], 1.0)
            nc.gpsimd.memset(qaug[64:66, :], 1.0)
            p.bar()
            for h in range(H):
                load_wcols(p, wcols, w_q3, D, h * DH, DH)
                _head_proj2(p, wcols, DC, xnb, T, qaug, 0, DH, scale=0.125)
                p.bar()
                load_wcols(p, wcols, w_k3, D, h * DH, DH)
                _head_proj2(p, wcols, DC, xnb, T, kaug, 0, DH)
                p.bar()
                emit_bbound(p, qaug[0:DH, :], kaug[0:DH, :], DH, T, T, negb)
                nc.vector.tensor_copy(qaug[64:65, :], negb[:1, :])
                p.bar()
                if h == 0 and dbg_outs and "qaug" in dbg_outs:
                    p.dma(dbg_outs["qaug"], qaug[:, :])
                    p.dma(dbg_outs["kaug"], kaug[:, :])
                    p.bar()
                vb_h = vbuf.ap().rearrange(
                    "p (i h x) -> p (i h) x", h=H, x=VX3)

                def out_cb(t0, w, o1, h=h):
                    if h == 0 and dbg_outs and "o1" in dbg_outs:
                        nc.vector.tensor_copy(zwb[:DH, :w], o1[:DH, :w])
                        p.bar()
                        p.dma(dbg_outs["o1"][:, t0:t0 + w], zwb[:DH, :w])
                        p.bar()
                    spz, sdz = p.sem("s_sim"), p.sem("s_exp")
                    bz_p, bz_d = p.cnt["s_sim"], p.cnt["s_exp"]
                    for e in range(E):
                        col = h * E + e
                        bank = p.pb[5 + e % 2]
                        if e >= 2:
                            nc.tensor.wait_ge(sdz, bz_d + e - 1)
                        nc.tensor.matmul(bank[:DH, :w],
                                         allsel[:, col * 64:col * 64 + DH],
                                         gmT[:, t0:t0 + w],
                                         start=True, stop=True).then_inc(spz, 1)
                        p.cnt["s_sim"] += 1
                        nc.vector.wait_ge(spz, bz_p + e + 1)
                        nc.vector.tensor_tensor(zwb[:DH, e * 512:e * 512 + w],
                                                o1[:DH, :w], bank[:DH, :w],
                                                ALU.mult).then_inc(sdz, 1)
                        p.cnt["s_exp"] += 1
                    p.bar()
                    p.dma(zw_hbm[h * 512:(h + 1) * 512, t0:t0 + w]
                          .rearrange("(e q) t -> q e t", q=DH),
                          zwb.ap().rearrange("q (e t) -> q e t", t=512)[:, :, :w])
                    p.bar()

                emit_attn_head2(p, kaug, qaug,
                                lambda i, h=h: vb_h[:, i * H + h, :],
                                NKV, T, out_cb,
                                dh=DH, dv=DH, causal_t0=lambda t0: t0,
                                dmaskadd=dmasks)
    if dbg_outs and "zw" in dbg_outs:
        p.dma(dbg_outs["zw"], zw_hbm[0:512, :])
        p.bar()
    with p.scope():
        if True:
            oex = p.sb([DH, H * E * D], BF16, name="f3oe")
            p.cdma(oex.ap().rearrange("v (e h d) -> v e h d", e=E, h=H),
                   oexp.rearrange("e h v d -> v e h d"))
            zwt = p.sb([DH, H * E * 512], BF16, name="f3zt")
            zw3 = zwt.ap().rearrange("p (cc n) -> p cc n", n=512)
            gsb = p.sb([128, 512], F32, name="f3gb")
            p.bar()
            oex4 = oex.ap().rearrange("v (e h d) -> v e h d", e=E, h=H)
            for t0 in range(0, T, 512):
                p.dma(zw3, zw_hbm.rearrange("(cc p) t -> p cc t", p=DH)
                      [:, :, t0:t0 + 512])
                p.bar()
                for ep in range(DC):
                    pbk = p.pb[2 + (ep % 2)]
                    n = 0
                    for h in range(H):
                        for e in range(E):
                            pair = h * E + e
                            nc.tensor.matmul(
                                pbk[:, :], oex4[:DH, e, h, ep * 128:(ep + 1) * 128],
                                zwt[0:DH, pair * 512:(pair + 1) * 512],
                                start=(n == 0), stop=(n == H * E - 1))
                            n += 1
                    p.bar()
                    nc.scalar.activation(gsb[:, :], pbk[:, :], AF.Silu)
                    p.bar()
                    p.dma(gT_hbm[ep * 128:(ep + 1) * 128, t0:t0 + 512], gsb[:, :])
                    p.bar()


def build_layer3(T=TT, dbg=False):
    nc = bass.Bass("TRN2", target_bir_lowering=False)
    xT = nc.dram_tensor("xT", [D, T], F32, kind="ExternalInput").ap()
    norm_w = nc.dram_tensor("norm_w", [D], F32, kind="ExternalInput").ap()
    Wq = nc.dram_tensor("Wq", [D, 1024], F32, kind="ExternalInput").ap()
    Wkv = nc.dram_tensor("Wkv", [D, 2048], F32, kind="ExternalInput").ap()
    Wout = nc.dram_tensor("Wout", [1024, D], F32, kind="ExternalInput").ap()
    Wg = nc.dram_tensor("Wg", [D, 16], F32, kind="ExternalInput").ap()
    w_q3 = nc.dram_tensor("w_q3", [D, 512], F32, kind="ExternalInput").ap()
    w_k3 = nc.dram_tensor("w_k3", [D, 512], F32, kind="ExternalInput").ap()
    w_gate3 = nc.dram_tensor("w_gate3", [D, 64], F32, kind="ExternalInput").ap()
    vexp = nc.dram_tensor("vexp", [8, 8, D, 64], F32, kind="ExternalInput").ap()
    oexp = nc.dram_tensor("oexp", [8, 8, 64, D], F32, kind="ExternalInput").ap()
    dmask = nc.dram_tensor("dmask", [128, 4 * 512], F32, kind="ExternalInput").ap()
    xoutT = nc.dram_tensor("xoutT", [D, T], F32, kind="ExternalOutput").ap()

    p = Prog(nc)
    xnT_f32 = p.dram([D, T], F32).ap()
    xnT_bf16 = p.dram([D, T], BF16).ap()
    sT = p.dram([D, T], F32).ap()
    gT = p.dram([D, T], F32).ap()
    emit_prelude(p, xT, xnT_f32, xnT_bf16, T)
    import os
    if not os.environ.get("SKIP_L3_ATTN"):
        emit_l3_attn(p, xnT_f32, xnT_bf16, norm_w, Wq, Wkv, Wout, Wg, dmask, sT, T)
    dbg_outs = None
    if dbg == 2:
        dbg_outs = {
            "vbuf": nc.dram_tensor("vbuf_dbg", [128, (T // 128) * 8 * 64], BF16,
                                   kind="ExternalOutput").ap(),
            "vbuf_in": nc.dram_tensor("vbuf_in", [128, (T // 128) * 8 * 64], F32,
                                      kind="ExternalInput").ap(),
            "gmT": nc.dram_tensor("gmT_dbg", [64, T], BF16,
                                  kind="ExternalOutput").ap(),
            "zw": nc.dram_tensor("zw_dbg", [512, T], BF16,
                                 kind="ExternalOutput").ap(),
            "qaug": nc.dram_tensor("qaug_dbg", [66, T], BF16,
                                   kind="ExternalOutput").ap(),
            "kaug": nc.dram_tensor("kaug_dbg", [66, T], BF16,
                                   kind="ExternalOutput").ap(),
            "o1": nc.dram_tensor("o1_dbg", [64, T], BF16,
                                 kind="ExternalOutput").ap(),
            "praw": nc.dram_tensor("praw_dbg", [128, 4 * 512 * 5], F32,
                                   kind="ExternalOutput").ap(),
            "ptil": nc.dram_tensor("ptil_dbg", [128, 4 * 512 * 5], BF16,
                                   kind="ExternalOutput").ap(),
            "pacc": nc.dram_tensor("pacc_dbg", [65, T], F32,
                                   kind="ExternalOutput").ap(),
        }
    emit_l3_ffn(p, xnT_bf16, w_q3, w_k3, w_gate3, vexp, oexp, dmask, gT, T,
                dbg_outs=dbg_outs)
    emit_glue(p, xT, sT, gT, xoutT, T)
    if dbg:
        for nm, src in (("s_dbg", sT), ("g_dbg", gT), ("xn_dbg", xnT_f32)):
            o = nc.dram_tensor(nm, [D, T], F32, kind="ExternalOutput").ap()
            p.dma(o, src)
        p.bar()
    p.close()
    return nc


def make_dmask(q_tile=512):
    m = np.zeros((128, 4 * q_tile), np.float32)
    for r in range(4):
        i = np.arange(128)[:, None]
        j = np.arange(q_tile)[None, :]
        m[:, r * q_tile:(r + 1) * q_tile] = (i + 128 * r <= j)
    return m


def make_dmaskadd(q_tile=512):
    return (make_dmask(q_tile) - 1.0) * 30000.0


def _run_layer3(xT_all, inputs, dbg=False, extra=None):
    nc = build_layer3(dbg=dbg)
    base = {
        "norm_w": inputs["L3_attn_norm_w"],
        "Wq": inputs["L3_attn_Wq"],
        "Wkv": inputs["L3_attn_Wkv"],
        "Wout": inputs["L3_attn_Wout"],
        "Wg": inputs["L3_attn_Wg"],
        "w_q3": inputs["L3_ffn_w_q"],
        "w_k3": inputs["L3_ffn_w_k"],
        "w_gate3": inputs["L3_ffn_w_gate"],
        "vexp": inputs["L3_ffn_value_experts"],
        "oexp": inputs["L3_ffn_output_experts"],
        "dmask": make_dmaskadd(),
    }
    base = {k: np.ascontiguousarray(v) for k, v in base.items()}
    if dbg == 2:
        base["vbuf_in"] = np.zeros((128, 32 * 8 * 64), np.float32)
        if extra:
            base.update({k: np.ascontiguousarray(v) for k, v in extra.items()})
    in_maps = []
    for c in range(8):
        m = dict(base)
        m["xT"] = xT_all[c % 4]
        in_maps.append(m)
    res = run_bass_kernel_spmd(nc, in_maps, core_ids=list(range(8)))
    if dbg:
        return res.results
    return [np.asarray(res.results[b]["xoutT"]) for b in range(4)]


def _to_fm(x):
    return [np.ascontiguousarray(x[b].T) for b in range(x.shape[0])]


def _run_layer2(xT_all, inputs, dbg=False):
    nc = build_layer2(dbg=dbg)
    base = {
        "WkT": np.ascontiguousarray(np.transpose(inputs["L2_attn_Wk"], (0, 2, 1))),
        "WvT": np.ascontiguousarray(np.transpose(inputs["L2_attn_Wv"], (0, 2, 1))),
        "WrT": np.ascontiguousarray(np.transpose(inputs["L2_attn_Wr"], (0, 2, 1))),
        "WoT": np.ascontiguousarray(np.transpose(inputs["L2_attn_Wo"], (0, 2, 1))),
        "lnw": np.ascontiguousarray(inputs["L2_attn_ln_w"]),
        "lnb": np.ascontiguousarray(inputs["L2_attn_ln_b"]),
        "wgT": np.ascontiguousarray(inputs["L2_ffn_w_gate"].T),
        "wuT": np.ascontiguousarray(inputs["L2_ffn_w_up"].T),
        "wdT": np.ascontiguousarray(inputs["L2_ffn_w_down"].T),
    }
    in_maps = []
    for c in range(8):
        m = dict(base)
        m["xT"] = xT_all[c % 4]
        in_maps.append(m)
    res = run_bass_kernel_spmd(nc, in_maps, core_ids=list(range(8)))
    if dbg:
        return res.results
    return [np.asarray(res.results[b]["xoutT"]) for b in range(4)]


def build_module(T=TT):
    nc = bass.Bass("TRN2", target_bir_lowering=False)
    EI = "ExternalInput"
    xT = nc.dram_tensor("xT", [D, T], F32, kind=EI).ap()
    gamma_h = nc.dram_tensor("gamma_h", [D], F32, kind=EI).ap()
    wq_h = nc.dram_tensor("wq_h", [D, 512], F32, kind=EI).ap()
    wkv_h = nc.dram_tensor("wkv_h", [D, 1024], F32, kind=EI).ap()
    wo_h = nc.dram_tensor("wo_h", [512, D], F32, kind=EI).ap()
    ln_g = nc.dram_tensor("ln_g", [D], F32, kind=EI).ap()
    ln_b = nc.dram_tensor("ln_b", [D], F32, kind=EI).ap()
    wqkv_l = nc.dram_tensor("wqkv_l", [D, 1536], F32, kind=EI).ap()
    wo_l = nc.dram_tensor("wo_l", [512, D], F32, kind=EI).ap()
    rts = nc.dram_tensor("rts", [D, 2], F32, kind=EI).ap()
    bandmask = nc.dram_tensor("bandmask", [128, 6 * 512], F32, kind=EI).ap()
    w_gate = nc.dram_tensor("w_gate", [2816, D], F32, kind=EI).ap()
    w_up = nc.dram_tensor("w_up", [2816, D], F32, kind=EI).ap()
    w_down = nc.dram_tensor("w_down", [D, 2816], F32, kind=EI).ap()
    W1T = nc.dram_tensor("W1T", [D, D], F32, kind=EI).ap()
    W2T = nc.dram_tensor("W2T", [D, D], F32, kind=EI).ap()
    W3T = nc.dram_tensor("W3T", [D, D], F32, kind=EI).ap()
    WoT2 = nc.dram_tensor("WoT2", [D, D], F32, kind=EI).ap()
    ropeC = nc.dram_tensor("ropeC", [128, T], F32, kind=EI).ap()
    ropeS = nc.dram_tensor("ropeS", [128, T], F32, kind=EI).ap()
    q_w = nc.dram_tensor("q_w", [D, D], F32, kind=EI).ap()
    q_b = nc.dram_tensor("q_b", [D], F32, kind=EI).ap()
    k_w = nc.dram_tensor("k_w", [D, 512], F32, kind=EI).ap()
    k_b = nc.dram_tensor("k_b", [512], F32, kind=EI).ap()
    v_w = nc.dram_tensor("v_w", [D, 512], F32, kind=EI).ap()
    v_b = nc.dram_tensor("v_b", [512], F32, kind=EI).ap()
    o_w = nc.dram_tensor("o_w", [D, D], F32, kind=EI).ap()
    o_b = nc.dram_tensor("o_b", [D], F32, kind=EI).ap()
    router_w = nc.dram_tensor("router_w", [D, 8], F32, kind=EI).ap()
    router_b = nc.dram_tensor("router_b", [8], F32, kind=EI).ap()
    w1 = nc.dram_tensor("w1", [8, D, 2048], F32, kind=EI).ap()
    b1 = nc.dram_tensor("b1", [8, 2048], F32, kind=EI).ap()
    w2 = nc.dram_tensor("w2", [8, 2048, D], F32, kind=EI).ap()
    b2 = nc.dram_tensor("b2", [8, D], F32, kind=EI).ap()
    WkT = nc.dram_tensor("WkT", [2, D, D], F32, kind=EI).ap()
    WvT = nc.dram_tensor("WvT", [2, D, D], F32, kind=EI).ap()
    WrT = nc.dram_tensor("WrT", [2, D, D], F32, kind=EI).ap()
    WoT = nc.dram_tensor("WoT", [2, D, D], F32, kind=EI).ap()
    lnw = nc.dram_tensor("lnw", [2, D], F32, kind=EI).ap()
    lnb = nc.dram_tensor("lnb", [2, D], F32, kind=EI).ap()
    wgT = nc.dram_tensor("wgT", [D, 3072], I32, kind=EI).ap()
    wuT = nc.dram_tensor("wuT", [D, 3072], I32, kind=EI).ap()
    wdT = nc.dram_tensor("wdT", [3072, D], I32, kind=EI).ap()
    norm_w = nc.dram_tensor("norm_w", [D], F32, kind=EI).ap()
    Wq = nc.dram_tensor("Wq", [D, 1024], F32, kind=EI).ap()
    Wkv = nc.dram_tensor("Wkv", [D, 2048], F32, kind=EI).ap()
    Wout = nc.dram_tensor("Wout", [1024, D], F32, kind=EI).ap()
    Wg = nc.dram_tensor("Wg", [D, 16], F32, kind=EI).ap()
    w_q3 = nc.dram_tensor("w_q3", [D, 512], F32, kind=EI).ap()
    w_k3 = nc.dram_tensor("w_k3", [D, 512], F32, kind=EI).ap()
    w_gate3 = nc.dram_tensor("w_gate3", [D, 64], F32, kind=EI).ap()
    vexp = nc.dram_tensor("vexp", [8, 8, D, 64], F32, kind=EI).ap()
    oexp = nc.dram_tensor("oexp", [8, 8, 64, D], F32, kind=EI).ap()
    dmask = nc.dram_tensor("dmask", [128, 4 * 512], F32, kind=EI).ap()
    xoutT = nc.dram_tensor("xoutT", [D, T], F32, kind="ExternalOutput").ap()

    p = Prog(nc)
    x1T = p.dram([D, T], F32).ap()
    x2T = p.dram([D, T], F32).ap()
    x3T = p.dram([D, T], F32).ap()
    xnT_f32 = p.dram([D, T], F32).ap()
    xnT_bf16 = p.dram([D, T], BF16).ap()
    sT = p.dram([D, T], F32).ap()
    gT = p.dram([D, T], F32).ap()
    emit_prelude(p, xT, xnT_f32, xnT_bf16, T)
    emit_l0_attn(p, xnT_f32, xnT_bf16, gamma_h, wq_h, wkv_h, wo_h, ln_g, ln_b,
                 wqkv_l, wo_l, rts, bandmask, sT, T)
    emit_l0_ffn(p, xnT_f32, w_gate, w_up, w_down, gT, T)
    emit_glue(p, xT, sT, gT, x1T, T)
    xnT_f32 = p.dram([D, T], F32).ap()
    xnT_bf16 = p.dram([D, T], BF16).ap()
    sT = p.dram([D, T], F32).ap()
    gT = p.dram([D, T], F32).ap()
    emit_prelude(p, x1T, xnT_f32, xnT_bf16, T)
    emit_l1_attn(p, xnT_bf16, W1T, W2T, W3T, WoT2, ropeC, ropeS, sT, T)
    emit_l1_ffn(p, xnT_f32, xnT_bf16, q_w, q_b, k_w, k_b, v_w, v_b, o_w, o_b,
                router_w, router_b, w1, b1, w2, b2, gT, T)
    emit_glue(p, x1T, sT, gT, x2T, T)
    xnT_f32 = p.dram([D, T], F32).ap()
    xnT_bf16 = p.dram([D, T], BF16).ap()
    sT_scr = p.dram([D, T], F32).ap()
    u_b = p.dram([D, T], F32).ap()
    gT = p.dram([D, T], F32).ap()
    emit_prelude(p, x2T, xnT_f32, xnT_bf16, T)
    sT2 = emit_l2_attn(p, xnT_f32, sT_scr, u_b, WkT, WvT, WrT, WoT, lnw, lnb, T)
    emit_l2_ffn(p, xnT_bf16, wgT, wuT, wdT, gT, T)
    emit_glue(p, x2T, sT2, gT, x3T, T)
    xnT_f32 = p.dram([D, T], F32).ap()
    xnT_bf16 = p.dram([D, T], BF16).ap()
    sT = p.dram([D, T], F32).ap()
    gT = p.dram([D, T], F32).ap()
    emit_prelude(p, x3T, xnT_f32, xnT_bf16, T)
    emit_l3_attn(p, xnT_f32, xnT_bf16, norm_w, Wq, Wkv, Wout, Wg, dmask, sT, T)
    emit_l3_ffn(p, xnT_bf16, w_q3, w_k3, w_gate3, vexp, oexp, dmask, gT, T)
    emit_glue(p, x3T, sT, gT, xoutT, T)
    p.close()
    return nc


def kernel(**inputs):
    x = np.ascontiguousarray(np.asarray(inputs["L0_attn_x"], np.float32))
    xT_all = _to_fm(x)
    nc = build_module()
    rc, rs = make_rope_tables()
    base = {
        "gamma_h": inputs["L0_attn_gamma_h"],
        "wq_h": inputs["L0_attn_wq_h"],
        "wkv_h": inputs["L0_attn_wkv_h"],
        "wo_h": inputs["L0_attn_wo_h"],
        "ln_g": inputs["L0_attn_ln_g"],
        "ln_b": inputs["L0_attn_ln_b"],
        "wqkv_l": inputs["L0_attn_wqkv_l"],
        "wo_l": inputs["L0_attn_wo_l"],
        "rts": np.stack([inputs["L0_attn_rt_q"], inputs["L0_attn_rt_kv"]], 1),
        "bandmask": make_bandmask(),
        "w_gate": inputs["L0_ffn_w_gate"],
        "w_up": inputs["L0_ffn_w_up"],
        "w_down": inputs["L0_ffn_w_down"],
        "W1T": inputs["L1_attn_W1"].T, "W2T": inputs["L1_attn_W2"].T,
        "W3T": inputs["L1_attn_W3"].T, "WoT2": inputs["L1_attn_Wo"].T,
        "ropeC": rc, "ropeS": rs,
        "q_w": inputs["L1_ffn_q_w"], "q_b": inputs["L1_ffn_q_b"],
        "k_w": inputs["L1_ffn_k_w"], "k_b": inputs["L1_ffn_k_b"],
        "v_w": inputs["L1_ffn_v_w"], "v_b": inputs["L1_ffn_v_b"],
        "o_w": inputs["L1_ffn_o_w"], "o_b": inputs["L1_ffn_o_b"],
        "router_w": inputs["L1_ffn_router_w"],
        "router_b": inputs["L1_ffn_router_b"],
        "w1": inputs["L1_ffn_w1"], "b1": inputs["L1_ffn_b1"],
        "w2": inputs["L1_ffn_w2"], "b2": inputs["L1_ffn_b2"],
        "WkT": np.transpose(inputs["L2_attn_Wk"], (0, 2, 1)),
        "WvT": np.transpose(inputs["L2_attn_Wv"], (0, 2, 1)),
        "WrT": np.transpose(inputs["L2_attn_Wr"], (0, 2, 1)),
        "WoT": np.transpose(inputs["L2_attn_Wo"], (0, 2, 1)),
        "lnw": inputs["L2_attn_ln_w"],
        "lnb": inputs["L2_attn_ln_b"],
        "wgT": inputs["L2_ffn_w_gate"].T,
        "wuT": inputs["L2_ffn_w_up"].T,
        "wdT": inputs["L2_ffn_w_down"].T,
        "norm_w": inputs["L3_attn_norm_w"],
        "Wq": inputs["L3_attn_Wq"],
        "Wkv": inputs["L3_attn_Wkv"],
        "Wout": inputs["L3_attn_Wout"],
        "Wg": inputs["L3_attn_Wg"],
        "w_q3": inputs["L3_ffn_w_q"],
        "w_k3": inputs["L3_ffn_w_k"],
        "w_gate3": inputs["L3_ffn_w_gate"],
        "vexp": inputs["L3_ffn_value_experts"],
        "oexp": inputs["L3_ffn_output_experts"],
        "dmask": make_dmaskadd(),
    }
    base = {k: np.ascontiguousarray(v) for k, v in base.items()}
    in_maps = []
    for c in range(8):
        m = dict(base)
        m["xT"] = xT_all[c % 4]
        in_maps.append(m)
    res = run_bass_kernel_spmd(nc, in_maps, core_ids=list(range(8)))
    out = np.stack(
        [np.asarray(res.results[b]["xoutT"], np.float32).T for b in range(4)],
        axis=0)
    return np.ascontiguousarray(out)
